# version 01
# baseline (speedup 1.0000x reference)
import numpy as np
import ml_dtypes
import concourse.bass as bass
import concourse.mybir as mybir
from concourse.bass_utils import run_bass_kernel_spmd
from contextlib import ExitStack

F32 = mybir.dt.float32
BF16 = mybir.dt.bfloat16
AF = mybir.ActivationFunctionType
ALU = mybir.AluOpType
AXX = mybir.AxisListType.X

BPC = 2
S = 2048
DM = 2048
NDT = DM // 128
NH = 16
HD = 128
H0 = 8192
NHT = H0 // 128
CK = 128
NCK = S // CK
INV_SQRT_HD = float(1.0 / np.sqrt(128.0))
LN_EPS = 1e-5
DSC = float(128.0 ** -0.5)
BF = ml_dtypes.bfloat16


def hada(n):
    H = np.array([[1.0]], dtype=np.float32)
    while H.shape[0] < n:
        H = np.block([[H, H], [H, -H]])
    return H


def relay_lhsT(w, din_tiles, dout_block):
    din, dout = w.shape
    nb = dout // dout_block
    assert din == din_tiles * 128
    r = w.reshape(din_tiles, 128, nb, dout_block).transpose(2, 1, 0, 3)
    return np.ascontiguousarray(r.reshape(nb, 128, din_tiles * dout_block))


class Prog:
    def __init__(self, upto=99, dbg=None):
        self.upto = upto
        self.dbg = dbg
        nc = bass.Bass("TRN2", target_bir_lowering=False, debug=False)
        self.nc = nc
        self.cnt = 0
        self.ops = {e: [] for e in ("tensor", "vector", "scalar", "gpsimd", "sync")}
        self.st = ExitStack()
        self._declare_dram()

    def _emit(self, eng, fn, inc=1):
        at = self.cnt
        sem = self.chain
        def run(h, at=at, fn=fn, inc=inc, sem=sem):
            h.wait_ge(sem, at)
            r = fn(h)
            r.then_inc(sem, inc)
        self.ops[eng].append(run)
        self.cnt += inc

    def t(self, fn): self._emit("tensor", fn)
    def v(self, fn): self._emit("vector", fn)
    def s(self, fn): self._emit("scalar", fn)
    def g(self, fn): self._emit("gpsimd", fn)
    def dma(self, out, in_):
        self._emit("sync", lambda h, o=out, i=in_: h.dma_start(out=o, in_=i), inc=16)
    def gdma(self, out, in_):
        self._emit("gpsimd", lambda h, o=out, i=in_: h.dma_start(out=o, in_=i), inc=16)

    def mm(self, out, lhsT, rhs, start=True, stop=True):
        self.t(lambda t, o=out, l=lhsT, r=rhs, st=start, sp=stop:
               t.matmul(o, l, r, start=st, stop=sp))
    def tr(self, out, in_, f32=False):
        k = in_.shape[0]
        ident = (lambda kk: self.ident_f[:kk, :kk]) if f32 else (lambda kk: self.ident_bf[:kk, :kk])
        self.t(lambda t, o=out, i=in_, g=ident, kk=k: t.transpose(o, i, g(kk)))
    def act(self, out, in_, func, bias=0.0, scale=1.0, accum_out=None):
        self.s(lambda sc, o=out, i=in_, f=func, b=bias, s_=scale, a=accum_out:
               sc.activation(o, i, f, bias=b, scale=s_, accum_out=a))
    def scopy(self, out, in_):
        self.s(lambda sc, o=out, i=in_: sc.copy(o, i))
    def vcopy(self, out, in_):
        self.v(lambda v, o=out, i=in_: v.tensor_copy(o, i))
    def tt(self, out, in0, in1, op):
        self.v(lambda v, o=out, a=in0, b=in1, p=op: v.tensor_tensor(o, a, b, p))
    def ts(self, out, in0, s1, s2=None, op0=ALU.mult, op1=ALU.bypass, accum_out=None):
        self.v(lambda v, o=out, i=in0, a=s1, b=s2, p0=op0, p1=op1, ac=accum_out:
               v.tensor_scalar(o, i, a, b, p0, p1, accum_out=ac))
    def stt(self, out, in0, s1, in1, op0, op1):
        self.v(lambda v, o=out, i=in0, a=s1, j=in1, p0=op0, p1=op1:
               v.scalar_tensor_tensor(o, i, a, j, p0, p1))
    def recip(self, out, in_):
        self.v(lambda v, o=out, i=in_: v.reciprocal(o, i))
    def rsum(self, out, in_):
        self.v(lambda v, o=out, i=in_: v.reduce_sum(o, i, AXX))
    def memset(self, ap, val):
        self.g(lambda g, a=ap, c=val: g.memset(a, c))
    def asel(self, out, in_, pattern, cmp, fill, base=0, chmul=1):
        self.g(lambda g, o=out, i=in_, p=pattern, c=cmp, f=fill, b=base, m=chmul:
               g.affine_select(o, i, p, c, f, base=b, channel_multiplier=m))

    def _declare_dram(self):
        nc = self.nc
        self.inp = {}
        def I(name, shape, dt=F32):
            self.inp[name] = nc.dram_tensor(name, list(shape), dt, kind="ExternalInput").ap()
            return self.inp[name]
        I("xin", (BPC, DM, S))
        I("wq0", (4, 128, NDT * 512)); I("wk0", (4, 128, NDT * 512)); I("wv0", (4, 128, NDT * 512))
        I("qb0", (128, 16)); I("kb0", (128, 16)); I("vb0_row", (1, DM))
        I("wo0", (16, 128, NDT * 128)); I("ob0", (128, 16))
        I("alibi_rep", (1, 256))
        I("w1t", (NHT, 128, DM)); I("w3t", (NHT, 128, DM)); I("w2t", (H0, DM))
        I("h128", (128, 128), BF16); I("h64", (64, 64), BF16); I("identity", (128, 128), BF16)
        I("identity_f", (128, 128), F32)
        I("wq1", (16, 128, NDT * 128)); I("wk1", (16, 128, NDT * 128))
        I("wv1", (16, 128, NDT * 128)); I("wg1", (DM, DM))
        I("wa1", (128, 256)); I("wb1", (128, 256)); I("ba1", (16, 1)); I("bb1", (16, 1))
        I("wo1", (16, 128, NDT * 128))
        I("qcw", (128, 64)); I("kcw", (128, 64)); I("vcw", (128, 64))
        I("qcb", (128, 16)); I("kcb", (128, 16)); I("vcb", (128, 16))
        I("lnw_rep", (1, DM)); I("lnb_rep", (1, DM))
        I("wgate", (DM, H0)); I("wup", (DM, H0)); I("wdown", (H0, DM))

        D = lambda name, shape, dt: nc.dram_tensor(name, list(shape), dt, kind="Internal").ap()
        self.w2eff = D("w2eff", (H0, DM), BF16)
        self.bh = D("bh", (NHT, 128, DM), BF16)
        self.aoh = D("aoh", (DM, S), BF16)
        self.x1f = D("x1f", (BPC, DM, S), F32)
        self.x1b = D("x1b", (DM, BPC * S), BF16)
        self.hid0 = D("hid0", (NHT, 128, BPC * S), BF16)
        self.part0 = D("part0", (NDT, 128, BPC * S), F32)
        self.x2f = D("x2f", (BPC, DM, S), F32)
        self.x2b = D("x2b", (DM, BPC * S), BF16)
        self.xwq = D("xwq", (DM, BPC * S), BF16)
        self.xwk = D("xwk", (DM, BPC * S), BF16)
        self.xwv = D("xwv", (DM, BPC * S), BF16)
        self.xwg_tm = D("xwg_tm", (BPC * S, DM), BF16)
        self.x3f = D("x3f", (BPC, DM, S), F32)
        self.x3b = D("x3b", (DM, BPC * S), BF16)
        self.hid1 = D("hid1", (NHT, 128, BPC * S), BF16)
        self.part1 = D("part1", (NDT, 128, BPC * S), F32)
        self.sigh = D("sigh", (16, 2 * BPC * S), F32)
        self.rowtmp = D("rowtmp", (16, CK), F32)
        self.rowtmp2 = D("rowtmp2", (16, CK), F32)

        self.out = nc.dram_tensor("out", [BPC, DM, S], F32, kind="ExternalOutput").ap()
        if self.dbg is not None:
            shp = {"x1f": (BPC, DM, S), "x2f": (BPC, DM, S), "x3f": (BPC, DM, S),
                   "w2eff": (H0, DM), "aoh": (DM, S), "x1b": (DM, BPC * S),
                   "hid0": (NHT, 128, BPC * S), "hid1": (NHT, 128, BPC * S),
                   "xwq": (DM, BPC * S), "xwg_tm": (BPC * S, DM),
                   "part0": (NDT, 128, BPC * S)}[self.dbg]
            dt = BF16 if self.dbg in ("w2eff", "aoh", "x1b", "hid0", "hid1", "xwq", "xwg_tm") else F32
            self.dbg_out = nc.dram_tensor("dbg", list(shp), dt, kind="ExternalOutput").ap()

    def alloc_consts(self):
        E = self.st.enter_context
        nc = self.nc
        self.chain = E(nc.semaphore("chain"))
        self.ident_bf = E(nc.sbuf_tensor("ident_bf", [128, 128], BF16))
        self.ident_f = E(nc.sbuf_tensor("ident_f", [128, 128], F32))
        self.h128_sb = E(nc.sbuf_tensor("h128_sb", [128, 128], BF16))
        self.h64_sb = E(nc.sbuf_tensor("h64_sb", [64, 64], BF16))
        self.ones_col_bf = E(nc.sbuf_tensor("ones_col_bf", [128, 1], BF16))
        self.ones_col_f = E(nc.sbuf_tensor("ones_col_f", [128, 1], F32))
        self.ones_row_f = E(nc.sbuf_tensor("ones_row_f", [1, 128], F32))
        self.iota_row = E(nc.sbuf_tensor("iota_row", [1, 128], F32))
        self.negoff_row = E(nc.sbuf_tensor("negoff_row", [1, 256], F32))
        self.slope_rep = E(nc.sbuf_tensor("slope_rep", [1, 256], F32))
        self.r2_row = E(nc.sbuf_tensor("r2_row", [1, 256], F32))
        self.bias_tab = E(nc.sbuf_tensor("bias_tab", [128, 256], F32))
        self.qb_sb = E(nc.sbuf_tensor("qb_sb", [128, 16], F32))
        self.kb_sb = E(nc.sbuf_tensor("kb_sb", [128, 16], F32))
        self.ob_sb = E(nc.sbuf_tensor("ob_sb", [128, 16], F32))
        self.ps = [E(nc.psum_tensor(f"ps{i}", [128, 512], F32)) for i in range(4)]
        self.ps_bf = E(nc.psum_tensor("ps_bf", [128, 512], BF16))
        self.ps_row = E(nc.psum_tensor("ps_row", [1, 512], F32))
        self.epsc = E(nc.sbuf_tensor("epsc", [128, 1], F32))
        nc.const_aps.register(F32, LN_EPS, self.epsc[:, :])
        self.ps_row2 = E(nc.psum_tensor("ps_row2", [1, 512], F32))

    def phase_consts(self):
        p = self
        p.dma(p.ident_bf[:, :], p.inp["identity"][:, :])
        p.dma(p.ident_f[:, :], p.inp["identity_f"][:, :])
        p.dma(p.h128_sb[:, :], p.inp["h128"][:, :])
        p.dma(p.h64_sb[:, :], p.inp["h64"][:, :])
        p.dma(p.slope_rep[:, :], p.inp["alibi_rep"][:, :])
        p.dma(p.qb_sb[:, :], p.inp["qb0"][:, :])
        p.dma(p.kb_sb[:, :], p.inp["kb0"][:, :])
        p.dma(p.ob_sb[:, :], p.inp["ob0"][:, :])
        p.memset(p.epsc[:, :], LN_EPS)
        p.memset(p.ones_col_bf[:, :], 1.0)
        p.memset(p.ones_col_f[:, :], 1.0)
        p.memset(p.ones_row_f[:, :], 1.0)
        p.g(lambda g: g.iota(p.iota_row[:, :], [[1, 128]], base=0, channel_multiplier=0,
                             allow_small_or_imprecise_dtypes=True))
        p.g(lambda g: g.iota(p.negoff_row[:, :], [[0, 16], [-128, 16]], base=-64,
                             channel_multiplier=0, allow_small_or_imprecise_dtypes=True))
        p.tt(p.r2_row[:, :], p.slope_rep[:, :], p.negoff_row[:, :], ALU.mult)
        p.mm(p.ps[0][:, :256], p.iota_row[:, :], p.slope_rep[:, :], start=True, stop=False)
        p.mm(p.ps[0][:, :256], p.ones_row_f[:, :], p.r2_row[:, :], start=False, stop=True)
        p.scopy(p.bias_tab[:, :], p.ps[0][:, :256])

    def bcast_row_2048(self, dst, srcname, row_tmp):
        p = self
        p.dma(row_tmp[:, :], p.inp[srcname][:, :])
        for n in range(4):
            p.mm(p.ps[1][:, :], p.ones_row_f[:, :], row_tmp[:, n * 512:(n + 1) * 512])
            p.scopy(dst[:, n * 512:(n + 1) * 512], p.ps[1][:, :])

    def phase_w2eff(self):
        p = self
        E = self.st
        with (self.nc.sbuf_tensor("w2tile", [128, DM], BF16) as w2tile,
              self.nc.sbuf_tensor("bh_sb", [128, DM], BF16) as bh_sb,
              self.nc.sbuf_tensor("bsteb", [64, 4 * DM], BF16) as bsteb,
              self.nc.sbuf_tensor("w2e_sb", [64, 4 * DM], BF16) as w2e_sb):
            for hi in range(NHT):
                p.gdma(w2tile[:, :], p.inp["w2t"][hi * 128:(hi + 1) * 128, :])
                for n in range(4):
                    p.mm(p.ps[0][:, :], p.h128_sb[:, :], w2tile[:, n * 512:(n + 1) * 512])
                    p.scopy(bh_sb[:, n * 512:(n + 1) * 512], p.ps[0][:, :])
                p.dma(p.bh[hi, :, :], bh_sb[:, :])
            bh4 = p.bh.rearrange("a p d -> a (p d)")
            w2r = p.w2eff.rearrange("(a b) d -> a b d", b=128)
            for lo0 in range(0, 128, 4):
                p.dma(bsteb[:, :], bh4[:, lo0 * DM:(lo0 + 4) * DM])
                for j in range(4):
                    for n in range(4):
                        sl = slice(j * DM + n * 512, j * DM + (n + 1) * 512)
                        p.mm(p.ps[0][:64, :], p.h64_sb[:, :], bsteb[:, sl])
                        p.act(w2e_sb[:64, sl], p.ps[0][:64, :], AF.Identity,
                              scale=float(1.0 / np.sqrt(8192.0)))
                    p.dma(w2r[:, lo0 + j, :], w2e_sb[:64, j * DM:(j + 1) * DM])

    def phase_l0_attn(self, b):
        p = self
        nc = self.nc
        with (nc.sbuf_tensor(f"xbf{b}", [128, NDT * S], BF16) as xbf,
              nc.sbuf_tensor(f"wbuf{b}", [128, NDT * 512], BF16) as wbuf,
              nc.sbuf_tensor(f"qg{b}", [128, 4 * S], BF16) as qg,
              nc.sbuf_tensor(f"kg{b}", [128, 4 * S], BF16) as kg,
              nc.sbuf_tensor(f"vtm{b}", [128, 16 * 512], BF16) as vtm,
              nc.sbuf_tensor(f"probs{b}", [128, 16 * 128], BF16) as probs,
              nc.sbuf_tensor(f"rec_row{b}", [1, 128], F32) as rec_row,
              nc.sbuf_tensor(f"recb{b}", [128, 128], F32) as recb,
              nc.sbuf_tensor(f"ao_sb{b}", [128, 512], BF16) as ao_sb,
              nc.sbuf_tensor(f"bv_bc{b}", [128, DM], F32) as bv_bc,
              nc.sbuf_tensor(f"rowt{b}", [1, DM], F32) as rowt):
            self.bcast_row_2048(bv_bc, "vb0_row", rowt)
            for dt in range(NDT):
                p.gdma(xbf[:, dt * S:(dt + 1) * S], p.inp["xin"][b, dt * 128:(dt + 1) * 128, :])
            for g in range(4):
                p.gdma(wbuf[:, :], p.inp["wq0"][g, :, :])
                for hl in range(4):
                    for n in range(4):
                        for dt in range(NDT):
                            p.mm(p.ps[0][:, :], wbuf[:, dt * 512 + hl * 128:dt * 512 + (hl + 1) * 128],
                                 xbf[:, dt * S + n * 512:dt * S + (n + 1) * 512],
                                 start=(dt == 0), stop=(dt == NDT - 1))
                        p.act(qg[:, hl * S + n * 512:hl * S + (n + 1) * 512], p.ps[0][:, :],
                              AF.Identity, bias=p.qb_sb[:, g * 4 + hl:g * 4 + hl + 1])
                p.gdma(wbuf[:, :], p.inp["wk0"][g, :, :])
                for hl in range(4):
                    for n in range(4):
                        for dt in range(NDT):
                            p.mm(p.ps[0][:, :], wbuf[:, dt * 512 + hl * 128:dt * 512 + (hl + 1) * 128],
                                 xbf[:, dt * S + n * 512:dt * S + (n + 1) * 512],
                                 start=(dt == 0), stop=(dt == NDT - 1))
                        p.act(kg[:, hl * S + n * 512:hl * S + (n + 1) * 512], p.ps[0][:, :],
                              AF.Identity, bias=p.kb_sb[:, g * 4 + hl:g * 4 + hl + 1])
                p.gdma(wbuf[:, :], p.inp["wv0"][g, :, :])
                for tt in range(16):
                    for dt in range(NDT):
                        p.mm(p.ps[0][:, :], xbf[:, dt * S + tt * 128:dt * S + (tt + 1) * 128],
                             wbuf[:, dt * 512:(dt + 1) * 512],
                             start=(dt == 0), stop=(dt == NDT - 1))
                    p.stt(vtm[:, tt * 512:(tt + 1) * 512], p.ps[0][:, :], 1.0,
                          bv_bc[:, g * 512:(g + 1) * 512], ALU.mult, ALU.add)
                for hl in range(4):
                    h = g * 4 + hl
                    for qc in range(16):
                        for kt in range(qc + 1):
                            p.mm(p.ps[0][:, :128], kg[:, hl * S + kt * 128:hl * S + (kt + 1) * 128],
                                 qg[:, hl * S + qc * 128:hl * S + (qc + 1) * 128])
                            p.act(probs[:, kt * 128:(kt + 1) * 128], p.ps[0][:, :128], AF.Exp,
                                  bias=p.bias_tab[:, h * 16 + (qc - kt):h * 16 + (qc - kt) + 1],
                                  scale=INV_SQRT_HD)
                            if kt == qc:
                                p.asel(probs[:, kt * 128:(kt + 1) * 128],
                                       probs[:, kt * 128:(kt + 1) * 128],
                                       [[1, 128]], ALU.is_ge, 0.0, base=0, chmul=-1)
                            p.mm(p.ps_row[:, :128], p.ones_col_bf[:, :],
                                 probs[:, kt * 128:(kt + 1) * 128],
                                 start=(kt == 0), stop=(kt == qc))
                            p.mm(p.ps[1][:, :128], vtm[:, kt * 512 + hl * 128:kt * 512 + (hl + 1) * 128],
                                 probs[:, kt * 128:(kt + 1) * 128],
                                 start=(kt == 0), stop=(kt == qc))
                        p.recip(rec_row[:, :], p.ps_row[:, :128])
                        p.mm(p.ps[2][:, :128], p.ones_row_f[:, :], rec_row[:, :])
                        p.scopy(recb[:, :], p.ps[2][:, :128])
                        p.tt(ao_sb[:, (qc % 4) * 128:(qc % 4 + 1) * 128], p.ps[1][:, :128],
                             recb[:, :], ALU.mult)
                        if qc % 4 == 3:
                            p.dma(p.aoh[h * 128:(h + 1) * 128, (qc - 3) * 128:(qc + 1) * 128],
                                  ao_sb[:, :])
        self._proj_epilogue(wname="wo0", act_hbm=p.aoh, act_cols=S, b=b,
                            res_in=p.inp["xin"], outf=p.x1f, outb=p.x1b,
                            bias_sb=p.ob_sb, tag=f"l0a{b}")

    def _proj_epilogue(self, wname, act_hbm, act_cols, b, res_in, outf, outb, bias_sb,
                       part_in=None, tokchunk=512, tag=""):
        p = self
        nc = self.nc
        nchunk = S // tokchunk
        with (nc.sbuf_tensor(f"wo_full{tag}", [128, 16 * NDT * 128], BF16) as wo_full,
              nc.sbuf_tensor(f"aobuf{tag}", [128, NDT * tokchunk], BF16) as aobuf,
              nc.sbuf_tensor(f"x1pre{tag}", [128, NDT * tokchunk], F32) as x1pre,
              nc.sbuf_tensor(f"xres{tag}", [128, tokchunk], F32) as xres,
              nc.sbuf_tensor(f"sqsb{tag}", [128, tokchunk], F32) as sqsb,
              nc.sbuf_tensor(f"rows{tag}", [1, 6 * tokchunk], F32) as rows,
              nc.sbuf_tensor(f"ab_sb{tag}", [128, 2 * tokchunk], F32) as ab_sb,
              nc.sbuf_tensor(f"xo_sb{tag}", [128, tokchunk], F32) as xo_sb,
              nc.sbuf_tensor(f"xb_sb{tag}", [128, tokchunk], BF16) as xb_sb):
            for ft in range(16):
                p.gdma(wo_full[:, ft * NDT * 128:(ft + 1) * NDT * 128], p.inp[wname][ft, :, :])
            TC = tokchunk
            for n in range(nchunk):
                for at in range(NDT):
                    p.dma(aobuf[:, at * TC:(at + 1) * TC],
                          act_hbm[at * 128:(at + 1) * 128, n * TC:(n + 1) * TC])
                for ft in range(NDT):
                    for at in range(NDT):
                        p.mm(p.ps[0][:, :TC],
                             wo_full[:, ft * NDT * 128 + at * 128:ft * NDT * 128 + (at + 1) * 128],
                             aobuf[:, at * TC:(at + 1) * TC],
                             start=(at == 0), stop=(at == NDT - 1))
                    p.dma(xres[:, :], res_in[b, ft * 128:(ft + 1) * 128, n * TC:(n + 1) * TC])
                    p.act(x1pre[:, ft * TC:(ft + 1) * TC], p.ps[0][:, :TC], AF.Identity,
                          bias=bias_sb[:, ft:ft + 1])
                    p.tt(x1pre[:, ft * TC:(ft + 1) * TC], x1pre[:, ft * TC:(ft + 1) * TC],
                         xres[:, :], ALU.add)
                    p.mm(p.ps_row[:, :TC], p.ones_col_f[:, :], x1pre[:, ft * TC:(ft + 1) * TC],
                         start=(ft == 0), stop=(ft == NDT - 1))
                    p.act(sqsb[:, :], x1pre[:, ft * TC:(ft + 1) * TC], AF.Square)
                    p.mm(p.ps_row2[:, :TC], p.ones_col_f[:, :], sqsb[:, :],
                         start=(ft == 0), stop=(ft == NDT - 1))
                MU, E2, VAR, STD, RA, RB = (rows[:, i * TC:(i + 1) * TC] for i in range(6))
                p.ts(MU, p.ps_row[:, :TC], 1.0 / DM, None, ALU.mult)
                p.ts(E2, p.ps_row2[:, :TC], 1.0 / DM, None, ALU.mult)
                p.tt(VAR, MU, MU, ALU.mult)
                p.tt(VAR, E2, VAR, ALU.subtract)
                p.act(STD, VAR, AF.Sqrt, bias=LN_EPS)
                p.recip(RA, STD)
                p.tt(RB, MU, RA, ALU.mult)
                p.ts(RB, RB, -1.0, None, ALU.mult)
                p.mm(p.ps[1][:, :TC], p.ones_row_f[:, :], RA)
                p.scopy(ab_sb[:, :TC], p.ps[1][:, :TC])
                p.mm(p.ps[1][:, :TC], p.ones_row_f[:, :], RB)
                p.scopy(ab_sb[:, TC:2 * TC], p.ps[1][:, :TC])
                for ft in range(NDT):
                    p.tt(xo_sb[:, :], x1pre[:, ft * TC:(ft + 1) * TC], ab_sb[:, :TC], ALU.mult)
                    p.tt(xo_sb[:, :], xo_sb[:, :], ab_sb[:, TC:2 * TC], ALU.add)
                    p.dma(outf[b, ft * 128:(ft + 1) * 128, n * TC:(n + 1) * TC], xo_sb[:, :])
                    p.scopy(xb_sb[:, :], xo_sb[:, :])
                    p.dma(outb[ft * 128:(ft + 1) * 128, b * S + n * TC:b * S + (n + 1) * TC],
                          xb_sb[:, :])

    def phase_l0_ffn(self):
        p = self
        nc = self.nc
        T = BPC * S
        with (nc.sbuf_tensor("ffa_x", [128, NDT * T], BF16) as xa,
              nc.sbuf_tensor("ffa_w1", [128, DM], BF16) as w1b,
              nc.sbuf_tensor("ffa_w3", [128, DM], BF16) as w3b,
              nc.sbuf_tensor("ffa_hid", [128, T], BF16) as hidsb,
              nc.sbuf_tensor("ffa_silu", [128, 512], BF16) as silu_sb):
            for dt in range(NDT):
                p.dma(xa[:, dt * T:(dt + 1) * T], p.x1b[dt * 128:(dt + 1) * 128, :])
            for ht in range(NHT):
                p.gdma(w1b[:, :], p.inp["w1t"][ht, :, :])
                p.gdma(w3b[:, :], p.inp["w3t"][ht, :, :])
                for n in range(T // 512):
                    for dt in range(NDT):
                        p.mm(p.ps[0][:, :], w1b[:, dt * 128:(dt + 1) * 128],
                             xa[:, dt * T + n * 512:dt * T + (n + 1) * 512],
                             start=(dt == 0), stop=(dt == NDT - 1))
                    p.act(silu_sb[:, :], p.ps[0][:, :], AF.Silu)
                    for dt in range(NDT):
                        p.mm(p.ps[1][:, :], w3b[:, dt * 128:(dt + 1) * 128],
                             xa[:, dt * T + n * 512:dt * T + (n + 1) * 512],
                             start=(dt == 0), stop=(dt == NDT - 1))
                    p.tt(hidsb[:, n * 512:(n + 1) * 512], p.ps[1][:, :], silu_sb[:, :], ALU.mult)
                p.dma(p.hid0[ht, :, :], hidsb[:, :])
        self._down_proj(p.hid0, p.w2eff, p.part0, p.x1f, p.x2f, p.x2b, tag="l0f")

    def _down_proj(self, hid, wsrc, part, resf, outf, outb, tag, wcast=False):
        p = self
        nc = self.nc
        T = BPC * S
        TC = 256
        for half in range(2):
            with ExitStack() as stk:
                E = stk.enter_context
                wsb = E(nc.sbuf_tensor(f"dp_w{tag}{half}", [128, 32 * DM], BF16))
                hch = E(nc.sbuf_tensor(f"dp_h{tag}{half}", [128, 32 * TC], BF16))
                aux = E(nc.sbuf_tensor(f"dp_aux{tag}{half}", [128, TC], F32))
                if half == 1:
                    pre = E(nc.sbuf_tensor(f"dp_pre{tag}{half}", [128, NDT * TC], F32))
                    sq_sb = E(nc.sbuf_tensor(f"dp_sq{tag}{half}", [128, TC], F32))
                    rows = E(nc.sbuf_tensor(f"dp_rows{tag}{half}", [1, 6 * TC], F32))
                    ab_sb = E(nc.sbuf_tensor(f"dp_ab{tag}{half}", [128, 2 * TC], F32))
                    xo_sb = E(nc.sbuf_tensor(f"dp_xo{tag}{half}", [128, TC], F32))
                    xb_sb = E(nc.sbuf_tensor(f"dp_xb{tag}{half}", [128, TC], BF16))
                for i in range(32):
                    ht = half * 32 + i
                    if wcast:
                        p.gdma(wsb[:, i * DM:(i + 1) * DM], wsrc[ht * 128:(ht + 1) * 128, :])
                    else:
                        p.dma(wsb[:, i * DM:(i + 1) * DM], wsrc[ht * 128:(ht + 1) * 128, :])
                for c in range(T // TC):
                    b = (c * TC) // S
                    nin = (c * TC) % S
                    for i in range(32):
                        p.dma(hch[:, i * TC:(i + 1) * TC],
                              hid[half * 32 + i, :, c * TC:(c + 1) * TC])
                    for ft in range(NDT):
                        for i in range(32):
                            p.mm(p.ps[ft % 2][:, :TC],
                                 wsb[:, i * DM + ft * 128:i * DM + (ft + 1) * 128],
                                 hch[:, i * TC:(i + 1) * TC],
                                 start=(i == 0), stop=(i == 31))
                        if half == 0:
                            p.vcopy(aux[:, :], p.ps[ft % 2][:, :TC])
                            p.dma(part[ft, :, c * TC:(c + 1) * TC], aux[:, :])
                        else:
                            p.dma(aux[:, :], part[ft, :, c * TC:(c + 1) * TC])
                            p.tt(pre[:, ft * TC:(ft + 1) * TC], p.ps[ft % 2][:, :TC],
                                 aux[:, :], ALU.add)
                            p.dma(aux[:, :], resf[b, ft * 128:(ft + 1) * 128, nin:nin + TC])
                            p.tt(pre[:, ft * TC:(ft + 1) * TC], pre[:, ft * TC:(ft + 1) * TC],
                                 aux[:, :], ALU.add)
                    if half == 1:
                        self._ln_store(pre, TC, sq_sb, rows, ab_sb, xo_sb, xb_sb,
                                       outf, outb, b, nin)

    def _ln_store(self, pre, TC, sq_sb, rows, ab_sb, xo_sb, xb_sb, outf, outb, b, nin):
        p = self
        for ft in range(NDT):
            p.mm(p.ps_row[:, :TC], p.ones_col_f[:, :], pre[:, ft * TC:(ft + 1) * TC],
                 start=(ft == 0), stop=(ft == NDT - 1))
            p.act(sq_sb[:, :], pre[:, ft * TC:(ft + 1) * TC], AF.Square)
            p.mm(p.ps_row2[:, :TC], p.ones_col_f[:, :], sq_sb[:, :],
                 start=(ft == 0), stop=(ft == NDT - 1))
        MU, E2, VAR, STD, RA, RB = (rows[:, i * TC:(i + 1) * TC] for i in range(6))
        p.ts(MU, p.ps_row[:, :TC], 1.0 / DM, None, ALU.mult)
        p.ts(E2, p.ps_row2[:, :TC], 1.0 / DM, None, ALU.mult)
        p.tt(VAR, MU, MU, ALU.mult)
        p.tt(VAR, E2, VAR, ALU.subtract)
        p.act(STD, VAR, AF.Sqrt, bias=LN_EPS)
        p.recip(RA, STD)
        p.tt(RB, MU, RA, ALU.mult)
        p.ts(RB, RB, -1.0, None, ALU.mult)
        p.mm(p.ps[2][:, :TC], p.ones_row_f[:, :], RA)
        p.scopy(ab_sb[:, :TC], p.ps[2][:, :TC])
        p.mm(p.ps[2][:, :TC], p.ones_row_f[:, :], RB)
        p.scopy(ab_sb[:, TC:2 * TC], p.ps[2][:, :TC])
        for ft in range(NDT):
            p.tt(xo_sb[:, :], pre[:, ft * TC:(ft + 1) * TC], ab_sb[:, :TC], ALU.mult)
            p.tt(xo_sb[:, :], xo_sb[:, :], ab_sb[:, TC:2 * TC], ALU.add)
            p.dma(outf[b, ft * 128:(ft + 1) * 128, nin:nin + TC], xo_sb[:, :])
            if outb is not None:
                p.scopy(xb_sb[:, :], xo_sb[:, :])
                p.dma(outb[ft * 128:(ft + 1) * 128, b * S + nin:b * S + nin + TC], xb_sb[:, :])

    def phase_l1_proj(self):
        p = self
        nc = self.nc
        T = BPC * S
        with (nc.sbuf_tensor("p3_x", [128, NDT * T], BF16) as xa,
              nc.sbuf_tensor("p3_w", [128, NDT * 512], BF16) as wb,
              nc.sbuf_tensor("p3_o", [128, 512], BF16) as ob,
              nc.sbuf_tensor("p3_ab", [128, 2 * 256], BF16) as wab,
              nc.sbuf_tensor("p3_sig", [16, 2 * T], F32) as sig,
              nc.sbuf_tensor("p3_bab", [16, 2], F32) as bab):
            for dt in range(NDT):
                p.dma(xa[:, dt * T:(dt + 1) * T], p.x2b[dt * 128:(dt + 1) * 128, :])
            for wname, dst in (("wq1", p.xwq), ("wk1", p.xwk), ("wv1", p.xwv)):
                for dot in range(16):
                    p.gdma(wb[:, :NDT * 128], p.inp[wname][dot, :, :])
                    for n in range(T // 512):
                        for dt in range(NDT):
                            p.mm(p.ps[0][:, :], wb[:, dt * 128:(dt + 1) * 128],
                                 xa[:, dt * T + n * 512:dt * T + (n + 1) * 512],
                                 start=(dt == 0), stop=(dt == NDT - 1))
                        p.scopy(ob[:, :], p.ps[0][:, :])
                        p.dma(dst[dot * 128:(dot + 1) * 128, n * 512:(n + 1) * 512], ob[:, :])
            for fc in range(4):
                for dt in range(NDT):
                    p.gdma(wb[:, dt * 512:(dt + 1) * 512],
                           p.inp["wg1"][dt * 128:(dt + 1) * 128, fc * 512:(fc + 1) * 512])
                for tt in range(T // 128):
                    for dt in range(NDT):
                        p.mm(p.ps[0][:, :], xa[:, dt * T + tt * 128:dt * T + (tt + 1) * 128],
                             wb[:, dt * 512:(dt + 1) * 512],
                             start=(dt == 0), stop=(dt == NDT - 1))
                    p.scopy(ob[:, :], p.ps[0][:, :])
                    p.dma(p.xwg_tm[tt * 128:(tt + 1) * 128, fc * 512:(fc + 1) * 512], ob[:, :])
            p.gdma(wab[:, :256], p.inp["wa1"][:, :])
            p.gdma(wab[:, 256:], p.inp["wb1"][:, :])
            p.dma(bab[:, 0:1], p.inp["ba1"][:, :])
            p.dma(bab[:, 1:2], p.inp["bb1"][:, :])
            for j, (coff, boff) in enumerate(((0, 0), (256, 1))):
                for n in range(T // 512):
                    for dt in range(NDT):
                        p.mm(p.ps[0][:16, :], wab[:, coff + dt * 16:coff + (dt + 1) * 16],
                             xa[:, dt * T + n * 512:dt * T + (n + 1) * 512],
                             start=(dt == 0), stop=(dt == NDT - 1))
                    p.act(sig[:, j * T + n * 512:j * T + (n + 1) * 512], p.ps[0][:16, :],
                          AF.Sigmoid, bias=bab[:, boff:boff + 1])

            p.dma(p.sigh[:, :], sig[:, :])

    def phase_l1_delta(self, b):
        p = self
        nc = self.nc
        QT = 512
        with ExitStack() as stk:
            E = stk.enter_context
            qhat = E(nc.sbuf_tensor(f"d_q{b}", [128, 16 * QT], BF16))
            khat = E(nc.sbuf_tensor(f"d_k{b}", [128, 16 * QT], BF16))
            vhat = E(nc.sbuf_tensor(f"d_v{b}", [128, 16 * QT], BF16))
            xwb = E(nc.sbuf_tensor(f"d_xw{b}", [128, 16 * (QT + 3)], BF16))
            cacc = E(nc.sbuf_tensor(f"d_cacc{b}", [128, QT], F32))
            cw = E(nc.sbuf_tensor(f"d_cw{b}", [128, 3 * 64], F32))
            cb = E(nc.sbuf_tensor(f"d_cb{b}", [128, 3 * 16], F32))
            sigq = E(nc.sbuf_tensor(f"d_sig{b}", [16, 2 * QT], F32))
            scal = E(nc.sbuf_tensor(f"d_scal{b}", [16, 8 * QT], F32))
            flat = E(nc.sbuf_tensor(f"d_flat{b}", [1, 2 * 16 * CK], F32))
            cols = E(nc.sbuf_tensor(f"d_cols{b}", [128, 6 * 16], F32))
            zf = E(nc.sbuf_tensor(f"d_zf{b}", [128, 16 * 128], F32))
            zb = E(nc.sbuf_tensor(f"d_zb{b}", [128, 16 * 128], BF16))
            ktm = E(nc.sbuf_tensor(f"d_ktm{b}", [128, 16 * 128], BF16))
            vtm = E(nc.sbuf_tensor(f"d_vtm{b}", [128, 16 * 128], BF16))
            expm = E(nc.sbuf_tensor(f"d_expm{b}", [128, 128], F32))
            mneg = E(nc.sbuf_tensor(f"d_mneg{b}", [128, 128], BF16))
            rcur = E(nc.sbuf_tensor(f"d_rcur{b}", [128, 2 * 128], BF16))
            r0sb = E(nc.sbuf_tensor(f"d_r0{b}", [128, 128], BF16))
            otm = E(nc.sbuf_tensor(f"d_otm{b}", [128, 16 * 128], F32))
            otmp = E(nc.sbuf_tensor(f"d_otmp{b}", [128, 128], F32))
            osq = E(nc.sbuf_tensor(f"d_osq{b}", [128, 16 * 128], F32))
            st16 = E(nc.sbuf_tensor(f"d_st16{b}", [128, 8 * 16], F32))
            gsb = E(nc.sbuf_tensor(f"d_gsb{b}", [128, DM], BF16))
            gss = E(nc.sbuf_tensor(f"d_gss{b}", [128, DM], BF16))
            ogbf = E(nc.sbuf_tensor(f"d_og{b}", [128, DM], BF16))
            ofm = E(nc.sbuf_tensor(f"d_ofm{b}", [128, 128], BF16))
            lnwb = E(nc.sbuf_tensor(f"d_lnw{b}", [128, DM], F32))
            lnbb = E(nc.sbuf_tensor(f"d_lnb{b}", [128, DM], F32))
            rowt = E(nc.sbuf_tensor(f"d_rowt{b}", [1, DM], F32))
            self.bcast_row_2048(lnwb, "lnw_rep", rowt)
            self.bcast_row_2048(lnbb, "lnb_rep", rowt)
            p.dma(cw[:, 0:64], p.inp["qcw"][:, :])
            p.dma(cw[:, 64:128], p.inp["kcw"][:, :])
            p.dma(cw[:, 128:192], p.inp["vcw"][:, :])
            p.dma(cb[:, 0:16], p.inp["qcb"][:, :])
            p.dma(cb[:, 16:32], p.inp["kcb"][:, :])
            p.dma(cb[:, 32:48], p.inp["vcb"][:, :])
            p.memset(zf[:, :], 0.0)
            p.memset(zb[:, :], 0.0)
            LNC2 = float(2.0 * np.log(DSC))
            for q in range(4):
                t0 = b * S + q * QT
                for ti, (src, dst, woff, boff) in enumerate(
                        ((p.xwq, qhat, 0, 0), (p.xwk, khat, 64, 16), (p.xwv, vhat, 128, 32))):
                    if q == 0:
                        p.memset(xwb[:, :].rearrange("p (d c) -> p d c", d=16)[:, :, 0:3], 0.0)
                        for dt in range(NDT):
                            p.dma(xwb[:, dt * (QT + 3) + 3:(dt + 1) * (QT + 3)],
                                  src[dt * 128:(dt + 1) * 128, t0:t0 + QT])
                    else:
                        for dt in range(NDT):
                            p.dma(xwb[:, dt * (QT + 3):(dt + 1) * (QT + 3)],
                                  src[dt * 128:(dt + 1) * 128, t0 - 3:t0 + QT])
                    for dt in range(NDT):
                        base = dt * (QT + 3)
                        p.ts(cacc[:, :], xwb[:, base:base + QT],
                             cw[:, woff + dt * 4:woff + dt * 4 + 1],
                             cb[:, boff + dt:boff + dt + 1], ALU.mult, ALU.add)
                        for j in (1, 2, 3):
                            p.stt(cacc[:, :], xwb[:, base + j:base + j + QT],
                                  cw[:, woff + dt * 4 + j:woff + dt * 4 + j + 1],
                                  cacc[:, :], ALU.mult, ALU.add)
                        p.act(dst[:, dt * QT:(dt + 1) * QT], cacc[:, :], AF.Silu)
                p.dma(sigq[:, 0:QT], p.sigh[:, t0:t0 + QT])
                p.dma(sigq[:, QT:2 * QT], p.sigh[:, BPC * S + t0:BPC * S + t0 + QT])
                A_, B_ = sigq[:, 0:QT], sigq[:, QT:2 * QT]
                LNA = scal[:, 0 * QT:1 * QT]; LL = scal[:, 1 * QT:2 * QT]
                LP = scal[:, 2 * QT:3 * QT]; LB2 = scal[:, 3 * QT:4 * QT]
                NL = scal[:, 4 * QT:5 * QT]; S1 = scal[:, 5 * QT:6 * QT]
                BC = scal[:, 6 * QT:7 * QT]; EL = scal[:, 7 * QT:8 * QT]
                p.act(LNA, A_, AF.Ln)
                p.vcopy(LL, LNA)
                cur, oth = LL, LNA
                for sft in (1, 2, 4, 8, 16, 32, 64):
                    c3_cur = cur.rearrange("p (c i) -> p c i", c=4)
                    c3_oth = oth.rearrange("p (c i) -> p c i", c=4)
                    p.vcopy(c3_oth[:, :, 0:sft], c3_cur[:, :, 0:sft])
                    p.tt(c3_oth[:, :, sft:CK], c3_cur[:, :, sft:CK],
                         c3_cur[:, :, 0:CK - sft], ALU.add)
                    cur, oth = oth, cur
                if cur is not LL:
                    p.vcopy(LL, cur)
                p.act(LNA, A_, AF.Ln)
                p.tt(LP, LL, LNA, ALU.subtract)
                p.act(LB2, B_, AF.Ln)
                p.tt(LB2, LP, LB2, ALU.add)
                p.ts(LB2, LB2, 1.0, LNC2, ALU.mult, ALU.add)
                p.ts(NL, LL, -1.0, None, ALU.mult)
                p.act(S1, LP, AF.Exp)
                p.tt(S1, B_, S1, ALU.mult)
                p.ts(S1, S1, float(-DSC * DSC), None, ALU.mult)
                p.ts(BC, B_, DSC, None, ALU.mult)
                p.act(EL, LL, AF.Exp)
                for ck4 in range(4):
                    cg = q * 4 + ck4
                    c0 = cg * CK
                    csl = slice(ck4 * CK, (ck4 + 1) * CK)
                    for k, qty in enumerate((NL, S1, BC, EL)):
                        p.tr(p.ps[3][:, :16], qty[:, csl], f32=True)
                        p.scopy(cols[:, k * 16:(k + 1) * 16], p.ps[3][:, :16])
                    p.tr(p.ps[3][0:1, 0:16], LL[:, ck4 * CK + CK - 1:ck4 * CK + CK], f32=True)
                    p.scopy(flat[:, 0:16], p.ps[3][0:1, 0:16])
                    p.mm(p.ps[3][:, :16], p.ones_row_f[:, :], flat[:, 0:16])
                    p.scopy(cols[:, 64:80], p.ps[3][:, :16])
                    p.tt(cols[:, 80:96], cols[:, 64:80], cols[:, 0:16], ALU.add)
                    p.act(cols[:, 80:96], cols[:, 80:96], AF.Exp)
                    p.act(cols[:, 64:80], cols[:, 64:80], AF.Exp)
                    p.dma(p.rowtmp[:, :], LB2[:, csl])
                    p.dma(flat[:, 0 * 16 * CK:1 * 16 * CK],
                          p.rowtmp.rearrange("h c -> (h c)")[None, :])
                    p.dma(p.rowtmp2[:, :], LL[:, csl])
                    p.dma(flat[:, 1 * 16 * CK:2 * 16 * CK],
                          p.rowtmp2.rearrange("h c -> (h c)")[None, :])
                    for h in range(16):
                        hq = h * QT + ck4 * CK
                        hsl = slice(hq, hq + CK)
                        zsl = slice(h * 128, (h + 1) * 128)
                        p.tr(p.ps_bf[:, :128], khat[:, hsl])
                        p.ts(ktm[:, zsl], p.ps_bf[:, :128], cols[:, 80 + h:80 + h + 1],
                             None, ALU.mult)
                        p.tr(p.ps_bf[:, :128], vhat[:, hsl])
                        p.ts(vtm[:, zsl], p.ps_bf[:, :128], cols[:, 32 + h:32 + h + 1],
                             None, ALU.mult)
                        p.mm(p.ps[0][:, :128], khat[:, hsl], khat[:, hsl])
                        p.mm(p.ps[1][:, :128], p.ones_row_f[:, :],
                             flat[:, h * CK:(h + 1) * CK])
                        p.act(expm[:, :], p.ps[1][:, :128], AF.Exp,
                              bias=cols[:, h:h + 1])
                        p.asel(expm[:, :], expm[:, :], [[1, 128]], ALU.is_gt, 0.0,
                               base=0, chmul=-1)
                        p.stt(mneg[:, :], p.ps[0][:, :128], -1.0, expm[:, :],
                              ALU.mult, ALU.mult)
                        p.mm(p.ps[2][:, :128], khat[:, hsl], zb[:, zsl])
                        p.stt(r0sb[:, :], p.ps[2][:, :128], cols[:, 16 + h:16 + h + 1],
                              vtm[:, zsl], ALU.mult, ALU.add)
                        p.vcopy(rcur[:, 0:128], r0sb[:, :])
                        for it in range(3):
                            cu = rcur[:, (it % 2) * 128:(it % 2) * 128 + 128]
                            nx = rcur[:, ((it + 1) % 2) * 128:((it + 1) % 2) * 128 + 128]
                            p.mm(p.ps[2][:, :128], mneg[:, :], cu)
                            p.tt(nx, p.ps[2][:, :128], r0sb[:, :], ALU.add)
                        Y = rcur[:, (3 % 2) * 128:(3 % 2) * 128 + 128]
                        p.mm(p.ps[2][:, :128], qhat[:, hsl], zb[:, zsl])
                        p.mm(p.ps[0][:, :128], khat[:, hsl], qhat[:, hsl])
                        p.mm(p.ps[1][:, :128], p.ones_row_f[:, :],
                             flat[:, 16 * CK + h * CK:16 * CK + (h + 1) * CK])
                        p.act(expm[:, :], p.ps[1][:, :128], AF.Exp,
                              bias=cols[:, h:h + 1])
                        p.asel(expm[:, :], expm[:, :], [[1, 128]], ALU.is_ge, 0.0,
                               base=0, chmul=-1)
                        p.tt(mneg[:, :], p.ps[0][:, :128], expm[:, :], ALU.mult)
                        p.mm(p.ps[0][:, :128], mneg[:, :], Y)
                        p.ts(otmp[:, :], p.ps[2][:, :128], cols[:, 48 + h:48 + h + 1],
                             None, ALU.mult)
                        p.tt(otm[:, zsl], otmp[:, :], p.ps[0][:, :128], ALU.add)
                        p.mm(p.ps[2][:, :128], ktm[:, zsl], Y)
                        p.stt(zf[:, zsl], zf[:, zsl], cols[:, 64 + h:64 + h + 1],
                              p.ps[2][:, :128], ALU.mult, ALU.add)
                        p.scopy(zb[:, zsl], zf[:, zsl])
                    o3 = otm[:, :].rearrange("p (h v) -> p h v", h=16)
                    p.rsum(st16[:, 0:16], o3)
                    p.ts(st16[:, 0:16], st16[:, 0:16], 1.0 / 128.0, None, ALU.mult)
                    p.act(osq[:, :], otm[:, :], AF.Square)
                    p.rsum(st16[:, 16:32], osq[:, :].rearrange("p (h v) -> p h v", h=16))
                    p.ts(st16[:, 16:32], st16[:, 16:32], 1.0 / 128.0, None, ALU.mult)
                    p.tt(st16[:, 32:48], st16[:, 0:16], st16[:, 0:16], ALU.mult)
                    p.tt(st16[:, 32:48], st16[:, 16:32], st16[:, 32:48], ALU.subtract)
                    p.act(st16[:, 32:48], st16[:, 32:48], AF.Sqrt, bias=LN_EPS)
                    p.recip(st16[:, 48:64], st16[:, 32:48])
                    p.ts(st16[:, 0:16], st16[:, 0:16], -1.0, None, ALU.mult)
                    for h in range(16):
                        zsl = slice(h * 128, (h + 1) * 128)
                        p.ts(otm[:, zsl], otm[:, zsl], st16[:, h:h + 1],
                             st16[:, 48 + h:48 + h + 1], ALU.add, ALU.mult)
                    p.tt(otm[:, :], otm[:, :], lnwb[:, :], ALU.mult)
                    p.tt(otm[:, :], otm[:, :], lnbb[:, :], ALU.add)
                    p.dma(gsb[:, :], p.xwg_tm[b * S + c0:b * S + c0 + CK, :])
                    p.act(gss[:, :], gsb[:, :], AF.Sigmoid)
                    p.tt(ogbf[:, :], otm[:, :], gss[:, :], ALU.mult)
                    for h in range(16):
                        p.tr(p.ps_bf[:, :128], ogbf[:, h * 128:(h + 1) * 128])
                        p.scopy(ofm[:, :], p.ps_bf[:, :128])
                        p.dma(p.aoh[h * 128:(h + 1) * 128, c0:c0 + CK], ofm[:, :])
        with self.nc.sbuf_tensor(f"zb0_{b}", [128, 16], F32) as zb0:
            p.memset(zb0[:, :], 0.0)
            self._proj_epilogue(wname="wo1", act_hbm=p.aoh, act_cols=S, b=b,
                                res_in=p.x2f, outf=p.x3f, outb=p.x3b,
                                bias_sb=zb0, tag=f"l1a{b}")

    def phase_l1_ffn(self):
        p = self
        nc = self.nc
        T = BPC * S
        with ExitStack() as stk:
            E = stk.enter_context
            xa = E(nc.sbuf_tensor("p5_x", [128, NDT * T], BF16))
            wgb = E(nc.sbuf_tensor("p5_wg", [128, NDT * 512], BF16))
            wub = E(nc.sbuf_tensor("p5_wu", [128, NDT * 512], BF16))
            slf = E(nc.sbuf_tensor("p5_sl", [128, 512], F32))
            hidf = E(nc.sbuf_tensor("p5_h", [128, 512], F32))
            absf = E(nc.sbuf_tensor("p5_a", [128, 512], F32))
            lad = E(nc.sbuf_tensor("p5_l", [128, 512], F32))
            tau = E(nc.sbuf_tensor("p5_t", [128, 2 * 128], F32))
            mk = E(nc.sbuf_tensor("p5_m", [128, 512], F32))
            hsp = E(nc.sbuf_tensor("p5_s", [128, 512], BF16))
            stag = E(nc.sbuf_tensor("p5_st", [128, 4 * 1024], BF16))
            for dt in range(NDT):
                p.dma(xa[:, dt * T:(dt + 1) * T], p.x3b[dt * 128:(dt + 1) * 128, :])
            for hb in range(16):
                for dt in range(NDT):
                    p.gdma(wgb[:, dt * 512:(dt + 1) * 512],
                           p.inp["wgate"][dt * 128:(dt + 1) * 128, hb * 512:(hb + 1) * 512])
                    p.gdma(wub[:, dt * 512:(dt + 1) * 512],
                           p.inp["wup"][dt * 128:(dt + 1) * 128, hb * 512:(hb + 1) * 512])
                for tt in range(T // 128):
                    for dt in range(NDT):
                        p.mm(p.ps[0][:, :], xa[:, dt * T + tt * 128:dt * T + (tt + 1) * 128],
                             wgb[:, dt * 512:(dt + 1) * 512],
                             start=(dt == 0), stop=(dt == NDT - 1))
                    p.act(slf[:, :], p.ps[0][:, :], AF.Silu)
                    for dt in range(NDT):
                        p.mm(p.ps[1][:, :], xa[:, dt * T + tt * 128:dt * T + (tt + 1) * 128],
                             wub[:, dt * 512:(dt + 1) * 512],
                             start=(dt == 0), stop=(dt == NDT - 1))
                    p.tt(hidf[:, :], p.ps[1][:, :], slf[:, :], ALU.mult)
                    p.act(absf[:, :], hidf[:, :], AF.Abs)
                    p.tt(lad[:, 0:512:4], absf[:, 0:512:4], absf[:, 1:512:4], ALU.max)
                    p.tt(lad[:, 1:512:4], absf[:, 2:512:4], absf[:, 3:512:4], ALU.max)
                    p.tt(lad[:, 2:512:4], absf[:, 0:512:4], absf[:, 1:512:4], ALU.min)
                    p.tt(lad[:, 3:512:4], absf[:, 2:512:4], absf[:, 3:512:4], ALU.min)
                    p.tt(tau[:, 0:128], lad[:, 0:512:4], lad[:, 1:512:4], ALU.min)
                    p.tt(tau[:, 128:256], lad[:, 2:512:4], lad[:, 3:512:4], ALU.max)
                    p.tt(tau[:, 0:128], tau[:, 0:128], tau[:, 128:256], ALU.max)
                    for j in range(4):
                        p.tt(mk[:, j:512:4], absf[:, j:512:4], tau[:, 0:128], ALU.is_ge)
                    p.tt(hsp[:, :], hidf[:, :], mk[:, :], ALU.mult)
                    for j in range(4):
                        p.tr(p.ps_bf[:, :128], hsp[:, j * 128:(j + 1) * 128])
                        p.scopy(stag[:, j * 1024 + (tt % 8) * 128:j * 1024 + (tt % 8 + 1) * 128],
                                p.ps_bf[:, :128])
                    if tt % 8 == 7:
                        for j in range(4):
                            p.dma(p.hid1[hb * 4 + j, :, (tt - 7) * 128:(tt + 1) * 128],
                                  stag[:, j * 1024:(j + 1) * 1024])
        self._down_proj(p.hid1, p.inp["wdown"], p.part1, p.x3f, self.out, None,
                        tag="l1f", wcast=True)

    def finish(self):
        p = self
        if self.dbg is not None:
            src = {"x1f": p.x1f, "x2f": p.x2f, "x3f": p.x3f, "w2eff": p.w2eff,
                   "aoh": p.aoh, "x1b": p.x1b, "hid0": p.hid0, "hid1": p.hid1,
                   "xwq": p.xwq, "xwg_tm": p.xwg_tm, "part0": p.part0}[self.dbg]
            flat_s = src.rearrange(" ".join(f"a{i}" for i in range(len(src.shape))) + " -> (" +
                                   " ".join(f"a{i}" for i in range(len(src.shape))) + ")") \
                if False else src
            p.dma(self.dbg_out[...], src[...])
        total = self.cnt
        self.ops["sync"].append(lambda h, n=total: h.wait_ge(self.chain, n))
        with self.nc.Block() as block:
            for eng in ("tensor", "vector", "scalar", "gpsimd", "sync"):
                def mk(e):
                    def f(h):
                        for op in self.ops[e]:
                            op(h)
                    return f
                getattr(block, eng)(mk(eng))
        self.st.close()
        return self.nc


def build_program(upto=99, dbg=None):
    p = Prog(upto=upto, dbg=dbg)
    p.alloc_consts()
    p.phase_consts()
    if upto >= 0:
        p.phase_w2eff()
    if upto >= 1:
        for b in range(BPC):
            p.phase_l0_attn(b)
    if upto >= 2:
        p.phase_l0_ffn()
    if upto >= 3:
        p.phase_l1_proj()
    if upto >= 4:
        for b in range(BPC):
            p.phase_l1_delta(b)
    if upto >= 5:
        p.phase_l1_ffn()
    nc = p.finish()
    return nc


def host_layout(inputs):
    f32 = np.float32
    qkv_w = np.asarray(inputs["L0_attn_qkv_w"], f32)
    qkv_b = np.asarray(inputs["L0_attn_qkv_b"], f32)
    d = {}
    d["wq0"] = relay_lhsT(np.ascontiguousarray(qkv_w[0:DM].T), NDT, 512)
    d["wk0"] = relay_lhsT(np.ascontiguousarray(qkv_w[DM:2 * DM].T), NDT, 512)
    d["wv0"] = relay_lhsT(np.ascontiguousarray(qkv_w[2 * DM:3 * DM].T), NDT, 512)
    d["qb0"] = np.ascontiguousarray(qkv_b[0:DM].reshape(16, 128).T)
    d["kb0"] = np.ascontiguousarray(qkv_b[DM:2 * DM].reshape(16, 128).T)
    d["vb0_row"] = np.ascontiguousarray(qkv_b[2 * DM:3 * DM].reshape(1, DM))
    d["wo0"] = relay_lhsT(np.ascontiguousarray(np.asarray(inputs["L0_attn_out_w"], f32).T), NDT, 128)
    d["ob0"] = np.ascontiguousarray(np.asarray(inputs["L0_attn_out_b"], f32).reshape(16, 128).T)
    d["alibi_rep"] = np.ascontiguousarray(
        np.repeat(np.asarray(inputs["L0_attn_alibi_slopes"], f32), 16).reshape(1, 256))
    W1 = np.asarray(inputs["L0_ffn_W1"], f32); W3 = np.asarray(inputs["L0_ffn_W3"], f32)
    d["w1t"] = np.ascontiguousarray(
        W1.reshape(NHT, 128, NDT, 128).transpose(0, 3, 2, 1).reshape(NHT, 128, DM))
    d["w3t"] = np.ascontiguousarray(
        W3.reshape(NHT, 128, NDT, 128).transpose(0, 3, 2, 1).reshape(NHT, 128, DM))
    d["w2t"] = np.ascontiguousarray(np.asarray(inputs["L0_ffn_W2"], f32).T)
    d["h128"] = hada(128).astype(BF)
    d["h64"] = hada(64).astype(BF)
    d["identity"] = np.eye(128, dtype=np.float32).astype(BF)
    d["identity_f"] = np.eye(128, dtype=np.float32)
    d["wq1"] = relay_lhsT(np.asarray(inputs["L1_attn_Wq"], f32), NDT, 128)
    d["wk1"] = relay_lhsT(np.asarray(inputs["L1_attn_Wk"], f32), NDT, 128)
    d["wv1"] = relay_lhsT(np.asarray(inputs["L1_attn_Wv"], f32), NDT, 128)
    d["wg1"] = np.ascontiguousarray(np.asarray(inputs["L1_attn_Wg"], f32))
    d["wa1"] = np.ascontiguousarray(
        np.asarray(inputs["L1_attn_Wa"], f32).reshape(16, 128, 16).transpose(1, 0, 2).reshape(128, 256))
    d["wb1"] = np.ascontiguousarray(
        np.asarray(inputs["L1_attn_Wb"], f32).reshape(16, 128, 16).transpose(1, 0, 2).reshape(128, 256))
    d["ba1"] = np.ascontiguousarray(np.asarray(inputs["L1_attn_ba"], f32).reshape(16, 1))
    d["bb1"] = np.ascontiguousarray(np.asarray(inputs["L1_attn_bb"], f32).reshape(16, 1))
    d["wo1"] = relay_lhsT(np.asarray(inputs["L1_attn_Wo"], f32), NDT, 128)
    for nm, key in (("qcw", "L1_attn_qcw"), ("kcw", "L1_attn_kcw"), ("vcw", "L1_attn_vcw")):
        w = np.asarray(inputs[key], f32)
        d[nm] = np.ascontiguousarray(w.reshape(16, 128, 4).transpose(1, 0, 2).reshape(128, 64))
    for nm, key in (("qcb", "L1_attn_qcb"), ("kcb", "L1_attn_kcb"), ("vcb", "L1_attn_vcb")):
        d[nm] = np.ascontiguousarray(np.asarray(inputs[key], f32).reshape(16, 128).T)
    d["lnw_rep"] = np.ascontiguousarray(np.tile(np.asarray(inputs["L1_attn_ln_w"], f32), 16).reshape(1, DM))
    d["lnb_rep"] = np.ascontiguousarray(np.tile(np.asarray(inputs["L1_attn_ln_b"], f32), 16).reshape(1, DM))
    d["wgate"] = np.ascontiguousarray(np.asarray(inputs["L1_ffn_w_gate"], f32))
    d["wup"] = np.ascontiguousarray(np.asarray(inputs["L1_ffn_w_up"], f32))
    d["wdown"] = np.ascontiguousarray(np.asarray(inputs["L1_ffn_w_down"], f32))
    return d


def kernel(**inputs):
    x = np.asarray(inputs["L0_attn_hidden_states"], np.float32)
    shared = host_layout(inputs)
    nc = build_program()
    in_maps = []
    for c in range(8):
        m = dict(shared)
        xs = np.ascontiguousarray(np.transpose(x[2 * c:2 * c + 2], (0, 2, 1)))
        m["xin"] = xs
        in_maps.append(m)
    res = run_bass_kernel_spmd(nc, in_maps, core_ids=list(range(8)))
    outs = []
    for c in range(8):
        o = np.asarray(res.results[c]["out"])
        outs.append(np.transpose(o, (0, 2, 1)))
    return np.concatenate(outs, axis=0).astype(np.float32)


if __name__ == "__main__":
    pass


# version 05
# speedup vs baseline: 2.5764x; 2.5764x over previous
import os
import numpy as np
import ml_dtypes
import concourse.bass as bass
import concourse.mybir as mybir
from concourse.bass_utils import run_bass_kernel_spmd
from contextlib import ExitStack

F32 = mybir.dt.float32
BF16 = mybir.dt.bfloat16
AF = mybir.ActivationFunctionType
ALU = mybir.AluOpType
AXX = mybir.AxisListType.X

BPC = 2
S = 2048
DM = 2048
NDT = DM // 128
NH = 16
HD = 128
H0 = 8192
NHT = H0 // 128
CK = 128
NCK = S // CK
INV_SQRT_HD = float(1.0 / np.sqrt(128.0))
LN_EPS = 1e-5
DSC = float(128.0 ** -0.5)
BF = ml_dtypes.bfloat16


def hada(n):
    H = np.array([[1.0]], dtype=np.float32)
    while H.shape[0] < n:
        H = np.block([[H, H], [H, -H]])
    return H


def relay_lhsT(w, din_tiles, dout_block):
    din, dout = w.shape
    nb = dout // dout_block
    assert din == din_tiles * 128
    r = w.reshape(din_tiles, 128, nb, dout_block).transpose(2, 1, 0, 3)
    return np.ascontiguousarray(r.reshape(nb, 128, din_tiles * dout_block))


class Prog:
    def __init__(self, upto=99, dbg=None, mode="auto"):
        self.upto = upto
        self.dbg = dbg
        nc = bass.Bass("TRN2", target_bir_lowering=False, debug=False)
        self.nc = nc
        self.cnt = 0
        self.mode = mode
        self.ops = {e: [] for e in ("tensor", "vector", "scalar", "gpsimd", "sync")}
        self.emitted = {e: 0 for e in self.ops}
        self.waited = {}
        self.acc = {}
        self.floor = {}
        self.input_names = set()
        self.psum_names = set()
        import os
        se = os.environ.get("KSE", "tensor")
        self.same_engine_ok = set(se.split(",")) if se else set()
        self.bar = {}
        orig_sbuf = nc.sbuf_tensor
        prog = self
        class _SbGuard:
            def __init__(self, cm):
                self.cm = cm
            def __enter__(self):
                return self.cm.__enter__()
            def __exit__(self, *a):
                r = self.cm.__exit__(*a)
                bar = dict(prog.bar)
                for e2 in prog.ops:
                    if prog.emitted[e2] > 0:
                        bar[e2] = prog.emitted[e2]
                for k2, u in prog.dslot_use.items():
                    if u > 0:
                        bar[k2] = 16 * u
                prog.bar = bar
                return r
        def wrapped(*args, **kw):
            return _SbGuard(orig_sbuf(*args, **kw))
        nc.sbuf_tensor = wrapped
        self.st = ExitStack()
        self._declare_dram()

    MAXREC = 160

    BIGP = 1 << 40
    BIGF = 1 << 60

    def _iv(self, ap):
        try:
            dims = list(ap.ap)
            off = ap.offset
            t = ap.tensor
            rowlen = 1
            for d in t.shape[1:]:
                rowlen *= int(d)
            if dims and rowlen > 0 and dims[0][0] == rowlen:
                p0 = off // rowlen
                f0 = off % rowlen
                pn = dims[0][1]
                fspan = sum(abs(st) * (cn - 1) for st, cn in dims[1:])
                if f0 + fspan < rowlen:
                    return (p0, p0 + pn - 1, f0, f0 + fspan)
            lo = off
            span = sum(abs(st) * (cn - 1) for st, cn in dims)
            return (0, self.BIGP, lo, lo + span)
        except Exception:
            return (0, self.BIGP, 0, self.BIGF)

    def _scan(self, eng, isdma, ap, want_writes_only, need):
        t = ap.tensor
        name = t.name
        if name in self.input_names:
            return
        recs = self.acc.get(name)
        flo = self.floor.get(name)
        if flo:
            for e2, c2 in flo.items():
                if not (e2 == eng and not isdma):
                    if need.get(e2, 0) < c2:
                        need[e2] = c2
        if not recs:
            return
        if name in self.psum_names:
            rect = (0, self.BIGP, 0, self.BIGF)
        else:
            rect = self._iv(ap)
        plo, phi, flo, fhi = rect
        for (w2, e2, c2, r2, dma2) in recs:
            if want_writes_only and not w2:
                continue
            if fhi < r2[2] or r2[3] < flo or phi < r2[0] or r2[1] < plo:
                continue
            if e2 == eng and not isdma and not dma2 and eng in self.same_engine_ok:
                continue
            if need.get(e2, 0) < c2:
                need[e2] = c2

    def _rec(self, name, entry):
        lst = self.acc.setdefault(name, [])
        lst.append(entry)
        if len(lst) > self.MAXREC:
            drop = lst[: len(lst) - self.MAXREC]
            del lst[: len(lst) - self.MAXREC]
            flo = self.floor.setdefault(name, {})
            for (w2, e2, c2, r2, dma2) in drop:
                if flo.get(e2, 0) < c2:
                    flo[e2] = c2

    def _emit(self, eng, fn, reads=(), writes=(), inc=1):
        if self.mode == "chain":
            at = self.cnt
            def run(h, at=at, fn=fn, inc=inc):
                h.wait_ge(self.chain, at)
                r = fn(h)
                r.then_inc(self.chain, inc)
            self.ops[eng].append(run)
            self.cnt += inc
            return
        import os
        isdma = inc == 16
        need = {}
        for ap in writes:
            self._scan(eng, isdma, ap, False, need)
        for ap in reads:
            self._scan(eng, isdma, ap, True, need)
        if getattr(self, "kbar_on", False):
            for e2 in self.ops:
                if self.emitted[e2] > 0 and (e2 != eng or os.environ.get("KSELF")):
                    need[e2] = self.emitted[e2]
            for k2, u in self.dslot_use.items():
                if u > 0:
                    need[k2] = max(need.get(k2, 0), 16 * u)
        for k2, v2 in self.bar.items():
            if k2 != eng and need.get(k2, 0) < v2:
                need[k2] = v2
        if isdma:
            q = eng
            skey = ("d", q, self.dma_idx[q] % self.NSLOT)
            self.dma_idx[q] += 1
            use = self.dslot_use[skey]
            self.dslot_use[skey] = use + 1
            cnt_after = 16 * (use + 1)
            ekey = skey
            if use > 0:
                need[skey] = max(need.get(skey, 0), 16 * use)
        else:
            ekey = eng
            cnt_after = self.emitted[eng] + inc
            self.emitted[eng] = cnt_after
        waits = []
        for e2, v in need.items():
            key = (ekey, e2) if isdma else (eng, e2)
            if self.waited.get(key, 0) >= v:
                continue
            self.waited[key] = v
            waits.append((self.esem[e2], v))
        mysem = self.esem[ekey]
        def run(h, waits=tuple(waits), fn=fn, mysem=mysem, inc=inc):
            for smm, v in waits:
                h.wait_ge(smm, v)
            r = fn(h)
            r.then_inc(mysem, inc)
        self.ops[eng].append(run)
        for ap in writes:
            name = ap.tensor.name
            if name in self.input_names:
                continue
            r = (0, self.BIGP, 0, self.BIGF) if name in self.psum_names else self._iv(ap)
            self._rec(name, (True, ekey, cnt_after, r, isdma))
        for ap in reads:
            name = ap.tensor.name
            if name in self.input_names:
                continue
            r = (0, self.BIGP, 0, self.BIGF) if name in self.psum_names else self._iv(ap)
            self._rec(name, (False, ekey, cnt_after, r, isdma))

    def t(self, fn, reads=(), writes=()): self._emit("tensor", fn, reads, writes)
    def v(self, fn, reads=(), writes=()): self._emit("vector", fn, reads, writes)
    def s(self, fn, reads=(), writes=()): self._emit("scalar", fn, reads, writes)
    def g(self, fn, reads=(), writes=()): self._emit("gpsimd", fn, reads, writes)
    def dma(self, out, in_, q="sync"):
        self._emit(q, lambda h, o=out, i=in_: h.dma_start(out=o, in_=i),
                   reads=(in_,), writes=(out,), inc=16)
    def gdma(self, out, in_):
        self.dma(out, in_, q="gpsimd")

    def mm(self, out, lhsT, rhs, start=True, stop=True):
        self.t(lambda t, o=out, l=lhsT, r=rhs, st=start, sp=stop:
               t.matmul(o, l, r, start=st, stop=sp),
               reads=(lhsT, rhs), writes=(out,))
    def tr(self, out, in_, f32=False):
        k = in_.shape[0]
        ident = self.ident_f if f32 else self.ident_bf
        self.t(lambda t, o=out, i=in_, g=ident, kk=k: t.transpose(o, i, g[:kk, :kk]),
               reads=(in_, ident[:, :]), writes=(out,))
    def act(self, out, in_, func, bias=0.0, scale=1.0, accum_out=None):
        rd = [in_]
        if not isinstance(bias, float):
            rd.append(bias)
        wr = [out] + ([accum_out] if accum_out is not None else [])
        self.s(lambda sc, o=out, i=in_, f=func, b=bias, s_=scale, a=accum_out:
               sc.activation(o, i, f, bias=b, scale=s_, accum_out=a),
               reads=tuple(rd), writes=tuple(wr))
    def scopy(self, out, in_):
        self.s(lambda sc, o=out, i=in_: sc.copy(o, i), reads=(in_,), writes=(out,))
    def vcopy(self, out, in_):
        self.v(lambda v, o=out, i=in_: v.tensor_copy(o, i), reads=(in_,), writes=(out,))
    def tt(self, out, in0, in1, op):
        self.v(lambda v, o=out, a=in0, b=in1, p=op: v.tensor_tensor(o, a, b, p),
               reads=(in0, in1), writes=(out,))
    def ts(self, out, in0, s1, s2=None, op0=ALU.mult, op1=ALU.bypass, accum_out=None):
        rd = [in0]
        if not isinstance(s1, float):
            rd.append(s1)
        if s2 is not None and not isinstance(s2, float):
            rd.append(s2)
        wr = [out] + ([accum_out] if accum_out is not None else [])
        self.v(lambda v, o=out, i=in0, a=s1, b=s2, p0=op0, p1=op1, ac=accum_out:
               v.tensor_scalar(o, i, a, b, p0, p1, accum_out=ac),
               reads=tuple(rd), writes=tuple(wr))
    def stt(self, out, in0, s1, in1, op0, op1):
        rd = [in0, in1]
        if not isinstance(s1, float):
            rd.append(s1)
        self.v(lambda v, o=out, i=in0, a=s1, j=in1, p0=op0, p1=op1:
               v.scalar_tensor_tensor(o, i, a, j, p0, p1),
               reads=tuple(rd), writes=(out,))
    def recip(self, out, in_):
        self.v(lambda v, o=out, i=in_: v.reciprocal(o, i), reads=(in_,), writes=(out,))
    def rsum(self, out, in_):
        self.v(lambda v, o=out, i=in_: v.reduce_sum(o, i, AXX),
               reads=(in_,), writes=(out,))
    def memset(self, ap, val):
        self.g(lambda g, a=ap, c=val: g.memset(a, c), reads=(), writes=(ap,))
    def asel(self, out, in_, pattern, cmp, fill, base=0, chmul=1):
        self.g(lambda g, o=out, i=in_, p=pattern, c=cmp, f=fill, b=base, m=chmul:
               g.affine_select(o, i, p, c, f, base=b, channel_multiplier=m),
               reads=(in_,), writes=(out,))

    def _declare_dram(self):
        nc = self.nc
        self.inp = {}
        def I(name, shape, dt=F32):
            self.inp[name] = nc.dram_tensor(name, list(shape), dt, kind="ExternalInput").ap()
            self.input_names.add(name)
            return self.inp[name]
        I("xin", (BPC, DM, S))
        I("wq0", (4, 128, NDT * 512)); I("wk0", (4, 128, NDT * 512)); I("wv0", (4, 128, NDT * 512))
        I("qb0", (128, 16)); I("kb0", (128, 16)); I("vb0_row", (1, DM))
        I("wo0", (16, 128, NDT * 128)); I("ob0", (128, 16))
        I("alibi_rep", (1, 256))
        I("w1t", (NHT, 128, DM)); I("w3t", (NHT, 128, DM)); I("w2t", (H0, DM))
        I("h128", (128, 128), BF16); I("h64", (64, 64), BF16); I("identity", (128, 128), BF16)
        I("identity_f", (128, 128), F32)
        I("wq1", (16, 128, NDT * 128)); I("wk1", (16, 128, NDT * 128))
        I("wv1", (16, 128, NDT * 128)); I("wg1", (DM, DM))
        I("wa1", (128, 256)); I("wb1", (128, 256)); I("ba1", (16, 1)); I("bb1", (16, 1))
        I("wo1", (16, 128, NDT * 128))
        I("qcw", (128, 64)); I("kcw", (128, 64)); I("vcw", (128, 64))
        I("qcb", (128, 16)); I("kcb", (128, 16)); I("vcb", (128, 16))
        I("lnw_rep", (1, DM)); I("lnb_rep", (1, DM))
        I("wgate", (DM, H0)); I("wup", (DM, H0)); I("wdown", (H0, DM))

        D = lambda name, shape, dt: nc.dram_tensor(name, list(shape), dt, kind="Internal").ap()
        self.w2eff = D("w2eff", (H0, DM), BF16)
        self.bh = D("bh", (NHT, 128, DM), BF16)
        self.aoh = D("aoh", (DM, S), BF16)
        self.x1f = D("x1f", (BPC, DM, S), F32)
        self.x1b = D("x1b", (DM, BPC * S), BF16)
        self.hid0 = D("hid0", (NHT, 128, BPC * S), BF16)
        self.part0 = D("part0", (NDT, 128, BPC * S), F32)
        self.x2f = D("x2f", (BPC, DM, S), F32)
        self.x2b = D("x2b", (DM, BPC * S), BF16)
        self.xwq = D("xwq", (DM, BPC * S), BF16)
        self.xwk = D("xwk", (DM, BPC * S), BF16)
        self.xwv = D("xwv", (DM, BPC * S), BF16)
        self.xwg_tm = D("xwg_tm", (BPC * S, DM), BF16)
        self.x3f = D("x3f", (BPC, DM, S), F32)
        self.x3b = D("x3b", (DM, BPC * S), BF16)
        self.hid1 = D("hid1", (NHT, 128, BPC * S), BF16)
        self.part1 = D("part1", (NDT, 128, BPC * S), F32)
        self.sigh = D("sigh", (16, 2 * BPC * S), F32)
        self.rowtmp = D("rowtmp", (16, CK), F32)
        self.rowtmp2 = D("rowtmp2", (16, CK), F32)

        self.out = nc.dram_tensor("out", [BPC, DM, S], F32, kind="ExternalOutput").ap()
        if self.dbg is not None:
            shp = {"x1f": (BPC, DM, S), "x2f": (BPC, DM, S), "x3f": (BPC, DM, S),
                   "sigh": (16, 2 * BPC * S),
                   "w2eff": (H0, DM), "aoh": (DM, S), "x1b": (DM, BPC * S),
                   "hid0": (NHT, 128, BPC * S), "hid1": (NHT, 128, BPC * S),
                   "xwq": (DM, BPC * S), "xwk": (DM, BPC * S), "xwv": (DM, BPC * S),
                   "xwg_tm": (BPC * S, DM),
                   "part0": (NDT, 128, BPC * S)}[self.dbg]
            dt = BF16 if self.dbg in ("w2eff", "aoh", "x1b", "hid0", "hid1", "xwq", "xwk", "xwv", "xwg_tm") else F32
            self.dbg_out = nc.dram_tensor("dbg", list(shp), dt, kind="ExternalOutput").ap()

    def alloc_consts(self):
        E = self.st.enter_context
        nc = self.nc
        self.chain = E(nc.semaphore("chain"))
        self.esem = {e: E(nc.semaphore(f"sem_{e}")) for e in self.ops}
        self.NSLOT = 24
        for q in ("sync", "gpsimd"):
            for si in range(self.NSLOT):
                self.esem[("d", q, si)] = E(nc.semaphore(f"ds_{q}_{si}"))
        self.dslot_use = {("d", q, si): 0 for q in ("sync", "gpsimd")
                          for si in range(self.NSLOT)}
        self.dma_idx = {"sync": 0, "gpsimd": 0}
        self.ident_bf = E(nc.sbuf_tensor("ident_bf", [128, 128], BF16))
        self.ident_f = E(nc.sbuf_tensor("ident_f", [128, 128], F32))
        self.h128_sb = E(nc.sbuf_tensor("h128_sb", [128, 128], BF16))
        self.h64_sb = E(nc.sbuf_tensor("h64_sb", [64, 64], BF16))
        self.ones_col_bf = E(nc.sbuf_tensor("ones_col_bf", [128, 1], BF16))
        self.ones_col_f = E(nc.sbuf_tensor("ones_col_f", [128, 1], F32))
        self.ones_row_f = E(nc.sbuf_tensor("ones_row_f", [1, 128], F32))
        self.iota_row = E(nc.sbuf_tensor("iota_row", [1, 128], F32))
        self.negoff_row = E(nc.sbuf_tensor("negoff_row", [1, 256], F32))
        self.slope_rep = E(nc.sbuf_tensor("slope_rep", [1, 256], F32))
        self.r2_row = E(nc.sbuf_tensor("r2_row", [1, 256], F32))
        self.bias_tab = E(nc.sbuf_tensor("bias_tab", [128, 256], F32))
        self.qb_sb = E(nc.sbuf_tensor("qb_sb", [128, 16], F32))
        self.kb_sb = E(nc.sbuf_tensor("kb_sb", [128, 16], F32))
        self.ob_sb = E(nc.sbuf_tensor("ob_sb", [128, 16], F32))
        self.ps = [E(nc.psum_tensor(f"ps{i}", [128, 512], F32)) for i in range(4)]
        self.ps_bf = E(nc.psum_tensor("ps_bf", [128, 1024], BF16))
        self.ps_row = E(nc.psum_tensor("ps_row", [1, 512], F32))
        for i in range(4):
            self.psum_names.add(f"ps{i}")
        self.psum_names.update(("ps_bf", "ps_row", "ps_row2"))
        self.epsc = E(nc.sbuf_tensor("epsc", [128, 1], F32))
        nc.const_aps.register(F32, LN_EPS, self.epsc[:, :])
        self.ps_row2 = E(nc.psum_tensor("ps_row2", [128, 512], F32))

    def phase_consts(self):
        p = self
        p.dma(p.ident_bf[:, :], p.inp["identity"][:, :])
        p.dma(p.ident_f[:, :], p.inp["identity_f"][:, :])
        p.dma(p.h128_sb[:, :], p.inp["h128"][:, :])
        p.dma(p.h64_sb[:, :], p.inp["h64"][:, :])
        p.dma(p.slope_rep[:, :], p.inp["alibi_rep"][:, :])
        p.dma(p.qb_sb[:, :], p.inp["qb0"][:, :])
        p.dma(p.kb_sb[:, :], p.inp["kb0"][:, :])
        p.dma(p.ob_sb[:, :], p.inp["ob0"][:, :])
        p.memset(p.epsc[:, :], LN_EPS)
        p.memset(p.ones_col_bf[:, :], 1.0)
        p.memset(p.ones_col_f[:, :], 1.0)
        p.memset(p.ones_row_f[:, :], 1.0)
        p.g(lambda g: g.iota(p.iota_row[:, :], [[1, 128]], base=0, channel_multiplier=0,
                             allow_small_or_imprecise_dtypes=True),
            writes=(p.iota_row[:, :],))
        p.g(lambda g: g.iota(p.negoff_row[:, :], [[0, 16], [-128, 16]], base=-64,
                             channel_multiplier=0, allow_small_or_imprecise_dtypes=True),
            writes=(p.negoff_row[:, :],))
        p.tt(p.r2_row[:, :], p.slope_rep[:, :], p.negoff_row[:, :], ALU.mult)
        p.mm(p.ps[0][:, :256], p.iota_row[:, :], p.slope_rep[:, :], start=True, stop=False)
        p.mm(p.ps[0][:, :256], p.ones_row_f[:, :], p.r2_row[:, :], start=False, stop=True)
        p.scopy(p.bias_tab[:, :], p.ps[0][:, :256])

    def bcast_row_2048(self, dst, srcname, row_tmp):
        p = self
        p.dma(row_tmp[:, :], p.inp[srcname][:, :])
        for n in range(4):
            p.mm(p.ps[1][:, :], p.ones_row_f[:, :], row_tmp[:, n * 512:(n + 1) * 512])
            p.scopy(dst[:, n * 512:(n + 1) * 512], p.ps[1][:, :])

    def phase_w2eff(self):
        p = self
        E = self.st
        with (self.nc.sbuf_tensor("w2tile", [128, DM], BF16) as w2tile,
              self.nc.sbuf_tensor("bh_sb", [128, DM], BF16) as bh_sb,
              self.nc.sbuf_tensor("bsteb", [64, 4 * DM], BF16) as bsteb,
              self.nc.sbuf_tensor("w2e_sb", [64, 4 * DM], BF16) as w2e_sb):
            for hi in range(NHT):
                p.gdma(w2tile[:, :], p.inp["w2t"][hi * 128:(hi + 1) * 128, :])
                for n in range(4):
                    p.mm(p.ps[0][:, :], p.h128_sb[:, :], w2tile[:, n * 512:(n + 1) * 512])
                    p.scopy(bh_sb[:, n * 512:(n + 1) * 512], p.ps[0][:, :])
                p.dma(p.bh[hi, :, :], bh_sb[:, :])
            bh4 = p.bh.rearrange("a p d -> a (p d)")
            w2r = p.w2eff.rearrange("(a b) d -> a b d", b=128)
            for lo0 in range(0, 128, 4):
                p.dma(bsteb[:, :], bh4[:, lo0 * DM:(lo0 + 4) * DM])
                for j in range(4):
                    for n in range(4):
                        sl = slice(j * DM + n * 512, j * DM + (n + 1) * 512)
                        p.mm(p.ps[0][:64, :], p.h64_sb[:, :], bsteb[:, sl])
                        p.act(w2e_sb[:64, sl], p.ps[0][:64, :], AF.Identity,
                              scale=float(1.0 / np.sqrt(8192.0)))
                    p.dma(w2r[:, lo0 + j, :], w2e_sb[:64, j * DM:(j + 1) * DM])

    def phase_l0_attn(self, b):
        p = self
        nc = self.nc
        with (nc.sbuf_tensor(f"xbf{b}", [128, NDT * S], BF16) as xbf,
              nc.sbuf_tensor(f"wbuf{b}", [128, NDT * 512], BF16) as wbuf,
              nc.sbuf_tensor(f"wbuf2{b}", [128, NDT * 512], BF16) as wbuf2,
              nc.sbuf_tensor(f"qg{b}", [128, 4 * S], BF16) as qg,
              nc.sbuf_tensor(f"kg{b}", [128, 4 * S], BF16) as kg,
              nc.sbuf_tensor(f"vtm{b}", [128, 16 * 512], BF16) as vtm,
              nc.sbuf_tensor(f"probs{b}", [128, 16 * 128], BF16) as probs,
              nc.sbuf_tensor(f"rec_row{b}", [1, 128], F32) as rec_row,
              nc.sbuf_tensor(f"recb{b}", [128, 128], F32) as recb,
              nc.sbuf_tensor(f"ao_sb{b}", [128, 512], BF16) as ao_sb,
              nc.sbuf_tensor(f"bv_bc{b}", [128, DM], F32) as bv_bc,
              nc.sbuf_tensor(f"rowt{b}", [1, DM], F32) as rowt):
            self.bcast_row_2048(bv_bc, "vb0_row", rowt)
            for dt in range(NDT):
                p.gdma(xbf[:, dt * S:(dt + 1) * S], p.inp["xin"][b, dt * 128:(dt + 1) * 128, :])
            for g in range(4):
                wb_ = [wbuf, wbuf2][(3 * g) % 2]
                p.gdma(wb_[:, :], p.inp["wq0"][g, :, :])
                for hl in range(4):
                    for n in range(4):
                        pp = p.ps[(hl * 4 + n) % 2 * 2][:, :]
                        for dt in range(NDT):
                            p.mm(pp, wb_[:, dt * 512 + hl * 128:dt * 512 + (hl + 1) * 128],
                                 xbf[:, dt * S + n * 512:dt * S + (n + 1) * 512],
                                 start=(dt == 0), stop=(dt == NDT - 1))
                        p.act(qg[:, hl * S + n * 512:hl * S + (n + 1) * 512], pp,
                              AF.Identity, bias=p.qb_sb[:, g * 4 + hl:g * 4 + hl + 1])
                wb_ = [wbuf, wbuf2][(3 * g + 1) % 2]
                p.gdma(wb_[:, :], p.inp["wk0"][g, :, :])
                for hl in range(4):
                    for n in range(4):
                        pp = p.ps[(hl * 4 + n) % 2 * 2][:, :]
                        for dt in range(NDT):
                            p.mm(pp, wb_[:, dt * 512 + hl * 128:dt * 512 + (hl + 1) * 128],
                                 xbf[:, dt * S + n * 512:dt * S + (n + 1) * 512],
                                 start=(dt == 0), stop=(dt == NDT - 1))
                        p.act(kg[:, hl * S + n * 512:hl * S + (n + 1) * 512], pp,
                              AF.Identity, bias=p.kb_sb[:, g * 4 + hl:g * 4 + hl + 1])
                wb_ = [wbuf, wbuf2][(3 * g + 2) % 2]
                p.gdma(wb_[:, :], p.inp["wv0"][g, :, :])
                for tt in range(16):
                    pp = p.ps[tt % 2 * 2][:, :]
                    for dt in range(NDT):
                        p.mm(pp, xbf[:, dt * S + tt * 128:dt * S + (tt + 1) * 128],
                             wb_[:, dt * 512:(dt + 1) * 512],
                             start=(dt == 0), stop=(dt == NDT - 1))
                    p.stt(vtm[:, tt * 512:(tt + 1) * 512], pp, 1.0,
                          bv_bc[:, g * 512:(g + 1) * 512], ALU.mult, ALU.add)
                import os
                if os.environ.get("KDUMPQKV") and g == 0 and b == 1:
                    for hl in range(4):
                        p.dma(p.aoh[hl * 128:(hl + 1) * 128, :], qg[:, hl * S:(hl + 1) * S])
                        p.dma(p.aoh[512 + hl * 128:512 + (hl + 1) * 128, :], kg[:, hl * S:(hl + 1) * S])
                    for tt in range(16):
                        p.dma(p.aoh.rearrange("d s -> (d s)").rearrange(
                              "(a p f) -> a p f", a=32, p=128)[8 + tt // 2, :,
                              (tt % 2) * 512:(tt % 2) * 512 + 512],
                              vtm[:, tt * 512:(tt + 1) * 512])
                for hl in range(4 if not os.environ.get("KDUMPQKV") else 0):
                    h = g * 4 + hl
                    for qc in range(16):
                        for kt in range(qc + 1):
                            pp = p.ps[(kt % 2) * 3][:, :128]
                            p.mm(pp, kg[:, hl * S + kt * 128:hl * S + (kt + 1) * 128],
                                 qg[:, hl * S + qc * 128:hl * S + (qc + 1) * 128])
                            p.act(probs[:, kt * 128:(kt + 1) * 128], pp, AF.Exp,
                                  bias=p.bias_tab[:, h * 16 + (qc - kt):h * 16 + (qc - kt) + 1],
                                  scale=INV_SQRT_HD)
                            if kt == qc:
                                p.asel(probs[:, kt * 128:(kt + 1) * 128],
                                       probs[:, kt * 128:(kt + 1) * 128],
                                       [[1, 128]], ALU.is_ge, 0.0, base=0, chmul=-1)
                            p.mm(p.ps_row[0:1, :128], p.ones_col_bf[:, :],
                                 probs[:, kt * 128:(kt + 1) * 128],
                                 start=(kt == 0), stop=(kt == qc))
                            p.mm(p.ps[1][:, :128], vtm[:, kt * 512 + hl * 128:kt * 512 + (hl + 1) * 128],
                                 probs[:, kt * 128:(kt + 1) * 128],
                                 start=(kt == 0), stop=(kt == qc))
                        p.recip(rec_row[:, :], p.ps_row[0:1, :128])
                        p.mm(p.ps[2][:, :128], p.ones_row_f[:, :], rec_row[:, :])
                        p.scopy(recb[:, :], p.ps[2][:, :128])
                        p.tt(ao_sb[:, (qc % 4) * 128:(qc % 4 + 1) * 128], p.ps[1][:, :128],
                             recb[:, :], ALU.mult)
                        if qc % 4 == 3:
                            p.dma(p.aoh[h * 128:(h + 1) * 128, (qc - 3) * 128:(qc + 1) * 128],
                                  ao_sb[:, :])
        self._proj_epilogue(wname="wo0", act_hbm=p.aoh, act_cols=S, b=b,
                            res_in=p.inp["xin"], outf=p.x1f, outb=p.x1b,
                            bias_sb=p.ob_sb, tag=f"l0a{b}")

    def _proj_epilogue(self, wname, act_hbm, act_cols, b, res_in, outf, outb, bias_sb,
                       part_in=None, tokchunk=512, tag=""):
        p = self
        nc = self.nc
        nchunk = S // tokchunk
        with (nc.sbuf_tensor(f"wo_full{tag}", [128, 16 * NDT * 128], BF16) as wo_full,
              nc.sbuf_tensor(f"aobuf{tag}", [128, NDT * tokchunk], BF16) as aobuf,
              nc.sbuf_tensor(f"x1pre{tag}", [128, NDT * tokchunk], F32) as x1pre,
              nc.sbuf_tensor(f"xres{tag}", [128, tokchunk], F32) as xres,
              nc.sbuf_tensor(f"sqsb{tag}", [128, tokchunk], F32) as sqsb,
              nc.sbuf_tensor(f"rows{tag}", [1, 6 * tokchunk], F32) as rows,
              nc.sbuf_tensor(f"ab_sb{tag}", [128, 2 * tokchunk], F32) as ab_sb,
              nc.sbuf_tensor(f"xo_sb{tag}", [128, tokchunk], F32) as xo_sb,
              nc.sbuf_tensor(f"xb_sb{tag}", [128, tokchunk], BF16) as xb_sb):
            for ft in range(16):
                p.gdma(wo_full[:, ft * NDT * 128:(ft + 1) * NDT * 128], p.inp[wname][ft, :, :])
            TC = tokchunk
            for n in range(nchunk):
                for at in range(NDT):
                    p.dma(aobuf[:, at * TC:(at + 1) * TC],
                          act_hbm[at * 128:(at + 1) * 128, n * TC:(n + 1) * TC])
                for ft in range(NDT):
                    for at in range(NDT):
                        p.mm(p.ps[0][:, :TC],
                             wo_full[:, ft * NDT * 128 + at * 128:ft * NDT * 128 + (at + 1) * 128],
                             aobuf[:, at * TC:(at + 1) * TC],
                             start=(at == 0), stop=(at == NDT - 1))
                    p.dma(xres[:, :], res_in[b, ft * 128:(ft + 1) * 128, n * TC:(n + 1) * TC])
                    p.act(x1pre[:, ft * TC:(ft + 1) * TC], p.ps[0][:, :TC], AF.Identity,
                          bias=bias_sb[:, ft:ft + 1])
                    p.tt(x1pre[:, ft * TC:(ft + 1) * TC], x1pre[:, ft * TC:(ft + 1) * TC],
                         xres[:, :], ALU.add)
                    p.mm(p.ps_row[0:1, :TC], p.ones_col_f[:, :], x1pre[:, ft * TC:(ft + 1) * TC],
                         start=(ft == 0), stop=(ft == NDT - 1))
                    p.act(sqsb[:, :], x1pre[:, ft * TC:(ft + 1) * TC], AF.Square)
                    p.mm(p.ps_row2[0:1, :TC], p.ones_col_f[:, :], sqsb[:, :],
                         start=(ft == 0), stop=(ft == NDT - 1))
                MU, E2, VAR, STD, RA, RB = (rows[:, i * TC:(i + 1) * TC] for i in range(6))
                p.ts(MU, p.ps_row[0:1, :TC], 1.0 / DM, None, ALU.mult)
                p.ts(E2, p.ps_row2[0:1, :TC], 1.0 / DM, None, ALU.mult)
                p.tt(VAR, MU, MU, ALU.mult)
                p.tt(VAR, E2, VAR, ALU.subtract)
                p.act(STD, VAR, AF.Sqrt, bias=LN_EPS)
                p.recip(RA, STD)
                p.tt(RB, MU, RA, ALU.mult)
                p.ts(RB, RB, -1.0, None, ALU.mult)
                p.mm(p.ps[1][:, :TC], p.ones_row_f[:, :], RA)
                p.scopy(ab_sb[:, :TC], p.ps[1][:, :TC])
                p.mm(p.ps[1][:, :TC], p.ones_row_f[:, :], RB)
                p.scopy(ab_sb[:, TC:2 * TC], p.ps[1][:, :TC])
                for ft in range(NDT):
                    p.tt(xo_sb[:, :], x1pre[:, ft * TC:(ft + 1) * TC], ab_sb[:, :TC], ALU.mult)
                    p.tt(xo_sb[:, :], xo_sb[:, :], ab_sb[:, TC:2 * TC], ALU.add)
                    p.dma(outf[b, ft * 128:(ft + 1) * 128, n * TC:(n + 1) * TC], xo_sb[:, :])
                    p.scopy(xb_sb[:, :], xo_sb[:, :])
                    p.dma(outb[ft * 128:(ft + 1) * 128, b * S + n * TC:b * S + (n + 1) * TC],
                          xb_sb[:, :])

    def phase_l0_ffn(self):
        p = self
        nc = self.nc
        T = BPC * S
        with (nc.sbuf_tensor("ffa_x", [128, NDT * T], BF16) as xa,
              nc.sbuf_tensor("ffa_w1", [128, 2 * DM], BF16) as w1b,
              nc.sbuf_tensor("ffa_w3", [128, 2 * DM], BF16) as w3b,
              nc.sbuf_tensor("ffa_hid", [128, 2 * T], BF16) as hidsb,
              nc.sbuf_tensor("ffa_silu", [128, 2 * 512], BF16) as silu_sb):
            for dt in range(NDT):
                p.dma(xa[:, dt * T:(dt + 1) * T], p.x1b[dt * 128:(dt + 1) * 128, :])
            for ht in range(NHT):
                par = ht % 2
                w1s = w1b[:, par * DM:(par + 1) * DM]
                w3s = w3b[:, par * DM:(par + 1) * DM]
                p.gdma(w1s, p.inp["w1t"][ht, :, :])
                p.gdma(w3s, p.inp["w3t"][ht, :, :])
                for n in range(T // 512):
                    pg = p.ps[(n % 2) * 2][:, :]
                    pl = p.ps[(n % 2) * 2 + 1][:, :]
                    sl = silu_sb[:, (n % 2) * 512:(n % 2 + 1) * 512]
                    for dt in range(NDT):
                        p.mm(pg, w1s[:, dt * 128:(dt + 1) * 128],
                             xa[:, dt * T + n * 512:dt * T + (n + 1) * 512],
                             start=(dt == 0), stop=(dt == NDT - 1))
                    p.act(sl, pg, AF.Silu)
                    for dt in range(NDT):
                        p.mm(pl, w3s[:, dt * 128:(dt + 1) * 128],
                             xa[:, dt * T + n * 512:dt * T + (n + 1) * 512],
                             start=(dt == 0), stop=(dt == NDT - 1))
                    p.tt(hidsb[:, par * T + n * 512:par * T + (n + 1) * 512], pl, sl, ALU.mult)
                p.dma(p.hid0[ht, :, :], hidsb[:, par * T:(par + 1) * T])
        self._down_proj(p.hid0, p.w2eff, p.part0, p.x1f, p.x2f, p.x2b, tag="l0f")

    def _down_proj(self, hid, wsrc, part, resf, outf, outb, tag, wcast=False):
        p = self
        nc = self.nc
        T = BPC * S
        TC = 256
        for half in range(2):
            with ExitStack() as stk:
                E = stk.enter_context
                wsb = E(nc.sbuf_tensor(f"dp_w{tag}{half}", [128, 32 * DM], BF16))
                hch = E(nc.sbuf_tensor(f"dp_h{tag}{half}", [128, 32 * TC], BF16))
                aux = E(nc.sbuf_tensor(f"dp_aux{tag}{half}", [128, TC], F32))
                if half == 1:
                    pre = E(nc.sbuf_tensor(f"dp_pre{tag}{half}", [128, NDT * TC], F32))
                    sq_sb = E(nc.sbuf_tensor(f"dp_sq{tag}{half}", [128, TC], F32))
                    rows = E(nc.sbuf_tensor(f"dp_rows{tag}{half}", [1, 6 * TC], F32))
                    ab_sb = E(nc.sbuf_tensor(f"dp_ab{tag}{half}", [128, 2 * TC], F32))
                    xo_sb = E(nc.sbuf_tensor(f"dp_xo{tag}{half}", [128, TC], F32))
                    xb_sb = E(nc.sbuf_tensor(f"dp_xb{tag}{half}", [128, TC], BF16))
                for i in range(32):
                    ht = half * 32 + i
                    if wcast:
                        p.gdma(wsb[:, i * DM:(i + 1) * DM], wsrc[ht * 128:(ht + 1) * 128, :])
                    else:
                        p.dma(wsb[:, i * DM:(i + 1) * DM], wsrc[ht * 128:(ht + 1) * 128, :])
                for c in range(T // TC):
                    b = (c * TC) // S
                    nin = (c * TC) % S
                    for i in range(32):
                        p.dma(hch[:, i * TC:(i + 1) * TC],
                              hid[half * 32 + i, :, c * TC:(c + 1) * TC])
                    for ft in range(NDT):
                        for i in range(32):
                            p.mm(p.ps[ft % 2][:, :TC],
                                 wsb[:, i * DM + ft * 128:i * DM + (ft + 1) * 128],
                                 hch[:, i * TC:(i + 1) * TC],
                                 start=(i == 0), stop=(i == 31))
                        if half == 0:
                            p.vcopy(aux[:, :], p.ps[ft % 2][:, :TC])
                            p.dma(part[ft, :, c * TC:(c + 1) * TC], aux[:, :])
                        else:
                            p.dma(aux[:, :], part[ft, :, c * TC:(c + 1) * TC])
                            p.tt(pre[:, ft * TC:(ft + 1) * TC], p.ps[ft % 2][:, :TC],
                                 aux[:, :], ALU.add)
                            p.dma(aux[:, :], resf[b, ft * 128:(ft + 1) * 128, nin:nin + TC])
                            p.tt(pre[:, ft * TC:(ft + 1) * TC], pre[:, ft * TC:(ft + 1) * TC],
                                 aux[:, :], ALU.add)
                    if half == 1:
                        self._ln_store(pre, TC, sq_sb, rows, ab_sb, xo_sb, xb_sb,
                                       outf, outb, b, nin)

    def _ln_store(self, pre, TC, sq_sb, rows, ab_sb, xo_sb, xb_sb, outf, outb, b, nin):
        p = self
        for ft in range(NDT):
            p.mm(p.ps_row[0:1, :TC], p.ones_col_f[:, :], pre[:, ft * TC:(ft + 1) * TC],
                 start=(ft == 0), stop=(ft == NDT - 1))
            p.act(sq_sb[:, :], pre[:, ft * TC:(ft + 1) * TC], AF.Square)
            p.mm(p.ps_row2[0:1, :TC], p.ones_col_f[:, :], sq_sb[:, :],
                 start=(ft == 0), stop=(ft == NDT - 1))
        MU, E2, VAR, STD, RA, RB = (rows[:, i * TC:(i + 1) * TC] for i in range(6))
        p.ts(MU, p.ps_row[0:1, :TC], 1.0 / DM, None, ALU.mult)
        p.ts(E2, p.ps_row2[0:1, :TC], 1.0 / DM, None, ALU.mult)
        p.tt(VAR, MU, MU, ALU.mult)
        p.tt(VAR, E2, VAR, ALU.subtract)
        p.act(STD, VAR, AF.Sqrt, bias=LN_EPS)
        p.recip(RA, STD)
        p.tt(RB, MU, RA, ALU.mult)
        p.ts(RB, RB, -1.0, None, ALU.mult)
        p.mm(p.ps[2][:, :TC], p.ones_row_f[:, :], RA)
        p.scopy(ab_sb[:, :TC], p.ps[2][:, :TC])
        p.mm(p.ps[2][:, :TC], p.ones_row_f[:, :], RB)
        p.scopy(ab_sb[:, TC:2 * TC], p.ps[2][:, :TC])
        for ft in range(NDT):
            p.tt(xo_sb[:, :], pre[:, ft * TC:(ft + 1) * TC], ab_sb[:, :TC], ALU.mult)
            p.tt(xo_sb[:, :], xo_sb[:, :], ab_sb[:, TC:2 * TC], ALU.add)
            p.dma(outf[b, ft * 128:(ft + 1) * 128, nin:nin + TC], xo_sb[:, :])
            if outb is not None:
                p.scopy(xb_sb[:, :], xo_sb[:, :])
                p.dma(outb[ft * 128:(ft + 1) * 128, b * S + nin:b * S + nin + TC], xb_sb[:, :])

    def phase_l1_proj(self):
        p = self
        nc = self.nc
        T = BPC * S
        with (nc.sbuf_tensor("p3_x", [128, NDT * T], BF16) as xa,
              nc.sbuf_tensor("p3_w", [128, 2 * NDT * 512], BF16) as wb0,
              nc.sbuf_tensor("p3_o", [128, 512], BF16) as ob,
              nc.sbuf_tensor("p3_o2", [128, 512], BF16) as ob2,
              nc.sbuf_tensor("p3_ab", [128, 2 * 256], BF16) as wab,
              nc.sbuf_tensor("p3_sig", [16, 2 * T], F32) as sig,
              nc.sbuf_tensor("p3_bab", [16, 2], F32) as bab):
            for dt in range(NDT):
                p.dma(xa[:, dt * T:(dt + 1) * T], p.x2b[dt * 128:(dt + 1) * 128, :])
            obuf = [ob, ob2]
            for iw, (wname, dst) in enumerate((("wq1", p.xwq), ("wk1", p.xwk), ("wv1", p.xwv))):
                for dot in range(16):
                    par = (iw * 16 + dot) % 2
                    wb = wb0[:, par * NDT * 512:par * NDT * 512 + NDT * 128]
                    p.gdma(wb, p.inp[wname][dot, :, :])
                    for n in range(T // 512):
                        pp = p.ps[(n % 2) * 2][:, :]
                        for dt in range(NDT):
                            p.mm(pp, wb[:, dt * 128:(dt + 1) * 128],
                                 xa[:, dt * T + n * 512:dt * T + (n + 1) * 512],
                                 start=(dt == 0), stop=(dt == NDT - 1))
                        p.scopy(obuf[n % 2][:, :], pp)
                        p.dma(dst[dot * 128:(dot + 1) * 128, n * 512:(n + 1) * 512], obuf[n % 2][:, :])
            for fc in range(4):
                par = fc % 2
                wb = wb0[:, par * NDT * 512:(par + 1) * NDT * 512]
                for dt in range(NDT):
                    p.gdma(wb[:, dt * 512:(dt + 1) * 512],
                           p.inp["wg1"][dt * 128:(dt + 1) * 128, fc * 512:(fc + 1) * 512])
                for tt in range(T // 128):
                    pp = p.ps[(tt % 2) * 2][:, :]
                    for dt in range(NDT):
                        p.mm(pp, xa[:, dt * T + tt * 128:dt * T + (tt + 1) * 128],
                             wb[:, dt * 512:(dt + 1) * 512],
                             start=(dt == 0), stop=(dt == NDT - 1))
                    p.scopy(obuf[tt % 2][:, :], pp)
                    p.dma(p.xwg_tm[tt * 128:(tt + 1) * 128, fc * 512:(fc + 1) * 512], obuf[tt % 2][:, :])
            p.gdma(wab[:, :256], p.inp["wa1"][:, :])
            p.gdma(wab[:, 256:], p.inp["wb1"][:, :])
            p.dma(bab[:, 0:1], p.inp["ba1"][:, :])
            p.dma(bab[:, 1:2], p.inp["bb1"][:, :])
            for j, (coff, boff) in enumerate(((0, 0), (256, 1))):
                for n in range(T // 512):
                    for dt in range(NDT):
                        p.mm(p.ps[0][:16, :], wab[:, coff + dt * 16:coff + (dt + 1) * 16],
                             xa[:, dt * T + n * 512:dt * T + (n + 1) * 512],
                             start=(dt == 0), stop=(dt == NDT - 1))
                    p.act(sig[:, j * T + n * 512:j * T + (n + 1) * 512], p.ps[0][:16, :],
                          AF.Sigmoid, bias=bab[:, boff:boff + 1])

            p.dma(p.sigh[:, :], sig[:, :])

    def phase_l1_delta(self, b):
        p = self
        nc = self.nc
        QT = 512
        with ExitStack() as stk:
            E = stk.enter_context
            qhat = E(nc.sbuf_tensor(f"d_q{b}", [128, 16 * QT], BF16))
            khat = E(nc.sbuf_tensor(f"d_k{b}", [128, 16 * QT], BF16))
            vhat = E(nc.sbuf_tensor(f"d_v{b}", [128, 16 * QT], BF16))
            xwb = E(nc.sbuf_tensor(f"d_xw{b}", [128, 16 * (QT + 3)], BF16))
            cacc = E(nc.sbuf_tensor(f"d_cacc{b}", [128, QT], F32))
            cw = E(nc.sbuf_tensor(f"d_cw{b}", [128, 3 * 64], F32))
            cb = E(nc.sbuf_tensor(f"d_cb{b}", [128, 3 * 16], F32))
            sigq = E(nc.sbuf_tensor(f"d_sig{b}", [16, 2 * QT], F32))
            scal = E(nc.sbuf_tensor(f"d_scal{b}", [16, 8 * QT], F32))
            flat = E(nc.sbuf_tensor(f"d_flat{b}", [1, 2 * 16 * CK], F32))
            cols = E(nc.sbuf_tensor(f"d_cols{b}", [128, 6 * 16], F32))
            zf = E(nc.sbuf_tensor(f"d_zf{b}", [128, 16 * 128], F32))
            zb = E(nc.sbuf_tensor(f"d_zb{b}", [128, 16 * 128], BF16))
            ktm = E(nc.sbuf_tensor(f"d_ktm{b}", [128, 16 * 128], BF16))
            vtm = E(nc.sbuf_tensor(f"d_vtm{b}", [128, 16 * 128], BF16))
            expm = E(nc.sbuf_tensor(f"d_expm{b}", [128, 128], F32))
            mneg = E(nc.sbuf_tensor(f"d_mneg{b}", [128, 128], BF16))
            rcur = E(nc.sbuf_tensor(f"d_rcur{b}", [128, 2 * 128], BF16))
            r0sb = E(nc.sbuf_tensor(f"d_r0{b}", [128, 128], BF16))
            otm = E(nc.sbuf_tensor(f"d_otm{b}", [128, 16 * 128], F32))
            otmp = E(nc.sbuf_tensor(f"d_otmp{b}", [128, 128], F32))
            osq = E(nc.sbuf_tensor(f"d_osq{b}", [128, 16 * 128], F32))
            st16 = E(nc.sbuf_tensor(f"d_st16{b}", [128, 8 * 16], F32))
            gsb = E(nc.sbuf_tensor(f"d_gsb{b}", [128, DM], BF16))
            gss = E(nc.sbuf_tensor(f"d_gss{b}", [128, DM], BF16))
            ogbf = E(nc.sbuf_tensor(f"d_og{b}", [128, DM], BF16))
            ofm = E(nc.sbuf_tensor(f"d_ofm{b}", [128, 128], BF16))
            lnwb = E(nc.sbuf_tensor(f"d_lnw{b}", [128, DM], F32))
            lnbb = E(nc.sbuf_tensor(f"d_lnb{b}", [128, DM], F32))
            rowt = E(nc.sbuf_tensor(f"d_rowt{b}", [1, DM], F32))
            self.bcast_row_2048(lnwb, "lnw_rep", rowt)
            self.bcast_row_2048(lnbb, "lnb_rep", rowt)
            p.dma(cw[:, 0:64], p.inp["qcw"][:, :])
            p.dma(cw[:, 64:128], p.inp["kcw"][:, :])
            p.dma(cw[:, 128:192], p.inp["vcw"][:, :])
            p.dma(cb[:, 0:16], p.inp["qcb"][:, :])
            p.dma(cb[:, 16:32], p.inp["kcb"][:, :])
            p.dma(cb[:, 32:48], p.inp["vcb"][:, :])
            p.memset(zf[:, :], 0.0)
            p.memset(zb[:, :], 0.0)
            LNC2 = float(2.0 * np.log(DSC))
            for q in range(4):
                t0 = b * S + q * QT
                for ti, (src, dst, woff, boff) in enumerate(
                        ((p.xwq, qhat, 0, 0), (p.xwk, khat, 64, 16), (p.xwv, vhat, 128, 32))):
                    if q == 0:
                        p.memset(xwb[:, :].rearrange("p (d c) -> p d c", d=16)[:, :, 0:3], 0.0)
                        for dt in range(NDT):
                            p.dma(xwb[:, dt * (QT + 3) + 3:(dt + 1) * (QT + 3)],
                                  src[dt * 128:(dt + 1) * 128, t0:t0 + QT])
                    else:
                        for dt in range(NDT):
                            p.dma(xwb[:, dt * (QT + 3):(dt + 1) * (QT + 3)],
                                  src[dt * 128:(dt + 1) * 128, t0 - 3:t0 + QT])
                    for dt in range(NDT):
                        base = dt * (QT + 3)
                        p.ts(cacc[:, :], xwb[:, base:base + QT],
                             cw[:, woff + dt * 4:woff + dt * 4 + 1],
                             cb[:, boff + dt:boff + dt + 1], ALU.mult, ALU.add)
                        for j in (1, 2, 3):
                            p.stt(cacc[:, :], xwb[:, base + j:base + j + QT],
                                  cw[:, woff + dt * 4 + j:woff + dt * 4 + j + 1],
                                  cacc[:, :], ALU.mult, ALU.add)
                        p.act(dst[:, dt * QT:(dt + 1) * QT], cacc[:, :], AF.Silu)
                if os.environ.get("KDUMPCONV") and b == 1 and q == 0:
                    for dt in range(16):
                        p.dma(p.aoh[dt * 128:(dt + 1) * 128, 0:512], qhat[:, dt * QT:(dt + 1) * QT])
                        p.dma(p.aoh[dt * 128:(dt + 1) * 128, 512:1024], khat[:, dt * QT:(dt + 1) * QT])
                        p.dma(p.aoh[dt * 128:(dt + 1) * 128, 1024:1536], vhat[:, dt * QT:(dt + 1) * QT])
                p.dma(sigq[:, 0:QT], p.sigh[:, t0:t0 + QT])
                p.dma(sigq[:, QT:2 * QT], p.sigh[:, BPC * S + t0:BPC * S + t0 + QT])
                A_, B_ = sigq[:, 0:QT], sigq[:, QT:2 * QT]
                LNA = scal[:, 0 * QT:1 * QT]; LL = scal[:, 1 * QT:2 * QT]
                LP = scal[:, 2 * QT:3 * QT]; LB2 = scal[:, 3 * QT:4 * QT]
                NL = scal[:, 4 * QT:5 * QT]; S1 = scal[:, 5 * QT:6 * QT]
                BC = scal[:, 6 * QT:7 * QT]; EL = scal[:, 7 * QT:8 * QT]
                p.act(LNA, A_, AF.Ln)
                p.vcopy(LL, LNA)
                cur, oth = LL, LNA
                for sft in (1, 2, 4, 8, 16, 32, 64):
                    c3_cur = cur.rearrange("p (c i) -> p c i", c=4)
                    c3_oth = oth.rearrange("p (c i) -> p c i", c=4)
                    p.vcopy(c3_oth[:, :, 0:sft], c3_cur[:, :, 0:sft])
                    p.tt(c3_oth[:, :, sft:CK], c3_cur[:, :, sft:CK],
                         c3_cur[:, :, 0:CK - sft], ALU.add)
                    cur, oth = oth, cur
                if cur is not LL:
                    p.vcopy(LL, cur)
                p.act(LNA, A_, AF.Ln)
                p.tt(LP, LL, LNA, ALU.subtract)
                p.act(LB2, B_, AF.Ln)
                p.tt(LB2, LP, LB2, ALU.add)
                p.ts(LB2, LB2, 1.0, LNC2, ALU.mult, ALU.add)
                p.ts(NL, LL, -1.0, None, ALU.mult)
                p.act(S1, LP, AF.Exp)
                p.tt(S1, B_, S1, ALU.mult)
                p.ts(S1, S1, float(-DSC * DSC), None, ALU.mult)
                p.ts(BC, B_, DSC, None, ALU.mult)
                p.act(EL, LL, AF.Exp)
                for ck4 in range(4 if not os.environ.get("KDUMPCONV") else 0):
                    cg = q * 4 + ck4
                    c0 = cg * CK
                    csl = slice(ck4 * CK, (ck4 + 1) * CK)
                    for k, qty in enumerate((NL, S1, BC, EL)):
                        p.tr(p.ps[3][:, :16], qty[:, csl], f32=True)
                        p.scopy(cols[:, k * 16:(k + 1) * 16], p.ps[3][:, :16])
                    p.tr(p.ps[3][0:1, 0:16], LL[:, ck4 * CK + CK - 1:ck4 * CK + CK], f32=True)
                    p.scopy(flat[:, 0:16], p.ps[3][0:1, 0:16])
                    p.mm(p.ps[3][:, :16], p.ones_row_f[:, :], flat[:, 0:16])
                    p.scopy(cols[:, 64:80], p.ps[3][:, :16])
                    p.tt(cols[:, 80:96], cols[:, 64:80], cols[:, 0:16], ALU.add)
                    p.act(cols[:, 80:96], cols[:, 80:96], AF.Exp)
                    p.act(cols[:, 64:80], cols[:, 64:80], AF.Exp)
                    p.dma(p.rowtmp[:, :], LB2[:, csl])
                    p.dma(flat[:, 0 * 16 * CK:1 * 16 * CK],
                          p.rowtmp.rearrange("h c -> (h c)")[None, :])
                    p.dma(p.rowtmp2[:, :], LL[:, csl])
                    p.dma(flat[:, 1 * 16 * CK:2 * 16 * CK],
                          p.rowtmp2.rearrange("h c -> (h c)")[None, :])
                    for h in range(16):
                        hq = h * QT + ck4 * CK
                        hsl = slice(hq, hq + CK)
                        zsl = slice(h * 128, (h + 1) * 128)
                        p.tr(p.ps_bf[:, :128], khat[:, hsl])
                        p.ts(ktm[:, zsl], p.ps_bf[:, :128], cols[:, 80 + h:80 + h + 1],
                             None, ALU.mult)
                        p.tr(p.ps_bf[:, :128], vhat[:, hsl])
                        p.ts(vtm[:, zsl], p.ps_bf[:, :128], cols[:, 32 + h:32 + h + 1],
                             None, ALU.mult)
                        p.mm(p.ps[0][:, :128], khat[:, hsl], khat[:, hsl])
                        p.mm(p.ps[1][:, :128], p.ones_row_f[:, :],
                             flat[:, h * CK:(h + 1) * CK])
                        p.act(expm[:, :], p.ps[1][:, :128], AF.Exp,
                              bias=cols[:, h:h + 1])
                        p.asel(expm[:, :], expm[:, :], [[1, 128]], ALU.is_gt, 0.0,
                               base=0, chmul=-1)
                        p.stt(mneg[:, :], p.ps[0][:, :128], -1.0, expm[:, :],
                              ALU.mult, ALU.mult)
                        p.mm(p.ps[2][:, :128], khat[:, hsl], zb[:, zsl])
                        p.stt(r0sb[:, :], p.ps[2][:, :128], cols[:, 16 + h:16 + h + 1],
                              vtm[:, zsl], ALU.mult, ALU.add)
                        p.vcopy(rcur[:, 0:128], r0sb[:, :])
                        for it in range(3):
                            cu = rcur[:, (it % 2) * 128:(it % 2) * 128 + 128]
                            nx = rcur[:, ((it + 1) % 2) * 128:((it + 1) % 2) * 128 + 128]
                            p.mm(p.ps[2][:, :128], mneg[:, :], cu)
                            p.tt(nx, p.ps[2][:, :128], r0sb[:, :], ALU.add)
                        Y = rcur[:, (3 % 2) * 128:(3 % 2) * 128 + 128]
                        p.mm(p.ps[2][:, :128], qhat[:, hsl], zb[:, zsl])
                        p.mm(p.ps[0][:, :128], khat[:, hsl], qhat[:, hsl])
                        p.mm(p.ps[1][:, :128], p.ones_row_f[:, :],
                             flat[:, 16 * CK + h * CK:16 * CK + (h + 1) * CK])
                        p.act(expm[:, :], p.ps[1][:, :128], AF.Exp,
                              bias=cols[:, h:h + 1])
                        p.asel(expm[:, :], expm[:, :], [[1, 128]], ALU.is_ge, 0.0,
                               base=0, chmul=-1)
                        p.tt(mneg[:, :], p.ps[0][:, :128], expm[:, :], ALU.mult)
                        p.mm(p.ps[0][:, :128], mneg[:, :], Y)
                        p.ts(otmp[:, :], p.ps[2][:, :128], cols[:, 48 + h:48 + h + 1],
                             None, ALU.mult)
                        p.tt(otm[:, zsl], otmp[:, :], p.ps[0][:, :128], ALU.add)
                        p.mm(p.ps[2][:, :128], ktm[:, zsl], Y)
                        p.stt(zf[:, zsl], zf[:, zsl], cols[:, 64 + h:64 + h + 1],
                              p.ps[2][:, :128], ALU.mult, ALU.add)
                        p.scopy(zb[:, zsl], zf[:, zsl])
                    if os.environ.get("KDUMPCHUNK") and b == 1 and cg == 0:
                        p.gdma(p.aoh[0:128, 0:2048], otm[:, :])
                        p.gdma(p.aoh[128:256, 0:96], cols[:, :])
                        p.gdma(p.aoh[300:301, 0:2048], flat[:, 0:2048])
                        p.gdma(p.aoh[301:302, 0:2048], flat[:, 2048:4096])
                        continue
                    o3 = otm[:, :].rearrange("p (h v) -> p h v", h=16)
                    p.rsum(st16[:, 0:16], o3)
                    p.ts(st16[:, 0:16], st16[:, 0:16], 1.0 / 128.0, None, ALU.mult)
                    p.act(osq[:, :], otm[:, :], AF.Square)
                    p.rsum(st16[:, 16:32], osq[:, :].rearrange("p (h v) -> p h v", h=16))
                    p.ts(st16[:, 16:32], st16[:, 16:32], 1.0 / 128.0, None, ALU.mult)
                    p.tt(st16[:, 32:48], st16[:, 0:16], st16[:, 0:16], ALU.mult)
                    p.tt(st16[:, 32:48], st16[:, 16:32], st16[:, 32:48], ALU.subtract)
                    p.act(st16[:, 32:48], st16[:, 32:48], AF.Sqrt, bias=LN_EPS)
                    p.recip(st16[:, 48:64], st16[:, 32:48])
                    p.ts(st16[:, 0:16], st16[:, 0:16], -1.0, None, ALU.mult)
                    for h in range(16):
                        zsl = slice(h * 128, (h + 1) * 128)
                        p.ts(otm[:, zsl], otm[:, zsl], st16[:, h:h + 1],
                             st16[:, 48 + h:48 + h + 1], ALU.add, ALU.mult)
                    p.tt(otm[:, :], otm[:, :], lnwb[:, :], ALU.mult)
                    p.tt(otm[:, :], otm[:, :], lnbb[:, :], ALU.add)
                    p.dma(gsb[:, :], p.xwg_tm[b * S + c0:b * S + c0 + CK, :])
                    p.act(gss[:, :], gsb[:, :], AF.Sigmoid)
                    p.tt(ogbf[:, :], otm[:, :], gss[:, :], ALU.mult)
                    for h in range(16):
                        p.tr(p.ps_bf[:, :128], ogbf[:, h * 128:(h + 1) * 128])
                        p.scopy(ofm[:, :], p.ps_bf[:, :128])
                        p.dma(p.aoh[h * 128:(h + 1) * 128, c0:c0 + CK], ofm[:, :])
        with self.nc.sbuf_tensor(f"zb0_{b}", [128, 16], F32) as zb0:
            p.memset(zb0[:, :], 0.0)
            self._proj_epilogue(wname="wo1", act_hbm=p.aoh, act_cols=S, b=b,
                                res_in=p.x2f, outf=p.x3f, outb=p.x3b,
                                bias_sb=zb0, tag=f"l1a{b}")

    def phase_l1_ffn(self):
        p = self
        nc = self.nc
        T = BPC * S
        with ExitStack() as stk:
            E = stk.enter_context
            xa = E(nc.sbuf_tensor("p5_x", [128, NDT * T], BF16))
            wgb = E(nc.sbuf_tensor("p5_wg", [128, NDT * 512], BF16))
            wub = E(nc.sbuf_tensor("p5_wu", [128, NDT * 512], BF16))
            slf = E(nc.sbuf_tensor("p5_sl", [128, 512], F32))
            hidf = E(nc.sbuf_tensor("p5_h", [128, 512], F32))
            absf = E(nc.sbuf_tensor("p5_a", [128, 512], F32))
            lad = E(nc.sbuf_tensor("p5_l", [128, 512], F32))
            tau = E(nc.sbuf_tensor("p5_t", [128, 2 * 128], F32))
            mk = E(nc.sbuf_tensor("p5_m", [128, 512], F32))
            hsp = E(nc.sbuf_tensor("p5_s", [128, 512], BF16))
            stag = E(nc.sbuf_tensor("p5_st", [128, 4 * 1024], BF16))
            for dt in range(NDT):
                p.dma(xa[:, dt * T:(dt + 1) * T], p.x3b[dt * 128:(dt + 1) * 128, :])
            for hb in range(16):
                for dt in range(NDT):
                    p.gdma(wgb[:, dt * 512:(dt + 1) * 512],
                           p.inp["wgate"][dt * 128:(dt + 1) * 128, hb * 512:(hb + 1) * 512])
                    p.gdma(wub[:, dt * 512:(dt + 1) * 512],
                           p.inp["wup"][dt * 128:(dt + 1) * 128, hb * 512:(hb + 1) * 512])
                for tt in range(T // 128):
                    for dt in range(NDT):
                        p.mm(p.ps[0][:, :], xa[:, dt * T + tt * 128:dt * T + (tt + 1) * 128],
                             wgb[:, dt * 512:(dt + 1) * 512],
                             start=(dt == 0), stop=(dt == NDT - 1))
                    p.act(slf[:, :], p.ps[0][:, :], AF.Silu)
                    for dt in range(NDT):
                        p.mm(p.ps[1][:, :], xa[:, dt * T + tt * 128:dt * T + (tt + 1) * 128],
                             wub[:, dt * 512:(dt + 1) * 512],
                             start=(dt == 0), stop=(dt == NDT - 1))
                    p.tt(hidf[:, :], p.ps[1][:, :], slf[:, :], ALU.mult)
                    p.act(absf[:, :], hidf[:, :], AF.Abs)
                    p.tt(lad[:, 0:512:4], absf[:, 0:512:4], absf[:, 1:512:4], ALU.max)
                    p.tt(lad[:, 1:512:4], absf[:, 2:512:4], absf[:, 3:512:4], ALU.max)
                    p.tt(lad[:, 2:512:4], absf[:, 0:512:4], absf[:, 1:512:4], ALU.min)
                    p.tt(lad[:, 3:512:4], absf[:, 2:512:4], absf[:, 3:512:4], ALU.min)
                    p.tt(tau[:, 0:128], lad[:, 0:512:4], lad[:, 1:512:4], ALU.min)
                    p.tt(tau[:, 128:256], lad[:, 2:512:4], lad[:, 3:512:4], ALU.max)
                    p.tt(tau[:, 0:128], tau[:, 0:128], tau[:, 128:256], ALU.max)
                    for j in range(4):
                        p.tt(mk[:, j:512:4], absf[:, j:512:4], tau[:, 0:128], ALU.is_ge)
                    p.tt(hsp[:, :], hidf[:, :], mk[:, :], ALU.mult)
                    for j in range(4):
                        p.tr(p.ps_bf[:, :128], hsp[:, j * 128:(j + 1) * 128])
                        p.scopy(stag[:, j * 1024 + (tt % 8) * 128:j * 1024 + (tt % 8 + 1) * 128],
                                p.ps_bf[:, :128])
                    if tt % 8 == 7:
                        for j in range(4):
                            p.dma(p.hid1[hb * 4 + j, :, (tt - 7) * 128:(tt + 1) * 128],
                                  stag[:, j * 1024:(j + 1) * 1024])
        self._down_proj(p.hid1, p.inp["wdown"], p.part1, p.x3f, self.out, None,
                        tag="l1f", wcast=True)

    def finish(self):
        p = self
        if self.dbg is not None:
            src = {"x1f": p.x1f, "x2f": p.x2f, "x3f": p.x3f, "w2eff": p.w2eff, "sigh": p.sigh,
                   "aoh": p.aoh, "x1b": p.x1b, "hid0": p.hid0, "hid1": p.hid1,
                   "xwq": p.xwq, "xwk": p.xwk, "xwv": p.xwv,
                   "xwg_tm": p.xwg_tm, "part0": p.part0}[self.dbg]
            flat_s = src.rearrange(" ".join(f"a{i}" for i in range(len(src.shape))) + " -> (" +
                                   " ".join(f"a{i}" for i in range(len(src.shape))) + ")") \
                if False else src
            p.dma(self.dbg_out[...], src[...])
        if self.mode == "chain":
            total = self.cnt
            self.ops["sync"].append(lambda h, n=total: h.wait_ge(self.chain, n))
        else:
            tots = [(self.esem[e], self.emitted[e]) for e in self.ops if self.emitted[e] > 0]
            tots += [(self.esem[k], 16 * u) for k, u in self.dslot_use.items() if u > 0]
            def fin(h, tots=tuple(tots)):
                for smm, v in tots:
                    h.wait_ge(smm, v)
            self.ops["sync"].append(fin)
        with self.nc.Block() as block:
            for eng in ("tensor", "vector", "scalar", "gpsimd", "sync"):
                def mk(e):
                    def f(h):
                        for op in self.ops[e]:
                            op(h)
                    return f
                getattr(block, eng)(mk(eng))
        self.st.close()
        return self.nc


def build_program(upto=99, dbg=None, mode=None):
    import os
    if mode is None:
        mode = os.environ.get("KMODE", "auto")
    relax = set(os.environ.get("KRELAX", "c,w,1,2,3,4,5").split(","))
    p = Prog(upto=upto, dbg=dbg, mode=mode)
    p.alloc_consts()
    p.kbar_on = "c" not in relax
    p.phase_consts()
    if upto >= 0:
        p.kbar_on = "w" not in relax
        p.phase_w2eff()
    if upto >= 1:
        p.kbar_on = "1" not in relax
        for b in range(BPC):
            p.phase_l0_attn(b)
    if upto >= 2:
        p.kbar_on = "2" not in relax
        p.phase_l0_ffn()
    if upto >= 3:
        p.kbar_on = "3" not in relax
        p.phase_l1_proj()
    if upto >= 4:
        p.kbar_on = "4" not in relax
        for b in range(BPC):
            p.phase_l1_delta(b)
    if upto >= 5:
        p.kbar_on = "5" not in relax
        p.phase_l1_ffn()
    p.kbar_on = False
    nc = p.finish()
    return nc


def host_layout(inputs):
    f32 = np.float32
    qkv_w = np.asarray(inputs["L0_attn_qkv_w"], f32)
    qkv_b = np.asarray(inputs["L0_attn_qkv_b"], f32)
    d = {}
    d["wq0"] = relay_lhsT(np.ascontiguousarray(qkv_w[0:DM].T), NDT, 512)
    d["wk0"] = relay_lhsT(np.ascontiguousarray(qkv_w[DM:2 * DM].T), NDT, 512)
    d["wv0"] = relay_lhsT(np.ascontiguousarray(qkv_w[2 * DM:3 * DM].T), NDT, 512)
    d["qb0"] = np.ascontiguousarray(qkv_b[0:DM].reshape(16, 128).T)
    d["kb0"] = np.ascontiguousarray(qkv_b[DM:2 * DM].reshape(16, 128).T)
    d["vb0_row"] = np.ascontiguousarray(qkv_b[2 * DM:3 * DM].reshape(1, DM))
    d["wo0"] = relay_lhsT(np.ascontiguousarray(np.asarray(inputs["L0_attn_out_w"], f32).T), NDT, 128)
    d["ob0"] = np.ascontiguousarray(np.asarray(inputs["L0_attn_out_b"], f32).reshape(16, 128).T)
    d["alibi_rep"] = np.ascontiguousarray(
        np.repeat(np.asarray(inputs["L0_attn_alibi_slopes"], f32), 16).reshape(1, 256))
    W1 = np.asarray(inputs["L0_ffn_W1"], f32); W3 = np.asarray(inputs["L0_ffn_W3"], f32)
    d["w1t"] = np.ascontiguousarray(
        W1.reshape(NHT, 128, NDT, 128).transpose(0, 3, 2, 1).reshape(NHT, 128, DM))
    d["w3t"] = np.ascontiguousarray(
        W3.reshape(NHT, 128, NDT, 128).transpose(0, 3, 2, 1).reshape(NHT, 128, DM))
    d["w2t"] = np.ascontiguousarray(np.asarray(inputs["L0_ffn_W2"], f32).T)
    d["h128"] = hada(128).astype(BF)
    d["h64"] = hada(64).astype(BF)
    d["identity"] = np.eye(128, dtype=np.float32).astype(BF)
    d["identity_f"] = np.eye(128, dtype=np.float32)
    d["wq1"] = relay_lhsT(np.asarray(inputs["L1_attn_Wq"], f32), NDT, 128)
    d["wk1"] = relay_lhsT(np.asarray(inputs["L1_attn_Wk"], f32), NDT, 128)
    d["wv1"] = relay_lhsT(np.asarray(inputs["L1_attn_Wv"], f32), NDT, 128)
    d["wg1"] = np.ascontiguousarray(np.asarray(inputs["L1_attn_Wg"], f32))
    d["wa1"] = np.ascontiguousarray(
        np.asarray(inputs["L1_attn_Wa"], f32).reshape(16, 128, 16).transpose(1, 0, 2).reshape(128, 256))
    d["wb1"] = np.ascontiguousarray(
        np.asarray(inputs["L1_attn_Wb"], f32).reshape(16, 128, 16).transpose(1, 0, 2).reshape(128, 256))
    d["ba1"] = np.ascontiguousarray(np.asarray(inputs["L1_attn_ba"], f32).reshape(16, 1))
    d["bb1"] = np.ascontiguousarray(np.asarray(inputs["L1_attn_bb"], f32).reshape(16, 1))
    d["wo1"] = relay_lhsT(np.asarray(inputs["L1_attn_Wo"], f32), NDT, 128)
    for nm, key in (("qcw", "L1_attn_qcw"), ("kcw", "L1_attn_kcw"), ("vcw", "L1_attn_vcw")):
        w = np.asarray(inputs[key], f32)
        d[nm] = np.ascontiguousarray(w.reshape(16, 128, 4).transpose(1, 0, 2).reshape(128, 64))
    for nm, key in (("qcb", "L1_attn_qcb"), ("kcb", "L1_attn_kcb"), ("vcb", "L1_attn_vcb")):
        d[nm] = np.ascontiguousarray(np.asarray(inputs[key], f32).reshape(16, 128).T)
    d["lnw_rep"] = np.ascontiguousarray(np.tile(np.asarray(inputs["L1_attn_ln_w"], f32), 16).reshape(1, DM))
    d["lnb_rep"] = np.ascontiguousarray(np.tile(np.asarray(inputs["L1_attn_ln_b"], f32), 16).reshape(1, DM))
    d["wgate"] = np.ascontiguousarray(np.asarray(inputs["L1_ffn_w_gate"], f32))
    d["wup"] = np.ascontiguousarray(np.asarray(inputs["L1_ffn_w_up"], f32))
    d["wdown"] = np.ascontiguousarray(np.asarray(inputs["L1_ffn_w_down"], f32))
    return d


def kernel(**inputs):
    x = np.asarray(inputs["L0_attn_hidden_states"], np.float32)
    shared = host_layout(inputs)
    nc = build_program()
    in_maps = []
    for c in range(8):
        m = dict(shared)
        xs = np.ascontiguousarray(np.transpose(x[2 * c:2 * c + 2], (0, 2, 1)))
        m["xin"] = xs
        in_maps.append(m)
    res = run_bass_kernel_spmd(nc, in_maps, core_ids=list(range(8)))
    outs = []
    for c in range(8):
        o = np.asarray(res.results[c]["out"])
        outs.append(np.transpose(o, (0, 2, 1)))
    return np.concatenate(outs, axis=0).astype(np.float32)


if __name__ == "__main__":
    pass


# version 06
# speedup vs baseline: 2.6274x; 1.0198x over previous
import os
import numpy as np
import ml_dtypes
import concourse.bass as bass
import concourse.mybir as mybir
from concourse.bass_utils import run_bass_kernel_spmd
from contextlib import ExitStack

F32 = mybir.dt.float32
BF16 = mybir.dt.bfloat16
AF = mybir.ActivationFunctionType
ALU = mybir.AluOpType
AXX = mybir.AxisListType.X

BPC = 2
S = 2048
DM = 2048
NDT = DM // 128
NH = 16
HD = 128
H0 = 8192
NHT = H0 // 128
CK = 128
NCK = S // CK
INV_SQRT_HD = float(1.0 / np.sqrt(128.0))
LN_EPS = 1e-5
DSC = float(128.0 ** -0.5)
BF = ml_dtypes.bfloat16


def hada(n):
    H = np.array([[1.0]], dtype=np.float32)
    while H.shape[0] < n:
        H = np.block([[H, H], [H, -H]])
    return H


def relay_lhsT(w, din_tiles, dout_block):
    din, dout = w.shape
    nb = dout // dout_block
    assert din == din_tiles * 128
    r = w.reshape(din_tiles, 128, nb, dout_block).transpose(2, 1, 0, 3)
    return np.ascontiguousarray(r.reshape(nb, 128, din_tiles * dout_block))


class Prog:
    def __init__(self, upto=99, dbg=None, mode="auto"):
        self.upto = upto
        self.dbg = dbg
        nc = bass.Bass("TRN2", target_bir_lowering=False, debug=False)
        self.nc = nc
        self.cnt = 0
        self.mode = mode
        self.ops = {e: [] for e in ("tensor", "vector", "scalar", "gpsimd", "sync")}
        self.emitted = {e: 0 for e in self.ops}
        self.waited = {}
        self.acc = {}
        self.floor = {}
        self.input_names = set()
        self.psum_names = set()
        import os
        se = os.environ.get("KSE", "tensor")
        self.same_engine_ok = set(se.split(",")) if se else set()
        self.bar = {}
        orig_sbuf = nc.sbuf_tensor
        prog = self
        class _SbGuard:
            def __init__(self, cm):
                self.cm = cm
            def __enter__(self):
                return self.cm.__enter__()
            def __exit__(self, *a):
                r = self.cm.__exit__(*a)
                bar = dict(prog.bar)
                for e2 in prog.ops:
                    if prog.emitted[e2] > 0:
                        bar[e2] = prog.emitted[e2]
                for k2, u in prog.dslot_use.items():
                    if u > 0:
                        bar[k2] = 16 * u
                prog.bar = bar
                return r
        def wrapped(*args, **kw):
            return _SbGuard(orig_sbuf(*args, **kw))
        nc.sbuf_tensor = wrapped
        self.st = ExitStack()
        self._declare_dram()

    MAXREC = 160

    BIGP = 1 << 40
    BIGF = 1 << 60

    def _iv(self, ap):
        try:
            dims = list(ap.ap)
            off = ap.offset
            t = ap.tensor
            rowlen = 1
            for d in t.shape[1:]:
                rowlen *= int(d)
            if dims and rowlen > 0 and dims[0][0] == rowlen:
                p0 = off // rowlen
                f0 = off % rowlen
                pn = dims[0][1]
                fspan = sum(abs(st) * (cn - 1) for st, cn in dims[1:])
                if f0 + fspan < rowlen:
                    return (p0, p0 + pn - 1, f0, f0 + fspan)
            lo = off
            span = sum(abs(st) * (cn - 1) for st, cn in dims)
            return (0, self.BIGP, lo, lo + span)
        except Exception:
            return (0, self.BIGP, 0, self.BIGF)

    def _scan(self, eng, isdma, ap, want_writes_only, need):
        t = ap.tensor
        name = t.name
        if name in self.input_names:
            return
        recs = self.acc.get(name)
        flo = self.floor.get(name)
        if flo:
            for e2, c2 in flo.items():
                if not (e2 == eng and not isdma):
                    if need.get(e2, 0) < c2:
                        need[e2] = c2
        if not recs:
            return
        if name in self.psum_names:
            rect = (0, self.BIGP, 0, self.BIGF)
        else:
            rect = self._iv(ap)
        plo, phi, flo, fhi = rect
        for (w2, e2, c2, r2, dma2) in recs:
            if want_writes_only and not w2:
                continue
            if fhi < r2[2] or r2[3] < flo or phi < r2[0] or r2[1] < plo:
                continue
            if e2 == eng and not isdma and not dma2 and eng in self.same_engine_ok:
                continue
            if need.get(e2, 0) < c2:
                need[e2] = c2

    def _rec(self, name, entry):
        lst = self.acc.setdefault(name, [])
        lst.append(entry)
        if len(lst) > self.MAXREC:
            drop = lst[: len(lst) - self.MAXREC]
            del lst[: len(lst) - self.MAXREC]
            flo = self.floor.setdefault(name, {})
            for (w2, e2, c2, r2, dma2) in drop:
                if flo.get(e2, 0) < c2:
                    flo[e2] = c2

    def _emit(self, eng, fn, reads=(), writes=(), inc=1):
        if self.mode == "chain":
            at = self.cnt
            def run(h, at=at, fn=fn, inc=inc):
                h.wait_ge(self.chain, at)
                r = fn(h)
                r.then_inc(self.chain, inc)
            self.ops[eng].append(run)
            self.cnt += inc
            return
        import os
        isdma = inc == 16
        need = {}
        for ap in writes:
            self._scan(eng, isdma, ap, False, need)
        for ap in reads:
            self._scan(eng, isdma, ap, True, need)
        if getattr(self, "kbar_on", False):
            for e2 in self.ops:
                if self.emitted[e2] > 0 and (e2 != eng or os.environ.get("KSELF")):
                    need[e2] = self.emitted[e2]
            for k2, u in self.dslot_use.items():
                if u > 0:
                    need[k2] = max(need.get(k2, 0), 16 * u)
        for k2, v2 in self.bar.items():
            if k2 != eng and need.get(k2, 0) < v2:
                need[k2] = v2
        if isdma:
            q = eng
            skey = ("d", q, self.dma_idx[q] % self.NSLOT)
            self.dma_idx[q] += 1
            use = self.dslot_use[skey]
            self.dslot_use[skey] = use + 1
            cnt_after = 16 * (use + 1)
            ekey = skey
            if use > 0:
                need[skey] = max(need.get(skey, 0), 16 * use)
        else:
            ekey = eng
            cnt_after = self.emitted[eng] + inc
            self.emitted[eng] = cnt_after
        waits = []
        for e2, v in need.items():
            key = (ekey, e2) if isdma else (eng, e2)
            if self.waited.get(key, 0) >= v:
                continue
            self.waited[key] = v
            waits.append((self.esem[e2], v))
        mysem = self.esem[ekey]
        def run(h, waits=tuple(waits), fn=fn, mysem=mysem, inc=inc):
            for smm, v in waits:
                h.wait_ge(smm, v)
            r = fn(h)
            r.then_inc(mysem, inc)
        self.ops[eng].append(run)
        for ap in writes:
            name = ap.tensor.name
            if name in self.input_names:
                continue
            r = (0, self.BIGP, 0, self.BIGF) if name in self.psum_names else self._iv(ap)
            self._rec(name, (True, ekey, cnt_after, r, isdma))
        for ap in reads:
            name = ap.tensor.name
            if name in self.input_names:
                continue
            r = (0, self.BIGP, 0, self.BIGF) if name in self.psum_names else self._iv(ap)
            self._rec(name, (False, ekey, cnt_after, r, isdma))

    def t(self, fn, reads=(), writes=()): self._emit("tensor", fn, reads, writes)
    def v(self, fn, reads=(), writes=()): self._emit("vector", fn, reads, writes)
    def s(self, fn, reads=(), writes=()): self._emit("scalar", fn, reads, writes)
    def g(self, fn, reads=(), writes=()): self._emit("gpsimd", fn, reads, writes)
    def dma(self, out, in_, q="sync"):
        self._emit(q, lambda h, o=out, i=in_: h.dma_start(out=o, in_=i),
                   reads=(in_,), writes=(out,), inc=16)
    def gdma(self, out, in_):
        self.dma(out, in_, q="gpsimd")

    def mm(self, out, lhsT, rhs, start=True, stop=True):
        self.t(lambda t, o=out, l=lhsT, r=rhs, st=start, sp=stop:
               t.matmul(o, l, r, start=st, stop=sp),
               reads=(lhsT, rhs), writes=(out,))
    def tr(self, out, in_, f32=False):
        k = in_.shape[0]
        ident = self.ident_f if f32 else self.ident_bf
        self.t(lambda t, o=out, i=in_, g=ident, kk=k: t.transpose(o, i, g[:kk, :kk]),
               reads=(in_, ident[:, :]), writes=(out,))
    def act(self, out, in_, func, bias=0.0, scale=1.0, accum_out=None):
        rd = [in_]
        if not isinstance(bias, float):
            rd.append(bias)
        wr = [out] + ([accum_out] if accum_out is not None else [])
        self.s(lambda sc, o=out, i=in_, f=func, b=bias, s_=scale, a=accum_out:
               sc.activation(o, i, f, bias=b, scale=s_, accum_out=a),
               reads=tuple(rd), writes=tuple(wr))
    def scopy(self, out, in_):
        self.s(lambda sc, o=out, i=in_: sc.copy(o, i), reads=(in_,), writes=(out,))
    def vcopy(self, out, in_):
        self.v(lambda v, o=out, i=in_: v.tensor_copy(o, i), reads=(in_,), writes=(out,))
    def tt(self, out, in0, in1, op):
        self.v(lambda v, o=out, a=in0, b=in1, p=op: v.tensor_tensor(o, a, b, p),
               reads=(in0, in1), writes=(out,))
    def ts(self, out, in0, s1, s2=None, op0=ALU.mult, op1=ALU.bypass, accum_out=None):
        rd = [in0]
        if not isinstance(s1, float):
            rd.append(s1)
        if s2 is not None and not isinstance(s2, float):
            rd.append(s2)
        wr = [out] + ([accum_out] if accum_out is not None else [])
        self.v(lambda v, o=out, i=in0, a=s1, b=s2, p0=op0, p1=op1, ac=accum_out:
               v.tensor_scalar(o, i, a, b, p0, p1, accum_out=ac),
               reads=tuple(rd), writes=tuple(wr))
    def stt(self, out, in0, s1, in1, op0, op1):
        rd = [in0, in1]
        if not isinstance(s1, float):
            rd.append(s1)
        self.v(lambda v, o=out, i=in0, a=s1, j=in1, p0=op0, p1=op1:
               v.scalar_tensor_tensor(o, i, a, j, p0, p1),
               reads=tuple(rd), writes=(out,))
    def recip(self, out, in_):
        self.v(lambda v, o=out, i=in_: v.reciprocal(o, i), reads=(in_,), writes=(out,))
    def rsum(self, out, in_):
        self.v(lambda v, o=out, i=in_: v.reduce_sum(o, i, AXX),
               reads=(in_,), writes=(out,))
    def memset(self, ap, val):
        self.g(lambda g, a=ap, c=val: g.memset(a, c), reads=(), writes=(ap,))
    def asel(self, out, in_, pattern, cmp, fill, base=0, chmul=1):
        self.g(lambda g, o=out, i=in_, p=pattern, c=cmp, f=fill, b=base, m=chmul:
               g.affine_select(o, i, p, c, f, base=b, channel_multiplier=m),
               reads=(in_,), writes=(out,))

    def _declare_dram(self):
        nc = self.nc
        self.inp = {}
        def I(name, shape, dt=F32):
            self.inp[name] = nc.dram_tensor(name, list(shape), dt, kind="ExternalInput").ap()
            self.input_names.add(name)
            return self.inp[name]
        I("xin", (BPC, DM, S))
        I("wq0", (4, 128, NDT * 512)); I("wk0", (4, 128, NDT * 512)); I("wv0", (4, 128, NDT * 512))
        I("qb0", (128, 16)); I("kb0", (128, 16)); I("vb0_row", (1, DM))
        I("wo0", (16, 128, NDT * 128)); I("ob0", (128, 16))
        I("alibi_rep", (1, 256))
        I("w1t", (NHT, 128, DM)); I("w3t", (NHT, 128, DM)); I("w2t", (H0, DM))
        I("h128", (128, 128), BF16); I("h64", (64, 64), BF16); I("identity", (128, 128), BF16)
        I("identity_f", (128, 128), F32)
        I("wq1", (16, 128, NDT * 128)); I("wk1", (16, 128, NDT * 128))
        I("wv1", (16, 128, NDT * 128)); I("wg1", (DM, DM))
        I("wa1", (128, 256)); I("wb1", (128, 256)); I("ba1", (16, 1)); I("bb1", (16, 1))
        I("wo1", (16, 128, NDT * 128))
        I("qcw", (128, 64)); I("kcw", (128, 64)); I("vcw", (128, 64))
        I("qcb", (128, 16)); I("kcb", (128, 16)); I("vcb", (128, 16))
        I("lnw_rep", (1, DM)); I("lnb_rep", (1, DM))
        I("wgate", (DM, H0)); I("wup", (DM, H0)); I("wdown", (H0, DM))

        D = lambda name, shape, dt: nc.dram_tensor(name, list(shape), dt, kind="Internal").ap()
        self.w2eff = D("w2eff", (H0, DM), BF16)
        self.bh = D("bh", (NHT, 128, DM), BF16)
        self.aoh = D("aoh", (DM, S), BF16)
        self.x1f = D("x1f", (BPC, DM, S), F32)
        self.x1b = D("x1b", (DM, BPC * S), BF16)
        self.hid0 = D("hid0", (NHT, 128, BPC * S), BF16)
        self.part0 = D("part0", (NDT, 128, BPC * S), F32)
        self.x2f = D("x2f", (BPC, DM, S), F32)
        self.x2b = D("x2b", (DM, BPC * S), BF16)
        self.xwq = D("xwq", (DM, BPC * S), BF16)
        self.xwk = D("xwk", (DM, BPC * S), BF16)
        self.xwv = D("xwv", (DM, BPC * S), BF16)
        self.xwg_tm = D("xwg_tm", (BPC * S, DM), BF16)
        self.x3f = D("x3f", (BPC, DM, S), F32)
        self.x3b = D("x3b", (DM, BPC * S), BF16)
        self.hid1 = D("hid1", (NHT, 128, BPC * S), BF16)
        self.part1 = D("part1", (NDT, 128, BPC * S), F32)
        self.sigh = D("sigh", (16, 2 * BPC * S), F32)
        self.rowtmp = D("rowtmp", (16, CK), F32)
        self.rowtmp2 = D("rowtmp2", (16, CK), F32)

        self.out = nc.dram_tensor("out", [BPC, DM, S], F32, kind="ExternalOutput").ap()
        if self.dbg is not None:
            shp = {"x1f": (BPC, DM, S), "x2f": (BPC, DM, S), "x3f": (BPC, DM, S),
                   "sigh": (16, 2 * BPC * S),
                   "w2eff": (H0, DM), "aoh": (DM, S), "x1b": (DM, BPC * S),
                   "hid0": (NHT, 128, BPC * S), "hid1": (NHT, 128, BPC * S),
                   "xwq": (DM, BPC * S), "xwk": (DM, BPC * S), "xwv": (DM, BPC * S),
                   "xwg_tm": (BPC * S, DM),
                   "part0": (NDT, 128, BPC * S)}[self.dbg]
            dt = BF16 if self.dbg in ("w2eff", "aoh", "x1b", "hid0", "hid1", "xwq", "xwk", "xwv", "xwg_tm") else F32
            self.dbg_out = nc.dram_tensor("dbg", list(shp), dt, kind="ExternalOutput").ap()

    def alloc_consts(self):
        E = self.st.enter_context
        nc = self.nc
        self.chain = E(nc.semaphore("chain"))
        self.esem = {e: E(nc.semaphore(f"sem_{e}")) for e in self.ops}
        self.NSLOT = 24
        for q in ("sync", "gpsimd"):
            for si in range(self.NSLOT):
                self.esem[("d", q, si)] = E(nc.semaphore(f"ds_{q}_{si}"))
        self.dslot_use = {("d", q, si): 0 for q in ("sync", "gpsimd")
                          for si in range(self.NSLOT)}
        self.dma_idx = {"sync": 0, "gpsimd": 0}
        self.ident_bf = E(nc.sbuf_tensor("ident_bf", [128, 128], BF16))
        self.ident_f = E(nc.sbuf_tensor("ident_f", [128, 128], F32))
        self.h128_sb = E(nc.sbuf_tensor("h128_sb", [128, 128], BF16))
        self.h64_sb = E(nc.sbuf_tensor("h64_sb", [64, 64], BF16))
        self.ones_col_bf = E(nc.sbuf_tensor("ones_col_bf", [128, 1], BF16))
        self.ones_col_f = E(nc.sbuf_tensor("ones_col_f", [128, 1], F32))
        self.ones_row_f = E(nc.sbuf_tensor("ones_row_f", [1, 128], F32))
        self.iota_row = E(nc.sbuf_tensor("iota_row", [1, 128], F32))
        self.negoff_row = E(nc.sbuf_tensor("negoff_row", [1, 256], F32))
        self.slope_rep = E(nc.sbuf_tensor("slope_rep", [1, 256], F32))
        self.r2_row = E(nc.sbuf_tensor("r2_row", [1, 256], F32))
        self.bias_tab = E(nc.sbuf_tensor("bias_tab", [128, 256], F32))
        self.qb_sb = E(nc.sbuf_tensor("qb_sb", [128, 16], F32))
        self.kb_sb = E(nc.sbuf_tensor("kb_sb", [128, 16], F32))
        self.ob_sb = E(nc.sbuf_tensor("ob_sb", [128, 16], F32))
        self.ps = [E(nc.psum_tensor(f"ps{i}", [128, 512], F32)) for i in range(4)]
        self.ps_bf = E(nc.psum_tensor("ps_bf", [128, 1024], BF16))
        self.ps_row = E(nc.psum_tensor("ps_row", [128, 512], F32))
        for i in range(4):
            self.psum_names.add(f"ps{i}")
        self.psum_names.update(("ps_bf", "ps_row", "ps_row2", "ps4"))
        self.epsc = E(nc.sbuf_tensor("epsc", [128, 1], F32))
        nc.const_aps.register(F32, LN_EPS, self.epsc[:, :])
        self.ps_row2 = E(nc.psum_tensor("ps_row2", [128, 512], F32))
        self.ps4 = E(nc.psum_tensor("ps4", [128, 512], F32))

    def phase_consts(self):
        p = self
        p.dma(p.ident_bf[:, :], p.inp["identity"][:, :])
        p.dma(p.ident_f[:, :], p.inp["identity_f"][:, :])
        p.dma(p.h128_sb[:, :], p.inp["h128"][:, :])
        p.dma(p.h64_sb[:, :], p.inp["h64"][:, :])
        p.dma(p.slope_rep[:, :], p.inp["alibi_rep"][:, :])
        p.dma(p.qb_sb[:, :], p.inp["qb0"][:, :])
        p.dma(p.kb_sb[:, :], p.inp["kb0"][:, :])
        p.dma(p.ob_sb[:, :], p.inp["ob0"][:, :])
        p.memset(p.epsc[:, :], LN_EPS)
        p.memset(p.ones_col_bf[:, :], 1.0)
        p.memset(p.ones_col_f[:, :], 1.0)
        p.memset(p.ones_row_f[:, :], 1.0)
        p.g(lambda g: g.iota(p.iota_row[:, :], [[1, 128]], base=0, channel_multiplier=0,
                             allow_small_or_imprecise_dtypes=True),
            writes=(p.iota_row[:, :],))
        p.g(lambda g: g.iota(p.negoff_row[:, :], [[0, 16], [-128, 16]], base=-64,
                             channel_multiplier=0, allow_small_or_imprecise_dtypes=True),
            writes=(p.negoff_row[:, :],))
        p.tt(p.r2_row[:, :], p.slope_rep[:, :], p.negoff_row[:, :], ALU.mult)
        p.mm(p.ps[0][:, :256], p.iota_row[:, :], p.slope_rep[:, :], start=True, stop=False)
        p.mm(p.ps[0][:, :256], p.ones_row_f[:, :], p.r2_row[:, :], start=False, stop=True)
        p.scopy(p.bias_tab[:, :], p.ps[0][:, :256])

    def bcast_row_2048(self, dst, srcname, row_tmp):
        p = self
        p.dma(row_tmp[:, :], p.inp[srcname][:, :])
        for n in range(4):
            p.mm(p.ps[1][:, :], p.ones_row_f[:, :], row_tmp[:, n * 512:(n + 1) * 512])
            p.scopy(dst[:, n * 512:(n + 1) * 512], p.ps[1][:, :])

    def phase_w2eff(self):
        p = self
        E = self.st
        with (self.nc.sbuf_tensor("w2tile", [128, DM], BF16) as w2tile,
              self.nc.sbuf_tensor("bh_sb", [128, DM], BF16) as bh_sb,
              self.nc.sbuf_tensor("bsteb", [64, 4 * DM], BF16) as bsteb,
              self.nc.sbuf_tensor("w2e_sb", [64, 4 * DM], BF16) as w2e_sb):
            for hi in range(NHT):
                p.gdma(w2tile[:, :], p.inp["w2t"][hi * 128:(hi + 1) * 128, :])
                for n in range(4):
                    p.mm(p.ps[0][:, :], p.h128_sb[:, :], w2tile[:, n * 512:(n + 1) * 512])
                    p.scopy(bh_sb[:, n * 512:(n + 1) * 512], p.ps[0][:, :])
                p.dma(p.bh[hi, :, :], bh_sb[:, :])
            bh4 = p.bh.rearrange("a p d -> a (p d)")
            w2r = p.w2eff.rearrange("(a b) d -> a b d", b=128)
            for lo0 in range(0, 128, 4):
                p.dma(bsteb[:, :], bh4[:, lo0 * DM:(lo0 + 4) * DM])
                for j in range(4):
                    for n in range(4):
                        sl = slice(j * DM + n * 512, j * DM + (n + 1) * 512)
                        p.mm(p.ps[0][:64, :], p.h64_sb[:, :], bsteb[:, sl])
                        p.act(w2e_sb[:64, sl], p.ps[0][:64, :], AF.Identity,
                              scale=float(1.0 / np.sqrt(8192.0)))
                    p.dma(w2r[:, lo0 + j, :], w2e_sb[:64, j * DM:(j + 1) * DM])

    def phase_l0_attn(self, b):
        p = self
        nc = self.nc
        with (nc.sbuf_tensor(f"xbf{b}", [128, NDT * S], BF16) as xbf,
              nc.sbuf_tensor(f"wbuf{b}", [128, NDT * 512], BF16) as wbuf,
              nc.sbuf_tensor(f"wbuf2{b}", [128, NDT * 512], BF16) as wbuf2,
              nc.sbuf_tensor(f"qg{b}", [128, 4 * S], BF16) as qg,
              nc.sbuf_tensor(f"kg{b}", [128, 4 * S], BF16) as kg,
              nc.sbuf_tensor(f"vtm{b}", [128, 16 * 512], BF16) as vtm,
              nc.sbuf_tensor(f"probs{b}", [128, 2 * 16 * 128], BF16) as probs0,
              nc.sbuf_tensor(f"rec_row{b}", [1, 256], F32) as rec_row,
              nc.sbuf_tensor(f"recb{b}", [128, 256], F32) as recb,
              nc.sbuf_tensor(f"ao_sb{b}", [128, 512], BF16) as ao_sb,
              nc.sbuf_tensor(f"bv_bc{b}", [128, DM], F32) as bv_bc,
              nc.sbuf_tensor(f"rowt{b}", [1, DM], F32) as rowt):
            self.bcast_row_2048(bv_bc, "vb0_row", rowt)
            for dt in range(NDT):
                p.gdma(xbf[:, dt * S:(dt + 1) * S], p.inp["xin"][b, dt * 128:(dt + 1) * 128, :])
            for g in range(4):
                wb_ = [wbuf, wbuf2][(3 * g) % 2]
                p.gdma(wb_[:, :], p.inp["wq0"][g, :, :])
                for hl in range(4):
                    for n in range(4):
                        pp = p.ps[(hl * 4 + n) % 2 * 2][:, :]
                        for dt in range(NDT):
                            p.mm(pp, wb_[:, dt * 512 + hl * 128:dt * 512 + (hl + 1) * 128],
                                 xbf[:, dt * S + n * 512:dt * S + (n + 1) * 512],
                                 start=(dt == 0), stop=(dt == NDT - 1))
                        p.act(qg[:, hl * S + n * 512:hl * S + (n + 1) * 512], pp,
                              AF.Identity, bias=p.qb_sb[:, g * 4 + hl:g * 4 + hl + 1])
                wb_ = [wbuf, wbuf2][(3 * g + 1) % 2]
                p.gdma(wb_[:, :], p.inp["wk0"][g, :, :])
                for hl in range(4):
                    for n in range(4):
                        pp = p.ps[(hl * 4 + n) % 2 * 2][:, :]
                        for dt in range(NDT):
                            p.mm(pp, wb_[:, dt * 512 + hl * 128:dt * 512 + (hl + 1) * 128],
                                 xbf[:, dt * S + n * 512:dt * S + (n + 1) * 512],
                                 start=(dt == 0), stop=(dt == NDT - 1))
                        p.act(kg[:, hl * S + n * 512:hl * S + (n + 1) * 512], pp,
                              AF.Identity, bias=p.kb_sb[:, g * 4 + hl:g * 4 + hl + 1])
                wb_ = [wbuf, wbuf2][(3 * g + 2) % 2]
                p.gdma(wb_[:, :], p.inp["wv0"][g, :, :])
                for tt in range(16):
                    pp = p.ps[tt % 2 * 2][:, :]
                    for dt in range(NDT):
                        p.mm(pp, xbf[:, dt * S + tt * 128:dt * S + (tt + 1) * 128],
                             wb_[:, dt * 512:(dt + 1) * 512],
                             start=(dt == 0), stop=(dt == NDT - 1))
                    p.stt(vtm[:, tt * 512:(tt + 1) * 512], pp, 1.0,
                          bv_bc[:, g * 512:(g + 1) * 512], ALU.mult, ALU.add)
                import os
                if os.environ.get("KDUMPQKV") and g == 0 and b == 1:
                    for hl in range(4):
                        p.dma(p.aoh[hl * 128:(hl + 1) * 128, :], qg[:, hl * S:(hl + 1) * S])
                        p.dma(p.aoh[512 + hl * 128:512 + (hl + 1) * 128, :], kg[:, hl * S:(hl + 1) * S])
                    for tt in range(16):
                        p.dma(p.aoh.rearrange("d s -> (d s)").rearrange(
                              "(a p f) -> a p f", a=32, p=128)[8 + tt // 2, :,
                              (tt % 2) * 512:(tt % 2) * 512 + 512],
                              vtm[:, tt * 512:(tt + 1) * 512])
                for hl in range(4 if not os.environ.get("KDUMPQKV") else 0):
                    h = g * 4 + hl
                    for qc in range(16):
                        probs = probs0[:, (qc % 2) * 2048:(qc % 2 + 1) * 2048]
                        for kt in range(qc + 1):
                            pp = p.ps[(kt % 2) * 3][:, :128]
                            p.mm(pp, kg[:, hl * S + kt * 128:hl * S + (kt + 1) * 128],
                                 qg[:, hl * S + qc * 128:hl * S + (qc + 1) * 128])
                            p.act(probs[:, kt * 128:(kt + 1) * 128], pp, AF.Exp,
                                  bias=p.bias_tab[:, h * 16 + (qc - kt):h * 16 + (qc - kt) + 1],
                                  scale=INV_SQRT_HD)
                            if kt == qc:
                                p.asel(probs[:, kt * 128:(kt + 1) * 128],
                                       probs[:, kt * 128:(kt + 1) * 128],
                                       [[1, 128]], ALU.is_ge, 0.0, base=0, chmul=-1)
                            pd = [p.ps_row, p.ps_row2][qc % 2]
                            po = [p.ps[1], p.ps4][qc % 2]
                            p.mm(pd[0:1, :128], p.ones_col_bf[:, :],
                                 probs[:, kt * 128:(kt + 1) * 128],
                                 start=(kt == 0), stop=(kt == qc))
                            p.mm(po[:, :128], vtm[:, kt * 512 + hl * 128:kt * 512 + (hl + 1) * 128],
                                 probs[:, kt * 128:(kt + 1) * 128],
                                 start=(kt == 0), stop=(kt == qc))
                        p.recip(rec_row[:, (qc % 2) * 128:(qc % 2) * 128 + 128], pd[0:1, :128])
                        p.mm(p.ps[2][:, (qc % 2) * 128:(qc % 2) * 128 + 128], p.ones_row_f[:, :],
                             rec_row[:, (qc % 2) * 128:(qc % 2) * 128 + 128])
                        p.scopy(recb[:, (qc % 2) * 128:(qc % 2) * 128 + 128],
                                p.ps[2][:, (qc % 2) * 128:(qc % 2) * 128 + 128])
                        p.tt(ao_sb[:, (qc % 4) * 128:(qc % 4 + 1) * 128], po[:, :128],
                             recb[:, (qc % 2) * 128:(qc % 2) * 128 + 128], ALU.mult)
                        if qc % 4 == 3:
                            p.dma(p.aoh[h * 128:(h + 1) * 128, (qc - 3) * 128:(qc + 1) * 128],
                                  ao_sb[:, :])
        self._proj_epilogue(wname="wo0", act_hbm=p.aoh, act_cols=S, b=b,
                            res_in=p.inp["xin"], outf=p.x1f, outb=p.x1b,
                            bias_sb=p.ob_sb, tag=f"l0a{b}")

    def _proj_epilogue(self, wname, act_hbm, act_cols, b, res_in, outf, outb, bias_sb,
                       part_in=None, tokchunk=512, tag=""):
        p = self
        nc = self.nc
        nchunk = S // tokchunk
        with (nc.sbuf_tensor(f"wo_full{tag}", [128, 16 * NDT * 128], BF16) as wo_full,
              nc.sbuf_tensor(f"aobuf{tag}", [128, NDT * tokchunk], BF16) as aobuf,
              nc.sbuf_tensor(f"x1pre{tag}", [128, NDT * tokchunk], F32) as x1pre,
              nc.sbuf_tensor(f"xres{tag}", [128, tokchunk], F32) as xres,
              nc.sbuf_tensor(f"sqsb{tag}", [128, tokchunk], F32) as sqsb,
              nc.sbuf_tensor(f"rows{tag}", [1, 6 * tokchunk], F32) as rows,
              nc.sbuf_tensor(f"ab_sb{tag}", [128, 2 * tokchunk], F32) as ab_sb,
              nc.sbuf_tensor(f"xo_sb{tag}", [128, tokchunk], F32) as xo_sb,
              nc.sbuf_tensor(f"xb_sb{tag}", [128, tokchunk], BF16) as xb_sb):
            for ft in range(16):
                p.gdma(wo_full[:, ft * NDT * 128:(ft + 1) * NDT * 128], p.inp[wname][ft, :, :])
            TC = tokchunk
            for n in range(nchunk):
                for at in range(NDT):
                    p.dma(aobuf[:, at * TC:(at + 1) * TC],
                          act_hbm[at * 128:(at + 1) * 128, n * TC:(n + 1) * TC])
                for ft in range(NDT):
                    for at in range(NDT):
                        p.mm(p.ps[0][:, :TC],
                             wo_full[:, ft * NDT * 128 + at * 128:ft * NDT * 128 + (at + 1) * 128],
                             aobuf[:, at * TC:(at + 1) * TC],
                             start=(at == 0), stop=(at == NDT - 1))
                    p.dma(xres[:, :], res_in[b, ft * 128:(ft + 1) * 128, n * TC:(n + 1) * TC])
                    p.act(x1pre[:, ft * TC:(ft + 1) * TC], p.ps[0][:, :TC], AF.Identity,
                          bias=bias_sb[:, ft:ft + 1])
                    p.tt(x1pre[:, ft * TC:(ft + 1) * TC], x1pre[:, ft * TC:(ft + 1) * TC],
                         xres[:, :], ALU.add)
                    p.mm(p.ps_row[0:1, :TC], p.ones_col_f[:, :], x1pre[:, ft * TC:(ft + 1) * TC],
                         start=(ft == 0), stop=(ft == NDT - 1))
                    p.act(sqsb[:, :], x1pre[:, ft * TC:(ft + 1) * TC], AF.Square)
                    p.mm(p.ps_row2[0:1, :TC], p.ones_col_f[:, :], sqsb[:, :],
                         start=(ft == 0), stop=(ft == NDT - 1))
                MU, E2, VAR, STD, RA, RB = (rows[:, i * TC:(i + 1) * TC] for i in range(6))
                p.ts(MU, p.ps_row[0:1, :TC], 1.0 / DM, None, ALU.mult)
                p.ts(E2, p.ps_row2[0:1, :TC], 1.0 / DM, None, ALU.mult)
                p.tt(VAR, MU, MU, ALU.mult)
                p.tt(VAR, E2, VAR, ALU.subtract)
                p.act(STD, VAR, AF.Sqrt, bias=LN_EPS)
                p.recip(RA, STD)
                p.tt(RB, MU, RA, ALU.mult)
                p.ts(RB, RB, -1.0, None, ALU.mult)
                p.mm(p.ps[1][:, :TC], p.ones_row_f[:, :], RA)
                p.scopy(ab_sb[:, :TC], p.ps[1][:, :TC])
                p.mm(p.ps[1][:, :TC], p.ones_row_f[:, :], RB)
                p.scopy(ab_sb[:, TC:2 * TC], p.ps[1][:, :TC])
                for ft in range(NDT):
                    p.tt(xo_sb[:, :], x1pre[:, ft * TC:(ft + 1) * TC], ab_sb[:, :TC], ALU.mult)
                    p.tt(xo_sb[:, :], xo_sb[:, :], ab_sb[:, TC:2 * TC], ALU.add)
                    p.dma(outf[b, ft * 128:(ft + 1) * 128, n * TC:(n + 1) * TC], xo_sb[:, :])
                    p.scopy(xb_sb[:, :], xo_sb[:, :])
                    p.dma(outb[ft * 128:(ft + 1) * 128, b * S + n * TC:b * S + (n + 1) * TC],
                          xb_sb[:, :])

    def phase_l0_ffn(self):
        p = self
        nc = self.nc
        T = BPC * S
        with (nc.sbuf_tensor("ffa_x", [128, NDT * T], BF16) as xa,
              nc.sbuf_tensor("ffa_w1", [128, 2 * DM], BF16) as w1b,
              nc.sbuf_tensor("ffa_w3", [128, 2 * DM], BF16) as w3b,
              nc.sbuf_tensor("ffa_hid", [128, 2 * T], BF16) as hidsb,
              nc.sbuf_tensor("ffa_silu", [128, 2 * 512], BF16) as silu_sb):
            for dt in range(NDT):
                p.dma(xa[:, dt * T:(dt + 1) * T], p.x1b[dt * 128:(dt + 1) * 128, :])
            for ht in range(NHT):
                par = ht % 2
                w1s = w1b[:, par * DM:(par + 1) * DM]
                w3s = w3b[:, par * DM:(par + 1) * DM]
                p.gdma(w1s, p.inp["w1t"][ht, :, :])
                p.gdma(w3s, p.inp["w3t"][ht, :, :])
                for n in range(T // 512):
                    pg = p.ps[(n % 2) * 2][:, :]
                    pl = p.ps[(n % 2) * 2 + 1][:, :]
                    sl = silu_sb[:, (n % 2) * 512:(n % 2 + 1) * 512]
                    for dt in range(NDT):
                        p.mm(pg, w1s[:, dt * 128:(dt + 1) * 128],
                             xa[:, dt * T + n * 512:dt * T + (n + 1) * 512],
                             start=(dt == 0), stop=(dt == NDT - 1))
                    p.act(sl, pg, AF.Silu)
                    for dt in range(NDT):
                        p.mm(pl, w3s[:, dt * 128:(dt + 1) * 128],
                             xa[:, dt * T + n * 512:dt * T + (n + 1) * 512],
                             start=(dt == 0), stop=(dt == NDT - 1))
                    p.tt(hidsb[:, par * T + n * 512:par * T + (n + 1) * 512], pl, sl, ALU.mult)
                p.dma(p.hid0[ht, :, :], hidsb[:, par * T:(par + 1) * T])
        self._down_proj(p.hid0, p.w2eff, p.part0, p.x1f, p.x2f, p.x2b, tag="l0f")

    def _down_proj(self, hid, wsrc, part, resf, outf, outb, tag, wcast=False):
        p = self
        nc = self.nc
        T = BPC * S
        TC = 256
        for half in range(2):
            with ExitStack() as stk:
                E = stk.enter_context
                wsb = E(nc.sbuf_tensor(f"dp_w{tag}{half}", [128, 32 * DM], BF16))
                hch = E(nc.sbuf_tensor(f"dp_h{tag}{half}", [128, 2 * 32 * TC], BF16))
                aux = E(nc.sbuf_tensor(f"dp_aux{tag}{half}", [128, TC], F32))
                if half == 1:
                    pre = E(nc.sbuf_tensor(f"dp_pre{tag}{half}", [128, NDT * TC], F32))
                    sq_sb = E(nc.sbuf_tensor(f"dp_sq{tag}{half}", [128, TC], F32))
                    rows = E(nc.sbuf_tensor(f"dp_rows{tag}{half}", [1, 6 * TC], F32))
                    ab_sb = E(nc.sbuf_tensor(f"dp_ab{tag}{half}", [128, 2 * TC], F32))
                    xo_sb = E(nc.sbuf_tensor(f"dp_xo{tag}{half}", [128, TC], F32))
                    xb_sb = E(nc.sbuf_tensor(f"dp_xb{tag}{half}", [128, TC], BF16))
                for i in range(32):
                    ht = half * 32 + i
                    if wcast:
                        p.gdma(wsb[:, i * DM:(i + 1) * DM], wsrc[ht * 128:(ht + 1) * 128, :])
                    else:
                        p.dma(wsb[:, i * DM:(i + 1) * DM], wsrc[ht * 128:(ht + 1) * 128, :])
                for c in range(T // TC):
                    b = (c * TC) // S
                    nin = (c * TC) % S
                    hb = hch[:, (c % 2) * 32 * TC:(c % 2 + 1) * 32 * TC]
                    for i in range(32):
                        p.dma(hb[:, i * TC:(i + 1) * TC],
                              hid[half * 32 + i, :, c * TC:(c + 1) * TC])
                    for ft in range(NDT):
                        for i in range(32):
                            p.mm(p.ps[ft % 2][:, :TC],
                                 wsb[:, i * DM + ft * 128:i * DM + (ft + 1) * 128],
                                 hb[:, i * TC:(i + 1) * TC],
                                 start=(i == 0), stop=(i == 31))
                        if half == 0:
                            p.vcopy(aux[:, :], p.ps[ft % 2][:, :TC])
                            p.dma(part[ft, :, c * TC:(c + 1) * TC], aux[:, :])
                        else:
                            p.dma(aux[:, :], part[ft, :, c * TC:(c + 1) * TC])
                            p.tt(pre[:, ft * TC:(ft + 1) * TC], p.ps[ft % 2][:, :TC],
                                 aux[:, :], ALU.add)
                            p.dma(aux[:, :], resf[b, ft * 128:(ft + 1) * 128, nin:nin + TC])
                            p.tt(pre[:, ft * TC:(ft + 1) * TC], pre[:, ft * TC:(ft + 1) * TC],
                                 aux[:, :], ALU.add)
                    if half == 1:
                        self._ln_store(pre, TC, sq_sb, rows, ab_sb, xo_sb, xb_sb,
                                       outf, outb, b, nin)

    def _ln_store(self, pre, TC, sq_sb, rows, ab_sb, xo_sb, xb_sb, outf, outb, b, nin):
        p = self
        for ft in range(NDT):
            p.mm(p.ps_row[0:1, :TC], p.ones_col_f[:, :], pre[:, ft * TC:(ft + 1) * TC],
                 start=(ft == 0), stop=(ft == NDT - 1))
            p.act(sq_sb[:, :], pre[:, ft * TC:(ft + 1) * TC], AF.Square)
            p.mm(p.ps_row2[0:1, :TC], p.ones_col_f[:, :], sq_sb[:, :],
                 start=(ft == 0), stop=(ft == NDT - 1))
        MU, E2, VAR, STD, RA, RB = (rows[:, i * TC:(i + 1) * TC] for i in range(6))
        p.ts(MU, p.ps_row[0:1, :TC], 1.0 / DM, None, ALU.mult)
        p.ts(E2, p.ps_row2[0:1, :TC], 1.0 / DM, None, ALU.mult)
        p.tt(VAR, MU, MU, ALU.mult)
        p.tt(VAR, E2, VAR, ALU.subtract)
        p.act(STD, VAR, AF.Sqrt, bias=LN_EPS)
        p.recip(RA, STD)
        p.tt(RB, MU, RA, ALU.mult)
        p.ts(RB, RB, -1.0, None, ALU.mult)
        p.mm(p.ps[2][:, :TC], p.ones_row_f[:, :], RA)
        p.scopy(ab_sb[:, :TC], p.ps[2][:, :TC])
        p.mm(p.ps[2][:, :TC], p.ones_row_f[:, :], RB)
        p.scopy(ab_sb[:, TC:2 * TC], p.ps[2][:, :TC])
        for ft in range(NDT):
            p.tt(xo_sb[:, :], pre[:, ft * TC:(ft + 1) * TC], ab_sb[:, :TC], ALU.mult)
            p.tt(xo_sb[:, :], xo_sb[:, :], ab_sb[:, TC:2 * TC], ALU.add)
            p.dma(outf[b, ft * 128:(ft + 1) * 128, nin:nin + TC], xo_sb[:, :])
            if outb is not None:
                p.scopy(xb_sb[:, :], xo_sb[:, :])
                p.dma(outb[ft * 128:(ft + 1) * 128, b * S + nin:b * S + nin + TC], xb_sb[:, :])

    def phase_l1_proj(self):
        p = self
        nc = self.nc
        T = BPC * S
        with (nc.sbuf_tensor("p3_x", [128, NDT * T], BF16) as xa,
              nc.sbuf_tensor("p3_w", [128, 2 * NDT * 512], BF16) as wb0,
              nc.sbuf_tensor("p3_o", [128, 512], BF16) as ob,
              nc.sbuf_tensor("p3_o2", [128, 512], BF16) as ob2,
              nc.sbuf_tensor("p3_ab", [128, 2 * 256], BF16) as wab,
              nc.sbuf_tensor("p3_sig", [16, 2 * T], F32) as sig,
              nc.sbuf_tensor("p3_bab", [16, 2], F32) as bab):
            for dt in range(NDT):
                p.dma(xa[:, dt * T:(dt + 1) * T], p.x2b[dt * 128:(dt + 1) * 128, :])
            obuf = [ob, ob2]
            for iw, (wname, dst) in enumerate((("wq1", p.xwq), ("wk1", p.xwk), ("wv1", p.xwv))):
                for dot in range(16):
                    par = (iw * 16 + dot) % 2
                    wb = wb0[:, par * NDT * 512:par * NDT * 512 + NDT * 128]
                    p.gdma(wb, p.inp[wname][dot, :, :])
                    for n in range(T // 512):
                        pp = p.ps[(n % 2) * 2][:, :]
                        for dt in range(NDT):
                            p.mm(pp, wb[:, dt * 128:(dt + 1) * 128],
                                 xa[:, dt * T + n * 512:dt * T + (n + 1) * 512],
                                 start=(dt == 0), stop=(dt == NDT - 1))
                        p.scopy(obuf[n % 2][:, :], pp)
                        p.dma(dst[dot * 128:(dot + 1) * 128, n * 512:(n + 1) * 512], obuf[n % 2][:, :])
            for fc in range(4):
                par = fc % 2
                wb = wb0[:, par * NDT * 512:(par + 1) * NDT * 512]
                for dt in range(NDT):
                    p.gdma(wb[:, dt * 512:(dt + 1) * 512],
                           p.inp["wg1"][dt * 128:(dt + 1) * 128, fc * 512:(fc + 1) * 512])
                for tt in range(T // 128):
                    pp = p.ps[(tt % 2) * 2][:, :]
                    for dt in range(NDT):
                        p.mm(pp, xa[:, dt * T + tt * 128:dt * T + (tt + 1) * 128],
                             wb[:, dt * 512:(dt + 1) * 512],
                             start=(dt == 0), stop=(dt == NDT - 1))
                    p.scopy(obuf[tt % 2][:, :], pp)
                    p.dma(p.xwg_tm[tt * 128:(tt + 1) * 128, fc * 512:(fc + 1) * 512], obuf[tt % 2][:, :])
            p.gdma(wab[:, :256], p.inp["wa1"][:, :])
            p.gdma(wab[:, 256:], p.inp["wb1"][:, :])
            p.dma(bab[:, 0:1], p.inp["ba1"][:, :])
            p.dma(bab[:, 1:2], p.inp["bb1"][:, :])
            for j, (coff, boff) in enumerate(((0, 0), (256, 1))):
                for n in range(T // 512):
                    for dt in range(NDT):
                        p.mm(p.ps[0][:16, :], wab[:, coff + dt * 16:coff + (dt + 1) * 16],
                             xa[:, dt * T + n * 512:dt * T + (n + 1) * 512],
                             start=(dt == 0), stop=(dt == NDT - 1))
                    p.act(sig[:, j * T + n * 512:j * T + (n + 1) * 512], p.ps[0][:16, :],
                          AF.Sigmoid, bias=bab[:, boff:boff + 1])

            p.dma(p.sigh[:, :], sig[:, :])

    def phase_l1_delta(self, b):
        p = self
        nc = self.nc
        QT = 512
        with ExitStack() as stk:
            E = stk.enter_context
            qhat = E(nc.sbuf_tensor(f"d_q{b}", [128, 16 * QT], BF16))
            khat = E(nc.sbuf_tensor(f"d_k{b}", [128, 16 * QT], BF16))
            vhat = E(nc.sbuf_tensor(f"d_v{b}", [128, 16 * QT], BF16))
            xwb = E(nc.sbuf_tensor(f"d_xw{b}", [128, 16 * (QT + 3)], BF16))
            dgt = E(nc.sbuf_tensor(f"d_dgt{b}", [128, 64 * 128], BF16))
            cw = E(nc.sbuf_tensor(f"d_cw{b}", [128, 3 * 64], F32))
            cb = E(nc.sbuf_tensor(f"d_cb{b}", [128, 3 * 16], F32))
            sigq = E(nc.sbuf_tensor(f"d_sig{b}", [16, 2 * QT], F32))
            scal = E(nc.sbuf_tensor(f"d_scal{b}", [16, 8 * QT], F32))
            flat = E(nc.sbuf_tensor(f"d_flat{b}", [1, 2 * 16 * CK], F32))
            cols = E(nc.sbuf_tensor(f"d_cols{b}", [128, 6 * 16], F32))
            zf = E(nc.sbuf_tensor(f"d_zf{b}", [128, 16 * 128], F32))
            zb = E(nc.sbuf_tensor(f"d_zb{b}", [128, 16 * 128], BF16))
            ktm = E(nc.sbuf_tensor(f"d_ktm{b}", [128, 16 * 128], BF16))
            vtm = E(nc.sbuf_tensor(f"d_vtm{b}", [128, 16 * 128], BF16))
            expm = E(nc.sbuf_tensor(f"d_expm{b}", [128, 2 * 128], F32))
            mneg = E(nc.sbuf_tensor(f"d_mneg{b}", [128, 2 * 128], BF16))
            rcur = E(nc.sbuf_tensor(f"d_rcur{b}", [128, 2 * 256], BF16))
            r0sb = E(nc.sbuf_tensor(f"d_r0{b}", [128, 2 * 128], BF16))
            otm = E(nc.sbuf_tensor(f"d_otm{b}", [128, 16 * 128], F32))
            otmp = E(nc.sbuf_tensor(f"d_otmp{b}", [128, 2 * 128], F32))
            osq = E(nc.sbuf_tensor(f"d_osq{b}", [128, 16 * 128], F32))
            st16 = E(nc.sbuf_tensor(f"d_st16{b}", [128, 8 * 16], F32))
            gsb = E(nc.sbuf_tensor(f"d_gsb{b}", [128, DM], BF16))
            gss = E(nc.sbuf_tensor(f"d_gss{b}", [128, DM], BF16))
            ogbf = E(nc.sbuf_tensor(f"d_og{b}", [128, DM], BF16))
            ofm = E(nc.sbuf_tensor(f"d_ofm{b}", [128, 128], BF16))
            lnwb = E(nc.sbuf_tensor(f"d_lnw{b}", [128, DM], F32))
            lnbb = E(nc.sbuf_tensor(f"d_lnb{b}", [128, DM], F32))
            rowt = E(nc.sbuf_tensor(f"d_rowt{b}", [1, DM], F32))
            self.bcast_row_2048(lnwb, "lnw_rep", rowt)
            self.bcast_row_2048(lnbb, "lnb_rep", rowt)
            p.dma(cw[:, 0:64], p.inp["qcw"][:, :])
            p.dma(cw[:, 64:128], p.inp["kcw"][:, :])
            p.dma(cw[:, 128:192], p.inp["vcw"][:, :])
            p.dma(cb[:, 0:16], p.inp["qcb"][:, :])
            p.dma(cb[:, 16:32], p.inp["kcb"][:, :])
            p.dma(cb[:, 32:48], p.inp["vcb"][:, :])
            p.memset(zf[:, :], 0.0)
            p.memset(zb[:, :], 0.0)
            LNC2 = float(2.0 * np.log(DSC))
            for q in range(4):
                t0 = b * S + q * QT
                for ti, (srch, dst, woff, boff) in enumerate(
                        ((p.xwq, qhat, 0, 0), (p.xwk, khat, 64, 16), (p.xwv, vhat, 128, 32))):
                    if q == 0:
                        p.memset(xwb[:, :].rearrange("p (d c) -> p d c", d=16)[:, :, 0:3], 0.0)
                        for dt in range(NDT):
                            p.dma(xwb[:, dt * (QT + 3) + 3:(dt + 1) * (QT + 3)],
                                  srch[dt * 128:(dt + 1) * 128, t0:t0 + QT])
                    else:
                        for dt in range(NDT):
                            p.dma(xwb[:, dt * (QT + 3):(dt + 1) * (QT + 3)],
                                  srch[dt * 128:(dt + 1) * 128, t0 - 3:t0 + QT])
                    for dt in range(NDT):
                        for j in range(4):
                            p.ts(dgt[:, (dt * 4 + j) * 128:(dt * 4 + j + 1) * 128],
                                 p.ident_bf[:, :], cw[:, woff + dt * 4 + j:woff + dt * 4 + j + 1],
                                 None, ALU.mult)
                    for dt in range(NDT):
                        base = dt * (QT + 3)
                        pp = [p.ps[0], p.ps[2]][dt % 2][:, :]
                        for j in range(4):
                            p.mm(pp, dgt[:, (dt * 4 + j) * 128:(dt * 4 + j + 1) * 128],
                                 xwb[:, base + j:base + j + QT],
                                 start=(j == 0), stop=(j == 3))
                        p.act(dst[:, dt * QT:(dt + 1) * QT], pp, AF.Silu,
                              bias=cb[:, boff + dt:boff + dt + 1])
                if os.environ.get("KDUMPCONV") and b == 1 and q == 0:
                    for dt in range(16):
                        p.dma(p.aoh[dt * 128:(dt + 1) * 128, 0:512], qhat[:, dt * QT:(dt + 1) * QT])
                        p.dma(p.aoh[dt * 128:(dt + 1) * 128, 512:1024], khat[:, dt * QT:(dt + 1) * QT])
                        p.dma(p.aoh[dt * 128:(dt + 1) * 128, 1024:1536], vhat[:, dt * QT:(dt + 1) * QT])
                p.dma(sigq[:, 0:QT], p.sigh[:, t0:t0 + QT])
                p.dma(sigq[:, QT:2 * QT], p.sigh[:, BPC * S + t0:BPC * S + t0 + QT])
                A_, B_ = sigq[:, 0:QT], sigq[:, QT:2 * QT]
                LNA = scal[:, 0 * QT:1 * QT]; LL = scal[:, 1 * QT:2 * QT]
                LP = scal[:, 2 * QT:3 * QT]; LB2 = scal[:, 3 * QT:4 * QT]
                NL = scal[:, 4 * QT:5 * QT]; S1 = scal[:, 5 * QT:6 * QT]
                BC = scal[:, 6 * QT:7 * QT]; EL = scal[:, 7 * QT:8 * QT]
                p.act(LNA, A_, AF.Ln)
                p.vcopy(LL, LNA)
                cur, oth = LL, LNA
                for sft in (1, 2, 4, 8, 16, 32, 64):
                    c3_cur = cur.rearrange("p (c i) -> p c i", c=4)
                    c3_oth = oth.rearrange("p (c i) -> p c i", c=4)
                    p.vcopy(c3_oth[:, :, 0:sft], c3_cur[:, :, 0:sft])
                    p.tt(c3_oth[:, :, sft:CK], c3_cur[:, :, sft:CK],
                         c3_cur[:, :, 0:CK - sft], ALU.add)
                    cur, oth = oth, cur
                if cur is not LL:
                    p.vcopy(LL, cur)
                p.act(LNA, A_, AF.Ln)
                p.tt(LP, LL, LNA, ALU.subtract)
                p.act(LB2, B_, AF.Ln)
                p.tt(LB2, LP, LB2, ALU.add)
                p.ts(LB2, LB2, 1.0, LNC2, ALU.mult, ALU.add)
                p.ts(NL, LL, -1.0, None, ALU.mult)
                p.act(S1, LP, AF.Exp)
                p.tt(S1, B_, S1, ALU.mult)
                p.ts(S1, S1, float(-DSC * DSC), None, ALU.mult)
                p.ts(BC, B_, DSC, None, ALU.mult)
                p.act(EL, LL, AF.Exp)
                for ck4 in range(4 if not os.environ.get("KDUMPCONV") else 0):
                    cg = q * 4 + ck4
                    c0 = cg * CK
                    csl = slice(ck4 * CK, (ck4 + 1) * CK)
                    for k, qty in enumerate((NL, S1, BC, EL)):
                        p.tr(p.ps[3][:, :16], qty[:, csl], f32=True)
                        p.scopy(cols[:, k * 16:(k + 1) * 16], p.ps[3][:, :16])
                    p.tr(p.ps[3][0:1, 0:16], LL[:, ck4 * CK + CK - 1:ck4 * CK + CK], f32=True)
                    p.scopy(flat[:, 0:16], p.ps[3][0:1, 0:16])
                    p.mm(p.ps[3][:, :16], p.ones_row_f[:, :], flat[:, 0:16])
                    p.scopy(cols[:, 64:80], p.ps[3][:, :16])
                    p.tt(cols[:, 80:96], cols[:, 64:80], cols[:, 0:16], ALU.add)
                    p.act(cols[:, 80:96], cols[:, 80:96], AF.Exp)
                    p.act(cols[:, 64:80], cols[:, 64:80], AF.Exp)
                    p.dma(p.rowtmp[:, :], LB2[:, csl])
                    p.dma(flat[:, 0 * 16 * CK:1 * 16 * CK],
                          p.rowtmp.rearrange("h c -> (h c)")[None, :])
                    p.dma(p.rowtmp2[:, :], LL[:, csl])
                    p.dma(flat[:, 1 * 16 * CK:2 * 16 * CK],
                          p.rowtmp2.rearrange("h c -> (h c)")[None, :])
                    for h in range(16):
                        hq = h * QT + ck4 * CK
                        hsl = slice(hq, hq + CK)
                        zsl = slice(h * 128, (h + 1) * 128)
                        par = h % 2
                        pA = [p.ps[0], p.ps[3]][par]
                        pB = [p.ps[1], p.ps4][par]
                        pC = [p.ps[2], p.ps_row][par]
                        pT = p.ps_bf[:, (par * 512):(par * 512) + 128]
                        exm = expm[:, par * 128:(par + 1) * 128]
                        mng = mneg[:, par * 128:(par + 1) * 128]
                        r0s = r0sb[:, par * 128:(par + 1) * 128]
                        rcu = rcur[:, par * 256:(par + 1) * 256]
                        otp = otmp[:, par * 128:(par + 1) * 128]
                        p.tr(pT, khat[:, hsl])
                        p.ts(ktm[:, zsl], pT, cols[:, 80 + h:80 + h + 1],
                             None, ALU.mult)
                        p.tr(pT, vhat[:, hsl])
                        p.ts(vtm[:, zsl], pT, cols[:, 32 + h:32 + h + 1],
                             None, ALU.mult)
                        p.mm(pA[:, :128], khat[:, hsl], khat[:, hsl])
                        p.mm(pB[:, :128], p.ones_row_f[:, :],
                             flat[:, h * CK:(h + 1) * CK])
                        p.act(exm, pB[:, :128], AF.Exp, bias=cols[:, h:h + 1])
                        p.asel(exm, exm, [[1, 128]], ALU.is_gt, 0.0,
                               base=0, chmul=-1)
                        p.stt(mng, pA[:, :128], -1.0, exm, ALU.mult, ALU.mult)
                        p.mm(pC[:, :128], khat[:, hsl], zb[:, zsl])
                        p.stt(r0s, pC[:, :128], cols[:, 16 + h:16 + h + 1],
                              vtm[:, zsl], ALU.mult, ALU.add)
                        p.vcopy(rcu[:, 0:128], r0s)
                        for it in range(3):
                            cu = rcu[:, (it % 2) * 128:(it % 2) * 128 + 128]
                            nx = rcu[:, ((it + 1) % 2) * 128:((it + 1) % 2) * 128 + 128]
                            p.mm(pC[:, :128], mng, cu)
                            p.tt(nx, pC[:, :128], r0s, ALU.add)
                        Y = rcu[:, 128:256]
                        p.mm(pC[:, :128], qhat[:, hsl], zb[:, zsl])
                        p.mm(pA[:, :128], khat[:, hsl], qhat[:, hsl])
                        p.mm(pB[:, :128], p.ones_row_f[:, :],
                             flat[:, 16 * CK + h * CK:16 * CK + (h + 1) * CK])
                        p.act(exm, pB[:, :128], AF.Exp, bias=cols[:, h:h + 1])
                        p.asel(exm, exm, [[1, 128]], ALU.is_ge, 0.0,
                               base=0, chmul=-1)
                        p.tt(mng, pA[:, :128], exm, ALU.mult)
                        p.mm(pA[:, :128], mng, Y)
                        p.ts(otp, pC[:, :128], cols[:, 48 + h:48 + h + 1],
                             None, ALU.mult)
                        p.tt(otm[:, zsl], otp, pA[:, :128], ALU.add)
                        p.mm(pC[:, :128], ktm[:, zsl], Y)
                        p.stt(zf[:, zsl], zf[:, zsl], cols[:, 64 + h:64 + h + 1],
                              pC[:, :128], ALU.mult, ALU.add)
                        p.scopy(zb[:, zsl], zf[:, zsl])
                    if os.environ.get("KDUMPCHUNK") and b == 1 and cg == 0:
                        p.gdma(p.aoh[0:128, 0:2048], otm[:, :])
                        p.gdma(p.aoh[128:256, 0:96], cols[:, :])
                        p.gdma(p.aoh[300:301, 0:2048], flat[:, 0:2048])
                        p.gdma(p.aoh[301:302, 0:2048], flat[:, 2048:4096])
                        continue
                    o3 = otm[:, :].rearrange("p (h v) -> p h v", h=16)
                    p.rsum(st16[:, 0:16], o3)
                    p.ts(st16[:, 0:16], st16[:, 0:16], 1.0 / 128.0, None, ALU.mult)
                    p.act(osq[:, :], otm[:, :], AF.Square)
                    p.rsum(st16[:, 16:32], osq[:, :].rearrange("p (h v) -> p h v", h=16))
                    p.ts(st16[:, 16:32], st16[:, 16:32], 1.0 / 128.0, None, ALU.mult)
                    p.tt(st16[:, 32:48], st16[:, 0:16], st16[:, 0:16], ALU.mult)
                    p.tt(st16[:, 32:48], st16[:, 16:32], st16[:, 32:48], ALU.subtract)
                    p.act(st16[:, 32:48], st16[:, 32:48], AF.Sqrt, bias=LN_EPS)
                    p.recip(st16[:, 48:64], st16[:, 32:48])
                    p.ts(st16[:, 0:16], st16[:, 0:16], -1.0, None, ALU.mult)
                    for h in range(16):
                        zsl = slice(h * 128, (h + 1) * 128)
                        p.ts(otm[:, zsl], otm[:, zsl], st16[:, h:h + 1],
                             st16[:, 48 + h:48 + h + 1], ALU.add, ALU.mult)
                    p.tt(otm[:, :], otm[:, :], lnwb[:, :], ALU.mult)
                    p.tt(otm[:, :], otm[:, :], lnbb[:, :], ALU.add)
                    p.dma(gsb[:, :], p.xwg_tm[b * S + c0:b * S + c0 + CK, :])
                    p.act(gss[:, :], gsb[:, :], AF.Sigmoid)
                    p.tt(ogbf[:, :], otm[:, :], gss[:, :], ALU.mult)
                    for h in range(16):
                        p.tr(p.ps_bf[:, :128], ogbf[:, h * 128:(h + 1) * 128])
                        p.scopy(ofm[:, :], p.ps_bf[:, :128])
                        p.dma(p.aoh[h * 128:(h + 1) * 128, c0:c0 + CK], ofm[:, :])
        with self.nc.sbuf_tensor(f"zb0_{b}", [128, 16], F32) as zb0:
            p.memset(zb0[:, :], 0.0)
            self._proj_epilogue(wname="wo1", act_hbm=p.aoh, act_cols=S, b=b,
                                res_in=p.x2f, outf=p.x3f, outb=p.x3b,
                                bias_sb=zb0, tag=f"l1a{b}")

    def phase_l1_ffn(self):
        p = self
        nc = self.nc
        T = BPC * S
        with ExitStack() as stk:
            E = stk.enter_context
            xa = E(nc.sbuf_tensor("p5_x", [128, NDT * T], BF16))
            wgb = E(nc.sbuf_tensor("p5_wg", [128, NDT * 512], BF16))
            wub = E(nc.sbuf_tensor("p5_wu", [128, NDT * 512], BF16))
            slf = E(nc.sbuf_tensor("p5_sl", [128, 512], F32))
            hidf = E(nc.sbuf_tensor("p5_h", [128, 512], F32))
            absf = E(nc.sbuf_tensor("p5_a", [128, 512], F32))
            lad = E(nc.sbuf_tensor("p5_l", [128, 512], F32))
            tau = E(nc.sbuf_tensor("p5_t", [128, 2 * 128], F32))
            mk = E(nc.sbuf_tensor("p5_m", [128, 512], F32))
            hsp = E(nc.sbuf_tensor("p5_s", [128, 512], BF16))
            stag = E(nc.sbuf_tensor("p5_st", [128, 4 * 1024], BF16))
            for dt in range(NDT):
                p.dma(xa[:, dt * T:(dt + 1) * T], p.x3b[dt * 128:(dt + 1) * 128, :])
            for hb in range(16):
                for dt in range(NDT):
                    p.gdma(wgb[:, dt * 512:(dt + 1) * 512],
                           p.inp["wgate"][dt * 128:(dt + 1) * 128, hb * 512:(hb + 1) * 512])
                    p.gdma(wub[:, dt * 512:(dt + 1) * 512],
                           p.inp["wup"][dt * 128:(dt + 1) * 128, hb * 512:(hb + 1) * 512])
                for tt in range(T // 128):
                    for dt in range(NDT):
                        p.mm(p.ps[0][:, :], xa[:, dt * T + tt * 128:dt * T + (tt + 1) * 128],
                             wgb[:, dt * 512:(dt + 1) * 512],
                             start=(dt == 0), stop=(dt == NDT - 1))
                    p.act(slf[:, :], p.ps[0][:, :], AF.Silu)
                    for dt in range(NDT):
                        p.mm(p.ps[1][:, :], xa[:, dt * T + tt * 128:dt * T + (tt + 1) * 128],
                             wub[:, dt * 512:(dt + 1) * 512],
                             start=(dt == 0), stop=(dt == NDT - 1))
                    p.tt(hidf[:, :], p.ps[1][:, :], slf[:, :], ALU.mult)
                    p.act(absf[:, :], hidf[:, :], AF.Abs)
                    p.tt(lad[:, 0:512:4], absf[:, 0:512:4], absf[:, 1:512:4], ALU.max)
                    p.tt(lad[:, 1:512:4], absf[:, 2:512:4], absf[:, 3:512:4], ALU.max)
                    p.tt(lad[:, 2:512:4], absf[:, 0:512:4], absf[:, 1:512:4], ALU.min)
                    p.tt(lad[:, 3:512:4], absf[:, 2:512:4], absf[:, 3:512:4], ALU.min)
                    p.tt(tau[:, 0:128], lad[:, 0:512:4], lad[:, 1:512:4], ALU.min)
                    p.tt(tau[:, 128:256], lad[:, 2:512:4], lad[:, 3:512:4], ALU.max)
                    p.tt(tau[:, 0:128], tau[:, 0:128], tau[:, 128:256], ALU.max)
                    for j in range(4):
                        p.tt(mk[:, j:512:4], absf[:, j:512:4], tau[:, 0:128], ALU.is_ge)
                    p.tt(hsp[:, :], hidf[:, :], mk[:, :], ALU.mult)
                    for j in range(4):
                        p.tr(p.ps_bf[:, :128], hsp[:, j * 128:(j + 1) * 128])
                        p.scopy(stag[:, j * 1024 + (tt % 8) * 128:j * 1024 + (tt % 8 + 1) * 128],
                                p.ps_bf[:, :128])
                    if tt % 8 == 7:
                        for j in range(4):
                            p.dma(p.hid1[hb * 4 + j, :, (tt - 7) * 128:(tt + 1) * 128],
                                  stag[:, j * 1024:(j + 1) * 1024])
        self._down_proj(p.hid1, p.inp["wdown"], p.part1, p.x3f, self.out, None,
                        tag="l1f", wcast=True)

    def finish(self):
        p = self
        if self.dbg is not None:
            src = {"x1f": p.x1f, "x2f": p.x2f, "x3f": p.x3f, "w2eff": p.w2eff, "sigh": p.sigh,
                   "aoh": p.aoh, "x1b": p.x1b, "hid0": p.hid0, "hid1": p.hid1,
                   "xwq": p.xwq, "xwk": p.xwk, "xwv": p.xwv,
                   "xwg_tm": p.xwg_tm, "part0": p.part0}[self.dbg]
            flat_s = src.rearrange(" ".join(f"a{i}" for i in range(len(src.shape))) + " -> (" +
                                   " ".join(f"a{i}" for i in range(len(src.shape))) + ")") \
                if False else src
            p.dma(self.dbg_out[...], src[...])
        if self.mode == "chain":
            total = self.cnt
            self.ops["sync"].append(lambda h, n=total: h.wait_ge(self.chain, n))
        else:
            tots = [(self.esem[e], self.emitted[e]) for e in self.ops if self.emitted[e] > 0]
            tots += [(self.esem[k], 16 * u) for k, u in self.dslot_use.items() if u > 0]
            def fin(h, tots=tuple(tots)):
                for smm, v in tots:
                    h.wait_ge(smm, v)
            self.ops["sync"].append(fin)
        with self.nc.Block() as block:
            for eng in ("tensor", "vector", "scalar", "gpsimd", "sync"):
                def mk(e):
                    def f(h):
                        for op in self.ops[e]:
                            op(h)
                    return f
                getattr(block, eng)(mk(eng))
        self.st.close()
        return self.nc


def build_program(upto=99, dbg=None, mode=None):
    import os
    if mode is None:
        mode = os.environ.get("KMODE", "auto")
    relax = set(os.environ.get("KRELAX", "c,w,1,2,3,4,5").split(","))
    p = Prog(upto=upto, dbg=dbg, mode=mode)
    p.alloc_consts()
    p.kbar_on = "c" not in relax
    p.phase_consts()
    if upto >= 0:
        p.kbar_on = "w" not in relax
        p.phase_w2eff()
    if upto >= 1:
        p.kbar_on = "1" not in relax
        for b in range(BPC):
            p.phase_l0_attn(b)
    if upto >= 2:
        p.kbar_on = "2" not in relax
        p.phase_l0_ffn()
    if upto >= 3:
        p.kbar_on = "3" not in relax
        p.phase_l1_proj()
    if upto >= 4:
        p.kbar_on = "4" not in relax
        for b in range(BPC):
            p.phase_l1_delta(b)
    if upto >= 5:
        p.kbar_on = "5" not in relax
        p.phase_l1_ffn()
    p.kbar_on = False
    nc = p.finish()
    return nc


def host_layout(inputs):
    f32 = np.float32
    qkv_w = np.asarray(inputs["L0_attn_qkv_w"], f32)
    qkv_b = np.asarray(inputs["L0_attn_qkv_b"], f32)
    d = {}
    d["wq0"] = relay_lhsT(np.ascontiguousarray(qkv_w[0:DM].T), NDT, 512)
    d["wk0"] = relay_lhsT(np.ascontiguousarray(qkv_w[DM:2 * DM].T), NDT, 512)
    d["wv0"] = relay_lhsT(np.ascontiguousarray(qkv_w[2 * DM:3 * DM].T), NDT, 512)
    d["qb0"] = np.ascontiguousarray(qkv_b[0:DM].reshape(16, 128).T)
    d["kb0"] = np.ascontiguousarray(qkv_b[DM:2 * DM].reshape(16, 128).T)
    d["vb0_row"] = np.ascontiguousarray(qkv_b[2 * DM:3 * DM].reshape(1, DM))
    d["wo0"] = relay_lhsT(np.ascontiguousarray(np.asarray(inputs["L0_attn_out_w"], f32).T), NDT, 128)
    d["ob0"] = np.ascontiguousarray(np.asarray(inputs["L0_attn_out_b"], f32).reshape(16, 128).T)
    d["alibi_rep"] = np.ascontiguousarray(
        np.repeat(np.asarray(inputs["L0_attn_alibi_slopes"], f32), 16).reshape(1, 256))
    W1 = np.asarray(inputs["L0_ffn_W1"], f32); W3 = np.asarray(inputs["L0_ffn_W3"], f32)
    d["w1t"] = np.ascontiguousarray(
        W1.reshape(NHT, 128, NDT, 128).transpose(0, 3, 2, 1).reshape(NHT, 128, DM))
    d["w3t"] = np.ascontiguousarray(
        W3.reshape(NHT, 128, NDT, 128).transpose(0, 3, 2, 1).reshape(NHT, 128, DM))
    d["w2t"] = np.ascontiguousarray(np.asarray(inputs["L0_ffn_W2"], f32).T)
    d["h128"] = hada(128).astype(BF)
    d["h64"] = hada(64).astype(BF)
    d["identity"] = np.eye(128, dtype=np.float32).astype(BF)
    d["identity_f"] = np.eye(128, dtype=np.float32)
    d["wq1"] = relay_lhsT(np.asarray(inputs["L1_attn_Wq"], f32), NDT, 128)
    d["wk1"] = relay_lhsT(np.asarray(inputs["L1_attn_Wk"], f32), NDT, 128)
    d["wv1"] = relay_lhsT(np.asarray(inputs["L1_attn_Wv"], f32), NDT, 128)
    d["wg1"] = np.ascontiguousarray(np.asarray(inputs["L1_attn_Wg"], f32))
    d["wa1"] = np.ascontiguousarray(
        np.asarray(inputs["L1_attn_Wa"], f32).reshape(16, 128, 16).transpose(1, 0, 2).reshape(128, 256))
    d["wb1"] = np.ascontiguousarray(
        np.asarray(inputs["L1_attn_Wb"], f32).reshape(16, 128, 16).transpose(1, 0, 2).reshape(128, 256))
    d["ba1"] = np.ascontiguousarray(np.asarray(inputs["L1_attn_ba"], f32).reshape(16, 1))
    d["bb1"] = np.ascontiguousarray(np.asarray(inputs["L1_attn_bb"], f32).reshape(16, 1))
    d["wo1"] = relay_lhsT(np.asarray(inputs["L1_attn_Wo"], f32), NDT, 128)
    for nm, key in (("qcw", "L1_attn_qcw"), ("kcw", "L1_attn_kcw"), ("vcw", "L1_attn_vcw")):
        w = np.asarray(inputs[key], f32)
        d[nm] = np.ascontiguousarray(w.reshape(16, 128, 4).transpose(1, 0, 2).reshape(128, 64))
    for nm, key in (("qcb", "L1_attn_qcb"), ("kcb", "L1_attn_kcb"), ("vcb", "L1_attn_vcb")):
        d[nm] = np.ascontiguousarray(np.asarray(inputs[key], f32).reshape(16, 128).T)
    d["lnw_rep"] = np.ascontiguousarray(np.tile(np.asarray(inputs["L1_attn_ln_w"], f32), 16).reshape(1, DM))
    d["lnb_rep"] = np.ascontiguousarray(np.tile(np.asarray(inputs["L1_attn_ln_b"], f32), 16).reshape(1, DM))
    d["wgate"] = np.ascontiguousarray(np.asarray(inputs["L1_ffn_w_gate"], f32))
    d["wup"] = np.ascontiguousarray(np.asarray(inputs["L1_ffn_w_up"], f32))
    d["wdown"] = np.ascontiguousarray(np.asarray(inputs["L1_ffn_w_down"], f32))
    return d


def kernel(**inputs):
    x = np.asarray(inputs["L0_attn_hidden_states"], np.float32)
    shared = host_layout(inputs)
    nc = build_program()
    in_maps = []
    for c in range(8):
        m = dict(shared)
        xs = np.ascontiguousarray(np.transpose(x[2 * c:2 * c + 2], (0, 2, 1)))
        m["xin"] = xs
        in_maps.append(m)
    res = run_bass_kernel_spmd(nc, in_maps, core_ids=list(range(8)))
    outs = []
    for c in range(8):
        o = np.asarray(res.results[c]["out"])
        outs.append(np.transpose(o, (0, 2, 1)))
    return np.concatenate(outs, axis=0).astype(np.float32)


if __name__ == "__main__":
    pass


# version 10
# speedup vs baseline: 2.7217x; 1.0359x over previous
import os
import numpy as np
import ml_dtypes
import concourse.bass as bass
import concourse.mybir as mybir
from concourse.bass_utils import run_bass_kernel_spmd
from contextlib import ExitStack

F32 = mybir.dt.float32
BF16 = mybir.dt.bfloat16
AF = mybir.ActivationFunctionType
ALU = mybir.AluOpType
AXX = mybir.AxisListType.X

BPC = 2
S = 2048
DM = 2048
NDT = DM // 128
NH = 16
HD = 128
H0 = 8192
NHT = H0 // 128
CK = 128
NCK = S // CK
INV_SQRT_HD = float(1.0 / np.sqrt(128.0))
LN_EPS = 1e-5
DSC = float(128.0 ** -0.5)
BF = ml_dtypes.bfloat16


def hada(n):
    H = np.array([[1.0]], dtype=np.float32)
    while H.shape[0] < n:
        H = np.block([[H, H], [H, -H]])
    return H


def relay_lhsT(w, din_tiles, dout_block):
    din, dout = w.shape
    nb = dout // dout_block
    assert din == din_tiles * 128
    r = w.reshape(din_tiles, 128, nb, dout_block).transpose(2, 1, 0, 3)
    return np.ascontiguousarray(r.reshape(nb, 128, din_tiles * dout_block))


class Prog:
    def __init__(self, upto=99, dbg=None, mode="auto"):
        self.upto = upto
        self.dbg = dbg
        nc = bass.Bass("TRN2", target_bir_lowering=False, debug=False)
        self.nc = nc
        self.cnt = 0
        self.mode = mode
        self.ops = {e: [] for e in ("tensor", "vector", "scalar", "gpsimd", "sync")}
        self.emitted = {e: 0 for e in self.ops}
        self.waited = {}
        self.acc = {}
        self.floor = {}
        self.input_names = set()
        self.psum_names = set()
        import os
        se = os.environ.get("KSE", "tensor")
        self.same_engine_ok = set(se.split(",")) if se else set()
        self.bar = {}
        orig_sbuf = nc.sbuf_tensor
        prog = self
        class _SbGuard:
            def __init__(self, cm):
                self.cm = cm
            def __enter__(self):
                return self.cm.__enter__()
            def __exit__(self, *a):
                r = self.cm.__exit__(*a)
                bar = dict(prog.bar)
                for e2 in prog.ops:
                    if prog.emitted[e2] > 0:
                        bar[e2] = prog.emitted[e2]
                for k2, u in prog.dslot_use.items():
                    if u > 0:
                        bar[k2] = 16 * u
                prog.bar = bar
                return r
        def wrapped(*args, **kw):
            return _SbGuard(orig_sbuf(*args, **kw))
        nc.sbuf_tensor = wrapped
        self.st = ExitStack()
        self._declare_dram()

    MAXREC = 160

    BIGP = 1 << 40
    BIGF = 1 << 60

    def _iv(self, ap):
        try:
            dims = list(ap.ap)
            off = ap.offset
            t = ap.tensor
            rowlen = 1
            for d in t.shape[1:]:
                rowlen *= int(d)
            if dims and rowlen > 0 and dims[0][0] == rowlen:
                p0 = off // rowlen
                f0 = off % rowlen
                pn = dims[0][1]
                fspan = sum(abs(st) * (cn - 1) for st, cn in dims[1:])
                if f0 + fspan < rowlen:
                    return (p0, p0 + pn - 1, f0, f0 + fspan)
            lo = off
            span = sum(abs(st) * (cn - 1) for st, cn in dims)
            return (0, self.BIGP, lo, lo + span)
        except Exception:
            return (0, self.BIGP, 0, self.BIGF)

    def _scan(self, eng, isdma, ap, want_writes_only, need):
        t = ap.tensor
        name = t.name
        if name in self.input_names:
            return
        recs = self.acc.get(name)
        flo = self.floor.get(name)
        if flo:
            for e2, c2 in flo.items():
                if not (e2 == eng and not isdma):
                    if need.get(e2, 0) < c2:
                        need[e2] = c2
        if not recs:
            return
        if name in self.psum_names:
            rect = (0, self.BIGP, 0, self.BIGF)
        else:
            rect = self._iv(ap)
        plo, phi, flo, fhi = rect
        for (w2, e2, c2, r2, dma2) in recs:
            if want_writes_only and not w2:
                continue
            if fhi < r2[2] or r2[3] < flo or phi < r2[0] or r2[1] < plo:
                continue
            if e2 == eng and not isdma and not dma2 and eng in self.same_engine_ok:
                continue
            if need.get(e2, 0) < c2:
                need[e2] = c2

    def _rec(self, name, entry):
        lst = self.acc.setdefault(name, [])
        lst.append(entry)
        if len(lst) > self.MAXREC:
            drop = lst[: len(lst) - self.MAXREC]
            del lst[: len(lst) - self.MAXREC]
            flo = self.floor.setdefault(name, {})
            for (w2, e2, c2, r2, dma2) in drop:
                if flo.get(e2, 0) < c2:
                    flo[e2] = c2

    def _emit(self, eng, fn, reads=(), writes=(), inc=1):
        if self.mode == "chain":
            at = self.cnt
            def run(h, at=at, fn=fn, inc=inc):
                h.wait_ge(self.chain, at)
                r = fn(h)
                r.then_inc(self.chain, inc)
            self.ops[eng].append(run)
            self.cnt += inc
            return
        import os
        isdma = inc == 16
        need = {}
        for ap in writes:
            self._scan(eng, isdma, ap, False, need)
        for ap in reads:
            self._scan(eng, isdma, ap, True, need)
        if getattr(self, "kbar_on", False):
            for e2 in self.ops:
                if self.emitted[e2] > 0 and (e2 != eng or os.environ.get("KSELF")):
                    need[e2] = self.emitted[e2]
            for k2, u in self.dslot_use.items():
                if u > 0:
                    need[k2] = max(need.get(k2, 0), 16 * u)
        for k2, v2 in self.bar.items():
            if k2 != eng and need.get(k2, 0) < v2:
                need[k2] = v2
        if isdma:
            q = eng
            skey = ("d", q, self.dma_idx[q] % self.NSLOT)
            self.dma_idx[q] += 1
            use = self.dslot_use[skey]
            self.dslot_use[skey] = use + 1
            cnt_after = 16 * (use + 1)
            ekey = skey
            if use > 0:
                need[skey] = max(need.get(skey, 0), 16 * use)
        else:
            ekey = eng
            cnt_after = self.emitted[eng] + inc
            self.emitted[eng] = cnt_after
        waits = []
        for e2, v in need.items():
            key = (ekey, e2) if isdma else (eng, e2)
            if self.waited.get(key, 0) >= v:
                continue
            self.waited[key] = v
            waits.append((self.esem[e2], v))
        mysem = self.esem[ekey]
        def run(h, waits=tuple(waits), fn=fn, mysem=mysem, inc=inc):
            for smm, v in waits:
                h.wait_ge(smm, v)
            r = fn(h)
            r.then_inc(mysem, inc)
        self.ops[eng].append(run)
        for ap in writes:
            name = ap.tensor.name
            if name in self.input_names:
                continue
            r = (0, self.BIGP, 0, self.BIGF) if name in self.psum_names else self._iv(ap)
            self._rec(name, (True, ekey, cnt_after, r, isdma))
        for ap in reads:
            name = ap.tensor.name
            if name in self.input_names:
                continue
            r = (0, self.BIGP, 0, self.BIGF) if name in self.psum_names else self._iv(ap)
            self._rec(name, (False, ekey, cnt_after, r, isdma))

    def t(self, fn, reads=(), writes=()): self._emit("tensor", fn, reads, writes)
    def v(self, fn, reads=(), writes=()): self._emit("vector", fn, reads, writes)
    def s(self, fn, reads=(), writes=()): self._emit("scalar", fn, reads, writes)
    def g(self, fn, reads=(), writes=()): self._emit("gpsimd", fn, reads, writes)
    def dma(self, out, in_, q="sync"):
        self._emit(q, lambda h, o=out, i=in_: h.dma_start(out=o, in_=i),
                   reads=(in_,), writes=(out,), inc=16)
    def gdma(self, out, in_):
        self.dma(out, in_, q="gpsimd")

    def mm(self, out, lhsT, rhs, start=True, stop=True):
        self.t(lambda t, o=out, l=lhsT, r=rhs, st=start, sp=stop:
               t.matmul(o, l, r, start=st, stop=sp),
               reads=(lhsT, rhs), writes=(out,))
    def tr(self, out, in_, f32=False):
        k = in_.shape[0]
        ident = self.ident_f if f32 else self.ident_bf
        self.t(lambda t, o=out, i=in_, g=ident, kk=k: t.transpose(o, i, g[:kk, :kk]),
               reads=(in_, ident[:, :]), writes=(out,))
    def act(self, out, in_, func, bias=0.0, scale=1.0, accum_out=None):
        rd = [in_]
        if not isinstance(bias, float):
            rd.append(bias)
        wr = [out] + ([accum_out] if accum_out is not None else [])
        self.s(lambda sc, o=out, i=in_, f=func, b=bias, s_=scale, a=accum_out:
               sc.activation(o, i, f, bias=b, scale=s_, accum_out=a),
               reads=tuple(rd), writes=tuple(wr))
    def scopy(self, out, in_):
        self.s(lambda sc, o=out, i=in_: sc.copy(o, i), reads=(in_,), writes=(out,))
    def vcopy(self, out, in_):
        self.v(lambda v, o=out, i=in_: v.tensor_copy(o, i), reads=(in_,), writes=(out,))
    def tt(self, out, in0, in1, op):
        self.v(lambda v, o=out, a=in0, b=in1, p=op: v.tensor_tensor(o, a, b, p),
               reads=(in0, in1), writes=(out,))
    def ts(self, out, in0, s1, s2=None, op0=ALU.mult, op1=ALU.bypass, accum_out=None):
        rd = [in0]
        if not isinstance(s1, float):
            rd.append(s1)
        if s2 is not None and not isinstance(s2, float):
            rd.append(s2)
        wr = [out] + ([accum_out] if accum_out is not None else [])
        self.v(lambda v, o=out, i=in0, a=s1, b=s2, p0=op0, p1=op1, ac=accum_out:
               v.tensor_scalar(o, i, a, b, p0, p1, accum_out=ac),
               reads=tuple(rd), writes=tuple(wr))
    def stt(self, out, in0, s1, in1, op0, op1):
        rd = [in0, in1]
        if not isinstance(s1, float):
            rd.append(s1)
        self.v(lambda v, o=out, i=in0, a=s1, j=in1, p0=op0, p1=op1:
               v.scalar_tensor_tensor(o, i, a, j, p0, p1),
               reads=tuple(rd), writes=(out,))
    def recip(self, out, in_):
        self.v(lambda v, o=out, i=in_: v.reciprocal(o, i), reads=(in_,), writes=(out,))
    def rsum(self, out, in_):
        self.v(lambda v, o=out, i=in_: v.reduce_sum(o, i, AXX),
               reads=(in_,), writes=(out,))
    def memset(self, ap, val):
        self.g(lambda g, a=ap, c=val: g.memset(a, c), reads=(), writes=(ap,))
    def asel(self, out, in_, pattern, cmp, fill, base=0, chmul=1):
        self.g(lambda g, o=out, i=in_, p=pattern, c=cmp, f=fill, b=base, m=chmul:
               g.affine_select(o, i, p, c, f, base=b, channel_multiplier=m),
               reads=(in_,), writes=(out,))

    def _declare_dram(self):
        nc = self.nc
        self.inp = {}
        def I(name, shape, dt=F32):
            self.inp[name] = nc.dram_tensor(name, list(shape), dt, kind="ExternalInput").ap()
            self.input_names.add(name)
            return self.inp[name]
        I("xin", (BPC, DM, S))
        I("wq0", (4, 128, NDT * 512)); I("wk0", (4, 128, NDT * 512)); I("wv0", (4, 128, NDT * 512))
        I("qb0", (128, 16)); I("kb0", (128, 16)); I("vb0_row", (1, DM))
        I("wo0", (16, 128, NDT * 128)); I("ob0", (128, 16))
        I("alibi_rep", (1, 256))
        I("w1t", (NHT, 128, DM)); I("w3t", (NHT, 128, DM)); I("w2t", (H0, DM))
        I("h128", (128, 128), BF16); I("h64", (64, 64), BF16); I("identity", (128, 128), BF16)
        I("identity_f", (128, 128), F32)
        I("wq1", (16, 128, NDT * 128)); I("wk1", (16, 128, NDT * 128))
        I("wv1", (16, 128, NDT * 128)); I("wg1", (DM, DM))
        I("wa1", (128, 256)); I("wb1", (128, 256)); I("ba1", (16, 1)); I("bb1", (16, 1))
        I("wo1", (16, 128, NDT * 128))
        I("qcw", (128, 64)); I("kcw", (128, 64)); I("vcw", (128, 64))
        I("qcb", (128, 16)); I("kcb", (128, 16)); I("vcb", (128, 16))
        I("lnw_rep", (1, DM)); I("lnb_rep", (1, DM))
        I("wgate", (DM, H0)); I("wup", (DM, H0)); I("wdown", (H0, DM))

        D = lambda name, shape, dt: nc.dram_tensor(name, list(shape), dt, kind="Internal").ap()
        self.w2eff = D("w2eff", (H0, DM), BF16)
        self.bh = D("bh", (NHT, 128, DM), BF16)
        self.aoh = D("aoh", (DM, S), BF16)
        self.x1f = D("x1f", (BPC, DM, S), F32)
        self.x1b = D("x1b", (DM, BPC * S), BF16)
        self.hid0 = D("hid0", (NHT, 128, BPC * S), BF16)
        self.part0 = D("part0", (NDT, 128, BPC * S), F32)
        self.x2f = D("x2f", (BPC, DM, S), F32)
        self.x2b = D("x2b", (DM, BPC * S), BF16)
        self.xwq = D("xwq", (DM, BPC * S), BF16)
        self.xwk = D("xwk", (DM, BPC * S), BF16)
        self.xwv = D("xwv", (DM, BPC * S), BF16)
        self.xwg_tm = D("xwg_tm", (BPC * S, DM), BF16)
        self.x3f = D("x3f", (BPC, DM, S), F32)
        self.x3b = D("x3b", (DM, BPC * S), BF16)
        self.hid1 = D("hid1", (NHT, 128, BPC * S), BF16)
        self.part1 = D("part1", (NDT, 128, BPC * S), F32)
        self.sigh = D("sigh", (16, 2 * BPC * S), F32)
        self.rowtmp = D("rowtmp", (16, CK), F32)
        self.rowtmp2 = D("rowtmp2", (16, CK), F32)

        self.out = nc.dram_tensor("out", [BPC, DM, S], F32, kind="ExternalOutput").ap()
        if self.dbg is not None:
            shp = {"x1f": (BPC, DM, S), "x2f": (BPC, DM, S), "x3f": (BPC, DM, S),
                   "sigh": (16, 2 * BPC * S),
                   "w2eff": (H0, DM), "aoh": (DM, S), "x1b": (DM, BPC * S),
                   "hid0": (NHT, 128, BPC * S), "hid1": (NHT, 128, BPC * S),
                   "xwq": (DM, BPC * S), "xwk": (DM, BPC * S), "xwv": (DM, BPC * S),
                   "xwg_tm": (BPC * S, DM),
                   "part0": (NDT, 128, BPC * S)}[self.dbg]
            dt = BF16 if self.dbg in ("w2eff", "aoh", "x1b", "hid0", "hid1", "xwq", "xwk", "xwv", "xwg_tm") else F32
            self.dbg_out = nc.dram_tensor("dbg", list(shp), dt, kind="ExternalOutput").ap()

    def alloc_consts(self):
        E = self.st.enter_context
        nc = self.nc
        self.chain = E(nc.semaphore("chain"))
        self.esem = {e: E(nc.semaphore(f"sem_{e}")) for e in self.ops}
        self.NSLOT = 24
        for q in ("sync", "gpsimd"):
            for si in range(self.NSLOT):
                self.esem[("d", q, si)] = E(nc.semaphore(f"ds_{q}_{si}"))
        self.dslot_use = {("d", q, si): 0 for q in ("sync", "gpsimd")
                          for si in range(self.NSLOT)}
        self.dma_idx = {"sync": 0, "gpsimd": 0}
        self.ident_bf = E(nc.sbuf_tensor("ident_bf", [128, 128], BF16))
        self.ident_f = E(nc.sbuf_tensor("ident_f", [128, 128], F32))
        self.h128_sb = E(nc.sbuf_tensor("h128_sb", [128, 128], BF16))
        self.h64_sb = E(nc.sbuf_tensor("h64_sb", [64, 64], BF16))
        self.ones_col_bf = E(nc.sbuf_tensor("ones_col_bf", [128, 1], BF16))
        self.ones_col_f = E(nc.sbuf_tensor("ones_col_f", [128, 1], F32))
        self.ones_row_f = E(nc.sbuf_tensor("ones_row_f", [1, 128], F32))
        self.iota_row = E(nc.sbuf_tensor("iota_row", [1, 128], F32))
        self.negoff_row = E(nc.sbuf_tensor("negoff_row", [1, 256], F32))
        self.slope_rep = E(nc.sbuf_tensor("slope_rep", [1, 256], F32))
        self.r2_row = E(nc.sbuf_tensor("r2_row", [1, 256], F32))
        self.bias_tab = E(nc.sbuf_tensor("bias_tab", [128, 256], F32))
        self.qb_sb = E(nc.sbuf_tensor("qb_sb", [128, 16], F32))
        self.kb_sb = E(nc.sbuf_tensor("kb_sb", [128, 16], F32))
        self.ob_sb = E(nc.sbuf_tensor("ob_sb", [128, 16], F32))
        self.ps = [E(nc.psum_tensor(f"ps{i}", [128, 512], F32)) for i in range(4)]
        self.ps_bf = E(nc.psum_tensor("ps_bf", [128, 1024], BF16))
        self.ps_row = E(nc.psum_tensor("ps_row", [128, 512], F32))
        for i in range(4):
            self.psum_names.add(f"ps{i}")
        self.psum_names.update(("ps_bf", "ps_row", "ps_row2", "ps4"))
        self.epsc = E(nc.sbuf_tensor("epsc", [128, 1], F32))
        nc.const_aps.register(F32, LN_EPS, self.epsc[:, :])
        self.ps_row2 = E(nc.psum_tensor("ps_row2", [128, 512], F32))
        self.ps4 = E(nc.psum_tensor("ps4", [128, 512], F32))
        self.mask_ge_bf = E(nc.sbuf_tensor("mask_ge_bf", [128, 128], BF16))
        self.mask_gt_f = E(nc.sbuf_tensor("mask_gt_f", [128, 128], F32))
        self.mask_ge_f = E(nc.sbuf_tensor("mask_ge_f", [128, 128], F32))

    def phase_consts(self):
        p = self
        p.dma(p.ident_bf[:, :], p.inp["identity"][:, :])
        p.dma(p.ident_f[:, :], p.inp["identity_f"][:, :])
        p.dma(p.h128_sb[:, :], p.inp["h128"][:, :])
        p.dma(p.h64_sb[:, :], p.inp["h64"][:, :])
        p.dma(p.slope_rep[:, :], p.inp["alibi_rep"][:, :])
        p.dma(p.qb_sb[:, :], p.inp["qb0"][:, :])
        p.dma(p.kb_sb[:, :], p.inp["kb0"][:, :])
        p.dma(p.ob_sb[:, :], p.inp["ob0"][:, :])
        p.memset(p.epsc[:, :], LN_EPS)
        p.memset(p.ones_col_bf[:, :], 1.0)
        p.memset(p.ones_col_f[:, :], 1.0)
        p.memset(p.ones_row_f[:, :], 1.0)
        p.g(lambda g: g.iota(p.iota_row[:, :], [[1, 128]], base=0, channel_multiplier=0,
                             allow_small_or_imprecise_dtypes=True),
            writes=(p.iota_row[:, :],))
        p.g(lambda g: g.iota(p.negoff_row[:, :], [[0, 16], [-128, 16]], base=-64,
                             channel_multiplier=0, allow_small_or_imprecise_dtypes=True),
            writes=(p.negoff_row[:, :],))
        p.memset(p.mask_ge_bf[:, :], 1e38)
        p.asel(p.mask_ge_bf[:, :], p.mask_ge_bf[:, :], [[1, 128]], ALU.is_ge, 0.0,
               base=0, chmul=-1)
        p.memset(p.mask_gt_f[:, :], 1e38)
        p.asel(p.mask_gt_f[:, :], p.mask_gt_f[:, :], [[1, 128]], ALU.is_gt, 0.0,
               base=0, chmul=-1)
        p.memset(p.mask_ge_f[:, :], 1e38)
        p.asel(p.mask_ge_f[:, :], p.mask_ge_f[:, :], [[1, 128]], ALU.is_ge, 0.0,
               base=0, chmul=-1)
        p.tt(p.r2_row[:, :], p.slope_rep[:, :], p.negoff_row[:, :], ALU.mult)
        p.mm(p.ps[0][:, :256], p.iota_row[:, :], p.slope_rep[:, :], start=True, stop=False)
        p.mm(p.ps[0][:, :256], p.ones_row_f[:, :], p.r2_row[:, :], start=False, stop=True)
        p.scopy(p.bias_tab[:, :], p.ps[0][:, :256])

    def bcast_row_2048(self, dst, srcname, row_tmp):
        p = self
        p.dma(row_tmp[:, :], p.inp[srcname][:, :])
        for n in range(4):
            p.mm(p.ps[1][:, :], p.ones_row_f[:, :], row_tmp[:, n * 512:(n + 1) * 512])
            p.scopy(dst[:, n * 512:(n + 1) * 512], p.ps[1][:, :])

    def phase_w2eff(self):
        p = self
        E = self.st
        with (self.nc.sbuf_tensor("w2tile", [128, 2 * DM], BF16) as w2tile,
              self.nc.sbuf_tensor("bh_sb", [128, 2 * DM], BF16) as bh_sb,
              self.nc.sbuf_tensor("bsteb", [64, 2 * 4 * DM], BF16) as bsteb,
              self.nc.sbuf_tensor("w2e_sb", [64, 2 * 4 * DM], BF16) as w2e_sb):
            for hi in range(NHT):
                par = hi % 2
                wt = w2tile[:, par * DM:(par + 1) * DM]
                bs = bh_sb[:, par * DM:(par + 1) * DM]
                p.gdma(wt, p.inp["w2t"][hi * 128:(hi + 1) * 128, :])
                for n in range(4):
                    p.mm(p.ps[par][:, :], p.h128_sb[:, :], wt[:, n * 512:(n + 1) * 512])
                    p.scopy(bs[:, n * 512:(n + 1) * 512], p.ps[par][:, :])
                p.dma(p.bh[hi, :, :], bs)
            bh4 = p.bh.rearrange("a p d -> a (p d)")
            w2r = p.w2eff.rearrange("(a b) d -> a b d", b=128)
            for lo0 in range(0, 128, 4):
                par = (lo0 // 4) % 2
                bst = bsteb[:, par * 4 * DM:(par + 1) * 4 * DM]
                w2e = w2e_sb[:, par * 4 * DM:(par + 1) * 4 * DM]
                p.dma(bst, bh4[:, lo0 * DM:(lo0 + 4) * DM])
                for j in range(4):
                    for n in range(4):
                        sl = slice(j * DM + n * 512, j * DM + (n + 1) * 512)
                        p.mm(p.ps[par][:64, :], p.h64_sb[:, :], bst[:, sl])
                        p.act(w2e[:64, sl], p.ps[par][:64, :], AF.Identity,
                              scale=float(1.0 / np.sqrt(8192.0)))
                    p.dma(w2r[:, lo0 + j, :], w2e[:64, j * DM:(j + 1) * DM])

    def phase_l0_attn(self, b):
        p = self
        nc = self.nc
        with (nc.sbuf_tensor(f"xbf{b}", [128, NDT * S], BF16) as xbf,
              nc.sbuf_tensor(f"wbuf{b}", [128, NDT * 512], BF16) as wbuf,
              nc.sbuf_tensor(f"wbuf2{b}", [128, NDT * 512], BF16) as wbuf2,
              nc.sbuf_tensor(f"qg{b}", [128, 4 * S], BF16) as qg,
              nc.sbuf_tensor(f"kg{b}", [128, 4 * S], BF16) as kg,
              nc.sbuf_tensor(f"vtm{b}", [128, 16 * 512], BF16) as vtm,
              nc.sbuf_tensor(f"probs{b}", [128, 2 * 16 * 128], BF16) as probs0,
              nc.sbuf_tensor(f"rec_row{b}", [1, 256], F32) as rec_row,
              nc.sbuf_tensor(f"recb{b}", [128, 256], F32) as recb,
              nc.sbuf_tensor(f"ao_sb{b}", [128, 512], BF16) as ao_sb,
              nc.sbuf_tensor(f"bv_bc{b}", [128, DM], F32) as bv_bc,
              nc.sbuf_tensor(f"rowt{b}", [1, DM], F32) as rowt):
            self.bcast_row_2048(bv_bc, "vb0_row", rowt)
            for dt in range(NDT):
                p.gdma(xbf[:, dt * S:(dt + 1) * S], p.inp["xin"][b, dt * 128:(dt + 1) * 128, :])
            for g in range(4):
                wb_ = [wbuf, wbuf2][(3 * g) % 2]
                p.gdma(wb_[:, :], p.inp["wq0"][g, :, :])
                for hl in range(4):
                    for n in range(4):
                        pp = p.ps[(hl * 4 + n) % 2 * 2][:, :]
                        for dt in range(NDT):
                            p.mm(pp, wb_[:, dt * 512 + hl * 128:dt * 512 + (hl + 1) * 128],
                                 xbf[:, dt * S + n * 512:dt * S + (n + 1) * 512],
                                 start=(dt == 0), stop=(dt == NDT - 1))
                        p.act(qg[:, hl * S + n * 512:hl * S + (n + 1) * 512], pp,
                              AF.Identity, bias=p.qb_sb[:, g * 4 + hl:g * 4 + hl + 1])
                wb_ = [wbuf, wbuf2][(3 * g + 1) % 2]
                p.gdma(wb_[:, :], p.inp["wk0"][g, :, :])
                for hl in range(4):
                    for n in range(4):
                        pp = p.ps[(hl * 4 + n) % 2 * 2][:, :]
                        for dt in range(NDT):
                            p.mm(pp, wb_[:, dt * 512 + hl * 128:dt * 512 + (hl + 1) * 128],
                                 xbf[:, dt * S + n * 512:dt * S + (n + 1) * 512],
                                 start=(dt == 0), stop=(dt == NDT - 1))
                        p.act(kg[:, hl * S + n * 512:hl * S + (n + 1) * 512], pp,
                              AF.Identity, bias=p.kb_sb[:, g * 4 + hl:g * 4 + hl + 1])
                wb_ = [wbuf, wbuf2][(3 * g + 2) % 2]
                p.gdma(wb_[:, :], p.inp["wv0"][g, :, :])
                for tt in range(16):
                    pp = p.ps[tt % 2 * 2][:, :]
                    for dt in range(NDT):
                        p.mm(pp, xbf[:, dt * S + tt * 128:dt * S + (tt + 1) * 128],
                             wb_[:, dt * 512:(dt + 1) * 512],
                             start=(dt == 0), stop=(dt == NDT - 1))
                    p.stt(vtm[:, tt * 512:(tt + 1) * 512], pp, 1.0,
                          bv_bc[:, g * 512:(g + 1) * 512], ALU.mult, ALU.add)
                import os
                if os.environ.get("KDUMPQKV") and g == 0 and b == 1:
                    for hl in range(4):
                        p.dma(p.aoh[hl * 128:(hl + 1) * 128, :], qg[:, hl * S:(hl + 1) * S])
                        p.dma(p.aoh[512 + hl * 128:512 + (hl + 1) * 128, :], kg[:, hl * S:(hl + 1) * S])
                    for tt in range(16):
                        p.dma(p.aoh.rearrange("d s -> (d s)").rearrange(
                              "(a p f) -> a p f", a=32, p=128)[8 + tt // 2, :,
                              (tt % 2) * 512:(tt % 2) * 512 + 512],
                              vtm[:, tt * 512:(tt + 1) * 512])
                for hl in range(4 if not os.environ.get("KDUMPQKV") else 0):
                    h = g * 4 + hl
                    for qc in range(16):
                        probs = probs0[:, (qc % 2) * 2048:(qc % 2 + 1) * 2048]
                        for kt in range(qc + 1):
                            pp = p.ps[(kt % 2) * 3][:, :128]
                            p.mm(pp, kg[:, hl * S + kt * 128:hl * S + (kt + 1) * 128],
                                 qg[:, hl * S + qc * 128:hl * S + (qc + 1) * 128])
                            p.act(probs[:, kt * 128:(kt + 1) * 128], pp, AF.Exp,
                                  bias=p.bias_tab[:, h * 16 + (qc - kt):h * 16 + (qc - kt) + 1],
                                  scale=INV_SQRT_HD)
                            if kt == qc:
                                p.tt(probs[:, kt * 128:(kt + 1) * 128],
                                     probs[:, kt * 128:(kt + 1) * 128],
                                     p.mask_ge_bf[:, :], ALU.min)
                            pd = [p.ps_row, p.ps_row2][qc % 2]
                            po = [p.ps[1], p.ps4][qc % 2]
                            p.mm(pd[0:1, :128], p.ones_col_bf[:, :],
                                 probs[:, kt * 128:(kt + 1) * 128],
                                 start=(kt == 0), stop=(kt == qc))
                            p.mm(po[:, :128], vtm[:, kt * 512 + hl * 128:kt * 512 + (hl + 1) * 128],
                                 probs[:, kt * 128:(kt + 1) * 128],
                                 start=(kt == 0), stop=(kt == qc))
                        p.recip(rec_row[:, (qc % 2) * 128:(qc % 2) * 128 + 128], pd[0:1, :128])
                        p.mm(p.ps[2][:, (qc % 2) * 128:(qc % 2) * 128 + 128], p.ones_row_f[:, :],
                             rec_row[:, (qc % 2) * 128:(qc % 2) * 128 + 128])
                        p.scopy(recb[:, (qc % 2) * 128:(qc % 2) * 128 + 128],
                                p.ps[2][:, (qc % 2) * 128:(qc % 2) * 128 + 128])
                        p.tt(ao_sb[:, (qc % 4) * 128:(qc % 4 + 1) * 128], po[:, :128],
                             recb[:, (qc % 2) * 128:(qc % 2) * 128 + 128], ALU.mult)
                        if qc % 4 == 3:
                            p.dma(p.aoh[h * 128:(h + 1) * 128, (qc - 3) * 128:(qc + 1) * 128],
                                  ao_sb[:, :])
        self._proj_epilogue(wname="wo0", act_hbm=p.aoh, act_cols=S, b=b,
                            res_in=p.inp["xin"], outf=p.x1f, outb=p.x1b,
                            bias_sb=p.ob_sb, tag=f"l0a{b}")

    def _proj_epilogue(self, wname, act_hbm, act_cols, b, res_in, outf, outb, bias_sb,
                       part_in=None, tokchunk=512, tag=""):
        p = self
        nc = self.nc
        nchunk = S // tokchunk
        with (nc.sbuf_tensor(f"wo_full{tag}", [128, 16 * NDT * 128], BF16) as wo_full,
              nc.sbuf_tensor(f"aobuf{tag}", [128, NDT * tokchunk], BF16) as aobuf,
              nc.sbuf_tensor(f"x1pre{tag}", [128, NDT * tokchunk], F32) as x1pre,
              nc.sbuf_tensor(f"xres{tag}", [128, tokchunk], F32) as xres,
              nc.sbuf_tensor(f"sqsb{tag}", [128, tokchunk], F32) as sqsb,
              nc.sbuf_tensor(f"rows{tag}", [1, 6 * tokchunk], F32) as rows,
              nc.sbuf_tensor(f"ab_sb{tag}", [128, 2 * tokchunk], F32) as ab_sb,
              nc.sbuf_tensor(f"xo_sb{tag}", [128, tokchunk], F32) as xo_sb,
              nc.sbuf_tensor(f"xb_sb{tag}", [128, tokchunk], BF16) as xb_sb):
            for ft in range(16):
                p.gdma(wo_full[:, ft * NDT * 128:(ft + 1) * NDT * 128], p.inp[wname][ft, :, :])
            TC = tokchunk
            for n in range(nchunk):
                for at in range(NDT):
                    p.dma(aobuf[:, at * TC:(at + 1) * TC],
                          act_hbm[at * 128:(at + 1) * 128, n * TC:(n + 1) * TC])
                for ft in range(NDT):
                    for at in range(NDT):
                        p.mm(p.ps[0][:, :TC],
                             wo_full[:, ft * NDT * 128 + at * 128:ft * NDT * 128 + (at + 1) * 128],
                             aobuf[:, at * TC:(at + 1) * TC],
                             start=(at == 0), stop=(at == NDT - 1))
                    p.dma(xres[:, :], res_in[b, ft * 128:(ft + 1) * 128, n * TC:(n + 1) * TC])
                    p.act(x1pre[:, ft * TC:(ft + 1) * TC], p.ps[0][:, :TC], AF.Identity,
                          bias=bias_sb[:, ft:ft + 1])
                    p.tt(x1pre[:, ft * TC:(ft + 1) * TC], x1pre[:, ft * TC:(ft + 1) * TC],
                         xres[:, :], ALU.add)
                    p.mm(p.ps_row[0:1, :TC], p.ones_col_f[:, :], x1pre[:, ft * TC:(ft + 1) * TC],
                         start=(ft == 0), stop=(ft == NDT - 1))
                    p.act(sqsb[:, :], x1pre[:, ft * TC:(ft + 1) * TC], AF.Square)
                    p.mm(p.ps_row2[0:1, :TC], p.ones_col_f[:, :], sqsb[:, :],
                         start=(ft == 0), stop=(ft == NDT - 1))
                MU, E2, VAR, STD, RA, RB = (rows[:, i * TC:(i + 1) * TC] for i in range(6))
                p.ts(MU, p.ps_row[0:1, :TC], 1.0 / DM, None, ALU.mult)
                p.ts(E2, p.ps_row2[0:1, :TC], 1.0 / DM, None, ALU.mult)
                p.tt(VAR, MU, MU, ALU.mult)
                p.tt(VAR, E2, VAR, ALU.subtract)
                p.act(STD, VAR, AF.Sqrt, bias=LN_EPS)
                p.recip(RA, STD)
                p.tt(RB, MU, RA, ALU.mult)
                p.ts(RB, RB, -1.0, None, ALU.mult)
                p.mm(p.ps[1][:, :TC], p.ones_row_f[:, :], RA)
                p.scopy(ab_sb[:, :TC], p.ps[1][:, :TC])
                p.mm(p.ps[1][:, :TC], p.ones_row_f[:, :], RB)
                p.scopy(ab_sb[:, TC:2 * TC], p.ps[1][:, :TC])
                for ft in range(NDT):
                    p.tt(xo_sb[:, :], x1pre[:, ft * TC:(ft + 1) * TC], ab_sb[:, :TC], ALU.mult)
                    p.tt(xo_sb[:, :], xo_sb[:, :], ab_sb[:, TC:2 * TC], ALU.add)
                    p.dma(outf[b, ft * 128:(ft + 1) * 128, n * TC:(n + 1) * TC], xo_sb[:, :])
                    p.scopy(xb_sb[:, :], xo_sb[:, :])
                    p.dma(outb[ft * 128:(ft + 1) * 128, b * S + n * TC:b * S + (n + 1) * TC],
                          xb_sb[:, :])

    def phase_l0_ffn(self):
        p = self
        nc = self.nc
        T = BPC * S
        with (nc.sbuf_tensor("ffa_x", [128, NDT * T], BF16) as xa,
              nc.sbuf_tensor("ffa_w1", [128, 2 * DM], BF16) as w1b,
              nc.sbuf_tensor("ffa_w3", [128, 2 * DM], BF16) as w3b,
              nc.sbuf_tensor("ffa_hid", [128, 2 * T], BF16) as hidsb,
              nc.sbuf_tensor("ffa_silu", [128, 2 * 512], BF16) as silu_sb):
            for dt in range(NDT):
                p.dma(xa[:, dt * T:(dt + 1) * T], p.x1b[dt * 128:(dt + 1) * 128, :])
            for ht in range(NHT):
                par = ht % 2
                w1s = w1b[:, par * DM:(par + 1) * DM]
                w3s = w3b[:, par * DM:(par + 1) * DM]
                p.gdma(w1s, p.inp["w1t"][ht, :, :])
                p.gdma(w3s, p.inp["w3t"][ht, :, :])
                for n in range(T // 512):
                    pg = p.ps[(n % 2) * 2][:, :]
                    pl = p.ps[(n % 2) * 2 + 1][:, :]
                    sl = silu_sb[:, (n % 2) * 512:(n % 2 + 1) * 512]
                    for dt in range(NDT):
                        p.mm(pg, w1s[:, dt * 128:(dt + 1) * 128],
                             xa[:, dt * T + n * 512:dt * T + (n + 1) * 512],
                             start=(dt == 0), stop=(dt == NDT - 1))
                    p.act(sl, pg, AF.Silu)
                    for dt in range(NDT):
                        p.mm(pl, w3s[:, dt * 128:(dt + 1) * 128],
                             xa[:, dt * T + n * 512:dt * T + (n + 1) * 512],
                             start=(dt == 0), stop=(dt == NDT - 1))
                    p.tt(hidsb[:, par * T + n * 512:par * T + (n + 1) * 512], pl, sl, ALU.mult)
                p.dma(p.hid0[ht, :, :], hidsb[:, par * T:(par + 1) * T])
        self._down_proj(p.hid0, p.w2eff, p.part0, p.x1f, p.x2f, p.x2b, tag="l0f")

    def _down_proj(self, hid, wsrc, part, resf, outf, outb, tag, wcast=False):
        p = self
        nc = self.nc
        T = BPC * S
        TC = 256
        for half in range(2):
            with ExitStack() as stk:
                E = stk.enter_context
                wsb = E(nc.sbuf_tensor(f"dp_w{tag}{half}", [128, 32 * DM], BF16))
                hch = E(nc.sbuf_tensor(f"dp_h{tag}{half}", [128, 2 * 32 * TC], BF16))
                aux = E(nc.sbuf_tensor(f"dp_aux{tag}{half}", [128, 4 * TC], F32))
                if half == 1:
                    raux = E(nc.sbuf_tensor(f"dp_raux{tag}{half}", [128, 4 * TC], F32))
                    pre = E(nc.sbuf_tensor(f"dp_pre{tag}{half}", [128, NDT * TC], F32))
                    sq_sb = E(nc.sbuf_tensor(f"dp_sq{tag}{half}", [128, TC], F32))
                    rows = E(nc.sbuf_tensor(f"dp_rows{tag}{half}", [1, 6 * TC], F32))
                    ab_sb = E(nc.sbuf_tensor(f"dp_ab{tag}{half}", [128, 2 * TC], F32))
                    xo_sb = E(nc.sbuf_tensor(f"dp_xo{tag}{half}", [128, TC], F32))
                    xb_sb = E(nc.sbuf_tensor(f"dp_xb{tag}{half}", [128, TC], BF16))
                for i in range(32):
                    ht = half * 32 + i
                    if wcast:
                        p.gdma(wsb[:, i * DM:(i + 1) * DM], wsrc[ht * 128:(ht + 1) * 128, :])
                    else:
                        p.dma(wsb[:, i * DM:(i + 1) * DM], wsrc[ht * 128:(ht + 1) * 128, :])
                for c in range(T // TC):
                    b = (c * TC) // S
                    nin = (c * TC) % S
                    hb = hch[:, (c % 2) * 32 * TC:(c % 2 + 1) * 32 * TC]
                    for i in range(32):
                        p.dma(hb[:, i * TC:(i + 1) * TC],
                              hid[half * 32 + i, :, c * TC:(c + 1) * TC])
                    pbank = (p.ps[0], p.ps[1], p.ps[3], p.ps4)
                    for ft in range(NDT):
                        ax = aux[:, (ft % 4) * TC:(ft % 4 + 1) * TC]
                        if half == 1:
                            p.dma(ax, part[ft, :, c * TC:(c + 1) * TC])
                            rx = raux[:, (ft % 4) * TC:(ft % 4 + 1) * TC]
                            p.dma(rx, resf[b, ft * 128:(ft + 1) * 128, nin:nin + TC])
                        for i in range(32):
                            p.mm(pbank[ft % 4][:, :TC],
                                 wsb[:, i * DM + ft * 128:i * DM + (ft + 1) * 128],
                                 hb[:, i * TC:(i + 1) * TC],
                                 start=(i == 0), stop=(i == 31))
                        if half == 0:
                            p.vcopy(ax, pbank[ft % 4][:, :TC])
                            p.dma(part[ft, :, c * TC:(c + 1) * TC], ax)
                        else:
                            p.tt(pre[:, ft * TC:(ft + 1) * TC], pbank[ft % 4][:, :TC],
                                 ax, ALU.add)
                            p.tt(pre[:, ft * TC:(ft + 1) * TC], pre[:, ft * TC:(ft + 1) * TC],
                                 rx, ALU.add)
                    if half == 1:
                        self._ln_store(pre, TC, sq_sb, rows, ab_sb, xo_sb, xb_sb,
                                       outf, outb, b, nin)

    def _ln_store(self, pre, TC, sq_sb, rows, ab_sb, xo_sb, xb_sb, outf, outb, b, nin):
        p = self
        for ft in range(NDT):
            p.mm(p.ps_row[0:1, :TC], p.ones_col_f[:, :], pre[:, ft * TC:(ft + 1) * TC],
                 start=(ft == 0), stop=(ft == NDT - 1))
            p.act(sq_sb[:, :], pre[:, ft * TC:(ft + 1) * TC], AF.Square)
            p.mm(p.ps_row2[0:1, :TC], p.ones_col_f[:, :], sq_sb[:, :],
                 start=(ft == 0), stop=(ft == NDT - 1))
        MU, E2, VAR, STD, RA, RB = (rows[:, i * TC:(i + 1) * TC] for i in range(6))
        p.ts(MU, p.ps_row[0:1, :TC], 1.0 / DM, None, ALU.mult)
        p.ts(E2, p.ps_row2[0:1, :TC], 1.0 / DM, None, ALU.mult)
        p.tt(VAR, MU, MU, ALU.mult)
        p.tt(VAR, E2, VAR, ALU.subtract)
        p.act(STD, VAR, AF.Sqrt, bias=LN_EPS)
        p.recip(RA, STD)
        p.tt(RB, MU, RA, ALU.mult)
        p.ts(RB, RB, -1.0, None, ALU.mult)
        p.mm(p.ps[2][:, :TC], p.ones_row_f[:, :], RA)
        p.scopy(ab_sb[:, :TC], p.ps[2][:, :TC])
        p.mm(p.ps[2][:, :TC], p.ones_row_f[:, :], RB)
        p.scopy(ab_sb[:, TC:2 * TC], p.ps[2][:, :TC])
        for ft in range(NDT):
            p.tt(xo_sb[:, :], pre[:, ft * TC:(ft + 1) * TC], ab_sb[:, :TC], ALU.mult)
            p.tt(xo_sb[:, :], xo_sb[:, :], ab_sb[:, TC:2 * TC], ALU.add)
            p.dma(outf[b, ft * 128:(ft + 1) * 128, nin:nin + TC], xo_sb[:, :])
            if outb is not None:
                p.scopy(xb_sb[:, :], xo_sb[:, :])
                p.dma(outb[ft * 128:(ft + 1) * 128, b * S + nin:b * S + nin + TC], xb_sb[:, :])

    def phase_l1_proj(self):
        p = self
        nc = self.nc
        T = BPC * S
        with (nc.sbuf_tensor("p3_x", [128, NDT * T], BF16) as xa,
              nc.sbuf_tensor("p3_w", [128, 2 * NDT * 512], BF16) as wb0,
              nc.sbuf_tensor("p3_o", [128, 512], BF16) as ob,
              nc.sbuf_tensor("p3_o2", [128, 512], BF16) as ob2,
              nc.sbuf_tensor("p3_ab", [128, 2 * 256], BF16) as wab,
              nc.sbuf_tensor("p3_sig", [16, 2 * T], F32) as sig,
              nc.sbuf_tensor("p3_bab", [16, 2], F32) as bab):
            for dt in range(NDT):
                p.dma(xa[:, dt * T:(dt + 1) * T], p.x2b[dt * 128:(dt + 1) * 128, :])
            obuf = [ob, ob2]
            for iw, (wname, dst) in enumerate((("wq1", p.xwq), ("wk1", p.xwk), ("wv1", p.xwv))):
                for dot in range(16):
                    par = (iw * 16 + dot) % 2
                    wb = wb0[:, par * NDT * 512:par * NDT * 512 + NDT * 128]
                    p.gdma(wb, p.inp[wname][dot, :, :])
                    for n in range(T // 512):
                        pp = p.ps[(n % 2) * 2][:, :]
                        for dt in range(NDT):
                            p.mm(pp, wb[:, dt * 128:(dt + 1) * 128],
                                 xa[:, dt * T + n * 512:dt * T + (n + 1) * 512],
                                 start=(dt == 0), stop=(dt == NDT - 1))
                        p.scopy(obuf[n % 2][:, :], pp)
                        p.dma(dst[dot * 128:(dot + 1) * 128, n * 512:(n + 1) * 512], obuf[n % 2][:, :])
            for fc in range(4):
                par = fc % 2
                wb = wb0[:, par * NDT * 512:(par + 1) * NDT * 512]
                for dt in range(NDT):
                    p.gdma(wb[:, dt * 512:(dt + 1) * 512],
                           p.inp["wg1"][dt * 128:(dt + 1) * 128, fc * 512:(fc + 1) * 512])
                for tt in range(T // 128):
                    pp = p.ps[(tt % 2) * 2][:, :]
                    for dt in range(NDT):
                        p.mm(pp, xa[:, dt * T + tt * 128:dt * T + (tt + 1) * 128],
                             wb[:, dt * 512:(dt + 1) * 512],
                             start=(dt == 0), stop=(dt == NDT - 1))
                    p.scopy(obuf[tt % 2][:, :], pp)
                    p.dma(p.xwg_tm[tt * 128:(tt + 1) * 128, fc * 512:(fc + 1) * 512], obuf[tt % 2][:, :])
            p.gdma(wab[:, :256], p.inp["wa1"][:, :])
            p.gdma(wab[:, 256:], p.inp["wb1"][:, :])
            p.dma(bab[:, 0:1], p.inp["ba1"][:, :])
            p.dma(bab[:, 1:2], p.inp["bb1"][:, :])
            for j, (coff, boff) in enumerate(((0, 0), (256, 1))):
                for n in range(T // 512):
                    for dt in range(NDT):
                        p.mm(p.ps[0][:16, :], wab[:, coff + dt * 16:coff + (dt + 1) * 16],
                             xa[:, dt * T + n * 512:dt * T + (n + 1) * 512],
                             start=(dt == 0), stop=(dt == NDT - 1))
                    p.act(sig[:, j * T + n * 512:j * T + (n + 1) * 512], p.ps[0][:16, :],
                          AF.Sigmoid, bias=bab[:, boff:boff + 1])

            p.dma(p.sigh[:, :], sig[:, :])

    def phase_l1_delta(self, b):
        p = self
        nc = self.nc
        QT = 512
        with ExitStack() as stk:
            E = stk.enter_context
            qhat = E(nc.sbuf_tensor(f"d_q{b}", [128, 16 * QT], BF16))
            khat = E(nc.sbuf_tensor(f"d_k{b}", [128, 16 * QT], BF16))
            vhat = E(nc.sbuf_tensor(f"d_v{b}", [128, 16 * QT], BF16))
            xwb = E(nc.sbuf_tensor(f"d_xw{b}", [128, 16 * (QT + 3)], BF16))
            dgt = E(nc.sbuf_tensor(f"d_dgt{b}", [128, 64 * 128], BF16))
            cw = E(nc.sbuf_tensor(f"d_cw{b}", [128, 3 * 64], F32))
            cb = E(nc.sbuf_tensor(f"d_cb{b}", [128, 3 * 16], F32))
            sigq = E(nc.sbuf_tensor(f"d_sig{b}", [16, 2 * QT], F32))
            scal = E(nc.sbuf_tensor(f"d_scal{b}", [16, 8 * QT], F32))
            flat = E(nc.sbuf_tensor(f"d_flat{b}", [1, 2 * 16 * CK], F32))
            cols = E(nc.sbuf_tensor(f"d_cols{b}", [128, 6 * 16], F32))
            zf = E(nc.sbuf_tensor(f"d_zf{b}", [128, 16 * 128], F32))
            zb = E(nc.sbuf_tensor(f"d_zb{b}", [128, 16 * 128], BF16))
            ktm = E(nc.sbuf_tensor(f"d_ktm{b}", [128, 16 * 128], BF16))
            vtm = E(nc.sbuf_tensor(f"d_vtm{b}", [128, 16 * 128], BF16))
            expm = E(nc.sbuf_tensor(f"d_expm{b}", [128, 2 * 128], F32))
            mneg = E(nc.sbuf_tensor(f"d_mneg{b}", [128, 2 * 128], BF16))
            rcur = E(nc.sbuf_tensor(f"d_rcur{b}", [128, 2 * 256], BF16))
            r0sb = E(nc.sbuf_tensor(f"d_r0{b}", [128, 2 * 128], BF16))
            otm = E(nc.sbuf_tensor(f"d_otm{b}", [128, 16 * 128], F32))
            otmp = E(nc.sbuf_tensor(f"d_otmp{b}", [128, 2 * 128], F32))
            osq = E(nc.sbuf_tensor(f"d_osq{b}", [128, 16 * 128], F32))
            st16 = E(nc.sbuf_tensor(f"d_st16{b}", [128, 8 * 16], F32))
            gsb = E(nc.sbuf_tensor(f"d_gsb{b}", [128, DM], BF16))
            gss = E(nc.sbuf_tensor(f"d_gss{b}", [128, DM], BF16))
            ogbf = E(nc.sbuf_tensor(f"d_og{b}", [128, DM], BF16))
            ofm = E(nc.sbuf_tensor(f"d_ofm{b}", [128, 128], BF16))
            lnwb = E(nc.sbuf_tensor(f"d_lnw{b}", [128, DM], F32))
            lnbb = E(nc.sbuf_tensor(f"d_lnb{b}", [128, DM], F32))
            rowt = E(nc.sbuf_tensor(f"d_rowt{b}", [1, DM], F32))
            self.bcast_row_2048(lnwb, "lnw_rep", rowt)
            self.bcast_row_2048(lnbb, "lnb_rep", rowt)
            p.dma(cw[:, 0:64], p.inp["qcw"][:, :])
            p.dma(cw[:, 64:128], p.inp["kcw"][:, :])
            p.dma(cw[:, 128:192], p.inp["vcw"][:, :])
            p.dma(cb[:, 0:16], p.inp["qcb"][:, :])
            p.dma(cb[:, 16:32], p.inp["kcb"][:, :])
            p.dma(cb[:, 32:48], p.inp["vcb"][:, :])
            p.memset(zf[:, :], 0.0)
            p.memset(zb[:, :], 0.0)
            LNC2 = float(2.0 * np.log(DSC))
            for q in range(4):
                t0 = b * S + q * QT
                for ti, (srch, dst, woff, boff) in enumerate(
                        ((p.xwq, qhat, 0, 0), (p.xwk, khat, 64, 16), (p.xwv, vhat, 128, 32))):
                    if q == 0:
                        p.memset(xwb[:, :].rearrange("p (d c) -> p d c", d=16)[:, :, 0:3], 0.0)
                        for dt in range(NDT):
                            p.dma(xwb[:, dt * (QT + 3) + 3:(dt + 1) * (QT + 3)],
                                  srch[dt * 128:(dt + 1) * 128, t0:t0 + QT])
                    else:
                        for dt in range(NDT):
                            p.dma(xwb[:, dt * (QT + 3):(dt + 1) * (QT + 3)],
                                  srch[dt * 128:(dt + 1) * 128, t0 - 3:t0 + QT])
                    for dt in range(NDT):
                        for j in range(4):
                            p.ts(dgt[:, (dt * 4 + j) * 128:(dt * 4 + j + 1) * 128],
                                 p.ident_bf[:, :], cw[:, woff + dt * 4 + j:woff + dt * 4 + j + 1],
                                 None, ALU.mult)
                    for dt in range(NDT):
                        base = dt * (QT + 3)
                        pp = [p.ps[0], p.ps[2]][dt % 2][:, :]
                        for j in range(4):
                            p.mm(pp, dgt[:, (dt * 4 + j) * 128:(dt * 4 + j + 1) * 128],
                                 xwb[:, base + j:base + j + QT],
                                 start=(j == 0), stop=(j == 3))
                        p.act(dst[:, dt * QT:(dt + 1) * QT], pp, AF.Silu,
                              bias=cb[:, boff + dt:boff + dt + 1])
                if os.environ.get("KDUMPCONV") and b == 1 and q == 0:
                    for dt in range(16):
                        p.dma(p.aoh[dt * 128:(dt + 1) * 128, 0:512], qhat[:, dt * QT:(dt + 1) * QT])
                        p.dma(p.aoh[dt * 128:(dt + 1) * 128, 512:1024], khat[:, dt * QT:(dt + 1) * QT])
                        p.dma(p.aoh[dt * 128:(dt + 1) * 128, 1024:1536], vhat[:, dt * QT:(dt + 1) * QT])
                p.dma(sigq[:, 0:QT], p.sigh[:, t0:t0 + QT])
                p.dma(sigq[:, QT:2 * QT], p.sigh[:, BPC * S + t0:BPC * S + t0 + QT])
                A_, B_ = sigq[:, 0:QT], sigq[:, QT:2 * QT]
                LNA = scal[:, 0 * QT:1 * QT]; LL = scal[:, 1 * QT:2 * QT]
                LP = scal[:, 2 * QT:3 * QT]; LB2 = scal[:, 3 * QT:4 * QT]
                NL = scal[:, 4 * QT:5 * QT]; S1 = scal[:, 5 * QT:6 * QT]
                BC = scal[:, 6 * QT:7 * QT]; EL = scal[:, 7 * QT:8 * QT]
                p.act(LNA, A_, AF.Ln)
                p.vcopy(LL, LNA)
                cur, oth = LL, LNA
                for sft in (1, 2, 4, 8, 16, 32, 64):
                    c3_cur = cur.rearrange("p (c i) -> p c i", c=4)
                    c3_oth = oth.rearrange("p (c i) -> p c i", c=4)
                    p.vcopy(c3_oth[:, :, 0:sft], c3_cur[:, :, 0:sft])
                    p.tt(c3_oth[:, :, sft:CK], c3_cur[:, :, sft:CK],
                         c3_cur[:, :, 0:CK - sft], ALU.add)
                    cur, oth = oth, cur
                if cur is not LL:
                    p.vcopy(LL, cur)
                p.act(LNA, A_, AF.Ln)
                p.tt(LP, LL, LNA, ALU.subtract)
                p.act(LB2, B_, AF.Ln)
                p.tt(LB2, LP, LB2, ALU.add)
                p.ts(LB2, LB2, 1.0, LNC2, ALU.mult, ALU.add)
                p.ts(NL, LL, -1.0, None, ALU.mult)
                p.act(S1, LP, AF.Exp)
                p.tt(S1, B_, S1, ALU.mult)
                p.ts(S1, S1, float(-DSC * DSC), None, ALU.mult)
                p.ts(BC, B_, DSC, None, ALU.mult)
                p.act(EL, LL, AF.Exp)
                for ck4 in range(4 if not os.environ.get("KDUMPCONV") else 0):
                    cg = q * 4 + ck4
                    c0 = cg * CK
                    csl = slice(ck4 * CK, (ck4 + 1) * CK)
                    for k, qty in enumerate((NL, S1, BC, EL)):
                        p.tr(p.ps[3][:, :16], qty[:, csl], f32=True)
                        p.scopy(cols[:, k * 16:(k + 1) * 16], p.ps[3][:, :16])
                    p.tr(p.ps[3][0:1, 0:16], LL[:, ck4 * CK + CK - 1:ck4 * CK + CK], f32=True)
                    p.scopy(flat[:, 0:16], p.ps[3][0:1, 0:16])
                    p.mm(p.ps[3][:, :16], p.ones_row_f[:, :], flat[:, 0:16])
                    p.scopy(cols[:, 64:80], p.ps[3][:, :16])
                    p.tt(cols[:, 80:96], cols[:, 64:80], cols[:, 0:16], ALU.add)
                    p.act(cols[:, 80:96], cols[:, 80:96], AF.Exp)
                    p.act(cols[:, 64:80], cols[:, 64:80], AF.Exp)
                    p.dma(p.rowtmp[:, :], LB2[:, csl])
                    p.dma(flat[:, 0 * 16 * CK:1 * 16 * CK],
                          p.rowtmp.rearrange("h c -> (h c)")[None, :])
                    p.dma(p.rowtmp2[:, :], LL[:, csl])
                    p.dma(flat[:, 1 * 16 * CK:2 * 16 * CK],
                          p.rowtmp2.rearrange("h c -> (h c)")[None, :])
                    for h in range(16):
                        hq = h * QT + ck4 * CK
                        hsl = slice(hq, hq + CK)
                        zsl = slice(h * 128, (h + 1) * 128)
                        par = h % 2
                        pA = [p.ps[0], p.ps[3]][par]
                        pB = [p.ps[1], p.ps4][par]
                        pC = [p.ps[2], p.ps_row][par]
                        pT = p.ps_bf[:, (par * 512):(par * 512) + 128]
                        exm = expm[:, par * 128:(par + 1) * 128]
                        mng = mneg[:, par * 128:(par + 1) * 128]
                        r0s = r0sb[:, par * 128:(par + 1) * 128]
                        rcu = rcur[:, par * 256:(par + 1) * 256]
                        otp = otmp[:, par * 128:(par + 1) * 128]
                        p.tr(pT, khat[:, hsl])
                        p.ts(ktm[:, zsl], pT, cols[:, 80 + h:80 + h + 1],
                             None, ALU.mult)
                        p.tr(pT, vhat[:, hsl])
                        p.ts(vtm[:, zsl], pT, cols[:, 32 + h:32 + h + 1],
                             None, ALU.mult)
                        p.mm(pA[:, :128], khat[:, hsl], khat[:, hsl])
                        p.mm(pB[:, :128], p.ones_row_f[:, :],
                             flat[:, h * CK:(h + 1) * CK])
                        p.act(exm, pB[:, :128], AF.Exp, bias=cols[:, h:h + 1])
                        p.tt(exm, exm, p.mask_gt_f[:, :], ALU.min)
                        p.stt(mng, pA[:, :128], -1.0, exm, ALU.mult, ALU.mult)
                        p.mm(pC[:, :128], khat[:, hsl], zb[:, zsl])
                        p.stt(r0s, pC[:, :128], cols[:, 16 + h:16 + h + 1],
                              vtm[:, zsl], ALU.mult, ALU.add)
                        p.vcopy(rcu[:, 0:128], r0s)
                        for it in range(3):
                            cu = rcu[:, (it % 2) * 128:(it % 2) * 128 + 128]
                            nx = rcu[:, ((it + 1) % 2) * 128:((it + 1) % 2) * 128 + 128]
                            p.mm(pC[:, :128], mng, cu)
                            p.tt(nx, pC[:, :128], r0s, ALU.add)
                        Y = rcu[:, 128:256]
                        p.mm(pC[:, :128], qhat[:, hsl], zb[:, zsl])
                        p.mm(pA[:, :128], khat[:, hsl], qhat[:, hsl])
                        p.mm(pB[:, :128], p.ones_row_f[:, :],
                             flat[:, 16 * CK + h * CK:16 * CK + (h + 1) * CK])
                        p.act(exm, pB[:, :128], AF.Exp, bias=cols[:, h:h + 1])
                        p.tt(exm, exm, p.mask_ge_f[:, :], ALU.min)
                        p.tt(mng, pA[:, :128], exm, ALU.mult)
                        p.mm(pA[:, :128], mng, Y)
                        p.ts(otp, pC[:, :128], cols[:, 48 + h:48 + h + 1],
                             None, ALU.mult)
                        p.tt(otm[:, zsl], otp, pA[:, :128], ALU.add)
                        p.mm(pC[:, :128], ktm[:, zsl], Y)
                        p.stt(zf[:, zsl], zf[:, zsl], cols[:, 64 + h:64 + h + 1],
                              pC[:, :128], ALU.mult, ALU.add)
                        p.scopy(zb[:, zsl], zf[:, zsl])
                    if os.environ.get("KDUMPCHUNK") and b == 1 and cg == 0:
                        p.gdma(p.aoh[0:128, 0:2048], otm[:, :])
                        p.gdma(p.aoh[128:256, 0:96], cols[:, :])
                        p.gdma(p.aoh[300:301, 0:2048], flat[:, 0:2048])
                        p.gdma(p.aoh[301:302, 0:2048], flat[:, 2048:4096])
                        continue
                    o3 = otm[:, :].rearrange("p (h v) -> p h v", h=16)
                    p.rsum(st16[:, 0:16], o3)
                    p.ts(st16[:, 0:16], st16[:, 0:16], 1.0 / 128.0, None, ALU.mult)
                    p.act(osq[:, :], otm[:, :], AF.Square)
                    p.rsum(st16[:, 16:32], osq[:, :].rearrange("p (h v) -> p h v", h=16))
                    p.ts(st16[:, 16:32], st16[:, 16:32], 1.0 / 128.0, None, ALU.mult)
                    p.tt(st16[:, 32:48], st16[:, 0:16], st16[:, 0:16], ALU.mult)
                    p.tt(st16[:, 32:48], st16[:, 16:32], st16[:, 32:48], ALU.subtract)
                    p.act(st16[:, 32:48], st16[:, 32:48], AF.Sqrt, bias=LN_EPS)
                    p.recip(st16[:, 48:64], st16[:, 32:48])
                    p.ts(st16[:, 0:16], st16[:, 0:16], -1.0, None, ALU.mult)
                    for h in range(16):
                        zsl = slice(h * 128, (h + 1) * 128)
                        p.ts(otm[:, zsl], otm[:, zsl], st16[:, h:h + 1],
                             st16[:, 48 + h:48 + h + 1], ALU.add, ALU.mult)
                    p.tt(otm[:, :], otm[:, :], lnwb[:, :], ALU.mult)
                    p.tt(otm[:, :], otm[:, :], lnbb[:, :], ALU.add)
                    p.dma(gsb[:, :], p.xwg_tm[b * S + c0:b * S + c0 + CK, :])
                    p.act(gss[:, :], gsb[:, :], AF.Sigmoid)
                    p.tt(ogbf[:, :], otm[:, :], gss[:, :], ALU.mult)
                    for h in range(16):
                        p.tr(p.ps_bf[:, :128], ogbf[:, h * 128:(h + 1) * 128])
                        p.scopy(ofm[:, :], p.ps_bf[:, :128])
                        p.dma(p.aoh[h * 128:(h + 1) * 128, c0:c0 + CK], ofm[:, :])
        with self.nc.sbuf_tensor(f"zb0_{b}", [128, 16], F32) as zb0:
            p.memset(zb0[:, :], 0.0)
            self._proj_epilogue(wname="wo1", act_hbm=p.aoh, act_cols=S, b=b,
                                res_in=p.x2f, outf=p.x3f, outb=p.x3b,
                                bias_sb=zb0, tag=f"l1a{b}")

    def phase_l1_ffn(self):
        p = self
        nc = self.nc
        T = BPC * S
        with ExitStack() as stk:
            E = stk.enter_context
            xa = E(nc.sbuf_tensor("p5_x", [128, NDT * T], BF16))
            wgb = E(nc.sbuf_tensor("p5_wg", [128, NDT * 512], BF16))
            wub = E(nc.sbuf_tensor("p5_wu", [128, NDT * 512], BF16))
            slf = E(nc.sbuf_tensor("p5_sl", [128, 512], F32))
            hidf = E(nc.sbuf_tensor("p5_h", [128, 512], F32))
            absf = E(nc.sbuf_tensor("p5_a", [128, 512], F32))
            lad = E(nc.sbuf_tensor("p5_l", [128, 512], F32))
            tau = E(nc.sbuf_tensor("p5_t", [128, 2 * 128], F32))
            mk = E(nc.sbuf_tensor("p5_m", [128, 512], F32))
            hsp = E(nc.sbuf_tensor("p5_s", [128, 512], BF16))
            stag = E(nc.sbuf_tensor("p5_st", [128, 4 * 1024], BF16))
            for dt in range(NDT):
                p.dma(xa[:, dt * T:(dt + 1) * T], p.x3b[dt * 128:(dt + 1) * 128, :])
            for hb in range(16):
                for dt in range(NDT):
                    p.gdma(wgb[:, dt * 512:(dt + 1) * 512],
                           p.inp["wgate"][dt * 128:(dt + 1) * 128, hb * 512:(hb + 1) * 512])
                    p.gdma(wub[:, dt * 512:(dt + 1) * 512],
                           p.inp["wup"][dt * 128:(dt + 1) * 128, hb * 512:(hb + 1) * 512])
                for tt in range(T // 128):
                    for dt in range(NDT):
                        p.mm(p.ps[0][:, :], xa[:, dt * T + tt * 128:dt * T + (tt + 1) * 128],
                             wgb[:, dt * 512:(dt + 1) * 512],
                             start=(dt == 0), stop=(dt == NDT - 1))
                    p.act(slf[:, :], p.ps[0][:, :], AF.Silu)
                    for dt in range(NDT):
                        p.mm(p.ps[1][:, :], xa[:, dt * T + tt * 128:dt * T + (tt + 1) * 128],
                             wub[:, dt * 512:(dt + 1) * 512],
                             start=(dt == 0), stop=(dt == NDT - 1))
                    p.tt(hidf[:, :], p.ps[1][:, :], slf[:, :], ALU.mult)
                    p.act(absf[:, :], hidf[:, :], AF.Abs)
                    p.tt(lad[:, 0:512:4], absf[:, 0:512:4], absf[:, 1:512:4], ALU.max)
                    p.tt(lad[:, 1:512:4], absf[:, 2:512:4], absf[:, 3:512:4], ALU.max)
                    p.tt(lad[:, 2:512:4], absf[:, 0:512:4], absf[:, 1:512:4], ALU.min)
                    p.tt(lad[:, 3:512:4], absf[:, 2:512:4], absf[:, 3:512:4], ALU.min)
                    p.tt(tau[:, 0:128], lad[:, 0:512:4], lad[:, 1:512:4], ALU.min)
                    p.tt(tau[:, 128:256], lad[:, 2:512:4], lad[:, 3:512:4], ALU.max)
                    p.tt(tau[:, 0:128], tau[:, 0:128], tau[:, 128:256], ALU.max)
                    for j in range(4):
                        p.tt(mk[:, j:512:4], absf[:, j:512:4], tau[:, 0:128], ALU.is_ge)
                    p.tt(hsp[:, :], hidf[:, :], mk[:, :], ALU.mult)
                    for j in range(4):
                        p.tr(p.ps_bf[:, :128], hsp[:, j * 128:(j + 1) * 128])
                        p.scopy(stag[:, j * 1024 + (tt % 8) * 128:j * 1024 + (tt % 8 + 1) * 128],
                                p.ps_bf[:, :128])
                    if tt % 8 == 7:
                        for j in range(4):
                            p.dma(p.hid1[hb * 4 + j, :, (tt - 7) * 128:(tt + 1) * 128],
                                  stag[:, j * 1024:(j + 1) * 1024])
        self._down_proj(p.hid1, p.inp["wdown"], p.part1, p.x3f, self.out, None,
                        tag="l1f", wcast=True)

    def finish(self):
        p = self
        if self.dbg is not None:
            src = {"x1f": p.x1f, "x2f": p.x2f, "x3f": p.x3f, "w2eff": p.w2eff, "sigh": p.sigh,
                   "aoh": p.aoh, "x1b": p.x1b, "hid0": p.hid0, "hid1": p.hid1,
                   "xwq": p.xwq, "xwk": p.xwk, "xwv": p.xwv,
                   "xwg_tm": p.xwg_tm, "part0": p.part0}[self.dbg]
            flat_s = src.rearrange(" ".join(f"a{i}" for i in range(len(src.shape))) + " -> (" +
                                   " ".join(f"a{i}" for i in range(len(src.shape))) + ")") \
                if False else src
            p.dma(self.dbg_out[...], src[...])
        if self.mode == "chain":
            total = self.cnt
            self.ops["sync"].append(lambda h, n=total: h.wait_ge(self.chain, n))
        else:
            tots = [(self.esem[e], self.emitted[e]) for e in self.ops if self.emitted[e] > 0]
            tots += [(self.esem[k], 16 * u) for k, u in self.dslot_use.items() if u > 0]
            def fin(h, tots=tuple(tots)):
                for smm, v in tots:
                    h.wait_ge(smm, v)
            self.ops["sync"].append(fin)
        with self.nc.Block() as block:
            for eng in ("tensor", "vector", "scalar", "gpsimd", "sync"):
                def mk(e):
                    def f(h):
                        for op in self.ops[e]:
                            op(h)
                    return f
                getattr(block, eng)(mk(eng))
        self.st.close()
        return self.nc


def build_program(upto=99, dbg=None, mode=None):
    import os
    if mode is None:
        mode = os.environ.get("KMODE", "auto")
    relax = set(os.environ.get("KRELAX", "c,w,1,2,3,4,5").split(","))
    p = Prog(upto=upto, dbg=dbg, mode=mode)
    p.alloc_consts()
    p.kbar_on = "c" not in relax
    p.phase_consts()
    if upto >= 0:
        p.kbar_on = "w" not in relax
        p.phase_w2eff()
    if upto >= 1:
        p.kbar_on = "1" not in relax
        for b in range(BPC):
            p.phase_l0_attn(b)
    if upto >= 2:
        p.kbar_on = "2" not in relax
        p.phase_l0_ffn()
    if upto >= 3:
        p.kbar_on = "3" not in relax
        p.phase_l1_proj()
    if upto >= 4:
        p.kbar_on = "4" not in relax
        for b in range(BPC):
            p.phase_l1_delta(b)
    if upto >= 5:
        p.kbar_on = "5" not in relax
        p.phase_l1_ffn()
    p.kbar_on = False
    nc = p.finish()
    return nc


def host_layout(inputs):
    f32 = np.float32
    qkv_w = np.asarray(inputs["L0_attn_qkv_w"], f32)
    qkv_b = np.asarray(inputs["L0_attn_qkv_b"], f32)
    d = {}
    d["wq0"] = relay_lhsT(np.ascontiguousarray(qkv_w[0:DM].T), NDT, 512)
    d["wk0"] = relay_lhsT(np.ascontiguousarray(qkv_w[DM:2 * DM].T), NDT, 512)
    d["wv0"] = relay_lhsT(np.ascontiguousarray(qkv_w[2 * DM:3 * DM].T), NDT, 512)
    d["qb0"] = np.ascontiguousarray(qkv_b[0:DM].reshape(16, 128).T)
    d["kb0"] = np.ascontiguousarray(qkv_b[DM:2 * DM].reshape(16, 128).T)
    d["vb0_row"] = np.ascontiguousarray(qkv_b[2 * DM:3 * DM].reshape(1, DM))
    d["wo0"] = relay_lhsT(np.ascontiguousarray(np.asarray(inputs["L0_attn_out_w"], f32).T), NDT, 128)
    d["ob0"] = np.ascontiguousarray(np.asarray(inputs["L0_attn_out_b"], f32).reshape(16, 128).T)
    d["alibi_rep"] = np.ascontiguousarray(
        np.repeat(np.asarray(inputs["L0_attn_alibi_slopes"], f32), 16).reshape(1, 256))
    W1 = np.asarray(inputs["L0_ffn_W1"], f32); W3 = np.asarray(inputs["L0_ffn_W3"], f32)
    d["w1t"] = np.ascontiguousarray(
        W1.reshape(NHT, 128, NDT, 128).transpose(0, 3, 2, 1).reshape(NHT, 128, DM))
    d["w3t"] = np.ascontiguousarray(
        W3.reshape(NHT, 128, NDT, 128).transpose(0, 3, 2, 1).reshape(NHT, 128, DM))
    d["w2t"] = np.ascontiguousarray(np.asarray(inputs["L0_ffn_W2"], f32).T)
    d["h128"] = hada(128).astype(BF)
    d["h64"] = hada(64).astype(BF)
    d["identity"] = np.eye(128, dtype=np.float32).astype(BF)
    d["identity_f"] = np.eye(128, dtype=np.float32)
    d["wq1"] = relay_lhsT(np.asarray(inputs["L1_attn_Wq"], f32), NDT, 128)
    d["wk1"] = relay_lhsT(np.asarray(inputs["L1_attn_Wk"], f32), NDT, 128)
    d["wv1"] = relay_lhsT(np.asarray(inputs["L1_attn_Wv"], f32), NDT, 128)
    d["wg1"] = np.ascontiguousarray(np.asarray(inputs["L1_attn_Wg"], f32))
    d["wa1"] = np.ascontiguousarray(
        np.asarray(inputs["L1_attn_Wa"], f32).reshape(16, 128, 16).transpose(1, 0, 2).reshape(128, 256))
    d["wb1"] = np.ascontiguousarray(
        np.asarray(inputs["L1_attn_Wb"], f32).reshape(16, 128, 16).transpose(1, 0, 2).reshape(128, 256))
    d["ba1"] = np.ascontiguousarray(np.asarray(inputs["L1_attn_ba"], f32).reshape(16, 1))
    d["bb1"] = np.ascontiguousarray(np.asarray(inputs["L1_attn_bb"], f32).reshape(16, 1))
    d["wo1"] = relay_lhsT(np.asarray(inputs["L1_attn_Wo"], f32), NDT, 128)
    for nm, key in (("qcw", "L1_attn_qcw"), ("kcw", "L1_attn_kcw"), ("vcw", "L1_attn_vcw")):
        w = np.asarray(inputs[key], f32)
        d[nm] = np.ascontiguousarray(w.reshape(16, 128, 4).transpose(1, 0, 2).reshape(128, 64))
    for nm, key in (("qcb", "L1_attn_qcb"), ("kcb", "L1_attn_kcb"), ("vcb", "L1_attn_vcb")):
        d[nm] = np.ascontiguousarray(np.asarray(inputs[key], f32).reshape(16, 128).T)
    d["lnw_rep"] = np.ascontiguousarray(np.tile(np.asarray(inputs["L1_attn_ln_w"], f32), 16).reshape(1, DM))
    d["lnb_rep"] = np.ascontiguousarray(np.tile(np.asarray(inputs["L1_attn_ln_b"], f32), 16).reshape(1, DM))
    d["wgate"] = np.ascontiguousarray(np.asarray(inputs["L1_ffn_w_gate"], f32))
    d["wup"] = np.ascontiguousarray(np.asarray(inputs["L1_ffn_w_up"], f32))
    d["wdown"] = np.ascontiguousarray(np.asarray(inputs["L1_ffn_w_down"], f32))
    return d


def kernel(**inputs):
    x = np.asarray(inputs["L0_attn_hidden_states"], np.float32)
    shared = host_layout(inputs)
    nc = build_program()
    in_maps = []
    for c in range(8):
        m = dict(shared)
        xs = np.ascontiguousarray(np.transpose(x[2 * c:2 * c + 2], (0, 2, 1)))
        m["xin"] = xs
        in_maps.append(m)
    res = run_bass_kernel_spmd(nc, in_maps, core_ids=list(range(8)))
    outs = []
    for c in range(8):
        o = np.asarray(res.results[c]["out"])
        outs.append(np.transpose(o, (0, 2, 1)))
    return np.concatenate(outs, axis=0).astype(np.float32)


if __name__ == "__main__":
    pass


# version 13
# speedup vs baseline: 3.0883x; 1.1347x over previous
import os
import numpy as np
import ml_dtypes
import concourse.bass as bass
import concourse.mybir as mybir
from concourse.bass_utils import run_bass_kernel_spmd
from contextlib import ExitStack

F32 = mybir.dt.float32
BF16 = mybir.dt.bfloat16
AF = mybir.ActivationFunctionType
ALU = mybir.AluOpType
AXX = mybir.AxisListType.X

BPC = 2
S = 2048
DM = 2048
NDT = DM // 128
NH = 16
HD = 128
H0 = 8192
NHT = H0 // 128
CK = 128
NCK = S // CK
INV_SQRT_HD = float(1.0 / np.sqrt(128.0))
LN_EPS = 1e-5
DSC = float(128.0 ** -0.5)
BF = ml_dtypes.bfloat16


def hada(n):
    H = np.array([[1.0]], dtype=np.float32)
    while H.shape[0] < n:
        H = np.block([[H, H], [H, -H]])
    return H


def relay_lhsT(w, din_tiles, dout_block):
    din, dout = w.shape
    nb = dout // dout_block
    assert din == din_tiles * 128
    r = w.reshape(din_tiles, 128, nb, dout_block).transpose(2, 1, 0, 3)
    return np.ascontiguousarray(r.reshape(nb, 128, din_tiles * dout_block))


class Prog:
    def __init__(self, upto=99, dbg=None, mode="auto"):
        self.upto = upto
        self.dbg = dbg
        nc = bass.Bass("TRN2", target_bir_lowering=False, debug=False)
        self.nc = nc
        self.cnt = 0
        self.mode = mode
        self.ops = {e: [] for e in ("tensor", "vector", "scalar", "gpsimd", "sync")}
        self.emitted = {e: 0 for e in self.ops}
        self.waited = {}
        self.acc = {}
        self.floor = {}
        self.input_names = set()
        self.psum_names = set()
        import os
        se = os.environ.get("KSE", "tensor")
        self.same_engine_ok = set(se.split(",")) if se else set()
        self.bar = {}
        orig_sbuf = nc.sbuf_tensor
        prog = self
        class _SbGuard:
            def __init__(self, cm):
                self.cm = cm
            def __enter__(self):
                return self.cm.__enter__()
            def __exit__(self, *a):
                r = self.cm.__exit__(*a)
                bar = dict(prog.bar)
                for e2 in prog.ops:
                    if prog.emitted[e2] > 0:
                        bar[e2] = prog.emitted[e2]
                for k2, u in prog.dslot_use.items():
                    if u > 0:
                        bar[k2] = 16 * u
                prog.bar = bar
                return r
        def wrapped(*args, **kw):
            return _SbGuard(orig_sbuf(*args, **kw))
        nc.sbuf_tensor = wrapped
        self.st = ExitStack()
        self._declare_dram()

    MAXREC = 160

    BIGP = 1 << 40
    BIGF = 1 << 60

    def _iv(self, ap):
        try:
            dims = list(ap.ap)
            off = ap.offset
            t = ap.tensor
            rowlen = 1
            for d in t.shape[1:]:
                rowlen *= int(d)
            if dims and rowlen > 0 and dims[0][0] == rowlen:
                p0 = off // rowlen
                f0 = off % rowlen
                pn = dims[0][1]
                fspan = sum(abs(st) * (cn - 1) for st, cn in dims[1:])
                if f0 + fspan < rowlen:
                    return (p0, p0 + pn - 1, f0, f0 + fspan)
            lo = off
            span = sum(abs(st) * (cn - 1) for st, cn in dims)
            return (0, self.BIGP, lo, lo + span)
        except Exception:
            return (0, self.BIGP, 0, self.BIGF)

    def _scan(self, eng, isdma, ap, want_writes_only, need):
        t = ap.tensor
        name = t.name
        if name in self.input_names:
            return
        recs = self.acc.get(name)
        flo = self.floor.get(name)
        if flo:
            for e2, c2 in flo.items():
                if not (e2 == eng and not isdma):
                    if need.get(e2, 0) < c2:
                        need[e2] = c2
        if not recs:
            return
        if name in self.psum_names:
            rect = (0, self.BIGP, 0, self.BIGF)
        else:
            rect = self._iv(ap)
        plo, phi, flo, fhi = rect
        for (w2, e2, c2, r2, dma2) in recs:
            if want_writes_only and not w2:
                continue
            if fhi < r2[2] or r2[3] < flo or phi < r2[0] or r2[1] < plo:
                continue
            if e2 == eng and not isdma and not dma2 and eng in self.same_engine_ok:
                continue
            if need.get(e2, 0) < c2:
                need[e2] = c2

    def _rec(self, name, entry):
        lst = self.acc.setdefault(name, [])
        lst.append(entry)
        if len(lst) > self.MAXREC:
            drop = lst[: len(lst) - self.MAXREC]
            del lst[: len(lst) - self.MAXREC]
            flo = self.floor.setdefault(name, {})
            for (w2, e2, c2, r2, dma2) in drop:
                if flo.get(e2, 0) < c2:
                    flo[e2] = c2

    def _emit(self, eng, fn, reads=(), writes=(), inc=1):
        if self.mode == "chain":
            at = self.cnt
            def run(h, at=at, fn=fn, inc=inc):
                h.wait_ge(self.chain, at)
                r = fn(h)
                r.then_inc(self.chain, inc)
            self.ops[eng].append(run)
            self.cnt += inc
            return
        import os
        isdma = inc == 16
        need = {}
        for ap in writes:
            self._scan(eng, isdma, ap, False, need)
        for ap in reads:
            self._scan(eng, isdma, ap, True, need)
        if getattr(self, "kbar_on", False):
            for e2 in self.ops:
                if self.emitted[e2] > 0 and (e2 != eng or os.environ.get("KSELF")):
                    need[e2] = self.emitted[e2]
            for k2, u in self.dslot_use.items():
                if u > 0:
                    need[k2] = max(need.get(k2, 0), 16 * u)
        for k2, v2 in self.bar.items():
            if k2 != eng and need.get(k2, 0) < v2:
                need[k2] = v2
        if isdma:
            q = eng
            skey = ("d", q, self.dma_idx[q] % self.NSLOT)
            self.dma_idx[q] += 1
            use = self.dslot_use[skey]
            self.dslot_use[skey] = use + 1
            cnt_after = 16 * (use + 1)
            ekey = skey
            if use > 0:
                need[skey] = max(need.get(skey, 0), 16 * use)
        else:
            ekey = eng
            cnt_after = self.emitted[eng] + inc
            self.emitted[eng] = cnt_after
        waits = []
        for e2, v in need.items():
            key = (ekey, e2) if isdma else (eng, e2)
            if self.waited.get(key, 0) >= v:
                continue
            self.waited[key] = v
            waits.append((self.esem[e2], v))
        mysem = self.esem[ekey]
        def run(h, waits=tuple(waits), fn=fn, mysem=mysem, inc=inc):
            for smm, v in waits:
                h.wait_ge(smm, v)
            r = fn(h)
            r.then_inc(mysem, inc)
        self.ops[eng].append(run)
        for ap in writes:
            name = ap.tensor.name
            if name in self.input_names:
                continue
            r = (0, self.BIGP, 0, self.BIGF) if name in self.psum_names else self._iv(ap)
            self._rec(name, (True, ekey, cnt_after, r, isdma))
        for ap in reads:
            name = ap.tensor.name
            if name in self.input_names:
                continue
            r = (0, self.BIGP, 0, self.BIGF) if name in self.psum_names else self._iv(ap)
            self._rec(name, (False, ekey, cnt_after, r, isdma))

    def t(self, fn, reads=(), writes=()): self._emit("tensor", fn, reads, writes)
    def v(self, fn, reads=(), writes=()): self._emit("vector", fn, reads, writes)
    def s(self, fn, reads=(), writes=()): self._emit("scalar", fn, reads, writes)
    def g(self, fn, reads=(), writes=()): self._emit("gpsimd", fn, reads, writes)
    def dma(self, out, in_, q="sync"):
        self._emit(q, lambda h, o=out, i=in_: h.dma_start(out=o, in_=i),
                   reads=(in_,), writes=(out,), inc=16)
    def gdma(self, out, in_):
        self.dma(out, in_, q="gpsimd")

    def mm(self, out, lhsT, rhs, start=True, stop=True):
        self.t(lambda t, o=out, l=lhsT, r=rhs, st=start, sp=stop:
               t.matmul(o, l, r, start=st, stop=sp),
               reads=(lhsT, rhs), writes=(out,))
    def tr(self, out, in_, f32=False):
        k = in_.shape[0]
        ident = self.ident_f if f32 else self.ident_bf
        self.t(lambda t, o=out, i=in_, g=ident, kk=k: t.transpose(o, i, g[:kk, :kk]),
               reads=(in_, ident[:, :]), writes=(out,))
    def act(self, out, in_, func, bias=0.0, scale=1.0, accum_out=None):
        rd = [in_]
        if not isinstance(bias, float):
            rd.append(bias)
        wr = [out] + ([accum_out] if accum_out is not None else [])
        self.s(lambda sc, o=out, i=in_, f=func, b=bias, s_=scale, a=accum_out:
               sc.activation(o, i, f, bias=b, scale=s_, accum_out=a),
               reads=tuple(rd), writes=tuple(wr))
    def scopy(self, out, in_):
        self.s(lambda sc, o=out, i=in_: sc.copy(o, i), reads=(in_,), writes=(out,))
    def vcopy(self, out, in_):
        self.v(lambda v, o=out, i=in_: v.tensor_copy(o, i), reads=(in_,), writes=(out,))
    def tt(self, out, in0, in1, op):
        self.v(lambda v, o=out, a=in0, b=in1, p=op: v.tensor_tensor(o, a, b, p),
               reads=(in0, in1), writes=(out,))
    def ts(self, out, in0, s1, s2=None, op0=ALU.mult, op1=ALU.bypass, accum_out=None):
        rd = [in0]
        if not isinstance(s1, float):
            rd.append(s1)
        if s2 is not None and not isinstance(s2, float):
            rd.append(s2)
        wr = [out] + ([accum_out] if accum_out is not None else [])
        self.v(lambda v, o=out, i=in0, a=s1, b=s2, p0=op0, p1=op1, ac=accum_out:
               v.tensor_scalar(o, i, a, b, p0, p1, accum_out=ac),
               reads=tuple(rd), writes=tuple(wr))
    def stt(self, out, in0, s1, in1, op0, op1):
        rd = [in0, in1]
        if not isinstance(s1, float):
            rd.append(s1)
        self.v(lambda v, o=out, i=in0, a=s1, j=in1, p0=op0, p1=op1:
               v.scalar_tensor_tensor(o, i, a, j, p0, p1),
               reads=tuple(rd), writes=(out,))
    def recip(self, out, in_):
        self.v(lambda v, o=out, i=in_: v.reciprocal(o, i), reads=(in_,), writes=(out,))
    def rsum(self, out, in_):
        self.v(lambda v, o=out, i=in_: v.reduce_sum(o, i, AXX),
               reads=(in_,), writes=(out,))
    def memset(self, ap, val):
        self.g(lambda g, a=ap, c=val: g.memset(a, c), reads=(), writes=(ap,))
    def asel(self, out, in_, pattern, cmp, fill, base=0, chmul=1):
        self.g(lambda g, o=out, i=in_, p=pattern, c=cmp, f=fill, b=base, m=chmul:
               g.affine_select(o, i, p, c, f, base=b, channel_multiplier=m),
               reads=(in_,), writes=(out,))

    def _declare_dram(self):
        nc = self.nc
        self.inp = {}
        def I(name, shape, dt=F32):
            self.inp[name] = nc.dram_tensor(name, list(shape), dt, kind="ExternalInput").ap()
            self.input_names.add(name)
            return self.inp[name]
        I("xin", (BPC, DM, S))
        I("wq0", (4, 128, NDT * 512)); I("wk0", (4, 128, NDT * 512)); I("wv0", (4, 128, NDT * 512))
        I("qb0", (128, 16)); I("kb0", (128, 16)); I("vb0_row", (1, DM))
        I("wo0", (16, 128, NDT * 128)); I("ob0", (128, 16))
        I("alibi_rep", (1, 256))
        I("w1t", (NHT, 128, DM)); I("w3t", (NHT, 128, DM)); I("w2t", (H0, DM))
        I("h128", (128, 128), BF16); I("h64", (64, 64), BF16); I("identity", (128, 128), BF16)
        I("identity_f", (128, 128), F32)
        I("wq1", (16, 128, NDT * 128)); I("wk1", (16, 128, NDT * 128))
        I("wv1", (16, 128, NDT * 128)); I("wg1", (DM, DM))
        I("wa1", (128, 256)); I("wb1", (128, 256)); I("ba1", (16, 1)); I("bb1", (16, 1))
        I("wo1", (16, 128, NDT * 128))
        I("qcw", (128, 64)); I("kcw", (128, 64)); I("vcw", (128, 64))
        I("qcb", (128, 16)); I("kcb", (128, 16)); I("vcb", (128, 16))
        I("lnw_rep", (1, DM)); I("lnb_rep", (1, DM))
        I("wgate_l", (16, 128, H0)); I("wup_l", (16, 128, H0)); I("wdown_l", (H0, DM))

        D = lambda name, shape, dt: nc.dram_tensor(name, list(shape), dt, kind="Internal").ap()
        self.w2eff = D("w2eff", (H0, DM), BF16)
        self.bh = D("bh", (NHT, 128, DM), BF16)
        self.aoh = D("aoh", (DM, S), BF16)
        self.x1f = D("x1f", (BPC, DM, S), F32)
        self.x1b = D("x1b", (DM, BPC * S), BF16)
        self.hid0 = D("hid0", (NHT, 128, BPC * S), BF16)
        self.part0 = D("part0", (NDT, 128, BPC * S), F32)
        self.x2f = D("x2f", (BPC, DM, S), F32)
        self.x2b = D("x2b", (DM, BPC * S), BF16)
        self.xwq = D("xwq", (DM, BPC * S), BF16)
        self.xwk = D("xwk", (DM, BPC * S), BF16)
        self.xwv = D("xwv", (DM, BPC * S), BF16)
        self.xwg_tm = D("xwg_tm", (BPC * S, DM), BF16)
        self.x3f = D("x3f", (BPC, DM, S), F32)
        self.x3b = D("x3b", (DM, BPC * S), BF16)
        self.hid1 = D("hid1", (NHT, 128, BPC * S), BF16)
        self.part1 = D("part1", (NDT, 128, BPC * S), F32)
        self.sigh = D("sigh", (16, 2 * BPC * S), F32)
        self.rowtmp = D("rowtmp", (16, CK), F32)
        self.rowtmp2 = D("rowtmp2", (16, CK), F32)
        self.wgate_b = D("wgate_b", (16, 128, H0), BF16)
        self.wup_b = D("wup_b", (16, 128, H0), BF16)
        self.wdown_b = D("wdown_b", (H0, DM), BF16)

        self.out = nc.dram_tensor("out", [BPC, DM, S], F32, kind="ExternalOutput").ap()
        if self.dbg is not None:
            shp = {"x1f": (BPC, DM, S), "x2f": (BPC, DM, S), "x3f": (BPC, DM, S),
                   "sigh": (16, 2 * BPC * S),
                   "w2eff": (H0, DM), "aoh": (DM, S), "x1b": (DM, BPC * S),
                   "hid0": (NHT, 128, BPC * S), "hid1": (NHT, 128, BPC * S),
                   "xwq": (DM, BPC * S), "xwk": (DM, BPC * S), "xwv": (DM, BPC * S),
                   "xwg_tm": (BPC * S, DM),
                   "part0": (NDT, 128, BPC * S)}[self.dbg]
            dt = BF16 if self.dbg in ("w2eff", "aoh", "x1b", "hid0", "hid1", "xwq", "xwk", "xwv", "xwg_tm") else F32
            self.dbg_out = nc.dram_tensor("dbg", list(shp), dt, kind="ExternalOutput").ap()

    def alloc_consts(self):
        E = self.st.enter_context
        nc = self.nc
        self.chain = E(nc.semaphore("chain"))
        self.esem = {e: E(nc.semaphore(f"sem_{e}")) for e in self.ops}
        self.NSLOT = 24
        for q in ("sync", "gpsimd"):
            for si in range(self.NSLOT):
                self.esem[("d", q, si)] = E(nc.semaphore(f"ds_{q}_{si}"))
        self.dslot_use = {("d", q, si): 0 for q in ("sync", "gpsimd")
                          for si in range(self.NSLOT)}
        self.dma_idx = {"sync": 0, "gpsimd": 0}
        self.ident_bf = E(nc.sbuf_tensor("ident_bf", [128, 128], BF16))
        self.ident_f = E(nc.sbuf_tensor("ident_f", [128, 128], F32))
        self.h128_sb = E(nc.sbuf_tensor("h128_sb", [128, 128], BF16))
        self.h64_sb = E(nc.sbuf_tensor("h64_sb", [64, 64], BF16))
        self.ones_col_bf = E(nc.sbuf_tensor("ones_col_bf", [128, 1], BF16))
        self.ones_col_f = E(nc.sbuf_tensor("ones_col_f", [128, 1], F32))
        self.ones_row_f = E(nc.sbuf_tensor("ones_row_f", [1, 128], F32))
        self.iota_row = E(nc.sbuf_tensor("iota_row", [1, 128], F32))
        self.negoff_row = E(nc.sbuf_tensor("negoff_row", [1, 256], F32))
        self.slope_rep = E(nc.sbuf_tensor("slope_rep", [1, 256], F32))
        self.r2_row = E(nc.sbuf_tensor("r2_row", [1, 256], F32))
        self.bias_tab = E(nc.sbuf_tensor("bias_tab", [128, 256], F32))
        self.qb_sb = E(nc.sbuf_tensor("qb_sb", [128, 16], F32))
        self.kb_sb = E(nc.sbuf_tensor("kb_sb", [128, 16], F32))
        self.ob_sb = E(nc.sbuf_tensor("ob_sb", [128, 16], F32))
        self.ps = [E(nc.psum_tensor(f"ps{i}", [128, 512], F32)) for i in range(4)]
        self.ps_bf = E(nc.psum_tensor("ps_bf", [128, 1024], BF16))
        self.ps_row = E(nc.psum_tensor("ps_row", [128, 512], F32))
        for i in range(1, 4):
            self.psum_names.add(f"ps{i}")
        self.psum_names.update(("ps_bf", "ps_row", "ps_row2", "ps4"))
        self.epsc = E(nc.sbuf_tensor("epsc", [128, 1], F32))
        nc.const_aps.register(F32, LN_EPS, self.epsc[:, :])
        self.ps_row2 = E(nc.psum_tensor("ps_row2", [128, 512], F32))
        self.ps4 = E(nc.psum_tensor("ps4", [128, 512], F32))
        self.mask_ge_bf = E(nc.sbuf_tensor("mask_ge_bf", [128, 128], BF16))
        self.mask_gt_f = E(nc.sbuf_tensor("mask_gt_f", [128, 128], F32))
        self.mask_ge_f = E(nc.sbuf_tensor("mask_ge_f", [128, 128], F32))
        self.sel16 = E(nc.sbuf_tensor("sel16", [16, 16 * 128], F32))

    def phase_consts(self):
        p = self
        p.dma(p.ident_bf[:, :], p.inp["identity"][:, :])
        p.dma(p.ident_f[:, :], p.inp["identity_f"][:, :])
        p.dma(p.h128_sb[:, :], p.inp["h128"][:, :])
        p.dma(p.h64_sb[:, :], p.inp["h64"][:, :])
        p.dma(p.slope_rep[:, :], p.inp["alibi_rep"][:, :])
        p.dma(p.qb_sb[:, :], p.inp["qb0"][:, :])
        p.dma(p.kb_sb[:, :], p.inp["kb0"][:, :])
        p.dma(p.ob_sb[:, :], p.inp["ob0"][:, :])
        p.memset(p.epsc[:, :], LN_EPS)
        p.memset(p.ones_col_bf[:, :], 1.0)
        p.memset(p.ones_col_f[:, :], 1.0)
        p.memset(p.ones_row_f[:, :], 1.0)
        p.g(lambda g: g.iota(p.iota_row[:, :], [[1, 128]], base=0, channel_multiplier=0,
                             allow_small_or_imprecise_dtypes=True),
            writes=(p.iota_row[:, :],))
        p.g(lambda g: g.iota(p.negoff_row[:, :], [[0, 16], [-128, 16]], base=-64,
                             channel_multiplier=0, allow_small_or_imprecise_dtypes=True),
            writes=(p.negoff_row[:, :],))
        p.memset(p.mask_ge_bf[:, :], 1e38)
        p.asel(p.mask_ge_bf[:, :], p.mask_ge_bf[:, :], [[1, 128]], ALU.is_ge, 0.0,
               base=0, chmul=-1)
        p.memset(p.mask_gt_f[:, :], 1e38)
        p.asel(p.mask_gt_f[:, :], p.mask_gt_f[:, :], [[1, 128]], ALU.is_gt, 0.0,
               base=0, chmul=-1)
        p.memset(p.mask_ge_f[:, :], 1e38)
        p.asel(p.mask_ge_f[:, :], p.mask_ge_f[:, :], [[1, 128]], ALU.is_ge, 0.0,
               base=0, chmul=-1)
        p.memset(p.sel16[:, :], 1.0)
        p.asel(p.sel16[:, :], p.sel16[:, :], [[1, 16], [0, 128]], ALU.is_ge, 0.0,
               base=0, chmul=-1)
        p.asel(p.sel16[:, :], p.sel16[:, :], [[-1, 16], [0, 128]], ALU.is_ge, 0.0,
               base=0, chmul=1)
        p.tt(p.r2_row[:, :], p.slope_rep[:, :], p.negoff_row[:, :], ALU.mult)
        p.mm(p.ps[0][:, :256], p.iota_row[:, :], p.slope_rep[:, :], start=True, stop=False)
        p.mm(p.ps[0][:, :256], p.ones_row_f[:, :], p.r2_row[:, :], start=False, stop=True)
        p.scopy(p.bias_tab[:, :], p.ps[0][:, :256])

    def bcast_row_2048(self, dst, srcname, row_tmp):
        p = self
        p.dma(row_tmp[:, :], p.inp[srcname][:, :])
        for n in range(4):
            p.mm(p.ps[1][:, :], p.ones_row_f[:, :], row_tmp[:, n * 512:(n + 1) * 512])
            p.scopy(dst[:, n * 512:(n + 1) * 512], p.ps[1][:, :])

    def phase_w2eff(self):
        p = self
        E = self.st
        with (self.nc.sbuf_tensor("w2tile", [128, 2 * DM], BF16) as w2tile,
              self.nc.sbuf_tensor("bh_sb", [128, 2 * DM], BF16) as bh_sb,
              self.nc.sbuf_tensor("bsteb", [64, 2 * 4 * DM], BF16) as bsteb,
              self.nc.sbuf_tensor("w2e_sb", [64, 2 * 4 * DM], BF16) as w2e_sb):
            for hi in range(NHT):
                par = hi % 2
                wt = w2tile[:, par * DM:(par + 1) * DM]
                bs = bh_sb[:, par * DM:(par + 1) * DM]
                p.gdma(wt, p.inp["w2t"][hi * 128:(hi + 1) * 128, :])
                for n in range(4):
                    p.mm(p.ps[par][:, :], p.h128_sb[:, :], wt[:, n * 512:(n + 1) * 512])
                    p.scopy(bs[:, n * 512:(n + 1) * 512], p.ps[par][:, :])
                p.dma(p.bh[hi, :, :], bs)
            bh4 = p.bh.rearrange("a p d -> a (p d)")
            w2r = p.w2eff.rearrange("(a b) d -> a b d", b=128)
            for lo0 in range(0, 128, 4):
                par = (lo0 // 4) % 2
                bst = bsteb[:, par * 4 * DM:(par + 1) * 4 * DM]
                w2e = w2e_sb[:, par * 4 * DM:(par + 1) * 4 * DM]
                p.dma(bst, bh4[:, lo0 * DM:(lo0 + 4) * DM])
                for j in range(4):
                    for n in range(4):
                        sl = slice(j * DM + n * 512, j * DM + (n + 1) * 512)
                        p.mm(p.ps[par][:64, :], p.h64_sb[:, :], bst[:, sl])
                        p.act(w2e[:64, sl], p.ps[par][:64, :], AF.Identity,
                              scale=float(1.0 / np.sqrt(8192.0)))
                    p.dma(w2r[:, lo0 + j, :], w2e[:64, j * DM:(j + 1) * DM])

    def phase_l0_attn(self, b):
        p = self
        nc = self.nc
        with (nc.sbuf_tensor(f"xbf{b}", [128, NDT * S], BF16) as xbf,
              nc.sbuf_tensor(f"wbuf{b}", [128, NDT * 512], BF16) as wbuf,
              nc.sbuf_tensor(f"wbuf2{b}", [128, NDT * 512], BF16) as wbuf2,
              nc.sbuf_tensor(f"qg{b}", [128, 4 * S], BF16) as qg,
              nc.sbuf_tensor(f"kg{b}", [128, 4 * S], BF16) as kg,
              nc.sbuf_tensor(f"vtm{b}", [128, 16 * 512], BF16) as vtm,
              nc.sbuf_tensor(f"probs{b}", [128, 2 * 16 * 128], BF16) as probs0,
              nc.sbuf_tensor(f"rec_row{b}", [1, 256], F32) as rec_row,
              nc.sbuf_tensor(f"recb{b}", [128, 256], F32) as recb,
              nc.sbuf_tensor(f"ao_sb{b}", [128, 512], BF16) as ao_sb,
              nc.sbuf_tensor(f"bv_bc{b}", [128, DM], F32) as bv_bc,
              nc.sbuf_tensor(f"rowt{b}", [1, DM], F32) as rowt):
            self.bcast_row_2048(bv_bc, "vb0_row", rowt)
            for dt in range(NDT):
                p.gdma(xbf[:, dt * S:(dt + 1) * S], p.inp["xin"][b, dt * 128:(dt + 1) * 128, :])
            for g in range(4):
                wb_ = [wbuf, wbuf2][(3 * g) % 2]
                p.gdma(wb_[:, :], p.inp["wq0"][g, :, :])
                for hl in range(4):
                    for n in range(4):
                        pp = p.ps[(hl * 4 + n) % 2 * 2][:, :]
                        for dt in range(NDT):
                            p.mm(pp, wb_[:, dt * 512 + hl * 128:dt * 512 + (hl + 1) * 128],
                                 xbf[:, dt * S + n * 512:dt * S + (n + 1) * 512],
                                 start=(dt == 0), stop=(dt == NDT - 1))
                        p.act(qg[:, hl * S + n * 512:hl * S + (n + 1) * 512], pp,
                              AF.Identity, bias=p.qb_sb[:, g * 4 + hl:g * 4 + hl + 1])
                wb_ = [wbuf, wbuf2][(3 * g + 1) % 2]
                p.gdma(wb_[:, :], p.inp["wk0"][g, :, :])
                for hl in range(4):
                    for n in range(4):
                        pp = p.ps[(hl * 4 + n) % 2 * 2][:, :]
                        for dt in range(NDT):
                            p.mm(pp, wb_[:, dt * 512 + hl * 128:dt * 512 + (hl + 1) * 128],
                                 xbf[:, dt * S + n * 512:dt * S + (n + 1) * 512],
                                 start=(dt == 0), stop=(dt == NDT - 1))
                        p.act(kg[:, hl * S + n * 512:hl * S + (n + 1) * 512], pp,
                              AF.Identity, bias=p.kb_sb[:, g * 4 + hl:g * 4 + hl + 1])
                wb_ = [wbuf, wbuf2][(3 * g + 2) % 2]
                p.gdma(wb_[:, :], p.inp["wv0"][g, :, :])
                for tt in range(16):
                    pp = p.ps[tt % 2 * 2][:, :]
                    for dt in range(NDT):
                        p.mm(pp, xbf[:, dt * S + tt * 128:dt * S + (tt + 1) * 128],
                             wb_[:, dt * 512:(dt + 1) * 512],
                             start=(dt == 0), stop=(dt == NDT - 1))
                    p.stt(vtm[:, tt * 512:(tt + 1) * 512], pp, 1.0,
                          bv_bc[:, g * 512:(g + 1) * 512], ALU.mult, ALU.add)
                import os
                if os.environ.get("KDUMPQKV") and g == 0 and b == 1:
                    for hl in range(4):
                        p.dma(p.aoh[hl * 128:(hl + 1) * 128, :], qg[:, hl * S:(hl + 1) * S])
                        p.dma(p.aoh[512 + hl * 128:512 + (hl + 1) * 128, :], kg[:, hl * S:(hl + 1) * S])
                    for tt in range(16):
                        p.dma(p.aoh.rearrange("d s -> (d s)").rearrange(
                              "(a p f) -> a p f", a=32, p=128)[8 + tt // 2, :,
                              (tt % 2) * 512:(tt % 2) * 512 + 512],
                              vtm[:, tt * 512:(tt + 1) * 512])
                for hl in range(4 if not os.environ.get("KDUMPQKV") else 0):
                    h = g * 4 + hl
                    for qc in range(16):
                        probs = probs0[:, (qc % 2) * 2048:(qc % 2 + 1) * 2048]
                        pd = [p.ps_row, p.ps_row2][qc % 2]
                        po = [p.ps[1], p.ps4][qc % 2]

                        def emit_kq(kt, probs=probs, hl=hl, h=h, qc=qc):
                            pp = p.ps[0][:, (kt % 4) * 128:(kt % 4 + 1) * 128]
                            p.mm(pp, kg[:, hl * S + kt * 128:hl * S + (kt + 1) * 128],
                                 qg[:, hl * S + qc * 128:hl * S + (qc + 1) * 128])
                            p.act(probs[:, kt * 128:(kt + 1) * 128], pp, AF.Exp,
                                  bias=p.bias_tab[:, h * 16 + (qc - kt):h * 16 + (qc - kt) + 1],
                                  scale=INV_SQRT_HD)
                            if kt == qc:
                                p.tt(probs[:, kt * 128:(kt + 1) * 128],
                                     probs[:, kt * 128:(kt + 1) * 128],
                                     p.mask_ge_bf[:, :], ALU.min)

                        LOOK = int(os.environ.get("KLOOK", "0"))
                        for kt in range(min(LOOK, qc + 1)):
                            emit_kq(kt)
                        for kt in range(qc + 1):
                            if LOOK == 0:
                                emit_kq(kt)
                            elif kt + LOOK <= qc:
                                emit_kq(kt + LOOK)
                            p.mm(pd[0:1, :128], p.ones_col_bf[:, :],
                                 probs[:, kt * 128:(kt + 1) * 128],
                                 start=(kt == 0), stop=(kt == qc))
                            p.mm(po[:, :128], vtm[:, kt * 512 + hl * 128:kt * 512 + (hl + 1) * 128],
                                 probs[:, kt * 128:(kt + 1) * 128],
                                 start=(kt == 0), stop=(kt == qc))
                        p.recip(rec_row[:, (qc % 2) * 128:(qc % 2) * 128 + 128], pd[0:1, :128])
                        p.mm(p.ps[2][:, (qc % 2) * 128:(qc % 2) * 128 + 128], p.ones_row_f[:, :],
                             rec_row[:, (qc % 2) * 128:(qc % 2) * 128 + 128])
                        p.scopy(recb[:, (qc % 2) * 128:(qc % 2) * 128 + 128],
                                p.ps[2][:, (qc % 2) * 128:(qc % 2) * 128 + 128])
                        p.tt(ao_sb[:, (qc % 4) * 128:(qc % 4 + 1) * 128], po[:, :128],
                             recb[:, (qc % 2) * 128:(qc % 2) * 128 + 128], ALU.mult)
                        if qc % 4 == 3:
                            p.dma(p.aoh[h * 128:(h + 1) * 128, (qc - 3) * 128:(qc + 1) * 128],
                                  ao_sb[:, :])
        self._proj_epilogue(wname="wo0", act_hbm=p.aoh, act_cols=S, b=b,
                            res_in=p.inp["xin"], outf=p.x1f, outb=p.x1b,
                            bias_sb=p.ob_sb, tag=f"l0a{b}")

    def _proj_epilogue(self, wname, act_hbm, act_cols, b, res_in, outf, outb, bias_sb,
                       part_in=None, tokchunk=512, tag=""):
        p = self
        nc = self.nc
        nchunk = S // tokchunk
        with (nc.sbuf_tensor(f"wo_full{tag}", [128, 16 * NDT * 128], BF16) as wo_full,
              nc.sbuf_tensor(f"aobuf{tag}", [128, NDT * tokchunk], BF16) as aobuf,
              nc.sbuf_tensor(f"x1pre{tag}", [128, NDT * tokchunk], F32) as x1pre,
              nc.sbuf_tensor(f"xres{tag}", [128, tokchunk], F32) as xres,
              nc.sbuf_tensor(f"sqsb{tag}", [128, tokchunk], F32) as sqsb,
              nc.sbuf_tensor(f"rows{tag}", [1, 6 * tokchunk], F32) as rows,
              nc.sbuf_tensor(f"ab_sb{tag}", [128, 2 * tokchunk], F32) as ab_sb,
              nc.sbuf_tensor(f"xo_sb{tag}", [128, tokchunk], F32) as xo_sb,
              nc.sbuf_tensor(f"xb_sb{tag}", [128, tokchunk], BF16) as xb_sb):
            for ft in range(16):
                p.gdma(wo_full[:, ft * NDT * 128:(ft + 1) * NDT * 128], p.inp[wname][ft, :, :])
            TC = tokchunk
            for n in range(nchunk):
                for at in range(NDT):
                    p.dma(aobuf[:, at * TC:(at + 1) * TC],
                          act_hbm[at * 128:(at + 1) * 128, n * TC:(n + 1) * TC])
                for ft in range(NDT):
                    for at in range(NDT):
                        p.mm(p.ps[0][:, :TC],
                             wo_full[:, ft * NDT * 128 + at * 128:ft * NDT * 128 + (at + 1) * 128],
                             aobuf[:, at * TC:(at + 1) * TC],
                             start=(at == 0), stop=(at == NDT - 1))
                    p.dma(xres[:, :], res_in[b, ft * 128:(ft + 1) * 128, n * TC:(n + 1) * TC])
                    p.act(x1pre[:, ft * TC:(ft + 1) * TC], p.ps[0][:, :TC], AF.Identity,
                          bias=bias_sb[:, ft:ft + 1])
                    p.tt(x1pre[:, ft * TC:(ft + 1) * TC], x1pre[:, ft * TC:(ft + 1) * TC],
                         xres[:, :], ALU.add)
                    p.mm(p.ps_row[0:1, :TC], p.ones_col_f[:, :], x1pre[:, ft * TC:(ft + 1) * TC],
                         start=(ft == 0), stop=(ft == NDT - 1))
                    p.act(sqsb[:, :], x1pre[:, ft * TC:(ft + 1) * TC], AF.Square)
                    p.mm(p.ps_row2[0:1, :TC], p.ones_col_f[:, :], sqsb[:, :],
                         start=(ft == 0), stop=(ft == NDT - 1))
                MU, E2, VAR, STD, RA, RB = (rows[:, i * TC:(i + 1) * TC] for i in range(6))
                p.ts(MU, p.ps_row[0:1, :TC], 1.0 / DM, None, ALU.mult)
                p.ts(E2, p.ps_row2[0:1, :TC], 1.0 / DM, None, ALU.mult)
                p.tt(VAR, MU, MU, ALU.mult)
                p.tt(VAR, E2, VAR, ALU.subtract)
                p.act(STD, VAR, AF.Sqrt, bias=LN_EPS)
                p.recip(RA, STD)
                p.tt(RB, MU, RA, ALU.mult)
                p.ts(RB, RB, -1.0, None, ALU.mult)
                p.mm(p.ps[1][:, :TC], p.ones_row_f[:, :], RA)
                p.scopy(ab_sb[:, :TC], p.ps[1][:, :TC])
                p.mm(p.ps[1][:, :TC], p.ones_row_f[:, :], RB)
                p.scopy(ab_sb[:, TC:2 * TC], p.ps[1][:, :TC])
                for ft in range(NDT):
                    p.tt(xo_sb[:, :], x1pre[:, ft * TC:(ft + 1) * TC], ab_sb[:, :TC], ALU.mult)
                    p.tt(xo_sb[:, :], xo_sb[:, :], ab_sb[:, TC:2 * TC], ALU.add)
                    p.dma(outf[b, ft * 128:(ft + 1) * 128, n * TC:(n + 1) * TC], xo_sb[:, :])
                    p.scopy(xb_sb[:, :], xo_sb[:, :])
                    p.dma(outb[ft * 128:(ft + 1) * 128, b * S + n * TC:b * S + (n + 1) * TC],
                          xb_sb[:, :])

    def phase_l0_ffn(self):
        p = self
        nc = self.nc
        T = BPC * S
        with (nc.sbuf_tensor("ffa_x", [128, NDT * T], BF16) as xa,
              nc.sbuf_tensor("ffa_w1", [128, 2 * DM], BF16) as w1b,
              nc.sbuf_tensor("ffa_w3", [128, 2 * DM], BF16) as w3b,
              nc.sbuf_tensor("ffa_hid", [128, 2 * T], BF16) as hidsb,
              nc.sbuf_tensor("ffa_silu", [128, 2 * 512], BF16) as silu_sb):
            for dt in range(NDT):
                p.dma(xa[:, dt * T:(dt + 1) * T], p.x1b[dt * 128:(dt + 1) * 128, :])
            for ht in range(NHT):
                if ht < 16:
                    p.gdma(p.wgate_b[ht, :, :], p.inp["wgate_l"][ht, :, :])
                elif ht < 32:
                    p.gdma(p.wup_b[ht - 16, :, :], p.inp["wup_l"][ht - 16, :, :])
                elif ht < 48:
                    i = ht - 32
                    p.gdma(p.wdown_b[i * 512:(i + 1) * 512, :],
                           p.inp["wdown_l"][i * 512:(i + 1) * 512, :])
                par = ht % 2
                w1s = w1b[:, par * DM:(par + 1) * DM]
                w3s = w3b[:, par * DM:(par + 1) * DM]
                p.gdma(w1s, p.inp["w1t"][ht, :, :])
                p.gdma(w3s, p.inp["w3t"][ht, :, :])
                for n in range(T // 512):
                    pg = p.ps[(n % 2) * 2][:, :]
                    pl = p.ps[(n % 2) * 2 + 1][:, :]
                    sl = silu_sb[:, (n % 2) * 512:(n % 2 + 1) * 512]
                    for dt in range(NDT):
                        p.mm(pg, w1s[:, dt * 128:(dt + 1) * 128],
                             xa[:, dt * T + n * 512:dt * T + (n + 1) * 512],
                             start=(dt == 0), stop=(dt == NDT - 1))
                    p.act(sl, pg, AF.Silu)
                    for dt in range(NDT):
                        p.mm(pl, w3s[:, dt * 128:(dt + 1) * 128],
                             xa[:, dt * T + n * 512:dt * T + (n + 1) * 512],
                             start=(dt == 0), stop=(dt == NDT - 1))
                    p.tt(hidsb[:, par * T + n * 512:par * T + (n + 1) * 512], pl, sl, ALU.mult)
                p.dma(p.hid0[ht, :, :], hidsb[:, par * T:(par + 1) * T])
        self._down_proj(p.hid0, p.w2eff, p.part0, p.x1f, p.x2f, p.x2b, tag="l0f")

    def _down_proj(self, hid, wsrc, part, resf, outf, outb, tag, wcast=False):
        p = self
        nc = self.nc
        T = BPC * S
        TC = 256
        for half in range(2):
            with ExitStack() as stk:
                E = stk.enter_context
                wsb = E(nc.sbuf_tensor(f"dp_w{tag}{half}", [128, 32 * DM], BF16))
                hch = E(nc.sbuf_tensor(f"dp_h{tag}{half}", [128, 2 * 32 * TC], BF16))
                aux = E(nc.sbuf_tensor(f"dp_aux{tag}{half}", [128, 2 * TC], F32))
                if half == 1:
                    raux = E(nc.sbuf_tensor(f"dp_raux{tag}{half}", [128, 2 * TC], F32))
                    pre = E(nc.sbuf_tensor(f"dp_pre{tag}{half}", [128, NDT * TC], F32))
                    sq_sb = E(nc.sbuf_tensor(f"dp_sq{tag}{half}", [128, TC], F32))
                    rows = E(nc.sbuf_tensor(f"dp_rows{tag}{half}", [1, 6 * TC], F32))
                    ab_sb = E(nc.sbuf_tensor(f"dp_ab{tag}{half}", [128, 2 * TC], F32))
                    xo_sb = E(nc.sbuf_tensor(f"dp_xo{tag}{half}", [128, TC], F32))
                    xb_sb = E(nc.sbuf_tensor(f"dp_xb{tag}{half}", [128, TC], BF16))
                for i in range(32):
                    ht = half * 32 + i
                    if wcast:
                        p.gdma(wsb[:, i * DM:(i + 1) * DM], wsrc[ht * 128:(ht + 1) * 128, :])
                    else:
                        p.dma(wsb[:, i * DM:(i + 1) * DM], wsrc[ht * 128:(ht + 1) * 128, :])
                for c in range(T // TC):
                    b = (c * TC) // S
                    nin = (c * TC) % S
                    hb = hch[:, (c % 2) * 32 * TC:(c % 2 + 1) * 32 * TC]
                    for i in range(32):
                        p.dma(hb[:, i * TC:(i + 1) * TC],
                              hid[half * 32 + i, :, c * TC:(c + 1) * TC])
                    pbank = (p.ps[0], p.ps[1], p.ps[3], p.ps4)
                    for ft in range(NDT):
                        ax = aux[:, (ft % 2) * TC:(ft % 2 + 1) * TC]
                        if half == 1:
                            p.dma(ax, part[ft, :, c * TC:(c + 1) * TC])
                            rx = raux[:, (ft % 2) * TC:(ft % 2 + 1) * TC]
                            p.dma(rx, resf[b, ft * 128:(ft + 1) * 128, nin:nin + TC])
                        for i in range(32):
                            p.mm(pbank[ft % 4][:, :TC],
                                 wsb[:, i * DM + ft * 128:i * DM + (ft + 1) * 128],
                                 hb[:, i * TC:(i + 1) * TC],
                                 start=(i == 0), stop=(i == 31))
                        if half == 0:
                            p.vcopy(ax, pbank[ft % 4][:, :TC])
                            p.dma(part[ft, :, c * TC:(c + 1) * TC], ax)
                        else:
                            p.tt(pre[:, ft * TC:(ft + 1) * TC], pbank[ft % 4][:, :TC],
                                 ax, ALU.add)
                            p.tt(pre[:, ft * TC:(ft + 1) * TC], pre[:, ft * TC:(ft + 1) * TC],
                                 rx, ALU.add)
                    if half == 1:
                        self._ln_store(pre, TC, sq_sb, rows, ab_sb, xo_sb, xb_sb,
                                       outf, outb, b, nin)

    def _ln_store(self, pre, TC, sq_sb, rows, ab_sb, xo_sb, xb_sb, outf, outb, b, nin):
        p = self
        for ft in range(NDT):
            p.mm(p.ps_row[0:1, :TC], p.ones_col_f[:, :], pre[:, ft * TC:(ft + 1) * TC],
                 start=(ft == 0), stop=(ft == NDT - 1))
            p.act(sq_sb[:, :], pre[:, ft * TC:(ft + 1) * TC], AF.Square)
            p.mm(p.ps_row2[0:1, :TC], p.ones_col_f[:, :], sq_sb[:, :],
                 start=(ft == 0), stop=(ft == NDT - 1))
        MU, E2, VAR, STD, RA, RB = (rows[:, i * TC:(i + 1) * TC] for i in range(6))
        p.ts(MU, p.ps_row[0:1, :TC], 1.0 / DM, None, ALU.mult)
        p.ts(E2, p.ps_row2[0:1, :TC], 1.0 / DM, None, ALU.mult)
        p.tt(VAR, MU, MU, ALU.mult)
        p.tt(VAR, E2, VAR, ALU.subtract)
        p.act(STD, VAR, AF.Sqrt, bias=LN_EPS)
        p.recip(RA, STD)
        p.tt(RB, MU, RA, ALU.mult)
        p.ts(RB, RB, -1.0, None, ALU.mult)
        p.mm(p.ps[2][:, :TC], p.ones_row_f[:, :], RA)
        p.scopy(ab_sb[:, :TC], p.ps[2][:, :TC])
        p.mm(p.ps[2][:, :TC], p.ones_row_f[:, :], RB)
        p.scopy(ab_sb[:, TC:2 * TC], p.ps[2][:, :TC])
        for ft in range(NDT):
            p.tt(xo_sb[:, :], pre[:, ft * TC:(ft + 1) * TC], ab_sb[:, :TC], ALU.mult)
            p.tt(xo_sb[:, :], xo_sb[:, :], ab_sb[:, TC:2 * TC], ALU.add)
            p.dma(outf[b, ft * 128:(ft + 1) * 128, nin:nin + TC], xo_sb[:, :])
            if outb is not None:
                p.scopy(xb_sb[:, :], xo_sb[:, :])
                p.dma(outb[ft * 128:(ft + 1) * 128, b * S + nin:b * S + nin + TC], xb_sb[:, :])

    def phase_l1_proj(self):
        p = self
        nc = self.nc
        T = BPC * S
        with (nc.sbuf_tensor("p3_x", [128, NDT * T], BF16) as xa,
              nc.sbuf_tensor("p3_w", [128, 2 * NDT * 512], BF16) as wb0,
              nc.sbuf_tensor("p3_o", [128, 512], BF16) as ob,
              nc.sbuf_tensor("p3_o2", [128, 512], BF16) as ob2,
              nc.sbuf_tensor("p3_ab", [128, 2 * 256], BF16) as wab,
              nc.sbuf_tensor("p3_sig", [16, T], F32) as sig,
              nc.sbuf_tensor("p3_bab", [16, 2], F32) as bab):
            for dt in range(NDT):
                p.dma(xa[:, dt * T:(dt + 1) * T], p.x2b[dt * 128:(dt + 1) * 128, :])
            obuf = [ob, ob2]
            for iw, (wname, dst) in enumerate((("wq1", p.xwq), ("wk1", p.xwk), ("wv1", p.xwv))):
                for dot in range(16):
                    par = (iw * 16 + dot) % 2
                    wb = wb0[:, par * NDT * 512:par * NDT * 512 + NDT * 128]
                    p.gdma(wb, p.inp[wname][dot, :, :])
                    for n in range(T // 512):
                        pp = p.ps[(n % 2) * 2][:, :]
                        for dt in range(NDT):
                            p.mm(pp, wb[:, dt * 128:(dt + 1) * 128],
                                 xa[:, dt * T + n * 512:dt * T + (n + 1) * 512],
                                 start=(dt == 0), stop=(dt == NDT - 1))
                        p.scopy(obuf[n % 2][:, :], pp)
                        p.dma(dst[dot * 128:(dot + 1) * 128, n * 512:(n + 1) * 512], obuf[n % 2][:, :])
            for fc in range(4):
                par = fc % 2
                wb = wb0[:, par * NDT * 512:(par + 1) * NDT * 512]
                for dt in range(NDT):
                    p.gdma(wb[:, dt * 512:(dt + 1) * 512],
                           p.inp["wg1"][dt * 128:(dt + 1) * 128, fc * 512:(fc + 1) * 512])
                for tt in range(T // 128):
                    pp = p.ps[(tt % 2) * 2][:, :]
                    for dt in range(NDT):
                        p.mm(pp, xa[:, dt * T + tt * 128:dt * T + (tt + 1) * 128],
                             wb[:, dt * 512:(dt + 1) * 512],
                             start=(dt == 0), stop=(dt == NDT - 1))
                    p.scopy(obuf[tt % 2][:, :], pp)
                    p.dma(p.xwg_tm[tt * 128:(tt + 1) * 128, fc * 512:(fc + 1) * 512], obuf[tt % 2][:, :])
            p.gdma(wab[:, :256], p.inp["wa1"][:, :])
            p.gdma(wab[:, 256:], p.inp["wb1"][:, :])
            p.dma(bab[:, 0:1], p.inp["ba1"][:, :])
            p.dma(bab[:, 1:2], p.inp["bb1"][:, :])
            for j, (coff, boff) in enumerate(((0, 0), (256, 1))):
                for n in range(T // 512):
                    for dt in range(NDT):
                        p.mm(p.ps[0][:16, :], wab[:, coff + dt * 16:coff + (dt + 1) * 16],
                             xa[:, dt * T + n * 512:dt * T + (n + 1) * 512],
                             start=(dt == 0), stop=(dt == NDT - 1))
                    p.act(sig[:, n * 512:(n + 1) * 512], p.ps[0][:16, :],
                          AF.Sigmoid, bias=bab[:, boff:boff + 1])
                p.dma(p.sigh[:, j * T:(j + 1) * T], sig[:, :])

    def phase_l1_delta(self, b):
        p = self
        nc = self.nc
        QT = 512
        with ExitStack() as stk:
            E = stk.enter_context
            qhat = E(nc.sbuf_tensor(f"d_q{b}", [128, 16 * QT], BF16))
            khat = E(nc.sbuf_tensor(f"d_k{b}", [128, 16 * QT], BF16))
            vhat = E(nc.sbuf_tensor(f"d_v{b}", [128, 16 * QT], BF16))
            xwb = E(nc.sbuf_tensor(f"d_xw{b}", [128, 16 * (QT + 3)], BF16))
            dgt = E(nc.sbuf_tensor(f"d_dgt{b}", [128, 2 * 4 * 128], BF16))
            cw = E(nc.sbuf_tensor(f"d_cw{b}", [128, 3 * 64], F32))
            cb = E(nc.sbuf_tensor(f"d_cb{b}", [128, 3 * 16], F32))
            sigq = E(nc.sbuf_tensor(f"d_sig{b}", [16, 2 * QT], F32))
            scal = E(nc.sbuf_tensor(f"d_scal{b}", [16, 8 * QT], F32))
            flat = E(nc.sbuf_tensor(f"d_flat{b}", [1, 16], F32))
            cols = E(nc.sbuf_tensor(f"d_cols{b}", [128, 6 * 16], F32))
            zf = E(nc.sbuf_tensor(f"d_zf{b}", [128, 16 * 128], F32))
            zb = E(nc.sbuf_tensor(f"d_zb{b}", [128, 16 * 128], BF16))
            ktm = E(nc.sbuf_tensor(f"d_ktm{b}", [128, 16 * 128], BF16))
            vtm = E(nc.sbuf_tensor(f"d_vtm{b}", [128, 16 * 128], BF16))
            expm = E(nc.sbuf_tensor(f"d_expm{b}", [128, 2 * 128], F32))
            mneg = E(nc.sbuf_tensor(f"d_mneg{b}", [128, 2 * 128], BF16))
            rcur = E(nc.sbuf_tensor(f"d_rcur{b}", [128, 2 * 256], BF16))
            r0sb = E(nc.sbuf_tensor(f"d_r0{b}", [128, 2 * 128], BF16))
            otm = E(nc.sbuf_tensor(f"d_otm{b}", [128, 16 * 128], F32))
            otmp = E(nc.sbuf_tensor(f"d_otmp{b}", [128, 2 * 128], F32))
            osq = E(nc.sbuf_tensor(f"d_osq{b}", [128, 16 * 128], F32))
            st16 = E(nc.sbuf_tensor(f"d_st16{b}", [128, 8 * 16], F32))
            gsb = E(nc.sbuf_tensor(f"d_gsb{b}", [128, DM], BF16))
            gss = E(nc.sbuf_tensor(f"d_gss{b}", [128, DM], BF16))
            ogbf = E(nc.sbuf_tensor(f"d_og{b}", [128, DM], BF16))
            ofm = E(nc.sbuf_tensor(f"d_ofm{b}", [128, 128], BF16))
            lnwb = E(nc.sbuf_tensor(f"d_lnw{b}", [128, DM], F32))
            lnbb = E(nc.sbuf_tensor(f"d_lnb{b}", [128, DM], F32))
            rowt = E(nc.sbuf_tensor(f"d_rowt{b}", [1, DM], F32))
            self.bcast_row_2048(lnwb, "lnw_rep", rowt)
            self.bcast_row_2048(lnbb, "lnb_rep", rowt)
            p.dma(cw[:, 0:64], p.inp["qcw"][:, :])
            p.dma(cw[:, 64:128], p.inp["kcw"][:, :])
            p.dma(cw[:, 128:192], p.inp["vcw"][:, :])
            p.dma(cb[:, 0:16], p.inp["qcb"][:, :])
            p.dma(cb[:, 16:32], p.inp["kcb"][:, :])
            p.dma(cb[:, 32:48], p.inp["vcb"][:, :])
            p.memset(zf[:, :], 0.0)
            p.memset(zb[:, :], 0.0)
            LNC2 = float(2.0 * np.log(DSC))
            for q in range(4):
                t0 = b * S + q * QT
                for ti, (srch, dst, woff, boff) in enumerate(
                        ((p.xwq, qhat, 0, 0), (p.xwk, khat, 64, 16), (p.xwv, vhat, 128, 32))):
                    if q == 0:
                        p.memset(xwb[:, :].rearrange("p (d c) -> p d c", d=16)[:, :, 0:3], 0.0)
                        for dt in range(NDT):
                            p.dma(xwb[:, dt * (QT + 3) + 3:(dt + 1) * (QT + 3)],
                                  srch[dt * 128:(dt + 1) * 128, t0:t0 + QT])
                    else:
                        for dt in range(NDT):
                            p.dma(xwb[:, dt * (QT + 3):(dt + 1) * (QT + 3)],
                                  srch[dt * 128:(dt + 1) * 128, t0 - 3:t0 + QT])
                    for dt in range(NDT):
                        base = dt * (QT + 3)
                        pp = [p.ps[0], p.ps[2]][dt % 2][:, :]
                        dg = dgt[:, (dt % 2) * 512:(dt % 2) * 512 + 512]
                        for j in range(4):
                            p.ts(dg[:, j * 128:(j + 1) * 128], p.ident_bf[:, :],
                                 cw[:, woff + dt * 4 + j:woff + dt * 4 + j + 1],
                                 None, ALU.mult)
                        for j in range(4):
                            p.mm(pp, dg[:, j * 128:(j + 1) * 128],
                                 xwb[:, base + j:base + j + QT],
                                 start=(j == 0), stop=(j == 3))
                        p.act(dst[:, dt * QT:(dt + 1) * QT], pp, AF.Silu,
                              bias=cb[:, boff + dt:boff + dt + 1])
                if os.environ.get("KDUMPCONV") and b == 1 and q == 0:
                    for dt in range(16):
                        p.dma(p.aoh[dt * 128:(dt + 1) * 128, 0:512], qhat[:, dt * QT:(dt + 1) * QT])
                        p.dma(p.aoh[dt * 128:(dt + 1) * 128, 512:1024], khat[:, dt * QT:(dt + 1) * QT])
                        p.dma(p.aoh[dt * 128:(dt + 1) * 128, 1024:1536], vhat[:, dt * QT:(dt + 1) * QT])
                p.dma(sigq[:, 0:QT], p.sigh[:, t0:t0 + QT])
                p.dma(sigq[:, QT:2 * QT], p.sigh[:, BPC * S + t0:BPC * S + t0 + QT])
                A_, B_ = sigq[:, 0:QT], sigq[:, QT:2 * QT]
                LNA = scal[:, 0 * QT:1 * QT]; LL = scal[:, 1 * QT:2 * QT]
                LP = scal[:, 2 * QT:3 * QT]; LB2 = scal[:, 3 * QT:4 * QT]
                NL = scal[:, 4 * QT:5 * QT]; S1 = scal[:, 5 * QT:6 * QT]
                BC = scal[:, 6 * QT:7 * QT]; EL = scal[:, 7 * QT:8 * QT]
                p.act(LNA, A_, AF.Ln)
                p.vcopy(LL, LNA)
                cur, oth = LL, LNA
                for sft in (1, 2, 4, 8, 16, 32, 64):
                    c3_cur = cur.rearrange("p (c i) -> p c i", c=4)
                    c3_oth = oth.rearrange("p (c i) -> p c i", c=4)
                    p.vcopy(c3_oth[:, :, 0:sft], c3_cur[:, :, 0:sft])
                    p.tt(c3_oth[:, :, sft:CK], c3_cur[:, :, sft:CK],
                         c3_cur[:, :, 0:CK - sft], ALU.add)
                    cur, oth = oth, cur
                if cur is not LL:
                    p.vcopy(LL, cur)
                p.act(LNA, A_, AF.Ln)
                p.tt(LP, LL, LNA, ALU.subtract)
                p.act(LB2, B_, AF.Ln)
                p.tt(LB2, LP, LB2, ALU.add)
                p.ts(LB2, LB2, 1.0, LNC2, ALU.mult, ALU.add)
                p.ts(NL, LL, -1.0, None, ALU.mult)
                p.act(S1, LP, AF.Exp)
                p.tt(S1, B_, S1, ALU.mult)
                p.ts(S1, S1, float(-DSC * DSC), None, ALU.mult)
                p.ts(BC, B_, DSC, None, ALU.mult)
                p.act(EL, LL, AF.Exp)
                for ck4 in range(4 if not os.environ.get("KDUMPCONV") else 0):
                    cg = q * 4 + ck4
                    c0 = cg * CK
                    csl = slice(ck4 * CK, (ck4 + 1) * CK)
                    for k, qty in enumerate((NL, S1, BC, EL)):
                        p.tr(p.ps[3][:, :16], qty[:, csl], f32=True)
                        p.scopy(cols[:, k * 16:(k + 1) * 16], p.ps[3][:, :16])
                    p.tr(p.ps[3][0:1, 0:16], LL[:, ck4 * CK + CK - 1:ck4 * CK + CK], f32=True)
                    p.scopy(flat[:, 0:16], p.ps[3][0:1, 0:16])
                    p.mm(p.ps[3][:, :16], p.ones_row_f[:, :], flat[:, 0:16])
                    p.scopy(cols[:, 64:80], p.ps[3][:, :16])
                    p.tt(cols[:, 80:96], cols[:, 64:80], cols[:, 0:16], ALU.add)
                    p.act(cols[:, 80:96], cols[:, 80:96], AF.Exp)
                    p.act(cols[:, 64:80], cols[:, 64:80], AF.Exp)
                    for h in range(16):
                        hq = h * QT + ck4 * CK
                        hsl = slice(hq, hq + CK)
                        zsl = slice(h * 128, (h + 1) * 128)
                        par = h % 2
                        pA = [p.ps[0], p.ps[3]][par]
                        pB = [p.ps[1], p.ps4][par]
                        pC = [p.ps[2], p.ps_row][par]
                        pT = p.ps_bf[:, (par * 512):(par * 512) + 128]
                        exm = expm[:, par * 128:(par + 1) * 128]
                        mng = mneg[:, par * 128:(par + 1) * 128]
                        r0s = r0sb[:, par * 128:(par + 1) * 128]
                        rcu = rcur[:, par * 256:(par + 1) * 256]
                        otp = otmp[:, par * 128:(par + 1) * 128]
                        p.tr(pT, khat[:, hsl])
                        p.ts(ktm[:, zsl], pT, cols[:, 80 + h:80 + h + 1],
                             None, ALU.mult)
                        p.tr(pT, vhat[:, hsl])
                        p.ts(vtm[:, zsl], pT, cols[:, 32 + h:32 + h + 1],
                             None, ALU.mult)
                        p.mm(pA[:, :128], khat[:, hsl], khat[:, hsl])
                        p.mm(pB[:, :128], p.sel16[:, h * 128:(h + 1) * 128],
                             LB2[:, csl])
                        p.act(exm, pB[:, :128], AF.Exp, bias=cols[:, h:h + 1])
                        p.tt(exm, exm, p.mask_gt_f[:, :], ALU.min)
                        p.stt(mng, pA[:, :128], -1.0, exm, ALU.mult, ALU.mult)
                        p.mm(pC[:, :128], khat[:, hsl], zb[:, zsl])
                        p.stt(r0s, pC[:, :128], cols[:, 16 + h:16 + h + 1],
                              vtm[:, zsl], ALU.mult, ALU.add)
                        p.vcopy(rcu[:, 0:128], r0s)
                        for it in range(3):
                            cu = rcu[:, (it % 2) * 128:(it % 2) * 128 + 128]
                            nx = rcu[:, ((it + 1) % 2) * 128:((it + 1) % 2) * 128 + 128]
                            p.mm(pC[:, :128], mng, cu)
                            p.tt(nx, pC[:, :128], r0s, ALU.add)
                        Y = rcu[:, 128:256]
                        p.mm(pC[:, :128], qhat[:, hsl], zb[:, zsl])
                        p.mm(pA[:, :128], khat[:, hsl], qhat[:, hsl])
                        p.mm(pB[:, :128], p.sel16[:, h * 128:(h + 1) * 128],
                             LL[:, csl])
                        p.act(exm, pB[:, :128], AF.Exp, bias=cols[:, h:h + 1])
                        p.tt(exm, exm, p.mask_ge_f[:, :], ALU.min)
                        p.tt(mng, pA[:, :128], exm, ALU.mult)
                        p.mm(pA[:, :128], mng, Y)
                        p.ts(otp, pC[:, :128], cols[:, 48 + h:48 + h + 1],
                             None, ALU.mult)
                        p.tt(otm[:, zsl], otp, pA[:, :128], ALU.add)
                        p.mm(pC[:, :128], ktm[:, zsl], Y)
                        p.stt(zf[:, zsl], zf[:, zsl], cols[:, 64 + h:64 + h + 1],
                              pC[:, :128], ALU.mult, ALU.add)
                        p.scopy(zb[:, zsl], zf[:, zsl])
                    if os.environ.get("KDUMPCHUNK") and b == 1 and cg == 0:
                        p.gdma(p.aoh[0:128, 0:2048], otm[:, :])
                        p.gdma(p.aoh[128:256, 0:96], cols[:, :])
                        p.gdma(p.aoh[300:301, 0:16], flat[:, 0:16])
                        continue
                    o3 = otm[:, :].rearrange("p (h v) -> p h v", h=16)
                    p.rsum(st16[:, 0:16], o3)
                    p.ts(st16[:, 0:16], st16[:, 0:16], 1.0 / 128.0, None, ALU.mult)
                    p.act(osq[:, :], otm[:, :], AF.Square)
                    p.rsum(st16[:, 16:32], osq[:, :].rearrange("p (h v) -> p h v", h=16))
                    p.ts(st16[:, 16:32], st16[:, 16:32], 1.0 / 128.0, None, ALU.mult)
                    p.tt(st16[:, 32:48], st16[:, 0:16], st16[:, 0:16], ALU.mult)
                    p.tt(st16[:, 32:48], st16[:, 16:32], st16[:, 32:48], ALU.subtract)
                    p.act(st16[:, 32:48], st16[:, 32:48], AF.Sqrt, bias=LN_EPS)
                    p.recip(st16[:, 48:64], st16[:, 32:48])
                    p.ts(st16[:, 0:16], st16[:, 0:16], -1.0, None, ALU.mult)
                    for h in range(16):
                        zsl = slice(h * 128, (h + 1) * 128)
                        p.ts(otm[:, zsl], otm[:, zsl], st16[:, h:h + 1],
                             st16[:, 48 + h:48 + h + 1], ALU.add, ALU.mult)
                    p.tt(otm[:, :], otm[:, :], lnwb[:, :], ALU.mult)
                    p.tt(otm[:, :], otm[:, :], lnbb[:, :], ALU.add)
                    p.dma(gsb[:, :], p.xwg_tm[b * S + c0:b * S + c0 + CK, :])
                    p.act(gss[:, :], gsb[:, :], AF.Sigmoid)
                    p.tt(ogbf[:, :], otm[:, :], gss[:, :], ALU.mult)
                    for h in range(16):
                        p.tr(p.ps_bf[:, :128], ogbf[:, h * 128:(h + 1) * 128])
                        p.scopy(ofm[:, :], p.ps_bf[:, :128])
                        p.dma(p.aoh[h * 128:(h + 1) * 128, c0:c0 + CK], ofm[:, :])
        with self.nc.sbuf_tensor(f"zb0_{b}", [128, 16], F32) as zb0:
            p.memset(zb0[:, :], 0.0)
            self._proj_epilogue(wname="wo1", act_hbm=p.aoh, act_cols=S, b=b,
                                res_in=p.x2f, outf=p.x3f, outb=p.x3b,
                                bias_sb=zb0, tag=f"l1a{b}")

    def phase_l1_ffn(self):
        p = self
        nc = self.nc
        T = BPC * S
        with ExitStack() as stk:
            E = stk.enter_context
            xa = E(nc.sbuf_tensor("p5_x", [128, NDT * T], BF16))
            wgb = E(nc.sbuf_tensor("p5_wg", [128, NDT * 512], BF16))
            wub = E(nc.sbuf_tensor("p5_wu", [128, NDT * 512], BF16))
            slf = E(nc.sbuf_tensor("p5_sl", [128, 2 * 512], F32))
            hidf = E(nc.sbuf_tensor("p5_h", [128, 2 * 512], F32))
            absf = E(nc.sbuf_tensor("p5_a", [128, 2 * 512], F32))
            mm4 = E(nc.sbuf_tensor("p5_l", [128, 4 * 128], F32))
            tau = E(nc.sbuf_tensor("p5_t", [128, 128], F32))
            mk = E(nc.sbuf_tensor("p5_m", [128, 512], F32))
            hsp = E(nc.sbuf_tensor("p5_s", [128, 2 * 512], BF16))
            stag = E(nc.sbuf_tensor("p5_st", [128, 4 * 1024], BF16))
            for dt in range(NDT):
                p.dma(xa[:, dt * T:(dt + 1) * T], p.x3b[dt * 128:(dt + 1) * 128, :])

            def emit_tr(hb0, tt0):
                hs0 = hsp[:, (tt0 % 2) * 512:(tt0 % 2) * 512 + 512]
                for j in range(4):
                    p.tr(p.ps_bf[:, :128], hs0[:, j * 128:(j + 1) * 128])
                    p.scopy(stag[:, j * 1024 + (tt0 % 8) * 128:j * 1024 + (tt0 % 8 + 1) * 128],
                            p.ps_bf[:, :128])
                if tt0 % 8 == 7:
                    for j in range(4):
                        p.dma(p.hid1[hb0 * 4 + j, :, (tt0 - 7) * 128:(tt0 + 1) * 128],
                              stag[:, j * 1024:(j + 1) * 1024])

            prev = None
            for hb in range(16):
                p.dma(wgb[:, :], p.wgate_b[hb, :, :])
                p.dma(wub[:, :], p.wup_b[hb, :, :])
                for tt in range(T // 128):
                    s = tt % 2
                    pg = p.ps[s][:, :]
                    pu = p.ps[2 + s][:, :]
                    sl = slf[:, s * 512:(s + 1) * 512]
                    hf = hidf[:, s * 512:(s + 1) * 512]
                    ab = absf[:, s * 512:(s + 1) * 512]
                    hs = hsp[:, s * 512:(s + 1) * 512]
                    for dt in range(NDT):
                        p.mm(pg, xa[:, dt * T + tt * 128:dt * T + (tt + 1) * 128],
                             wgb[:, dt * 512:(dt + 1) * 512],
                             start=(dt == 0), stop=(dt == NDT - 1))
                    p.act(sl, pg, AF.Silu)
                    for dt in range(NDT):
                        p.mm(pu, xa[:, dt * T + tt * 128:dt * T + (tt + 1) * 128],
                             wub[:, dt * 512:(dt + 1) * 512],
                             start=(dt == 0), stop=(dt == NDT - 1))
                    if prev is not None:
                        emit_tr(*prev)
                    prev = (hb, tt)
                    p.tt(hf, pu, sl, ALU.mult)
                    p.act(ab, hf, AF.Abs)
                    A = lambda j: ab[:, j * 128:(j + 1) * 128]
                    p.tt(mm4[:, 0:128], A(0), A(1), ALU.max)
                    p.tt(mm4[:, 128:256], A(2), A(3), ALU.max)
                    p.tt(mm4[:, 256:384], A(0), A(1), ALU.min)
                    p.tt(mm4[:, 384:512], A(2), A(3), ALU.min)
                    p.tt(tau[:, :], mm4[:, 0:128], mm4[:, 128:256], ALU.min)
                    p.tt(mm4[:, 256:384], mm4[:, 256:384], mm4[:, 384:512], ALU.max)
                    p.tt(tau[:, :], tau[:, :], mm4[:, 256:384], ALU.max)
                    for j in range(4):
                        p.tt(mk[:, j * 128:(j + 1) * 128], A(j), tau[:, :], ALU.is_ge)
                    p.tt(hs, hf, mk[:, :], ALU.mult)
            emit_tr(*prev)
        self._down_proj(p.hid1, p.wdown_b, p.part1, p.x3f, self.out, None,
                        tag="l1f", wcast=False)

    def finish(self):
        p = self
        if self.dbg is not None:
            src = {"x1f": p.x1f, "x2f": p.x2f, "x3f": p.x3f, "w2eff": p.w2eff, "sigh": p.sigh,
                   "aoh": p.aoh, "x1b": p.x1b, "hid0": p.hid0, "hid1": p.hid1,
                   "xwq": p.xwq, "xwk": p.xwk, "xwv": p.xwv,
                   "xwg_tm": p.xwg_tm, "part0": p.part0}[self.dbg]
            flat_s = src.rearrange(" ".join(f"a{i}" for i in range(len(src.shape))) + " -> (" +
                                   " ".join(f"a{i}" for i in range(len(src.shape))) + ")") \
                if False else src
            p.dma(self.dbg_out[...], src[...])
        if self.mode == "chain":
            total = self.cnt
            self.ops["sync"].append(lambda h, n=total: h.wait_ge(self.chain, n))
        else:
            tots = [(self.esem[e], self.emitted[e]) for e in self.ops if self.emitted[e] > 0]
            tots += [(self.esem[k], 16 * u) for k, u in self.dslot_use.items() if u > 0]
            def fin(h, tots=tuple(tots)):
                for smm, v in tots:
                    h.wait_ge(smm, v)
            self.ops["sync"].append(fin)
        with self.nc.Block() as block:
            for eng in ("tensor", "vector", "scalar", "gpsimd", "sync"):
                def mk(e):
                    def f(h):
                        for op in self.ops[e]:
                            op(h)
                    return f
                getattr(block, eng)(mk(eng))
        self.st.close()
        return self.nc


def build_program(upto=99, dbg=None, mode=None):
    import os
    if mode is None:
        mode = os.environ.get("KMODE", "auto")
    relax = set(os.environ.get("KRELAX", "c,w,1,2,3,4,5").split(","))
    p = Prog(upto=upto, dbg=dbg, mode=mode)
    p.alloc_consts()
    p.kbar_on = "c" not in relax
    p.phase_consts()
    if upto >= 0:
        p.kbar_on = "w" not in relax
        p.phase_w2eff()
    if upto >= 1:
        p.kbar_on = "1" not in relax
        for b in range(BPC):
            p.phase_l0_attn(b)
    if upto >= 2:
        p.kbar_on = "2" not in relax
        p.phase_l0_ffn()
    if upto >= 3:
        p.kbar_on = "3" not in relax
        p.phase_l1_proj()
    if upto >= 4:
        p.kbar_on = "4" not in relax
        for b in range(BPC):
            p.phase_l1_delta(b)
    if upto >= 5:
        p.kbar_on = "5" not in relax
        p.phase_l1_ffn()
    p.kbar_on = False
    nc = p.finish()
    return nc


def host_layout(inputs):
    f32 = np.float32
    qkv_w = np.asarray(inputs["L0_attn_qkv_w"], f32)
    qkv_b = np.asarray(inputs["L0_attn_qkv_b"], f32)
    d = {}
    d["wq0"] = relay_lhsT(np.ascontiguousarray(qkv_w[0:DM].T), NDT, 512)
    d["wk0"] = relay_lhsT(np.ascontiguousarray(qkv_w[DM:2 * DM].T), NDT, 512)
    d["wv0"] = relay_lhsT(np.ascontiguousarray(qkv_w[2 * DM:3 * DM].T), NDT, 512)
    d["qb0"] = np.ascontiguousarray(qkv_b[0:DM].reshape(16, 128).T)
    d["kb0"] = np.ascontiguousarray(qkv_b[DM:2 * DM].reshape(16, 128).T)
    d["vb0_row"] = np.ascontiguousarray(qkv_b[2 * DM:3 * DM].reshape(1, DM))
    d["wo0"] = relay_lhsT(np.ascontiguousarray(np.asarray(inputs["L0_attn_out_w"], f32).T), NDT, 128)
    d["ob0"] = np.ascontiguousarray(np.asarray(inputs["L0_attn_out_b"], f32).reshape(16, 128).T)
    d["alibi_rep"] = np.ascontiguousarray(
        np.repeat(np.asarray(inputs["L0_attn_alibi_slopes"], f32), 16).reshape(1, 256))
    W1 = np.asarray(inputs["L0_ffn_W1"], f32); W3 = np.asarray(inputs["L0_ffn_W3"], f32)
    d["w1t"] = np.ascontiguousarray(
        W1.reshape(NHT, 128, NDT, 128).transpose(0, 3, 2, 1).reshape(NHT, 128, DM))
    d["w3t"] = np.ascontiguousarray(
        W3.reshape(NHT, 128, NDT, 128).transpose(0, 3, 2, 1).reshape(NHT, 128, DM))
    d["w2t"] = np.ascontiguousarray(np.asarray(inputs["L0_ffn_W2"], f32).T)
    d["h128"] = hada(128).astype(BF)
    d["h64"] = hada(64).astype(BF)
    d["identity"] = np.eye(128, dtype=np.float32).astype(BF)
    d["identity_f"] = np.eye(128, dtype=np.float32)
    d["wq1"] = relay_lhsT(np.asarray(inputs["L1_attn_Wq"], f32), NDT, 128)
    d["wk1"] = relay_lhsT(np.asarray(inputs["L1_attn_Wk"], f32), NDT, 128)
    d["wv1"] = relay_lhsT(np.asarray(inputs["L1_attn_Wv"], f32), NDT, 128)
    d["wg1"] = np.ascontiguousarray(np.asarray(inputs["L1_attn_Wg"], f32))
    d["wa1"] = np.ascontiguousarray(
        np.asarray(inputs["L1_attn_Wa"], f32).reshape(16, 128, 16).transpose(1, 0, 2).reshape(128, 256))
    d["wb1"] = np.ascontiguousarray(
        np.asarray(inputs["L1_attn_Wb"], f32).reshape(16, 128, 16).transpose(1, 0, 2).reshape(128, 256))
    d["ba1"] = np.ascontiguousarray(np.asarray(inputs["L1_attn_ba"], f32).reshape(16, 1))
    d["bb1"] = np.ascontiguousarray(np.asarray(inputs["L1_attn_bb"], f32).reshape(16, 1))
    d["wo1"] = relay_lhsT(np.asarray(inputs["L1_attn_Wo"], f32), NDT, 128)
    for nm, key in (("qcw", "L1_attn_qcw"), ("kcw", "L1_attn_kcw"), ("vcw", "L1_attn_vcw")):
        w = np.asarray(inputs[key], f32)
        d[nm] = np.ascontiguousarray(w.reshape(16, 128, 4).transpose(1, 0, 2).reshape(128, 64))
    for nm, key in (("qcb", "L1_attn_qcb"), ("kcb", "L1_attn_kcb"), ("vcb", "L1_attn_vcb")):
        d[nm] = np.ascontiguousarray(np.asarray(inputs[key], f32).reshape(16, 128).T)
    d["lnw_rep"] = np.ascontiguousarray(np.tile(np.asarray(inputs["L1_attn_ln_w"], f32), 16).reshape(1, DM))
    d["lnb_rep"] = np.ascontiguousarray(np.tile(np.asarray(inputs["L1_attn_ln_b"], f32), 16).reshape(1, DM))
    def _permute_up(w):
        return np.ascontiguousarray(
            w.reshape(16, 128, 16, 128, 4).transpose(2, 1, 0, 4, 3).reshape(16, 128, H0))
    d["wgate_l"] = _permute_up(np.asarray(inputs["L1_ffn_w_gate"], f32))
    d["wup_l"] = _permute_up(np.asarray(inputs["L1_ffn_w_up"], f32))
    d["wdown_l"] = np.ascontiguousarray(
        np.asarray(inputs["L1_ffn_w_down"], f32).reshape(16, 128, 4, DM)
        .transpose(0, 2, 1, 3).reshape(H0, DM))
    return d


def kernel(**inputs):
    x = np.asarray(inputs["L0_attn_hidden_states"], np.float32)
    shared = host_layout(inputs)
    nc = build_program()
    in_maps = []
    for c in range(8):
        m = dict(shared)
        xs = np.ascontiguousarray(np.transpose(x[2 * c:2 * c + 2], (0, 2, 1)))
        m["xin"] = xs
        in_maps.append(m)
    res = run_bass_kernel_spmd(nc, in_maps, core_ids=list(range(8)))
    outs = []
    for c in range(8):
        o = np.asarray(res.results[c]["out"])
        outs.append(np.transpose(o, (0, 2, 1)))
    return np.concatenate(outs, axis=0).astype(np.float32)


if __name__ == "__main__":
    pass


# version 14 (final)
# speedup vs baseline: 3.1706x; 1.0266x over previous
import os
import numpy as np
import ml_dtypes
import concourse.bass as bass
import concourse.mybir as mybir
from concourse.bass_utils import run_bass_kernel_spmd
from contextlib import ExitStack

F32 = mybir.dt.float32
BF16 = mybir.dt.bfloat16
AF = mybir.ActivationFunctionType
ALU = mybir.AluOpType
AXX = mybir.AxisListType.X

BPC = 2
S = 2048
DM = 2048
NDT = DM // 128
NH = 16
HD = 128
H0 = 8192
NHT = H0 // 128
CK = 128
NCK = S // CK
INV_SQRT_HD = float(1.0 / np.sqrt(128.0))
LN_EPS = 1e-5
DSC = float(128.0 ** -0.5)
BF = ml_dtypes.bfloat16


def hada(n):
    H = np.array([[1.0]], dtype=np.float32)
    while H.shape[0] < n:
        H = np.block([[H, H], [H, -H]])
    return H


def relay_lhsT(w, din_tiles, dout_block):
    din, dout = w.shape
    nb = dout // dout_block
    assert din == din_tiles * 128
    r = w.reshape(din_tiles, 128, nb, dout_block).transpose(2, 1, 0, 3)
    return np.ascontiguousarray(r.reshape(nb, 128, din_tiles * dout_block))


class Prog:
    def __init__(self, upto=99, dbg=None, mode="auto"):
        self.upto = upto
        self.dbg = dbg
        nc = bass.Bass("TRN2", target_bir_lowering=False, debug=False)
        self.nc = nc
        self.cnt = 0
        self.mode = mode
        self.ops = {e: [] for e in ("tensor", "vector", "scalar", "gpsimd", "sync")}
        self.emitted = {e: 0 for e in self.ops}
        self.waited = {}
        self.acc = {}
        self.floor = {}
        self.input_names = set()
        self.psum_names = set()
        import os
        se = os.environ.get("KSE", "tensor")
        self.same_engine_ok = set(se.split(",")) if se else set()
        self.bar = {}
        orig_sbuf = nc.sbuf_tensor
        prog = self
        class _SbGuard:
            def __init__(self, cm):
                self.cm = cm
            def __enter__(self):
                return self.cm.__enter__()
            def __exit__(self, *a):
                r = self.cm.__exit__(*a)
                bar = dict(prog.bar)
                for e2 in prog.ops:
                    if prog.emitted[e2] > 0:
                        bar[e2] = prog.emitted[e2]
                for k2, u in prog.dslot_use.items():
                    if u > 0:
                        bar[k2] = 16 * u
                prog.bar = bar
                return r
        def wrapped(*args, **kw):
            return _SbGuard(orig_sbuf(*args, **kw))
        nc.sbuf_tensor = wrapped
        self.st = ExitStack()
        self._declare_dram()

    MAXREC = 160

    BIGP = 1 << 40
    BIGF = 1 << 60

    def _iv(self, ap):
        try:
            dims = list(ap.ap)
            off = ap.offset
            t = ap.tensor
            rowlen = 1
            for d in t.shape[1:]:
                rowlen *= int(d)
            if dims and rowlen > 0 and dims[0][0] == rowlen:
                p0 = off // rowlen
                f0 = off % rowlen
                pn = dims[0][1]
                fspan = sum(abs(st) * (cn - 1) for st, cn in dims[1:])
                if f0 + fspan < rowlen:
                    return (p0, p0 + pn - 1, f0, f0 + fspan)
            lo = off
            span = sum(abs(st) * (cn - 1) for st, cn in dims)
            return (0, self.BIGP, lo, lo + span)
        except Exception:
            return (0, self.BIGP, 0, self.BIGF)

    def _scan(self, eng, isdma, ap, want_writes_only, need):
        t = ap.tensor
        name = t.name
        if name in self.input_names:
            return
        recs = self.acc.get(name)
        flo = self.floor.get(name)
        if flo:
            for e2, c2 in flo.items():
                if not (e2 == eng and not isdma):
                    if need.get(e2, 0) < c2:
                        need[e2] = c2
        if not recs:
            return
        if name in self.psum_names:
            rect = (0, self.BIGP, 0, self.BIGF)
        else:
            rect = self._iv(ap)
        plo, phi, flo, fhi = rect
        for (w2, e2, c2, r2, dma2) in recs:
            if want_writes_only and not w2:
                continue
            if fhi < r2[2] or r2[3] < flo or phi < r2[0] or r2[1] < plo:
                continue
            if e2 == eng and not isdma and not dma2 and eng in self.same_engine_ok:
                continue
            if need.get(e2, 0) < c2:
                need[e2] = c2

    def _rec(self, name, entry):
        lst = self.acc.setdefault(name, [])
        lst.append(entry)
        if len(lst) > self.MAXREC:
            drop = lst[: len(lst) - self.MAXREC]
            del lst[: len(lst) - self.MAXREC]
            flo = self.floor.setdefault(name, {})
            for (w2, e2, c2, r2, dma2) in drop:
                if flo.get(e2, 0) < c2:
                    flo[e2] = c2

    def _emit(self, eng, fn, reads=(), writes=(), inc=1):
        if self.mode == "chain":
            at = self.cnt
            def run(h, at=at, fn=fn, inc=inc):
                h.wait_ge(self.chain, at)
                r = fn(h)
                r.then_inc(self.chain, inc)
            self.ops[eng].append(run)
            self.cnt += inc
            return
        import os
        isdma = inc == 16
        need = {}
        for ap in writes:
            self._scan(eng, isdma, ap, False, need)
        for ap in reads:
            self._scan(eng, isdma, ap, True, need)
        if getattr(self, "kbar_on", False):
            for e2 in self.ops:
                if self.emitted[e2] > 0 and (e2 != eng or os.environ.get("KSELF")):
                    need[e2] = self.emitted[e2]
            for k2, u in self.dslot_use.items():
                if u > 0:
                    need[k2] = max(need.get(k2, 0), 16 * u)
        for k2, v2 in self.bar.items():
            if k2 != eng and need.get(k2, 0) < v2:
                need[k2] = v2
        if isdma:
            q = eng
            skey = ("d", q, self.dma_idx[q] % self.NSLOT)
            self.dma_idx[q] += 1
            use = self.dslot_use[skey]
            self.dslot_use[skey] = use + 1
            cnt_after = 16 * (use + 1)
            ekey = skey
            if use > 0:
                need[skey] = max(need.get(skey, 0), 16 * use)
        else:
            ekey = eng
            cnt_after = self.emitted[eng] + inc
            self.emitted[eng] = cnt_after
        waits = []
        for e2, v in need.items():
            key = (ekey, e2) if isdma else (eng, e2)
            if self.waited.get(key, 0) >= v:
                continue
            self.waited[key] = v
            waits.append((self.esem[e2], v))
        mysem = self.esem[ekey]
        def run(h, waits=tuple(waits), fn=fn, mysem=mysem, inc=inc):
            for smm, v in waits:
                h.wait_ge(smm, v)
            r = fn(h)
            r.then_inc(mysem, inc)
        self.ops[eng].append(run)
        for ap in writes:
            name = ap.tensor.name
            if name in self.input_names:
                continue
            r = (0, self.BIGP, 0, self.BIGF) if name in self.psum_names else self._iv(ap)
            self._rec(name, (True, ekey, cnt_after, r, isdma))
        for ap in reads:
            name = ap.tensor.name
            if name in self.input_names:
                continue
            r = (0, self.BIGP, 0, self.BIGF) if name in self.psum_names else self._iv(ap)
            self._rec(name, (False, ekey, cnt_after, r, isdma))

    def t(self, fn, reads=(), writes=()): self._emit("tensor", fn, reads, writes)
    def v(self, fn, reads=(), writes=()): self._emit("vector", fn, reads, writes)
    def s(self, fn, reads=(), writes=()): self._emit("scalar", fn, reads, writes)
    def g(self, fn, reads=(), writes=()): self._emit("gpsimd", fn, reads, writes)
    def dma(self, out, in_, q="sync"):
        self._emit(q, lambda h, o=out, i=in_: h.dma_start(out=o, in_=i),
                   reads=(in_,), writes=(out,), inc=16)
    def gdma(self, out, in_):
        self.dma(out, in_, q="gpsimd")

    def mm(self, out, lhsT, rhs, start=True, stop=True):
        self.t(lambda t, o=out, l=lhsT, r=rhs, st=start, sp=stop:
               t.matmul(o, l, r, start=st, stop=sp),
               reads=(lhsT, rhs), writes=(out,))
    def tr(self, out, in_, f32=False):
        k = in_.shape[0]
        ident = self.ident_f if f32 else self.ident_bf
        self.t(lambda t, o=out, i=in_, g=ident, kk=k: t.transpose(o, i, g[:kk, :kk]),
               reads=(in_, ident[:, :]), writes=(out,))
    def act(self, out, in_, func, bias=0.0, scale=1.0, accum_out=None):
        rd = [in_]
        if not isinstance(bias, float):
            rd.append(bias)
        wr = [out] + ([accum_out] if accum_out is not None else [])
        self.s(lambda sc, o=out, i=in_, f=func, b=bias, s_=scale, a=accum_out:
               sc.activation(o, i, f, bias=b, scale=s_, accum_out=a),
               reads=tuple(rd), writes=tuple(wr))
    def scopy(self, out, in_):
        self.s(lambda sc, o=out, i=in_: sc.copy(o, i), reads=(in_,), writes=(out,))
    def vcopy(self, out, in_):
        self.v(lambda v, o=out, i=in_: v.tensor_copy(o, i), reads=(in_,), writes=(out,))
    def tt(self, out, in0, in1, op):
        self.v(lambda v, o=out, a=in0, b=in1, p=op: v.tensor_tensor(o, a, b, p),
               reads=(in0, in1), writes=(out,))
    def ts(self, out, in0, s1, s2=None, op0=ALU.mult, op1=ALU.bypass, accum_out=None):
        rd = [in0]
        if not isinstance(s1, float):
            rd.append(s1)
        if s2 is not None and not isinstance(s2, float):
            rd.append(s2)
        wr = [out] + ([accum_out] if accum_out is not None else [])
        self.v(lambda v, o=out, i=in0, a=s1, b=s2, p0=op0, p1=op1, ac=accum_out:
               v.tensor_scalar(o, i, a, b, p0, p1, accum_out=ac),
               reads=tuple(rd), writes=tuple(wr))
    def stt(self, out, in0, s1, in1, op0, op1):
        rd = [in0, in1]
        if not isinstance(s1, float):
            rd.append(s1)
        self.v(lambda v, o=out, i=in0, a=s1, j=in1, p0=op0, p1=op1:
               v.scalar_tensor_tensor(o, i, a, j, p0, p1),
               reads=tuple(rd), writes=(out,))
    def recip(self, out, in_):
        self.v(lambda v, o=out, i=in_: v.reciprocal(o, i), reads=(in_,), writes=(out,))
    def rsum(self, out, in_):
        self.v(lambda v, o=out, i=in_: v.reduce_sum(o, i, AXX),
               reads=(in_,), writes=(out,))
    def memset(self, ap, val):
        self.g(lambda g, a=ap, c=val: g.memset(a, c), reads=(), writes=(ap,))
    def asel(self, out, in_, pattern, cmp, fill, base=0, chmul=1):
        self.g(lambda g, o=out, i=in_, p=pattern, c=cmp, f=fill, b=base, m=chmul:
               g.affine_select(o, i, p, c, f, base=b, channel_multiplier=m),
               reads=(in_,), writes=(out,))

    def _declare_dram(self):
        nc = self.nc
        self.inp = {}
        def I(name, shape, dt=F32):
            self.inp[name] = nc.dram_tensor(name, list(shape), dt, kind="ExternalInput").ap()
            self.input_names.add(name)
            return self.inp[name]
        I("xin", (BPC, DM, S))
        I("wq0", (4, 128, NDT * 512)); I("wk0", (4, 128, NDT * 512)); I("wv0", (4, 128, NDT * 512))
        I("qb0", (128, 16)); I("kb0", (128, 16)); I("vb0_row", (1, DM))
        I("wo0", (16, 128, NDT * 128)); I("ob0", (128, 16))
        I("alibi_rep", (1, 256))
        I("w1t", (NHT, 128, DM)); I("w3t", (NHT, 128, DM)); I("w2t", (H0, DM))
        I("h128", (128, 128), BF16); I("h64", (64, 64), BF16); I("identity", (128, 128), BF16)
        I("identity_f", (128, 128), F32)
        I("wq1", (16, 128, NDT * 128)); I("wk1", (16, 128, NDT * 128))
        I("wv1", (16, 128, NDT * 128)); I("wg1", (DM, DM))
        I("wa1", (128, 256)); I("wb1", (128, 256)); I("ba1", (16, 1)); I("bb1", (16, 1))
        I("wo1", (16, 128, NDT * 128))
        I("qcw", (128, 64)); I("kcw", (128, 64)); I("vcw", (128, 64))
        I("qcb", (128, 16)); I("kcb", (128, 16)); I("vcb", (128, 16))
        I("lnw_rep", (1, DM)); I("lnb_rep", (1, DM))
        I("wgate_l", (16, 128, H0)); I("wup_l", (16, 128, H0)); I("wdown_l", (H0, DM))

        D = lambda name, shape, dt: nc.dram_tensor(name, list(shape), dt, kind="Internal").ap()
        self.w2eff = D("w2eff", (H0, DM), BF16)
        self.bh = D("bh", (NHT, 128, DM), BF16)
        self.aoh = D("aoh", (DM, S), BF16)
        self.x1f = D("x1f", (BPC, DM, S), F32)
        self.x1b = D("x1b", (DM, BPC * S), BF16)
        self.hid0 = D("hid0", (NHT, 128, BPC * S), BF16)
        self.part0 = D("part0", (NDT, 128, BPC * S), F32)
        self.x2f = D("x2f", (BPC, DM, S), F32)
        self.x2b = D("x2b", (DM, BPC * S), BF16)
        self.xwq = D("xwq", (DM, BPC * S), BF16)
        self.xwk = D("xwk", (DM, BPC * S), BF16)
        self.xwv = D("xwv", (DM, BPC * S), BF16)
        self.xwg_tm = D("xwg_tm", (BPC * S, DM), BF16)
        self.x3f = D("x3f", (BPC, DM, S), F32)
        self.x3b = D("x3b", (DM, BPC * S), BF16)
        self.hid1 = D("hid1", (NHT, 128, BPC * S), BF16)
        self.part1 = D("part1", (NDT, 128, BPC * S), F32)
        self.sigh = D("sigh", (16, 2 * BPC * S), F32)
        self.rowtmp = D("rowtmp", (16, CK), F32)
        self.rowtmp2 = D("rowtmp2", (16, CK), F32)
        self.wgate_b = D("wgate_b", (16, 128, H0), BF16)
        self.wup_b = D("wup_b", (16, 128, H0), BF16)
        self.wdown_b = D("wdown_b", (H0, DM), BF16)

        self.out = nc.dram_tensor("out", [BPC, DM, S], F32, kind="ExternalOutput").ap()
        if self.dbg is not None:
            shp = {"x1f": (BPC, DM, S), "x2f": (BPC, DM, S), "x3f": (BPC, DM, S),
                   "sigh": (16, 2 * BPC * S),
                   "w2eff": (H0, DM), "aoh": (DM, S), "x1b": (DM, BPC * S),
                   "hid0": (NHT, 128, BPC * S), "hid1": (NHT, 128, BPC * S),
                   "xwq": (DM, BPC * S), "xwk": (DM, BPC * S), "xwv": (DM, BPC * S),
                   "xwg_tm": (BPC * S, DM),
                   "part0": (NDT, 128, BPC * S)}[self.dbg]
            dt = BF16 if self.dbg in ("w2eff", "aoh", "x1b", "hid0", "hid1", "xwq", "xwk", "xwv", "xwg_tm") else F32
            self.dbg_out = nc.dram_tensor("dbg", list(shp), dt, kind="ExternalOutput").ap()

    def alloc_consts(self):
        E = self.st.enter_context
        nc = self.nc
        self.chain = E(nc.semaphore("chain"))
        self.esem = {e: E(nc.semaphore(f"sem_{e}")) for e in self.ops}
        self.NSLOT = 24
        for q in ("sync", "gpsimd"):
            for si in range(self.NSLOT):
                self.esem[("d", q, si)] = E(nc.semaphore(f"ds_{q}_{si}"))
        self.dslot_use = {("d", q, si): 0 for q in ("sync", "gpsimd")
                          for si in range(self.NSLOT)}
        self.dma_idx = {"sync": 0, "gpsimd": 0}
        self.ident_bf = E(nc.sbuf_tensor("ident_bf", [128, 128], BF16))
        self.ident_f = E(nc.sbuf_tensor("ident_f", [128, 128], F32))
        self.h128_sb = E(nc.sbuf_tensor("h128_sb", [128, 128], BF16))
        self.h64_sb = E(nc.sbuf_tensor("h64_sb", [64, 64], BF16))
        self.ones_col_bf = E(nc.sbuf_tensor("ones_col_bf", [128, 1], BF16))
        self.ones_col_f = E(nc.sbuf_tensor("ones_col_f", [128, 1], F32))
        self.ones_row_f = E(nc.sbuf_tensor("ones_row_f", [1, 128], F32))
        self.iota_row = E(nc.sbuf_tensor("iota_row", [1, 128], F32))
        self.negoff_row = E(nc.sbuf_tensor("negoff_row", [1, 256], F32))
        self.slope_rep = E(nc.sbuf_tensor("slope_rep", [1, 256], F32))
        self.r2_row = E(nc.sbuf_tensor("r2_row", [1, 256], F32))
        self.bias_tab = E(nc.sbuf_tensor("bias_tab", [128, 256], F32))
        self.qb_sb = E(nc.sbuf_tensor("qb_sb", [128, 16], F32))
        self.kb_sb = E(nc.sbuf_tensor("kb_sb", [128, 16], F32))
        self.ob_sb = E(nc.sbuf_tensor("ob_sb", [128, 16], F32))
        self.ps = [E(nc.psum_tensor(f"ps{i}", [128, 512], F32)) for i in range(4)]
        self.ps_bf = E(nc.psum_tensor("ps_bf", [128, 1024], BF16))
        self.ps_row = E(nc.psum_tensor("ps_row", [128, 512], F32))
        for i in range(1, 4):
            self.psum_names.add(f"ps{i}")
        self.psum_names.update(("ps_bf", "ps_row", "ps_row2", "ps4"))
        self.epsc = E(nc.sbuf_tensor("epsc", [128, 1], F32))
        nc.const_aps.register(F32, LN_EPS, self.epsc[:, :])
        self.ps_row2 = E(nc.psum_tensor("ps_row2", [128, 512], F32))
        self.ps4 = E(nc.psum_tensor("ps4", [128, 512], F32))
        self.mask_ge_bf = E(nc.sbuf_tensor("mask_ge_bf", [128, 128], BF16))
        self.mask_gt_f = E(nc.sbuf_tensor("mask_gt_f", [128, 128], F32))
        self.mask_ge_f = E(nc.sbuf_tensor("mask_ge_f", [128, 128], F32))
        self.sel16 = E(nc.sbuf_tensor("sel16", [16, 16 * 128], F32))

    def phase_consts(self):
        p = self
        p.dma(p.ident_bf[:, :], p.inp["identity"][:, :])
        p.dma(p.ident_f[:, :], p.inp["identity_f"][:, :])
        p.dma(p.h128_sb[:, :], p.inp["h128"][:, :])
        p.dma(p.h64_sb[:, :], p.inp["h64"][:, :])
        p.dma(p.slope_rep[:, :], p.inp["alibi_rep"][:, :])
        p.dma(p.qb_sb[:, :], p.inp["qb0"][:, :])
        p.dma(p.kb_sb[:, :], p.inp["kb0"][:, :])
        p.dma(p.ob_sb[:, :], p.inp["ob0"][:, :])
        p.memset(p.epsc[:, :], LN_EPS)
        p.memset(p.ones_col_bf[:, :], 1.0)
        p.memset(p.ones_col_f[:, :], 1.0)
        p.memset(p.ones_row_f[:, :], 1.0)
        p.g(lambda g: g.iota(p.iota_row[:, :], [[1, 128]], base=0, channel_multiplier=0,
                             allow_small_or_imprecise_dtypes=True),
            writes=(p.iota_row[:, :],))
        p.g(lambda g: g.iota(p.negoff_row[:, :], [[0, 16], [-128, 16]], base=-64,
                             channel_multiplier=0, allow_small_or_imprecise_dtypes=True),
            writes=(p.negoff_row[:, :],))
        p.memset(p.mask_ge_bf[:, :], 1e38)
        p.asel(p.mask_ge_bf[:, :], p.mask_ge_bf[:, :], [[1, 128]], ALU.is_ge, 0.0,
               base=0, chmul=-1)
        p.memset(p.mask_gt_f[:, :], 1e38)
        p.asel(p.mask_gt_f[:, :], p.mask_gt_f[:, :], [[1, 128]], ALU.is_gt, 0.0,
               base=0, chmul=-1)
        p.memset(p.mask_ge_f[:, :], 1e38)
        p.asel(p.mask_ge_f[:, :], p.mask_ge_f[:, :], [[1, 128]], ALU.is_ge, 0.0,
               base=0, chmul=-1)
        p.memset(p.sel16[:, :], 1.0)
        p.asel(p.sel16[:, :], p.sel16[:, :], [[1, 16], [0, 128]], ALU.is_ge, 0.0,
               base=0, chmul=-1)
        p.asel(p.sel16[:, :], p.sel16[:, :], [[-1, 16], [0, 128]], ALU.is_ge, 0.0,
               base=0, chmul=1)
        p.tt(p.r2_row[:, :], p.slope_rep[:, :], p.negoff_row[:, :], ALU.mult)
        p.mm(p.ps[0][:, :256], p.iota_row[:, :], p.slope_rep[:, :], start=True, stop=False)
        p.mm(p.ps[0][:, :256], p.ones_row_f[:, :], p.r2_row[:, :], start=False, stop=True)
        p.scopy(p.bias_tab[:, :], p.ps[0][:, :256])

    def bcast_row_2048(self, dst, srcname, row_tmp):
        p = self
        p.dma(row_tmp[:, :], p.inp[srcname][:, :])
        for n in range(4):
            p.mm(p.ps[1][:, :], p.ones_row_f[:, :], row_tmp[:, n * 512:(n + 1) * 512])
            p.scopy(dst[:, n * 512:(n + 1) * 512], p.ps[1][:, :])

    def phase_w2eff(self):
        p = self
        E = self.st
        with (self.nc.sbuf_tensor("w2tile", [128, 2 * DM], BF16) as w2tile,
              self.nc.sbuf_tensor("bh_sb", [128, 2 * DM], BF16) as bh_sb,
              self.nc.sbuf_tensor("bsteb", [64, 2 * 4 * DM], BF16) as bsteb,
              self.nc.sbuf_tensor("w2e_sb", [64, 2 * 4 * DM], BF16) as w2e_sb):
            for hi in range(NHT):
                par = hi % 2
                wt = w2tile[:, par * DM:(par + 1) * DM]
                bs = bh_sb[:, par * DM:(par + 1) * DM]
                p.gdma(wt, p.inp["w2t"][hi * 128:(hi + 1) * 128, :])
                for n in range(4):
                    p.mm(p.ps[par][:, :], p.h128_sb[:, :], wt[:, n * 512:(n + 1) * 512])
                    p.scopy(bs[:, n * 512:(n + 1) * 512], p.ps[par][:, :])
                p.dma(p.bh[hi, :, :], bs)
            bh4 = p.bh.rearrange("a p d -> a (p d)")
            w2r = p.w2eff.rearrange("(a b) d -> a b d", b=128)
            for lo0 in range(0, 128, 4):
                par = (lo0 // 4) % 2
                bst = bsteb[:, par * 4 * DM:(par + 1) * 4 * DM]
                w2e = w2e_sb[:, par * 4 * DM:(par + 1) * 4 * DM]
                p.dma(bst, bh4[:, lo0 * DM:(lo0 + 4) * DM])
                for j in range(4):
                    for n in range(4):
                        sl = slice(j * DM + n * 512, j * DM + (n + 1) * 512)
                        p.mm(p.ps[par][:64, :], p.h64_sb[:, :], bst[:, sl])
                        p.act(w2e[:64, sl], p.ps[par][:64, :], AF.Identity,
                              scale=float(1.0 / np.sqrt(8192.0)))
                    p.dma(w2r[:, lo0 + j, :], w2e[:64, j * DM:(j + 1) * DM])

    def phase_l0_attn(self, b):
        p = self
        nc = self.nc
        with (nc.sbuf_tensor(f"xbf{b}", [128, NDT * S], BF16) as xbf,
              nc.sbuf_tensor(f"wbuf{b}", [128, NDT * 512], BF16) as wbuf,
              nc.sbuf_tensor(f"wbuf2{b}", [128, NDT * 512], BF16) as wbuf2,
              nc.sbuf_tensor(f"qg{b}", [128, 4 * S], BF16) as qg,
              nc.sbuf_tensor(f"kg{b}", [128, 4 * S], BF16) as kg,
              nc.sbuf_tensor(f"vtm{b}", [128, 16 * 512], BF16) as vtm,
              nc.sbuf_tensor(f"probs{b}", [128, 2 * 16 * 128], BF16) as probs0,
              nc.sbuf_tensor(f"rec_row{b}", [1, 256], F32) as rec_row,
              nc.sbuf_tensor(f"recb{b}", [128, 256], F32) as recb,
              nc.sbuf_tensor(f"ao_sb{b}", [128, 512], BF16) as ao_sb,
              nc.sbuf_tensor(f"bv_bc{b}", [128, DM], F32) as bv_bc,
              nc.sbuf_tensor(f"rowt{b}", [1, DM], F32) as rowt):
            self.bcast_row_2048(bv_bc, "vb0_row", rowt)
            for dt in range(NDT):
                p.gdma(xbf[:, dt * S:(dt + 1) * S], p.inp["xin"][b, dt * 128:(dt + 1) * 128, :])
            for g in range(4):
                wb_ = [wbuf, wbuf2][(3 * g) % 2]
                p.gdma(wb_[:, :], p.inp["wq0"][g, :, :])
                for hl in range(4):
                    for n in range(4):
                        pp = p.ps[(hl * 4 + n) % 2 * 2][:, :]
                        for dt in range(NDT):
                            p.mm(pp, wb_[:, dt * 512 + hl * 128:dt * 512 + (hl + 1) * 128],
                                 xbf[:, dt * S + n * 512:dt * S + (n + 1) * 512],
                                 start=(dt == 0), stop=(dt == NDT - 1))
                        p.act(qg[:, hl * S + n * 512:hl * S + (n + 1) * 512], pp,
                              AF.Identity, bias=p.qb_sb[:, g * 4 + hl:g * 4 + hl + 1])
                wb_ = [wbuf, wbuf2][(3 * g + 1) % 2]
                p.gdma(wb_[:, :], p.inp["wk0"][g, :, :])
                for hl in range(4):
                    for n in range(4):
                        pp = p.ps[(hl * 4 + n) % 2 * 2][:, :]
                        for dt in range(NDT):
                            p.mm(pp, wb_[:, dt * 512 + hl * 128:dt * 512 + (hl + 1) * 128],
                                 xbf[:, dt * S + n * 512:dt * S + (n + 1) * 512],
                                 start=(dt == 0), stop=(dt == NDT - 1))
                        p.act(kg[:, hl * S + n * 512:hl * S + (n + 1) * 512], pp,
                              AF.Identity, bias=p.kb_sb[:, g * 4 + hl:g * 4 + hl + 1])
                wb_ = [wbuf, wbuf2][(3 * g + 2) % 2]
                p.gdma(wb_[:, :], p.inp["wv0"][g, :, :])
                for tt in range(16):
                    pp = p.ps[tt % 2 * 2][:, :]
                    for dt in range(NDT):
                        p.mm(pp, xbf[:, dt * S + tt * 128:dt * S + (tt + 1) * 128],
                             wb_[:, dt * 512:(dt + 1) * 512],
                             start=(dt == 0), stop=(dt == NDT - 1))
                    p.stt(vtm[:, tt * 512:(tt + 1) * 512], pp, 1.0,
                          bv_bc[:, g * 512:(g + 1) * 512], ALU.mult, ALU.add)
                import os
                if os.environ.get("KDUMPQKV") and g == 0 and b == 1:
                    for hl in range(4):
                        p.dma(p.aoh[hl * 128:(hl + 1) * 128, :], qg[:, hl * S:(hl + 1) * S])
                        p.dma(p.aoh[512 + hl * 128:512 + (hl + 1) * 128, :], kg[:, hl * S:(hl + 1) * S])
                    for tt in range(16):
                        p.dma(p.aoh.rearrange("d s -> (d s)").rearrange(
                              "(a p f) -> a p f", a=32, p=128)[8 + tt // 2, :,
                              (tt % 2) * 512:(tt % 2) * 512 + 512],
                              vtm[:, tt * 512:(tt + 1) * 512])
                for hl in range(4 if not os.environ.get("KDUMPQKV") else 0):
                    h = g * 4 + hl
                    for qc in range(16):
                        probs = probs0[:, (qc % 2) * 2048:(qc % 2 + 1) * 2048]
                        pd = [p.ps_row, p.ps_row2][qc % 2]
                        po = [p.ps[1], p.ps4][qc % 2]

                        def emit_kq(kt, probs=probs, hl=hl, h=h, qc=qc):
                            pp = p.ps[0][:, (kt % 4) * 128:(kt % 4 + 1) * 128]
                            p.mm(pp, kg[:, hl * S + kt * 128:hl * S + (kt + 1) * 128],
                                 qg[:, hl * S + qc * 128:hl * S + (qc + 1) * 128])
                            p.act(probs[:, kt * 128:(kt + 1) * 128], pp, AF.Exp,
                                  bias=p.bias_tab[:, h * 16 + (qc - kt):h * 16 + (qc - kt) + 1],
                                  scale=INV_SQRT_HD)
                            if kt == qc:
                                p.tt(probs[:, kt * 128:(kt + 1) * 128],
                                     probs[:, kt * 128:(kt + 1) * 128],
                                     p.mask_ge_bf[:, :], ALU.min)

                        LOOK = int(os.environ.get("KLOOK", "0"))
                        for kt in range(min(LOOK, qc + 1)):
                            emit_kq(kt)
                        for kt in range(qc + 1):
                            if LOOK == 0:
                                emit_kq(kt)
                            elif kt + LOOK <= qc:
                                emit_kq(kt + LOOK)
                            p.mm(pd[0:1, :128], p.ones_col_bf[:, :],
                                 probs[:, kt * 128:(kt + 1) * 128],
                                 start=(kt == 0), stop=(kt == qc))
                            p.mm(po[:, :128], vtm[:, kt * 512 + hl * 128:kt * 512 + (hl + 1) * 128],
                                 probs[:, kt * 128:(kt + 1) * 128],
                                 start=(kt == 0), stop=(kt == qc))
                        p.recip(rec_row[:, (qc % 2) * 128:(qc % 2) * 128 + 128], pd[0:1, :128])
                        p.mm(p.ps[2][:, (qc % 2) * 128:(qc % 2) * 128 + 128], p.ones_row_f[:, :],
                             rec_row[:, (qc % 2) * 128:(qc % 2) * 128 + 128])
                        p.scopy(recb[:, (qc % 2) * 128:(qc % 2) * 128 + 128],
                                p.ps[2][:, (qc % 2) * 128:(qc % 2) * 128 + 128])
                        p.tt(ao_sb[:, (qc % 4) * 128:(qc % 4 + 1) * 128], po[:, :128],
                             recb[:, (qc % 2) * 128:(qc % 2) * 128 + 128], ALU.mult)
                        if qc % 4 == 3:
                            p.dma(p.aoh[h * 128:(h + 1) * 128, (qc - 3) * 128:(qc + 1) * 128],
                                  ao_sb[:, :])
        self._proj_epilogue(wname="wo0", act_hbm=p.aoh, act_cols=S, b=b,
                            res_in=p.inp["xin"], outf=p.x1f, outb=p.x1b,
                            bias_sb=p.ob_sb, tag=f"l0a{b}")

    def _proj_epilogue(self, wname, act_hbm, act_cols, b, res_in, outf, outb, bias_sb,
                       part_in=None, tokchunk=512, tag=""):
        p = self
        nc = self.nc
        nchunk = S // tokchunk
        with (nc.sbuf_tensor(f"wo_full{tag}", [128, 16 * NDT * 128], BF16) as wo_full,
              nc.sbuf_tensor(f"aobuf{tag}", [128, 2 * NDT * tokchunk], BF16) as aobuf0,
              nc.sbuf_tensor(f"x1pre{tag}", [128, NDT * tokchunk], F32) as x1pre,
              nc.sbuf_tensor(f"xres{tag}", [128, 2 * tokchunk], F32) as xres0,
              nc.sbuf_tensor(f"sqsb{tag}", [128, tokchunk], F32) as sqsb,
              nc.sbuf_tensor(f"rows{tag}", [1, 6 * tokchunk], F32) as rows,
              nc.sbuf_tensor(f"ab_sb{tag}", [128, 2 * tokchunk], F32) as ab_sb,
              nc.sbuf_tensor(f"xo_sb{tag}", [128, 2 * tokchunk], F32) as xo_sb0,
              nc.sbuf_tensor(f"xb_sb{tag}", [128, 2 * tokchunk], BF16) as xb_sb0):
            for ft in range(16):
                p.gdma(wo_full[:, ft * NDT * 128:(ft + 1) * NDT * 128], p.inp[wname][ft, :, :])
            TC = tokchunk
            for n in range(nchunk):
                aobuf = aobuf0[:, (n % 2) * NDT * TC:(n % 2 + 1) * NDT * TC]
                for at in range(NDT):
                    p.dma(aobuf[:, at * TC:(at + 1) * TC],
                          act_hbm[at * 128:(at + 1) * 128, n * TC:(n + 1) * TC])
                for ft in range(NDT):
                    xres = xres0[:, (ft % 2) * TC:(ft % 2 + 1) * TC]
                    p.dma(xres, res_in[b, ft * 128:(ft + 1) * 128, n * TC:(n + 1) * TC])
                    pmm = [p.ps[0], p.ps[3]][ft % 2]
                    for at in range(NDT):
                        p.mm(pmm[:, :TC],
                             wo_full[:, ft * NDT * 128 + at * 128:ft * NDT * 128 + (at + 1) * 128],
                             aobuf[:, at * TC:(at + 1) * TC],
                             start=(at == 0), stop=(at == NDT - 1))
                    p.act(x1pre[:, ft * TC:(ft + 1) * TC], pmm[:, :TC], AF.Identity,
                          bias=bias_sb[:, ft:ft + 1])
                    p.tt(x1pre[:, ft * TC:(ft + 1) * TC], x1pre[:, ft * TC:(ft + 1) * TC],
                         xres, ALU.add)
                    p.mm(p.ps_row[0:1, :TC], p.ones_col_f[:, :], x1pre[:, ft * TC:(ft + 1) * TC],
                         start=(ft == 0), stop=(ft == NDT - 1))
                    p.act(sqsb[:, :], x1pre[:, ft * TC:(ft + 1) * TC], AF.Square)
                    p.mm(p.ps_row2[0:1, :TC], p.ones_col_f[:, :], sqsb[:, :],
                         start=(ft == 0), stop=(ft == NDT - 1))
                MU, E2, VAR, STD, RA, RB = (rows[:, i * TC:(i + 1) * TC] for i in range(6))
                p.ts(MU, p.ps_row[0:1, :TC], 1.0 / DM, None, ALU.mult)
                p.ts(E2, p.ps_row2[0:1, :TC], 1.0 / DM, None, ALU.mult)
                p.tt(VAR, MU, MU, ALU.mult)
                p.tt(VAR, E2, VAR, ALU.subtract)
                p.act(STD, VAR, AF.Sqrt, bias=LN_EPS)
                p.recip(RA, STD)
                p.tt(RB, MU, RA, ALU.mult)
                p.ts(RB, RB, -1.0, None, ALU.mult)
                p.mm(p.ps[1][:, :TC], p.ones_row_f[:, :], RA)
                p.scopy(ab_sb[:, :TC], p.ps[1][:, :TC])
                p.mm(p.ps[1][:, :TC], p.ones_row_f[:, :], RB)
                p.scopy(ab_sb[:, TC:2 * TC], p.ps[1][:, :TC])
                for ft in range(NDT):
                    xo_sb = xo_sb0[:, (ft % 2) * TC:(ft % 2 + 1) * TC]
                    xb_sb = xb_sb0[:, (ft % 2) * TC:(ft % 2 + 1) * TC]
                    p.tt(xo_sb, x1pre[:, ft * TC:(ft + 1) * TC], ab_sb[:, :TC], ALU.mult)
                    p.tt(xo_sb, xo_sb, ab_sb[:, TC:2 * TC], ALU.add)
                    p.dma(outf[b, ft * 128:(ft + 1) * 128, n * TC:(n + 1) * TC], xo_sb)
                    p.scopy(xb_sb, xo_sb)
                    p.dma(outb[ft * 128:(ft + 1) * 128, b * S + n * TC:b * S + (n + 1) * TC],
                          xb_sb)

    def phase_l0_ffn(self):
        p = self
        nc = self.nc
        T = BPC * S
        with (nc.sbuf_tensor("ffa_x", [128, NDT * T], BF16) as xa,
              nc.sbuf_tensor("ffa_w1", [128, 2 * DM], BF16) as w1b,
              nc.sbuf_tensor("ffa_w3", [128, 2 * DM], BF16) as w3b,
              nc.sbuf_tensor("ffa_hid", [128, 2 * T], BF16) as hidsb,
              nc.sbuf_tensor("ffa_silu", [128, 2 * 512], BF16) as silu_sb):
            for dt in range(NDT):
                p.dma(xa[:, dt * T:(dt + 1) * T], p.x1b[dt * 128:(dt + 1) * 128, :])
            for ht in range(NHT):
                if ht < 16:
                    p.gdma(p.wgate_b[ht, :, :], p.inp["wgate_l"][ht, :, :])
                elif ht < 32:
                    p.gdma(p.wup_b[ht - 16, :, :], p.inp["wup_l"][ht - 16, :, :])
                elif ht < 48:
                    i = ht - 32
                    p.gdma(p.wdown_b[i * 512:(i + 1) * 512, :],
                           p.inp["wdown_l"][i * 512:(i + 1) * 512, :])
                par = ht % 2
                w1s = w1b[:, par * DM:(par + 1) * DM]
                w3s = w3b[:, par * DM:(par + 1) * DM]
                p.gdma(w1s, p.inp["w1t"][ht, :, :])
                p.gdma(w3s, p.inp["w3t"][ht, :, :])
                for n in range(T // 512):
                    pg = p.ps[(n % 2) * 2][:, :]
                    pl = p.ps[(n % 2) * 2 + 1][:, :]
                    sl = silu_sb[:, (n % 2) * 512:(n % 2 + 1) * 512]
                    for dt in range(NDT):
                        p.mm(pg, w1s[:, dt * 128:(dt + 1) * 128],
                             xa[:, dt * T + n * 512:dt * T + (n + 1) * 512],
                             start=(dt == 0), stop=(dt == NDT - 1))
                    p.act(sl, pg, AF.Silu)
                    for dt in range(NDT):
                        p.mm(pl, w3s[:, dt * 128:(dt + 1) * 128],
                             xa[:, dt * T + n * 512:dt * T + (n + 1) * 512],
                             start=(dt == 0), stop=(dt == NDT - 1))
                    p.tt(hidsb[:, par * T + n * 512:par * T + (n + 1) * 512], pl, sl, ALU.mult)
                p.dma(p.hid0[ht, :, :], hidsb[:, par * T:(par + 1) * T])
        self._down_proj(p.hid0, p.w2eff, p.part0, p.x1f, p.x2f, p.x2b, tag="l0f")

    def _down_proj(self, hid, wsrc, part, resf, outf, outb, tag, wcast=False):
        p = self
        nc = self.nc
        T = BPC * S
        TC = 256
        for half in range(2):
            with ExitStack() as stk:
                E = stk.enter_context
                wsb = E(nc.sbuf_tensor(f"dp_w{tag}{half}", [128, 32 * DM], BF16))
                hch = E(nc.sbuf_tensor(f"dp_h{tag}{half}", [128, 2 * 32 * TC], BF16))
                aux = E(nc.sbuf_tensor(f"dp_aux{tag}{half}", [128, 2 * TC], F32))
                if half == 1:
                    raux = E(nc.sbuf_tensor(f"dp_raux{tag}{half}", [128, 2 * TC], F32))
                    pre = E(nc.sbuf_tensor(f"dp_pre{tag}{half}", [128, NDT * TC], F32))
                    sq_sb = E(nc.sbuf_tensor(f"dp_sq{tag}{half}", [128, TC], F32))
                    rows = E(nc.sbuf_tensor(f"dp_rows{tag}{half}", [1, 6 * TC], F32))
                    ab_sb = E(nc.sbuf_tensor(f"dp_ab{tag}{half}", [128, 2 * TC], F32))
                    xo_sb = E(nc.sbuf_tensor(f"dp_xo{tag}{half}", [128, TC], F32))
                    xb_sb = E(nc.sbuf_tensor(f"dp_xb{tag}{half}", [128, TC], BF16))
                for i in range(32):
                    ht = half * 32 + i
                    if wcast:
                        p.gdma(wsb[:, i * DM:(i + 1) * DM], wsrc[ht * 128:(ht + 1) * 128, :])
                    else:
                        p.dma(wsb[:, i * DM:(i + 1) * DM], wsrc[ht * 128:(ht + 1) * 128, :])
                for c in range(T // TC):
                    b = (c * TC) // S
                    nin = (c * TC) % S
                    hb = hch[:, (c % 2) * 32 * TC:(c % 2 + 1) * 32 * TC]
                    for i in range(32):
                        p.dma(hb[:, i * TC:(i + 1) * TC],
                              hid[half * 32 + i, :, c * TC:(c + 1) * TC])
                    pbank = (p.ps[0], p.ps[1], p.ps[3], p.ps4)
                    for ft in range(NDT):
                        ax = aux[:, (ft % 2) * TC:(ft % 2 + 1) * TC]
                        if half == 1:
                            p.dma(ax, part[ft, :, c * TC:(c + 1) * TC])
                            rx = raux[:, (ft % 2) * TC:(ft % 2 + 1) * TC]
                            p.dma(rx, resf[b, ft * 128:(ft + 1) * 128, nin:nin + TC])
                        for i in range(32):
                            p.mm(pbank[ft % 4][:, :TC],
                                 wsb[:, i * DM + ft * 128:i * DM + (ft + 1) * 128],
                                 hb[:, i * TC:(i + 1) * TC],
                                 start=(i == 0), stop=(i == 31))
                        if half == 0:
                            p.vcopy(ax, pbank[ft % 4][:, :TC])
                            p.dma(part[ft, :, c * TC:(c + 1) * TC], ax)
                        else:
                            p.tt(pre[:, ft * TC:(ft + 1) * TC], pbank[ft % 4][:, :TC],
                                 ax, ALU.add)
                            p.tt(pre[:, ft * TC:(ft + 1) * TC], pre[:, ft * TC:(ft + 1) * TC],
                                 rx, ALU.add)
                    if half == 1:
                        self._ln_store(pre, TC, sq_sb, rows, ab_sb, xo_sb, xb_sb,
                                       outf, outb, b, nin)

    def _ln_store(self, pre, TC, sq_sb, rows, ab_sb, xo_sb, xb_sb, outf, outb, b, nin):
        p = self
        for ft in range(NDT):
            p.mm(p.ps_row[0:1, :TC], p.ones_col_f[:, :], pre[:, ft * TC:(ft + 1) * TC],
                 start=(ft == 0), stop=(ft == NDT - 1))
            p.act(sq_sb[:, :], pre[:, ft * TC:(ft + 1) * TC], AF.Square)
            p.mm(p.ps_row2[0:1, :TC], p.ones_col_f[:, :], sq_sb[:, :],
                 start=(ft == 0), stop=(ft == NDT - 1))
        MU, E2, VAR, STD, RA, RB = (rows[:, i * TC:(i + 1) * TC] for i in range(6))
        p.ts(MU, p.ps_row[0:1, :TC], 1.0 / DM, None, ALU.mult)
        p.ts(E2, p.ps_row2[0:1, :TC], 1.0 / DM, None, ALU.mult)
        p.tt(VAR, MU, MU, ALU.mult)
        p.tt(VAR, E2, VAR, ALU.subtract)
        p.act(STD, VAR, AF.Sqrt, bias=LN_EPS)
        p.recip(RA, STD)
        p.tt(RB, MU, RA, ALU.mult)
        p.ts(RB, RB, -1.0, None, ALU.mult)
        p.mm(p.ps[2][:, :TC], p.ones_row_f[:, :], RA)
        p.scopy(ab_sb[:, :TC], p.ps[2][:, :TC])
        p.mm(p.ps[2][:, :TC], p.ones_row_f[:, :], RB)
        p.scopy(ab_sb[:, TC:2 * TC], p.ps[2][:, :TC])
        for ft in range(NDT):
            p.tt(xo_sb[:, :], pre[:, ft * TC:(ft + 1) * TC], ab_sb[:, :TC], ALU.mult)
            p.tt(xo_sb[:, :], xo_sb[:, :], ab_sb[:, TC:2 * TC], ALU.add)
            p.dma(outf[b, ft * 128:(ft + 1) * 128, nin:nin + TC], xo_sb[:, :])
            if outb is not None:
                p.scopy(xb_sb[:, :], xo_sb[:, :])
                p.dma(outb[ft * 128:(ft + 1) * 128, b * S + nin:b * S + nin + TC], xb_sb[:, :])

    def phase_l1_proj(self):
        p = self
        nc = self.nc
        T = BPC * S
        with (nc.sbuf_tensor("p3_x", [128, NDT * T], BF16) as xa,
              nc.sbuf_tensor("p3_w", [128, 2 * NDT * 512], BF16) as wb0,
              nc.sbuf_tensor("p3_o", [128, 512], BF16) as ob,
              nc.sbuf_tensor("p3_o2", [128, 512], BF16) as ob2,
              nc.sbuf_tensor("p3_ab", [128, 2 * 256], BF16) as wab,
              nc.sbuf_tensor("p3_sig", [16, T], F32) as sig,
              nc.sbuf_tensor("p3_bab", [16, 2], F32) as bab):
            for dt in range(NDT):
                p.dma(xa[:, dt * T:(dt + 1) * T], p.x2b[dt * 128:(dt + 1) * 128, :])
            obuf = [ob, ob2]
            for iw, (wname, dst) in enumerate((("wq1", p.xwq), ("wk1", p.xwk), ("wv1", p.xwv))):
                for dot in range(16):
                    par = (iw * 16 + dot) % 2
                    wb = wb0[:, par * NDT * 512:par * NDT * 512 + NDT * 128]
                    p.gdma(wb, p.inp[wname][dot, :, :])
                    for n in range(T // 512):
                        pp = p.ps[(n % 2) * 2][:, :]
                        for dt in range(NDT):
                            p.mm(pp, wb[:, dt * 128:(dt + 1) * 128],
                                 xa[:, dt * T + n * 512:dt * T + (n + 1) * 512],
                                 start=(dt == 0), stop=(dt == NDT - 1))
                        p.scopy(obuf[n % 2][:, :], pp)
                        p.dma(dst[dot * 128:(dot + 1) * 128, n * 512:(n + 1) * 512], obuf[n % 2][:, :])
            for fc in range(4):
                par = fc % 2
                wb = wb0[:, par * NDT * 512:(par + 1) * NDT * 512]
                for dt in range(NDT):
                    p.gdma(wb[:, dt * 512:(dt + 1) * 512],
                           p.inp["wg1"][dt * 128:(dt + 1) * 128, fc * 512:(fc + 1) * 512])
                for tt in range(T // 128):
                    pp = p.ps[(tt % 2) * 2][:, :]
                    for dt in range(NDT):
                        p.mm(pp, xa[:, dt * T + tt * 128:dt * T + (tt + 1) * 128],
                             wb[:, dt * 512:(dt + 1) * 512],
                             start=(dt == 0), stop=(dt == NDT - 1))
                    p.scopy(obuf[tt % 2][:, :], pp)
                    p.dma(p.xwg_tm[tt * 128:(tt + 1) * 128, fc * 512:(fc + 1) * 512], obuf[tt % 2][:, :])
            p.gdma(wab[:, :256], p.inp["wa1"][:, :])
            p.gdma(wab[:, 256:], p.inp["wb1"][:, :])
            p.dma(bab[:, 0:1], p.inp["ba1"][:, :])
            p.dma(bab[:, 1:2], p.inp["bb1"][:, :])
            for j, (coff, boff) in enumerate(((0, 0), (256, 1))):
                for n in range(T // 512):
                    for dt in range(NDT):
                        p.mm(p.ps[0][:16, :], wab[:, coff + dt * 16:coff + (dt + 1) * 16],
                             xa[:, dt * T + n * 512:dt * T + (n + 1) * 512],
                             start=(dt == 0), stop=(dt == NDT - 1))
                    p.act(sig[:, n * 512:(n + 1) * 512], p.ps[0][:16, :],
                          AF.Sigmoid, bias=bab[:, boff:boff + 1])
                p.dma(p.sigh[:, j * T:(j + 1) * T], sig[:, :])

    def phase_l1_delta(self, b):
        p = self
        nc = self.nc
        QT = 512
        with ExitStack() as stk:
            E = stk.enter_context
            qhat = E(nc.sbuf_tensor(f"d_q{b}", [128, 16 * QT], BF16))
            khat = E(nc.sbuf_tensor(f"d_k{b}", [128, 16 * QT], BF16))
            vhat = E(nc.sbuf_tensor(f"d_v{b}", [128, 16 * QT], BF16))
            xwb = E(nc.sbuf_tensor(f"d_xw{b}", [128, 16 * (QT + 3)], BF16))
            dgt = E(nc.sbuf_tensor(f"d_dgt{b}", [128, 2 * 4 * 128], BF16))
            cw = E(nc.sbuf_tensor(f"d_cw{b}", [128, 3 * 64], F32))
            cb = E(nc.sbuf_tensor(f"d_cb{b}", [128, 3 * 16], F32))
            sigq = E(nc.sbuf_tensor(f"d_sig{b}", [16, 2 * QT], F32))
            scal = E(nc.sbuf_tensor(f"d_scal{b}", [16, 8 * QT], F32))
            flat = E(nc.sbuf_tensor(f"d_flat{b}", [1, 16], F32))
            cols = E(nc.sbuf_tensor(f"d_cols{b}", [128, 6 * 16], F32))
            zf = E(nc.sbuf_tensor(f"d_zf{b}", [128, 16 * 128], F32))
            zb = E(nc.sbuf_tensor(f"d_zb{b}", [128, 16 * 128], BF16))
            ktm = E(nc.sbuf_tensor(f"d_ktm{b}", [128, 16 * 128], BF16))
            vtm = E(nc.sbuf_tensor(f"d_vtm{b}", [128, 16 * 128], BF16))
            expm = E(nc.sbuf_tensor(f"d_expm{b}", [128, 2 * 128], F32))
            mneg = E(nc.sbuf_tensor(f"d_mneg{b}", [128, 2 * 128], BF16))
            rcur = E(nc.sbuf_tensor(f"d_rcur{b}", [128, 2 * 256], BF16))
            r0sb = E(nc.sbuf_tensor(f"d_r0{b}", [128, 2 * 128], BF16))
            otm = E(nc.sbuf_tensor(f"d_otm{b}", [128, 16 * 128], F32))
            otmp = E(nc.sbuf_tensor(f"d_otmp{b}", [128, 2 * 128], F32))
            osq = E(nc.sbuf_tensor(f"d_osq{b}", [128, 16 * 128], F32))
            st16 = E(nc.sbuf_tensor(f"d_st16{b}", [128, 8 * 16], F32))
            gsb = E(nc.sbuf_tensor(f"d_gsb{b}", [128, DM], BF16))
            gss = E(nc.sbuf_tensor(f"d_gss{b}", [128, DM], BF16))
            ogbf = E(nc.sbuf_tensor(f"d_og{b}", [128, DM], BF16))
            ofm = E(nc.sbuf_tensor(f"d_ofm{b}", [128, 128], BF16))
            lnwb = E(nc.sbuf_tensor(f"d_lnw{b}", [128, DM], F32))
            lnbb = E(nc.sbuf_tensor(f"d_lnb{b}", [128, DM], F32))
            rowt = E(nc.sbuf_tensor(f"d_rowt{b}", [1, DM], F32))
            self.bcast_row_2048(lnwb, "lnw_rep", rowt)
            self.bcast_row_2048(lnbb, "lnb_rep", rowt)
            p.dma(cw[:, 0:64], p.inp["qcw"][:, :])
            p.dma(cw[:, 64:128], p.inp["kcw"][:, :])
            p.dma(cw[:, 128:192], p.inp["vcw"][:, :])
            p.dma(cb[:, 0:16], p.inp["qcb"][:, :])
            p.dma(cb[:, 16:32], p.inp["kcb"][:, :])
            p.dma(cb[:, 32:48], p.inp["vcb"][:, :])
            p.memset(zf[:, :], 0.0)
            p.memset(zb[:, :], 0.0)
            LNC2 = float(2.0 * np.log(DSC))
            for q in range(4):
                t0 = b * S + q * QT
                for ti, (srch, dst, woff, boff) in enumerate(
                        ((p.xwq, qhat, 0, 0), (p.xwk, khat, 64, 16), (p.xwv, vhat, 128, 32))):
                    if q == 0:
                        p.memset(xwb[:, :].rearrange("p (d c) -> p d c", d=16)[:, :, 0:3], 0.0)
                        for dt in range(NDT):
                            p.dma(xwb[:, dt * (QT + 3) + 3:(dt + 1) * (QT + 3)],
                                  srch[dt * 128:(dt + 1) * 128, t0:t0 + QT])
                    else:
                        for dt in range(NDT):
                            p.dma(xwb[:, dt * (QT + 3):(dt + 1) * (QT + 3)],
                                  srch[dt * 128:(dt + 1) * 128, t0 - 3:t0 + QT])
                    for dt in range(NDT):
                        base = dt * (QT + 3)
                        pp = [p.ps[0], p.ps[2]][dt % 2][:, :]
                        dg = dgt[:, (dt % 2) * 512:(dt % 2) * 512 + 512]
                        for j in range(4):
                            p.ts(dg[:, j * 128:(j + 1) * 128], p.ident_bf[:, :],
                                 cw[:, woff + dt * 4 + j:woff + dt * 4 + j + 1],
                                 None, ALU.mult)
                        for j in range(4):
                            p.mm(pp, dg[:, j * 128:(j + 1) * 128],
                                 xwb[:, base + j:base + j + QT],
                                 start=(j == 0), stop=(j == 3))
                        p.act(dst[:, dt * QT:(dt + 1) * QT], pp, AF.Silu,
                              bias=cb[:, boff + dt:boff + dt + 1])
                if os.environ.get("KDUMPCONV") and b == 1 and q == 0:
                    for dt in range(16):
                        p.dma(p.aoh[dt * 128:(dt + 1) * 128, 0:512], qhat[:, dt * QT:(dt + 1) * QT])
                        p.dma(p.aoh[dt * 128:(dt + 1) * 128, 512:1024], khat[:, dt * QT:(dt + 1) * QT])
                        p.dma(p.aoh[dt * 128:(dt + 1) * 128, 1024:1536], vhat[:, dt * QT:(dt + 1) * QT])
                p.dma(sigq[:, 0:QT], p.sigh[:, t0:t0 + QT])
                p.dma(sigq[:, QT:2 * QT], p.sigh[:, BPC * S + t0:BPC * S + t0 + QT])
                A_, B_ = sigq[:, 0:QT], sigq[:, QT:2 * QT]
                LNA = scal[:, 0 * QT:1 * QT]; LL = scal[:, 1 * QT:2 * QT]
                LP = scal[:, 2 * QT:3 * QT]; LB2 = scal[:, 3 * QT:4 * QT]
                NL = scal[:, 4 * QT:5 * QT]; S1 = scal[:, 5 * QT:6 * QT]
                BC = scal[:, 6 * QT:7 * QT]; EL = scal[:, 7 * QT:8 * QT]
                p.act(LNA, A_, AF.Ln)
                p.vcopy(LL, LNA)
                cur, oth = LL, LNA
                for sft in (1, 2, 4, 8, 16, 32, 64):
                    c3_cur = cur.rearrange("p (c i) -> p c i", c=4)
                    c3_oth = oth.rearrange("p (c i) -> p c i", c=4)
                    p.vcopy(c3_oth[:, :, 0:sft], c3_cur[:, :, 0:sft])
                    p.tt(c3_oth[:, :, sft:CK], c3_cur[:, :, sft:CK],
                         c3_cur[:, :, 0:CK - sft], ALU.add)
                    cur, oth = oth, cur
                if cur is not LL:
                    p.vcopy(LL, cur)
                p.act(LNA, A_, AF.Ln)
                p.tt(LP, LL, LNA, ALU.subtract)
                p.act(LB2, B_, AF.Ln)
                p.tt(LB2, LP, LB2, ALU.add)
                p.ts(LB2, LB2, 1.0, LNC2, ALU.mult, ALU.add)
                p.ts(NL, LL, -1.0, None, ALU.mult)
                p.act(S1, LP, AF.Exp)
                p.tt(S1, B_, S1, ALU.mult)
                p.ts(S1, S1, float(-DSC * DSC), None, ALU.mult)
                p.ts(BC, B_, DSC, None, ALU.mult)
                p.act(EL, LL, AF.Exp)
                for ck4 in range(4 if not os.environ.get("KDUMPCONV") else 0):
                    cg = q * 4 + ck4
                    c0 = cg * CK
                    csl = slice(ck4 * CK, (ck4 + 1) * CK)
                    for k, qty in enumerate((NL, S1, BC, EL)):
                        p.tr(p.ps[3][:, :16], qty[:, csl], f32=True)
                        p.scopy(cols[:, k * 16:(k + 1) * 16], p.ps[3][:, :16])
                    p.tr(p.ps[3][0:1, 0:16], LL[:, ck4 * CK + CK - 1:ck4 * CK + CK], f32=True)
                    p.scopy(flat[:, 0:16], p.ps[3][0:1, 0:16])
                    p.mm(p.ps[3][:, :16], p.ones_row_f[:, :], flat[:, 0:16])
                    p.scopy(cols[:, 64:80], p.ps[3][:, :16])
                    p.tt(cols[:, 80:96], cols[:, 64:80], cols[:, 0:16], ALU.add)
                    p.act(cols[:, 80:96], cols[:, 80:96], AF.Exp)
                    p.act(cols[:, 64:80], cols[:, 64:80], AF.Exp)
                    for h in range(16):
                        hq = h * QT + ck4 * CK
                        hsl = slice(hq, hq + CK)
                        zsl = slice(h * 128, (h + 1) * 128)
                        par = h % 2
                        pA = [p.ps[0], p.ps[3]][par]
                        pB = [p.ps[1], p.ps4][par]
                        pC = [p.ps[2], p.ps_row][par]
                        pT = p.ps_bf[:, (par * 512):(par * 512) + 128]
                        exm = expm[:, par * 128:(par + 1) * 128]
                        mng = mneg[:, par * 128:(par + 1) * 128]
                        r0s = r0sb[:, par * 128:(par + 1) * 128]
                        rcu = rcur[:, par * 256:(par + 1) * 256]
                        otp = otmp[:, par * 128:(par + 1) * 128]
                        p.tr(pT, khat[:, hsl])
                        p.ts(ktm[:, zsl], pT, cols[:, 80 + h:80 + h + 1],
                             None, ALU.mult)
                        p.tr(pT, vhat[:, hsl])
                        p.ts(vtm[:, zsl], pT, cols[:, 32 + h:32 + h + 1],
                             None, ALU.mult)
                        p.mm(pA[:, :128], khat[:, hsl], khat[:, hsl])
                        p.mm(pB[:, :128], p.sel16[:, h * 128:(h + 1) * 128],
                             LB2[:, csl])
                        p.act(exm, pB[:, :128], AF.Exp, bias=cols[:, h:h + 1])
                        p.tt(exm, exm, p.mask_gt_f[:, :], ALU.min)
                        p.stt(mng, pA[:, :128], -1.0, exm, ALU.mult, ALU.mult)
                        p.mm(pC[:, :128], khat[:, hsl], zb[:, zsl])
                        p.stt(r0s, pC[:, :128], cols[:, 16 + h:16 + h + 1],
                              vtm[:, zsl], ALU.mult, ALU.add)
                        p.vcopy(rcu[:, 0:128], r0s)
                        for it in range(3):
                            cu = rcu[:, (it % 2) * 128:(it % 2) * 128 + 128]
                            nx = rcu[:, ((it + 1) % 2) * 128:((it + 1) % 2) * 128 + 128]
                            p.mm(pC[:, :128], mng, cu)
                            p.tt(nx, pC[:, :128], r0s, ALU.add)
                        Y = rcu[:, 128:256]
                        p.mm(pC[:, :128], qhat[:, hsl], zb[:, zsl])
                        p.mm(pA[:, :128], khat[:, hsl], qhat[:, hsl])
                        p.mm(pB[:, :128], p.sel16[:, h * 128:(h + 1) * 128],
                             LL[:, csl])
                        p.act(exm, pB[:, :128], AF.Exp, bias=cols[:, h:h + 1])
                        p.tt(exm, exm, p.mask_ge_f[:, :], ALU.min)
                        p.tt(mng, pA[:, :128], exm, ALU.mult)
                        p.mm(pA[:, :128], mng, Y)
                        p.ts(otp, pC[:, :128], cols[:, 48 + h:48 + h + 1],
                             None, ALU.mult)
                        p.tt(otm[:, zsl], otp, pA[:, :128], ALU.add)
                        p.mm(pC[:, :128], ktm[:, zsl], Y)
                        p.stt(zf[:, zsl], zf[:, zsl], cols[:, 64 + h:64 + h + 1],
                              pC[:, :128], ALU.mult, ALU.add)
                        p.scopy(zb[:, zsl], zf[:, zsl])
                    if os.environ.get("KDUMPCHUNK") and b == 1 and cg == 0:
                        p.gdma(p.aoh[0:128, 0:2048], otm[:, :])
                        p.gdma(p.aoh[128:256, 0:96], cols[:, :])
                        p.gdma(p.aoh[300:301, 0:16], flat[:, 0:16])
                        continue
                    o3 = otm[:, :].rearrange("p (h v) -> p h v", h=16)
                    p.rsum(st16[:, 0:16], o3)
                    p.ts(st16[:, 0:16], st16[:, 0:16], 1.0 / 128.0, None, ALU.mult)
                    p.act(osq[:, :], otm[:, :], AF.Square)
                    p.rsum(st16[:, 16:32], osq[:, :].rearrange("p (h v) -> p h v", h=16))
                    p.ts(st16[:, 16:32], st16[:, 16:32], 1.0 / 128.0, None, ALU.mult)
                    p.tt(st16[:, 32:48], st16[:, 0:16], st16[:, 0:16], ALU.mult)
                    p.tt(st16[:, 32:48], st16[:, 16:32], st16[:, 32:48], ALU.subtract)
                    p.act(st16[:, 32:48], st16[:, 32:48], AF.Sqrt, bias=LN_EPS)
                    p.recip(st16[:, 48:64], st16[:, 32:48])
                    p.ts(st16[:, 0:16], st16[:, 0:16], -1.0, None, ALU.mult)
                    for h in range(16):
                        zsl = slice(h * 128, (h + 1) * 128)
                        p.ts(otm[:, zsl], otm[:, zsl], st16[:, h:h + 1],
                             st16[:, 48 + h:48 + h + 1], ALU.add, ALU.mult)
                    p.tt(otm[:, :], otm[:, :], lnwb[:, :], ALU.mult)
                    p.tt(otm[:, :], otm[:, :], lnbb[:, :], ALU.add)
                    p.dma(gsb[:, :], p.xwg_tm[b * S + c0:b * S + c0 + CK, :])
                    p.act(gss[:, :], gsb[:, :], AF.Sigmoid)
                    p.tt(ogbf[:, :], otm[:, :], gss[:, :], ALU.mult)
                    for h in range(16):
                        p.tr(p.ps_bf[:, :128], ogbf[:, h * 128:(h + 1) * 128])
                        p.scopy(ofm[:, :], p.ps_bf[:, :128])
                        p.dma(p.aoh[h * 128:(h + 1) * 128, c0:c0 + CK], ofm[:, :])
        with self.nc.sbuf_tensor(f"zb0_{b}", [128, 16], F32) as zb0:
            p.memset(zb0[:, :], 0.0)
            self._proj_epilogue(wname="wo1", act_hbm=p.aoh, act_cols=S, b=b,
                                res_in=p.x2f, outf=p.x3f, outb=p.x3b,
                                bias_sb=zb0, tag=f"l1a{b}")

    def phase_l1_ffn(self):
        p = self
        nc = self.nc
        T = BPC * S
        with ExitStack() as stk:
            E = stk.enter_context
            xa = E(nc.sbuf_tensor("p5_x", [128, NDT * T], BF16))
            wgb = E(nc.sbuf_tensor("p5_wg", [128, NDT * 512], BF16))
            wub = E(nc.sbuf_tensor("p5_wu", [128, NDT * 512], BF16))
            slf = E(nc.sbuf_tensor("p5_sl", [128, 2 * 512], F32))
            hidf = E(nc.sbuf_tensor("p5_h", [128, 2 * 512], F32))
            absf = E(nc.sbuf_tensor("p5_a", [128, 2 * 512], F32))
            mm4 = E(nc.sbuf_tensor("p5_l", [128, 4 * 128], F32))
            tau = E(nc.sbuf_tensor("p5_t", [128, 128], F32))
            mk = E(nc.sbuf_tensor("p5_m", [128, 512], F32))
            hsp = E(nc.sbuf_tensor("p5_s", [128, 2 * 512], BF16))
            stag = E(nc.sbuf_tensor("p5_st", [128, 4 * 1024], BF16))
            for dt in range(NDT):
                p.dma(xa[:, dt * T:(dt + 1) * T], p.x3b[dt * 128:(dt + 1) * 128, :])

            def emit_tr(hb0, tt0):
                hs0 = hsp[:, (tt0 % 2) * 512:(tt0 % 2) * 512 + 512]
                for j in range(4):
                    p.tr(p.ps_bf[:, :128], hs0[:, j * 128:(j + 1) * 128])
                    p.scopy(stag[:, j * 1024 + (tt0 % 8) * 128:j * 1024 + (tt0 % 8 + 1) * 128],
                            p.ps_bf[:, :128])
                if tt0 % 8 == 7:
                    for j in range(4):
                        p.dma(p.hid1[hb0 * 4 + j, :, (tt0 - 7) * 128:(tt0 + 1) * 128],
                              stag[:, j * 1024:(j + 1) * 1024])

            prev = None
            for hb in range(16):
                p.dma(wgb[:, :], p.wgate_b[hb, :, :])
                p.dma(wub[:, :], p.wup_b[hb, :, :])
                for tt in range(T // 128):
                    s = tt % 2
                    pg = p.ps[s][:, :]
                    pu = p.ps[2 + s][:, :]
                    sl = slf[:, s * 512:(s + 1) * 512]
                    hf = hidf[:, s * 512:(s + 1) * 512]
                    ab = absf[:, s * 512:(s + 1) * 512]
                    hs = hsp[:, s * 512:(s + 1) * 512]
                    for dt in range(NDT):
                        p.mm(pg, xa[:, dt * T + tt * 128:dt * T + (tt + 1) * 128],
                             wgb[:, dt * 512:(dt + 1) * 512],
                             start=(dt == 0), stop=(dt == NDT - 1))
                    p.act(sl, pg, AF.Silu)
                    for dt in range(NDT):
                        p.mm(pu, xa[:, dt * T + tt * 128:dt * T + (tt + 1) * 128],
                             wub[:, dt * 512:(dt + 1) * 512],
                             start=(dt == 0), stop=(dt == NDT - 1))
                    if prev is not None:
                        emit_tr(*prev)
                    prev = (hb, tt)
                    p.tt(hf, pu, sl, ALU.mult)
                    p.act(ab, hf, AF.Abs)
                    A = lambda j: ab[:, j * 128:(j + 1) * 128]
                    p.tt(mm4[:, 0:128], A(0), A(1), ALU.max)
                    p.tt(mm4[:, 128:256], A(2), A(3), ALU.max)
                    p.tt(mm4[:, 256:384], A(0), A(1), ALU.min)
                    p.tt(mm4[:, 384:512], A(2), A(3), ALU.min)
                    p.tt(tau[:, :], mm4[:, 0:128], mm4[:, 128:256], ALU.min)
                    p.tt(mm4[:, 256:384], mm4[:, 256:384], mm4[:, 384:512], ALU.max)
                    p.tt(tau[:, :], tau[:, :], mm4[:, 256:384], ALU.max)
                    for j in range(4):
                        p.tt(mk[:, j * 128:(j + 1) * 128], A(j), tau[:, :], ALU.is_ge)
                    p.tt(hs, hf, mk[:, :], ALU.mult)
            emit_tr(*prev)
        self._down_proj(p.hid1, p.wdown_b, p.part1, p.x3f, self.out, None,
                        tag="l1f", wcast=False)

    def finish(self):
        p = self
        if self.dbg is not None:
            src = {"x1f": p.x1f, "x2f": p.x2f, "x3f": p.x3f, "w2eff": p.w2eff, "sigh": p.sigh,
                   "aoh": p.aoh, "x1b": p.x1b, "hid0": p.hid0, "hid1": p.hid1,
                   "xwq": p.xwq, "xwk": p.xwk, "xwv": p.xwv,
                   "xwg_tm": p.xwg_tm, "part0": p.part0}[self.dbg]
            flat_s = src.rearrange(" ".join(f"a{i}" for i in range(len(src.shape))) + " -> (" +
                                   " ".join(f"a{i}" for i in range(len(src.shape))) + ")") \
                if False else src
            p.dma(self.dbg_out[...], src[...])
        if self.mode == "chain":
            total = self.cnt
            self.ops["sync"].append(lambda h, n=total: h.wait_ge(self.chain, n))
        else:
            tots = [(self.esem[e], self.emitted[e]) for e in self.ops if self.emitted[e] > 0]
            tots += [(self.esem[k], 16 * u) for k, u in self.dslot_use.items() if u > 0]
            def fin(h, tots=tuple(tots)):
                for smm, v in tots:
                    h.wait_ge(smm, v)
            self.ops["sync"].append(fin)
        with self.nc.Block() as block:
            for eng in ("tensor", "vector", "scalar", "gpsimd", "sync"):
                def mk(e):
                    def f(h):
                        for op in self.ops[e]:
                            op(h)
                    return f
                getattr(block, eng)(mk(eng))
        self.st.close()
        return self.nc


def build_program(upto=99, dbg=None, mode=None):
    import os
    if mode is None:
        mode = os.environ.get("KMODE", "auto")
    relax = set(os.environ.get("KRELAX", "c,w,1,2,3,4,5").split(","))
    p = Prog(upto=upto, dbg=dbg, mode=mode)
    p.alloc_consts()
    p.kbar_on = "c" not in relax
    p.phase_consts()
    if upto >= 0:
        p.kbar_on = "w" not in relax
        p.phase_w2eff()
    if upto >= 1:
        p.kbar_on = "1" not in relax
        for b in range(BPC):
            p.phase_l0_attn(b)
    if upto >= 2:
        p.kbar_on = "2" not in relax
        p.phase_l0_ffn()
    if upto >= 3:
        p.kbar_on = "3" not in relax
        p.phase_l1_proj()
    if upto >= 4:
        p.kbar_on = "4" not in relax
        for b in range(BPC):
            p.phase_l1_delta(b)
    if upto >= 5:
        p.kbar_on = "5" not in relax
        p.phase_l1_ffn()
    p.kbar_on = False
    nc = p.finish()
    return nc


def host_layout(inputs):
    f32 = np.float32
    qkv_w = np.asarray(inputs["L0_attn_qkv_w"], f32)
    qkv_b = np.asarray(inputs["L0_attn_qkv_b"], f32)
    d = {}
    d["wq0"] = relay_lhsT(np.ascontiguousarray(qkv_w[0:DM].T), NDT, 512)
    d["wk0"] = relay_lhsT(np.ascontiguousarray(qkv_w[DM:2 * DM].T), NDT, 512)
    d["wv0"] = relay_lhsT(np.ascontiguousarray(qkv_w[2 * DM:3 * DM].T), NDT, 512)
    d["qb0"] = np.ascontiguousarray(qkv_b[0:DM].reshape(16, 128).T)
    d["kb0"] = np.ascontiguousarray(qkv_b[DM:2 * DM].reshape(16, 128).T)
    d["vb0_row"] = np.ascontiguousarray(qkv_b[2 * DM:3 * DM].reshape(1, DM))
    d["wo0"] = relay_lhsT(np.ascontiguousarray(np.asarray(inputs["L0_attn_out_w"], f32).T), NDT, 128)
    d["ob0"] = np.ascontiguousarray(np.asarray(inputs["L0_attn_out_b"], f32).reshape(16, 128).T)
    d["alibi_rep"] = np.ascontiguousarray(
        np.repeat(np.asarray(inputs["L0_attn_alibi_slopes"], f32), 16).reshape(1, 256))
    W1 = np.asarray(inputs["L0_ffn_W1"], f32); W3 = np.asarray(inputs["L0_ffn_W3"], f32)
    d["w1t"] = np.ascontiguousarray(
        W1.reshape(NHT, 128, NDT, 128).transpose(0, 3, 2, 1).reshape(NHT, 128, DM))
    d["w3t"] = np.ascontiguousarray(
        W3.reshape(NHT, 128, NDT, 128).transpose(0, 3, 2, 1).reshape(NHT, 128, DM))
    d["w2t"] = np.ascontiguousarray(np.asarray(inputs["L0_ffn_W2"], f32).T)
    d["h128"] = hada(128).astype(BF)
    d["h64"] = hada(64).astype(BF)
    d["identity"] = np.eye(128, dtype=np.float32).astype(BF)
    d["identity_f"] = np.eye(128, dtype=np.float32)
    d["wq1"] = relay_lhsT(np.asarray(inputs["L1_attn_Wq"], f32), NDT, 128)
    d["wk1"] = relay_lhsT(np.asarray(inputs["L1_attn_Wk"], f32), NDT, 128)
    d["wv1"] = relay_lhsT(np.asarray(inputs["L1_attn_Wv"], f32), NDT, 128)
    d["wg1"] = np.ascontiguousarray(np.asarray(inputs["L1_attn_Wg"], f32))
    d["wa1"] = np.ascontiguousarray(
        np.asarray(inputs["L1_attn_Wa"], f32).reshape(16, 128, 16).transpose(1, 0, 2).reshape(128, 256))
    d["wb1"] = np.ascontiguousarray(
        np.asarray(inputs["L1_attn_Wb"], f32).reshape(16, 128, 16).transpose(1, 0, 2).reshape(128, 256))
    d["ba1"] = np.ascontiguousarray(np.asarray(inputs["L1_attn_ba"], f32).reshape(16, 1))
    d["bb1"] = np.ascontiguousarray(np.asarray(inputs["L1_attn_bb"], f32).reshape(16, 1))
    d["wo1"] = relay_lhsT(np.asarray(inputs["L1_attn_Wo"], f32), NDT, 128)
    for nm, key in (("qcw", "L1_attn_qcw"), ("kcw", "L1_attn_kcw"), ("vcw", "L1_attn_vcw")):
        w = np.asarray(inputs[key], f32)
        d[nm] = np.ascontiguousarray(w.reshape(16, 128, 4).transpose(1, 0, 2).reshape(128, 64))
    for nm, key in (("qcb", "L1_attn_qcb"), ("kcb", "L1_attn_kcb"), ("vcb", "L1_attn_vcb")):
        d[nm] = np.ascontiguousarray(np.asarray(inputs[key], f32).reshape(16, 128).T)
    d["lnw_rep"] = np.ascontiguousarray(np.tile(np.asarray(inputs["L1_attn_ln_w"], f32), 16).reshape(1, DM))
    d["lnb_rep"] = np.ascontiguousarray(np.tile(np.asarray(inputs["L1_attn_ln_b"], f32), 16).reshape(1, DM))
    def _permute_up(w):
        return np.ascontiguousarray(
            w.reshape(16, 128, 16, 128, 4).transpose(2, 1, 0, 4, 3).reshape(16, 128, H0))
    d["wgate_l"] = _permute_up(np.asarray(inputs["L1_ffn_w_gate"], f32))
    d["wup_l"] = _permute_up(np.asarray(inputs["L1_ffn_w_up"], f32))
    d["wdown_l"] = np.ascontiguousarray(
        np.asarray(inputs["L1_ffn_w_down"], f32).reshape(16, 128, 4, DM)
        .transpose(0, 2, 1, 3).reshape(H0, DM))
    return d


def kernel(**inputs):
    x = np.asarray(inputs["L0_attn_hidden_states"], np.float32)
    shared = host_layout(inputs)
    nc = build_program()
    in_maps = []
    for c in range(8):
        m = dict(shared)
        xs = np.ascontiguousarray(np.transpose(x[2 * c:2 * c + 2], (0, 2, 1)))
        m["xin"] = xs
        in_maps.append(m)
    res = run_bass_kernel_spmd(nc, in_maps, core_ids=list(range(8)))
    outs = []
    for c in range(8):
        o = np.asarray(res.results[c]["out"])
        outs.append(np.transpose(o, (0, 2, 1)))
    return np.concatenate(outs, axis=0).astype(np.float32)


if __name__ == "__main__":
    pass
